# version 02
# baseline (speedup 1.0000x reference)
import numpy as np
import ml_dtypes
import concourse.bass as bass
import concourse.mybir as mybir
from concourse.bass_utils import run_bass_kernel_spmd
from contextlib import ExitStack

F32 = mybir.dt.float32
BF16 = mybir.dt.bfloat16
I32 = mybir.dt.int32
AF = mybir.ActivationFunctionType
ALU = mybir.AluOpType
AX = mybir.AxisListType
BF = ml_dtypes.bfloat16

NCORES = 8
T = 4096
D = 1024
OWNT = 2048
NEG = -1e30

OWN = [[0, 3, 4, 7], [1, 2, 5, 6]]


def own_rows(par):
    return np.concatenate([np.arange(c * 512, (c + 1) * 512) for c in OWN[par]])


KK_TAB = [5, 6, 7, 8, 13, 14, 15, 16, 21, 22, 23, 24, 29, 30, 31, 32]


def KK(j):
    return KK_TAB[j]


class Sem:
    def __init__(self, h):
        self.h = h
        self.n = 0


class Rec:
    ENGINES = ("sync", "gpsimd", "tensor", "vector", "scalar")

    def __init__(self, nc):
        self.nc = nc
        self.ops = {e: [] for e in self.ENGINES}

    def on(self, eng, f):
        self.ops[eng].append(f)

    def wait(self, eng, sem, n=None):
        v = sem.n if n is None else n
        if v > 0:
            self.on(eng, lambda e, s=sem.h, v=v: e.wait_ge(s, v))

    def op(self, eng, meth, *a, inc=None, inck=1, **kw):
        if inc is not None:
            inc.n += inck

        import traceback
        site = traceback.extract_stack(limit=3)[0]

        def f(e, meth=meth, a=a, kw=kw, inc=inc, inck=inck, site=site):
            try:
                ins = getattr(e, meth)(*a, **kw)
            except Exception as ex:
                import traceback as _tb
                _tb.print_exc()
                raise RuntimeError(
                    f"op {meth} recorded at {site.filename}:{site.lineno} failed: {ex}"
                ) from ex
            if inc is not None:
                ins.then_inc(inc.h, inck)
        self.on(eng, f)

    def dma(self, eng, out, in_, inc, **kw):
        self.op(eng, "dma_start", inc=inc, inck=16, out=out, in_=in_, **kw)

    def flush(self, block):
        for eng in self.ENGINES:
            if self.ops[eng]:
                ops = self.ops[eng]
                getattr(block, eng)(lambda e, ops=ops: [f(e) for f in ops])
        self.ops = {e: [] for e in self.ENGINES}


class LnTiles:

    def __init__(self, nc, ctx, tag=""):
        sb = lambda n, s, dt=F32: ctx.enter_context(nc.sbuf_tensor(n + tag, s, dt))
        self.stats = sb("ln_stats", [128, 12])
        self.mv = sb("ln_mv", [128, 2])
        self.negm = sb("ln_negm", [128, 1])
        self.va = sb("ln_va", [128, 1])
        self.rstd = sb("ln_rstd", [128, 1])


def emit_ln_stats(R, lt, src, width, VE, SE, eps=1e-5):
    h = width // 2
    R.op("vector", "bn_stats", lt.stats[:, 0:6], src[:, 0:h])
    R.op("vector", "bn_stats", lt.stats[:, 6:12], src[:, h:width], inc=VE)
    R.wait("vector", VE)
    R.op("vector", "bn_aggr", lt.mv[:], lt.stats[:], inc=VE)
    R.wait("vector", VE)
    R.op("vector", "tensor_scalar", lt.negm[:], lt.mv[:, 0:1], -1.0, None, ALU.mult)
    R.op("vector", "tensor_scalar", lt.va[:], lt.mv[:, 1:2], eps, None, ALU.add, inc=VE)
    R.wait("scalar", VE)
    R.op("scalar", "activation", lt.rstd[:], lt.va[:], AF.Sqrt, inc=SE)
    R.wait("vector", SE)
    R.op("vector", "reciprocal", lt.rstd[:], lt.rstd[:], inc=VE)
    R.wait("vector", VE)


def build_launch_A(debug=False, fc=None):
    pre = "" if fc is None else "a_"
    if fc is None:
        nc = bass.Bass("TRN2", target_bir_lowering=False)
        R = Rec(nc)
        xfull = nc.dram_tensor("xfull", [T, D], F32, kind="ExternalInput").ap()
        xq = nc.dram_tensor("xq", [OWNT, D], F32, kind="ExternalInput").ap()
        wq = nc.dram_tensor("wq", [D, D], F32, kind="ExternalInput").ap()
        wk = nc.dram_tensor("wk", [D, D], F32, kind="ExternalInput").ap()
        wv = nc.dram_tensor("wv", [D, D], F32, kind="ExternalInput").ap()
        wo = nc.dram_tensor("wo", [D, D], F32, kind="ExternalInput").ap()
        nullk_p = nc.dram_tensor("nullk_p", [128, 8], F32, kind="ExternalInput").ap()
        nullv_r = nc.dram_tensor("nullv_r", [1, D], F32, kind="ExternalInput").ap()
        lsc_r = nc.dram_tensor("lsc_r", [1, 16], F32, kind="ExternalInput").ap()
        ident_in = nc.dram_tensor("ident_in", [128, 128], F32, kind="ExternalInput").ap()
        mtail = nc.dram_tensor("mtail", [16, 5, 128, 128], F32, kind="ExternalInput").ap()
        x1out = nc.dram_tensor("x1", [OWNT, D], F32, kind="ExternalOutput").ap()
    else:
        nc, R = fc["nc"], fc["R"]
        io = fc["io"]
        xfull, xq = io["a_xfull"], io["a_xq"]
        wq, wk, wv, wo = io["a_wq"], io["a_wk"], io["a_wv"], io["a_wo"]
        nullk_p, nullv_r, lsc_r = io["a_nullk_p"], io["a_nullv_r"], io["a_lsc_r"]
        ident_in, mtail = io["ident_in"], io["a_mtail"]
        x1out = fc["x1_d"].ap()

    kT_d = nc.dram_tensor(pre + "kT_d", [128, 8 * 4096], BF16)
    v_d = nc.dram_tensor(pre + "v_d", [128, 8 * 4096], BF16)
    if debug:
        dbg_k = nc.dram_tensor("dbg_k", [128, 8 * 4096], BF16, kind="ExternalOutput").ap()
        dbg_v = nc.dram_tensor("dbg_v", [128, 8 * 4096], BF16, kind="ExternalOutput").ap()
        dbg_q = nc.dram_tensor("dbg_q", [128, 8 * 2048], BF16, kind="ExternalOutput").ap()
        dbg_yn = nc.dram_tensor("dbg_yn", [4, 128, 8 * 512], BF16, kind="ExternalOutput").ap()
        dbg_xqt = nc.dram_tensor("dbg_xqt", [4, 128, 8 * 512], BF16, kind="ExternalOutput").ap()

    with ExitStack() as CTX:
        sb = lambda n, s, dt: CTX.enter_context(nc.sbuf_tensor(pre + n, s, dt))
        ps = lambda stk, n, s, dt=F32: stk.enter_context(nc.psum_tensor(pre + n, s, dt))
        if fc is None:
            sem = lambda n: Sem(CTX.enter_context(nc.semaphore(n)))
            LD = sem("LD"); GD = sem("GD"); SP = sem("SP")
            TE = sem("TE"); VE = sem("VE"); SE = sem("SE"); OT = sem("OT")
        else:
            sm = fc["sems"]
            LD, GD, SP = sm["LD"], sm["GD"], sm["SP"]
            TE, VE, SE, OT = sm["TE"], sm["VE"], sm["SE"], sm["OT"]

        xq_sb = sb("xq_sb", [128, 16 * 1024], F32)
        qT = sb("qT", [128, 8 * 2048], BF16)
        ident = sb("ident", [128, 128], BF16)
        ident_f = sb("ident_f", [128, 128], F32)
        ones_f = sb("ones_f", [1, 128], F32)
        nullk_sb = sb("nullk_sb", [128, 8], BF16)
        nullv_sb = sb("nullv_sb", [1, D], BF16)
        lsc_sb = sb("lsc_sb", [1, 16], F32)
        scale_b = sb("scale_b", [128, 16], F32)
        lt = LnTiles(nc, CTX, tag=pre)

        Tpack = ps(CTX, "Tpack", [128, 1024], BF16)
        smalls = ps(CTX, "smalls", [128, 512], F32)
        bc = ps(CTX, "bc", [128, 512], F32)
        yps = ps(CTX, "yps", [128, 512], F32)

        block = fc["block"] if fc else CTX.enter_context(nc.Block())

        R.dma("gpsimd", ident[:], ident_in, GD)
        R.dma("gpsimd", nullk_sb[:], nullk_p, GD)
        R.dma("gpsimd", nullv_sb[:], nullv_r, GD)
        R.dma("sync", ident_f[:], ident_in, LD)
        R.dma("sync", lsc_sb[:], lsc_r, LD)
        R.dma("sync", xq_sb[:].rearrange("p (t n) -> p t n", n=1024),
              xq.rearrange("(t p) n -> p t n", p=128), LD)
        R.op("vector", "memset", ones_f[:], 1.0, inc=VE)
        R.wait("scalar", LD)
        R.op("scalar", "activation", lsc_sb[:], lsc_sb[:], AF.Exp, inc=SE)
        R.wait("vector", SE)
        R.op("vector", "tensor_scalar", lsc_sb[:], lsc_sb[:], 0.125, None, ALU.mult, inc=VE)
        R.wait("tensor", VE)
        R.op("tensor", "matmul", bc[:, 0:16], ones_f[:], lsc_sb[:], start=True, stop=True, inc=TE)
        R.wait("vector", TE)
        R.op("vector", "tensor_copy", scale_b[:], bc[:, 0:16], inc=VE)

        with ExitStack() as PH1:
            sb1 = lambda n, s, dt: PH1.enter_context(nc.sbuf_tensor(pre + n, s, dt))
            xb = [sb1(f"xb{i}", [128, 1024], F32) for i in range(2)]
            xhat = [sb1(f"xhat{i}", [128, 1024], BF16) for i in range(2)]
            xTc = [sb1(f"xTc{i}", [128, 8 * 512], BF16) for i in range(2)]
            wk_sb = sb1("wk_sb", [128, 8 * 1024], BF16)
            wv_sb = sb1("wv_sb", [128, 8 * 1024], BF16)
            wq_sb = sb1("wq_sb", [128, 8 * 1024], BF16)
            kspill = sb1("kspill", [128, 8 * 512], BF16)
            vbuf = [sb1(f"vbuf{i}", [128, 1024], BF16) for i in range(2)]
            pps = ps(PH1, "pps", [128, 512], F32)
            pp2 = ps(PH1, "pp2", [128, 512], F32)

            for w_ap, w_sb in ((wk, wk_sb), (wv, wv_sb), (wq, wq_sb)):
                R.dma("gpsimd", w_sb[:].rearrange("p (c n) -> p c n", n=1024),
                      w_ap.rearrange("(c p) n -> p c n", p=128), GD)
            wready = GD.n

            def ln_transpose_tile(src_f32, xhat_t, xT_t, t4):
                emit_ln_stats(R, lt, src_f32, 1024, VE, SE)
                R.wait("vector", TE)
                R.op("vector", "tensor_scalar", xhat_t[:], src_f32, lt.negm[:],
                     lt.rstd[:], ALU.add, op1=ALU.mult, inc=VE)
                R.wait("tensor", VE)
                xT3 = xT_t[:].rearrange("p (c n) -> p c n", n=512)
                for half in range(2):
                    for c4 in range(4):
                        c = half * 4 + c4
                        R.op("tensor", "transpose", Tpack[:, c4 * 128:(c4 + 1) * 128],
                             xhat_t[:, c * 128:(c + 1) * 128], ident[:],
                             inc=(TE if c4 == 3 else None))
                    R.wait("vector", TE)
                    R.op("vector", "tensor_copy",
                         xT3[:, half * 4:half * 4 + 4, t4 * 128:(t4 + 1) * 128],
                         Tpack[:, 0:512].rearrange("p (k n) -> p k n", n=128), inc=VE)
                    R.wait("tensor", VE)

            R.wait("tensor", GD, wready)
            v3 = v_d.ap().rearrange("p (c n) -> p c n", n=4096)
            for tc in range(8):
                for t4 in range(4):
                    tt = tc * 4 + t4
                    buf = xb[tt % 2]
                    R.wait("sync", VE)
                    R.dma("sync", buf[:], xfull[tt * 128:(tt + 1) * 128, :], LD)
                    R.wait("vector", LD)
                    ln_transpose_tile(buf[:], xhat[tt % 2], xTc[tc % 2], t4)
                for ic in range(8):
                    for kc in range(8):
                        R.op("tensor", "matmul", pps[:, 0:512],
                             wk_sb[:, kc * 1024 + ic * 128: kc * 1024 + ic * 128 + 128],
                             xTc[tc % 2][:, kc * 512:(kc + 1) * 512],
                             start=(kc == 0), stop=(kc == 7),
                             inc=(TE if kc == 7 else None))
                    R.wait("vector", TE)
                    R.wait("vector", SP)
                    R.op("vector", "tensor_copy", kspill[:, ic * 512:(ic + 1) * 512],
                         pps[:, 0:512], inc=VE)
                    R.wait("tensor", VE)
                R.wait("gpsimd", VE)
                for ic in range(8):
                    R.dma("gpsimd",
                          kT_d.ap()[:, ic * 4096 + tc * 512: ic * 4096 + (tc + 1) * 512],
                          kspill[:, ic * 512:(ic + 1) * 512], SP)
                for t4 in range(4):
                    tt = tc * 4 + t4
                    for half in range(2):
                        bank = pps if half == 0 else pp2
                        for kc in range(8):
                            R.op("tensor", "matmul", bank[:, 0:512],
                                 xTc[tc % 2][:, kc * 512 + t4 * 128: kc * 512 + t4 * 128 + 128],
                                 wv_sb[:, kc * 1024 + half * 512: kc * 1024 + half * 512 + 512],
                                 start=(kc == 0), stop=(kc == 7),
                                 inc=(TE if kc == 7 else None))
                    R.wait("vector", TE)
                    R.wait("vector", SP)
                    R.op("vector", "tensor_copy", vbuf[tt % 2][:, 0:512], pps[:, 0:512])
                    R.op("vector", "tensor_copy", vbuf[tt % 2][:, 512:1024], pp2[:, 0:512],
                         inc=VE)
                    R.wait("tensor", VE)
                    R.wait("gpsimd", VE)
                    R.dma("gpsimd", v3[:, :, tt * 128:(tt + 1) * 128],
                          vbuf[tt % 2][:].rearrange("p (c n) -> p c n", n=128), SP)

            R.wait("vector", LD)
            for qc in range(4):
                for t4 in range(4):
                    tt = qc * 4 + t4
                    R.wait("vector", VE)
                    ln_transpose_tile(xq_sb[:, tt * 1024:(tt + 1) * 1024],
                                      xhat[tt % 2], xTc[qc % 2], t4)
                if debug:
                    R.wait("gpsimd", VE)
                    R.dma("gpsimd", dbg_xqt[qc], xTc[qc % 2][:], SP)
                    R.wait("tensor", SP)
                for ic in range(8):
                    for kc in range(8):
                        R.op("tensor", "matmul", pps[:, 0:512],
                             wq_sb[:, kc * 1024 + ic * 128: kc * 1024 + ic * 128 + 128],
                             xTc[qc % 2][:, kc * 512:(kc + 1) * 512],
                             start=(kc == 0), stop=(kc == 7),
                             inc=(TE if kc == 7 else None))
                    R.wait("vector", TE)
                    R.op("vector", "tensor_copy",
                         qT[:, ic * 2048 + qc * 512: ic * 2048 + (qc + 1) * 512],
                         pps[:, 0:512], inc=VE)
                    R.wait("tensor", VE)
            R.flush(block)

        with ExitStack() as PH2:
            sb2 = lambda n, s, dt: PH2.enter_context(nc.sbuf_tensor(pre + n, s, dt))
            Kg = sb2("Kg", [128, 2048], BF16)
            Vg = sb2("Vg", [128, 2048], BF16)
            P = sb2("P", [128, 2048], BF16)
            PT = sb2("PT", [128, 16 * 512], BF16)
            ynT = sb2("ynT", [128, 8 * 512], BF16)
            y_acc = sb2("y_acc", [128, 512], F32)
            tmp_y = sb2("tmp_y", [128, 512], F32)
            rb_sb = sb2("rb_sb", [128, 512], F32)
            row_a = sb2("row_a", [1, 512], F32)
            row_b = sb2("row_b", [1, 512], F32)
            mt_sb = sb2("mt_sb", [128, 4 * 5 * 128], BF16)
            rg = sb2("rg", [128, 4], F32)
            mg = sb2("mg", [128, 4], F32)
            r_acc = sb2("r_acc", [128, 4], F32)
            m_acc = sb2("m_acc", [128, 4], F32)
            sm1 = sb2("sm1", [128, 4], F32)
            sm2 = sb2("sm2", [128, 4], F32)
            pnull = sb2("pnull", [128, 4], BF16)
            pnullT = sb2("pnullT", [1, 512], BF16)
            wo_sb = sb2("wo_sb", [128, 8 * 1024], BF16)
            attn_f32 = sb2("attn_f32", [128, 1024], F32)
            x1b = sb2("x1b", [128, 1024], F32)
            Sps = ps(PH2, "Sps", [128, 2048], F32)
            PT3 = PT[:].rearrange("p (k n) -> p k n", n=512)

            R.wait("gpsimd", TE)
            R.wait("gpsimd", VE)
            R.wait("sync", TE)
            R.wait("sync", VE)
            R.dma("gpsimd", wo_sb[:].rearrange("p (c n) -> p c n", n=1024),
                  wo.rearrange("(c p) n -> p c n", p=128), GD)

            for qb in range(4):
                KKs = [KK(4 * qb + qs) for qs in range(4)]
                Wmax = 128 * KKs[3]
                ngroups = 1 if Wmax <= 2048 else 2
                R.wait("gpsimd", VE)
                R.dma("gpsimd", mt_sb[:].rearrange("p (j t n) -> p j t n", t=5, n=128),
                      mtail[4 * qb:4 * qb + 4, :, :, :].rearrange("j t p n -> p j t n"), GD)
                R.wait("vector", GD)
                for ic in range(8):
                    for hh in range(2):
                        h = 2 * ic + hh
                        rows = slice(64 * hh, 64 * hh + 64)
                        scol = scale_b[:, h:h + 1]
                        for g in range(ngroups):
                            g0 = g * 2048
                            gW = min(Wmax, g0 + 2048) - g0
                            if hh == 0 or ngroups > 1:
                                R.wait("sync", TE)
                                R.wait("sync", SE)
                                R.dma("sync", Kg[:, 0:gW],
                                      kT_d.ap()[:, ic * 4096 + g0: ic * 4096 + g0 + gW], LD)
                                R.dma("sync", Vg[:, 0:gW],
                                      v_d.ap()[:, ic * 4096 + g0: ic * 4096 + g0 + gW], LD)
                            R.wait("tensor", LD)
                            for qs in range(4):
                                j = 4 * qb + qs
                                Wj = 128 * KKs[qs]
                                lo, hi = g0, min(Wj, g0 + 2048)
                                if hi <= lo:
                                    continue
                                w = hi - lo
                                lhsT = qT[rows, ic * 2048 + j * 128: ic * 2048 + (j + 1) * 128]
                                R.wait("tensor", SE)
                                R.wait("tensor", VE)
                                mm_list = []
                                c = 0
                                while c < w:
                                    n = min(512, w - c)
                                    mm_list.append((c, n))
                                    c += n
                                for i, (c, n) in enumerate(mm_list):
                                    last = (i == len(mm_list) - 1) and g != 0
                                    R.op("tensor", "matmul", Sps[:, c:c + n], lhsT,
                                         Kg[rows, c:c + n], start=True, stop=True,
                                         inc=(TE if last else None))
                                if g == 0:
                                    R.op("tensor", "matmul", smalls[:, qs:qs + 1], lhsT,
                                         nullk_sb[rows, ic:ic + 1], start=True, stop=True,
                                         inc=TE)
                                R.wait("vector", TE)
                                tail0 = (KKs[qs] - 5) * 128
                                if lo <= tail0 and tail0 + 640 <= hi:
                                    R.op("vector", "tensor_tensor",
                                         Sps[:, tail0 - lo: tail0 - lo + 640],
                                         Sps[:, tail0 - lo: tail0 - lo + 640],
                                         mt_sb[:, qs * 640: qs * 640 + 640],
                                         ALU.add)
                                R.op("vector", "reduce_max", mg[:, qs:qs + 1], Sps[:, 0:w],
                                     axis=AX.X, inc=VE)
                                R.wait("vector", VE)
                                if g == 0:
                                    R.op("vector", "tensor_max", mg[:, qs:qs + 1],
                                         mg[:, qs:qs + 1], smalls[:, qs:qs + 1], inc=VE)
                                    R.wait("vector", VE)
                                R.op("vector", "tensor_scalar", sm1[:, qs:qs + 1],
                                     mg[:, qs:qs + 1], scol, -1.0, ALU.mult, op1=ALU.mult,
                                     inc=VE)
                                R.wait("scalar", VE)
                                R.wait("scalar", TE)
                                R.op("scalar", "activation", P[:, 0:w], Sps[:, 0:w], AF.Exp,
                                     bias=sm1[:, qs:qs + 1], scale=scol,
                                     accum_out=rg[:, qs:qs + 1], inc=SE)
                                if g == 0:
                                    R.op("scalar", "activation", pnull[:, qs:qs + 1],
                                         smalls[:, qs:qs + 1], AF.Exp,
                                         bias=sm1[:, qs:qs + 1], scale=scol, inc=SE)
                                    R.wait("vector", SE)
                                    R.op("vector", "tensor_add", rg[:, qs:qs + 1],
                                         rg[:, qs:qs + 1], pnull[:, qs:qs + 1], inc=VE)
                                R.wait("tensor", SE)
                                nk = (w + 127) // 128
                                for kb in range(0, nk, 8):
                                    ke = min(kb + 8, nk)
                                    R.wait("tensor", VE)
                                    for kt in range(kb, ke):
                                        R.op("tensor", "transpose",
                                             Tpack[:, (kt - kb) * 128:(kt - kb + 1) * 128],
                                             P[:, kt * 128: min((kt + 1) * 128, w)], ident[:],
                                             inc=(TE if kt == ke - 1 else None))
                                    R.wait("vector", TE)
                                    R.op("vector", "tensor_copy",
                                         PT3[:, kb:ke, qs * 128:(qs + 1) * 128],
                                         Tpack[:, 0:(ke - kb) * 128].rearrange(
                                             "p (k n) -> p k n", n=128), inc=VE)
                            if g == 0:
                                R.wait("tensor", VE)
                                for qs in range(4):
                                    R.op("tensor", "transpose",
                                         Tpack[0:1, qs * 128:(qs + 1) * 128],
                                         pnull[:, qs:qs + 1], ident[:],
                                         inc=(TE if qs == 3 else None))
                                R.wait("vector", TE)
                                R.op("vector", "tensor_copy", pnullT[:],
                                     Tpack[0:1, 0:512], inc=VE)
                            R.wait("tensor", VE)
                            nkg = gW // 128
                            av = []
                            for ktL in range(nkg):
                                ktA = g0 // 128 + ktL
                                qsmin = next((qs for qs in range(4) if KKs[qs] > ktA), None)
                                if qsmin is None:
                                    continue
                                av.append((ktL, qsmin * 128))
                            for i, (ktL, qoff) in enumerate(av):
                                last = (i == len(av) - 1) and g != 0
                                R.op("tensor", "matmul", yps[rows, qoff:512],
                                     Vg[:, ktL * 128 + hh * 64: ktL * 128 + hh * 64 + 64],
                                     PT3[:, ktL, qoff:512],
                                     start=(i == 0), stop=last,
                                     inc=(TE if last else None))
                            if g == 0:
                                R.op("tensor", "matmul", yps[rows, 0:512],
                                     nullv_sb[0:1, h * 64: h * 64 + 64], pnullT[:],
                                     start=False, stop=True, inc=TE)
                            R.wait("vector", TE)
                            R.wait("vector", SE)
                            if g == 0:
                                R.op("vector", "tensor_copy", y_acc[rows, :], yps[rows, :])
                                R.op("vector", "tensor_copy", r_acc[:], rg[:])
                                R.op("vector", "tensor_copy", m_acc[:], mg[:], inc=VE)
                                R.wait("tensor", VE)
                            else:
                                R.op("vector", "tensor_max", sm2[:], m_acc[:], mg[:], inc=VE)
                                R.wait("vector", VE)
                                R.op("vector", "tensor_sub", sm1[:], m_acc[:], sm2[:])
                                R.op("vector", "tensor_sub", mg[:], mg[:], sm2[:])
                                R.op("vector", "tensor_copy", m_acc[:], sm2[:])
                                R.op("vector", "tensor_scalar", sm1[:], sm1[:], scol, None,
                                     ALU.mult)
                                R.op("vector", "tensor_scalar", mg[:], mg[:], scol, None,
                                     ALU.mult, inc=VE)
                                R.wait("scalar", VE)
                                R.op("scalar", "activation", sm1[:], sm1[:], AF.Exp)
                                R.op("scalar", "activation", mg[:], mg[:], AF.Exp, inc=SE)
                                R.wait("vector", SE)
                                R.op("vector", "tensor_mul", r_acc[:], r_acc[:], sm1[:])
                                R.op("vector", "tensor_mul", rg[:], rg[:], mg[:], inc=VE)
                                R.wait("vector", VE)
                                R.op("vector", "tensor_add", r_acc[:], r_acc[:], rg[:],
                                     inc=VE)
                                R.wait("tensor", VE)
                                for rowt, srcontainter in ((row_a, sm1), (row_b, mg)):
                                    for qs in range(4):
                                        R.op("tensor", "transpose",
                                             smalls[0:1, qs * 128:(qs + 1) * 128],
                                             srcontainter[:, qs:qs + 1], ident_f[:],
                                             inc=(TE if qs == 3 else None))
                                    R.wait("vector", TE)
                                    R.op("vector", "tensor_copy",
                                         rowt[:], smalls[0:1, 0:512], inc=VE)
                                    R.wait("tensor", VE)
                                R.op("tensor", "matmul", bc[:, 0:512],
                                     ones_f[:], row_a[:],
                                     start=True, stop=True, inc=TE)
                                R.wait("vector", TE)
                                R.op("vector", "tensor_copy", rb_sb[:],
                                     bc[:, 0:512], inc=VE)
                                R.wait("tensor", VE)
                                R.op("tensor", "matmul", bc[:, 0:512],
                                     ones_f[:], row_b[:],
                                     start=True, stop=True, inc=TE)
                                R.wait("vector", TE)
                                R.op("vector", "tensor_copy", tmp_y[:], bc[:, 0:512])
                                R.op("vector", "tensor_mul", y_acc[rows, :], y_acc[rows, :],
                                     rb_sb[rows, :], inc=VE)
                                R.wait("vector", VE)
                                R.op("vector", "tensor_mul", tmp_y[rows, :], yps[rows, :],
                                     tmp_y[rows, :], inc=VE)
                                R.wait("vector", VE)
                                R.op("vector", "tensor_add", y_acc[rows, :], y_acc[rows, :],
                                     tmp_y[rows, :], inc=VE)
                                R.wait("tensor", VE)
                        R.op("vector", "reciprocal", r_acc[:], r_acc[:], inc=VE)
                        R.wait("tensor", VE)
                        for qs in range(4):
                            R.op("tensor", "transpose", smalls[0:1, qs * 128:(qs + 1) * 128],
                                 r_acc[:, qs:qs + 1], ident_f[:],
                                 inc=(TE if qs == 3 else None))
                        R.wait("vector", TE)
                        R.op("vector", "tensor_copy", row_a[:], smalls[0:1, 0:512],
                             inc=VE)
                        R.wait("tensor", VE)
                        R.op("tensor", "matmul", bc[:, 0:512], ones_f[:],
                             row_a[:], start=True, stop=True, inc=TE)
                        R.wait("vector", TE)
                        R.op("vector", "tensor_copy", rb_sb[:], bc[:, 0:512], inc=VE)
                        R.wait("vector", VE)
                        R.op("vector", "tensor_mul", ynT[rows, ic * 512:(ic + 1) * 512],
                             y_acc[rows, :], rb_sb[rows, :], inc=VE)
                        R.wait("tensor", VE)

                if debug:
                    R.wait("gpsimd", VE)
                    R.dma("gpsimd", dbg_yn[qb], ynT[:], OT)
                R.wait("tensor", GD)
                for ts in range(4):
                    tt = qb * 4 + ts
                    R.wait("scalar", VE)
                    for half in range(2):
                        for ic in range(8):
                            R.op("tensor", "matmul", bc[:, 0:512],
                                 ynT[:, ic * 512 + ts * 128: ic * 512 + ts * 128 + 128],
                                 wo_sb[:, ic * 1024 + half * 512: ic * 1024 + half * 512 + 512],
                                 start=(ic == 0), stop=(ic == 7),
                                 inc=(TE if ic == 7 else None))
                        R.wait("scalar", TE)
                        R.op("scalar", "copy", attn_f32[:, half * 512:(half + 1) * 512],
                             bc[:, 0:512], inc=SE)
                        R.wait("tensor", SE)
                    R.wait("vector", SE)
                    emit_ln_stats(R, lt, attn_f32[:], 1024, VE, SE)
                    R.op("vector", "tensor_scalar", attn_f32[:], attn_f32[:], lt.negm[:],
                         lt.rstd[:], ALU.add, op1=ALU.mult, inc=VE)
                    R.wait("vector", VE)
                    R.op("vector", "tensor_add", x1b[:], attn_f32[:],
                         xq_sb[:, tt * 1024:(tt + 1) * 1024], inc=VE)
                    R.wait("gpsimd", VE)
                    R.dma("gpsimd", x1out[tt * 128:(tt + 1) * 128, :], x1b[:], OT)
                    R.wait("vector", OT)

            if debug:
                R.dma("gpsimd", dbg_k, kT_d.ap(), OT)
                R.dma("gpsimd", dbg_v, v_d.ap(), OT)
                R.dma("gpsimd", dbg_q, qT[:], OT)
            R.wait("gpsimd", OT)
            R.wait("gpsimd", SP)
            R.flush(block)
    return nc


def make_mtail(par):
    tri = np.triu(np.full((128, 128), NEG, np.float32), 1)
    zero = np.zeros((128, 128), np.float32)
    allm = np.full((128, 128), NEG, np.float32)
    out = np.empty((16, 5, 128, 128), np.float32)
    for j in range(16):
        A = OWN[par][j // 4]
        V = 4 * A + (j % 4) + 1
        for s in range(5):
            kt = KK(j) - 5 + s
            out[j, s] = zero if kt < V - 1 else (tri if kt == V - 1 else allm)
    return out


def stage_A(inputs):
    x = np.ascontiguousarray(inputs["L0_attn_x"], np.float32)
    nk = np.ascontiguousarray(inputs["L0_attn_null_k"], np.float32).reshape(16, 64)
    nv = np.ascontiguousarray(inputs["L0_attn_null_v"], np.float32).reshape(16, 64)
    nullk_p = np.ascontiguousarray(
        nk.reshape(8, 2, 64).transpose(1, 2, 0).reshape(128, 8))
    nullv_r = np.ascontiguousarray(nv.reshape(1, 1024))
    lsc_r = np.ascontiguousarray(
        inputs["L0_attn_logit_scale"], np.float32).reshape(1, 16)
    ident = np.eye(128, dtype=np.float32)
    mt = [make_mtail(0), make_mtail(1)]
    maps = []
    for c in range(NCORES):
        b, par = c // 2, c % 2
        rows = own_rows(par)
        maps.append({
            "xfull": x[b],
            "xq": np.ascontiguousarray(x[b][rows]),
            "wq": np.ascontiguousarray(inputs["L0_attn_Wq"], np.float32),
            "wk": np.ascontiguousarray(inputs["L0_attn_Wk"], np.float32),
            "wv": np.ascontiguousarray(inputs["L0_attn_Wv"], np.float32),
            "wo": np.ascontiguousarray(inputs["L0_attn_Wo"], np.float32),
            "nullk_p": nullk_p, "nullv_r": nullv_r, "lsc_r": lsc_r,
            "ident_in": ident, "mtail": mt[par],
        })
    return maps


def gather_own(results, key):
    out = np.empty((4, T, D), np.float32)
    for c in range(NCORES):
        b, par = c // 2, c % 2
        out[b][own_rows(par)] = results[c][key]
    return out


_NC_CACHE = {}


def run_A(inputs, debug=False):
    key = ("A", debug)
    if key not in _NC_CACHE:
        _NC_CACHE[key] = build_launch_A(debug)
    maps = stage_A(inputs)
    res = run_bass_kernel_spmd(_NC_CACHE[key], maps, core_ids=list(range(NCORES)))
    if debug:
        return gather_own(res.results, "x1"), res.results
    return gather_own(res.results, "x1")



def build_launch_B(fc=None):
    pre = "" if fc is None else "b_"
    S127 = 1.0 / 127.0
    if fc is None:
        nc = bass.Bass("TRN2", target_bir_lowering=False)
        R = Rec(nc)
        x1i = nc.dram_tensor("x1i", [OWNT, D], F32, kind="ExternalInput").ap()
        wgT = nc.dram_tensor("wgT", [D, 3072], I32, kind="ExternalInput").ap()
        wuT = nc.dram_tensor("wuT", [D, 3072], I32, kind="ExternalInput").ap()
        wdT = nc.dram_tensor("wdT", [3072, D], I32, kind="ExternalInput").ap()
        ident_in = nc.dram_tensor("ident_in", [128, 128], F32, kind="ExternalInput").ap()
        x2out = nc.dram_tensor("x2", [OWNT, D], F32, kind="ExternalOutput").ap()
    else:
        nc, R = fc["nc"], fc["R"]
        io = fc["io"]
        x1i = fc["x1_d"].ap()
        wgT, wuT, wdT = io["b_wgT"], io["b_wuT"], io["b_wdT"]
        ident_in = io["ident_in"]
        x2out = fc["x2_d"].ap()

    with ExitStack() as CTX:
        sb = lambda n, s, dt: CTX.enter_context(nc.sbuf_tensor(pre + n, s, dt))
        ps = lambda n, s, dt=F32: CTX.enter_context(nc.psum_tensor(pre + n, s, dt))
        if fc is None:
            sem = lambda n: Sem(CTX.enter_context(nc.semaphore(n)))
            LD = sem("LD"); GD = sem("GD")
            TE = sem("TE"); VE = sem("VE"); SE = sem("SE"); OT = sem("OT")
        else:
            sm = fc["sems"]
            LD, GD = sm["LD"], sm["GD"]
            TE, VE, SE, OT = sm["TE"], sm["VE"], sm["SE"], sm["OT"]

        wg_sb = sb("wg_sb", [128, 8 * 3072], BF16)
        wu_sb = sb("wu_sb", [128, 8 * 3072], BF16)
        wd_sb = [sb(f"wd_sb{i}", [128, 24 * 128], BF16) for i in range(2)]
        xc = sb("xc", [128, 4 * 1024], F32)
        xhat = [sb(f"xhat{i}", [128, 1024], BF16) for i in range(2)]
        xTc = sb("xTc", [128, 8 * 512], BF16)
        Bt = sb("Bt", [128, 24 * 512], BF16)
        asb = sb("asb", [128, 512], BF16)
        ytb = sb("ytb", [128, 8 * 512], BF16)
        yf = sb("yf", [128, 1024], F32)
        x2b = sb("x2b", [128, 1024], F32)
        ident = sb("ident", [128, 128], BF16)
        lt = LnTiles(nc, CTX, tag=pre)
        pps = ps("pps", [128, 512])
        pp2 = ps("pp2", [128, 512])
        Tpack = ps("Tpack", [128, 1024], BF16)
        dps = ps("dps", [128, 512])

        block = fc["block"] if fc else CTX.enter_context(nc.Block())

        R.dma("gpsimd", ident[:], ident_in, GD)
        R.dma("gpsimd", wg_sb[:].rearrange("p (c n) -> p c n", n=3072),
              wgT.rearrange("(c p) n -> p c n", p=128), GD)
        R.dma("gpsimd", wu_sb[:].rearrange("p (c n) -> p c n", n=3072),
              wuT.rearrange("(c p) n -> p c n", p=128), GD)
        R.wait("tensor", GD)
        wdT3 = wdT.rearrange("(c p) n -> p c n", p=128)

        def ln_transpose_tile(src_f32, xhat_t, xT_t, t4):
            emit_ln_stats(R, lt, src_f32, 1024, VE, SE)
            R.wait("vector", TE)
            R.op("vector", "tensor_scalar", xhat_t[:], src_f32, lt.negm[:],
                 lt.rstd[:], ALU.add, op1=ALU.mult, inc=VE)
            R.wait("tensor", VE)
            xT3 = xT_t[:].rearrange("p (c n) -> p c n", n=512)
            for half in range(2):
                for c4 in range(4):
                    c = half * 4 + c4
                    R.op("tensor", "transpose", Tpack[:, c4 * 128:(c4 + 1) * 128],
                         xhat_t[:, c * 128:(c + 1) * 128], ident[:],
                         inc=(TE if c4 == 3 else None))
                R.wait("vector", TE)
                R.op("vector", "tensor_copy",
                     xT3[:, half * 4:half * 4 + 4, t4 * 128:(t4 + 1) * 128],
                     Tpack[:, 0:512].rearrange("p (k n) -> p k n", n=128), inc=VE)
                R.wait("tensor", VE)

        for qc in range(4):
            R.wait("sync", VE)
            R.wait("sync", OT)
            R.dma("sync", xc[:].rearrange("p (t n) -> p t n", n=1024),
                  x1i[qc * 512:(qc + 1) * 512, :].rearrange("(t p) n -> p t n", p=128), LD)
            R.wait("vector", LD)
            for t4 in range(4):
                ln_transpose_tile(xc[:, t4 * 1024:(t4 + 1) * 1024], xhat[t4 % 2], xTc, t4)
            for hc in range(24):
                R.wait("tensor", SE)
                R.wait("tensor", VE)
                for kc in range(8):
                    R.op("tensor", "matmul", pps[:, 0:512],
                         wg_sb[:, kc * 3072 + hc * 128: kc * 3072 + hc * 128 + 128],
                         xTc[:, kc * 512:(kc + 1) * 512],
                         start=(kc == 0), stop=(kc == 7))
                for kc in range(8):
                    R.op("tensor", "matmul", pp2[:, 0:512],
                         wu_sb[:, kc * 3072 + hc * 128: kc * 3072 + hc * 128 + 128],
                         xTc[:, kc * 512:(kc + 1) * 512],
                         start=(kc == 0), stop=(kc == 7),
                         inc=(TE if kc == 7 else None))
                R.wait("scalar", TE)
                R.op("scalar", "activation", asb[:], pps[:, 0:512], AF.Silu,
                     scale=S127, inc=SE)
                R.wait("vector", SE)
                R.op("vector", "tensor_tensor", Bt[:, hc * 512:(hc + 1) * 512],
                     pp2[:, 0:512], asb[:], ALU.mult, inc=VE)
            R.wait("tensor", VE)
            for oc in range(8):
                wdb = wd_sb[oc % 2]
                R.wait("gpsimd", TE)
                R.dma("gpsimd", wdb[:].rearrange("p (c n) -> p c n", n=128),
                      wdT3[:, :, oc * 128:(oc + 1) * 128], GD)
                R.wait("tensor", GD)
                R.wait("tensor", SE)
                for hc in range(24):
                    R.op("tensor", "matmul", dps[:, 0:512],
                         wdb[:, hc * 128:(hc + 1) * 128],
                         Bt[:, hc * 512:(hc + 1) * 512],
                         start=(hc == 0), stop=(hc == 23),
                         inc=(TE if hc == 23 else None))
                R.wait("scalar", TE)
                R.op("scalar", "mul", ytb[:, oc * 512:(oc + 1) * 512], dps[:, 0:512],
                     S127 * S127, inc=SE)
            for ts in range(4):
                R.wait("tensor", SE)
                R.wait("tensor", VE)
                for half in range(2):
                    for c4 in range(4):
                        oc = half * 4 + c4
                        R.op("tensor", "transpose", Tpack[:, c4 * 128:(c4 + 1) * 128],
                             ytb[:, oc * 512 + ts * 128: oc * 512 + ts * 128 + 128],
                             ident[:], inc=(TE if c4 == 3 else None))
                    R.wait("vector", TE)
                    R.op("vector", "tensor_copy",
                         yf[:, half * 512:(half + 1) * 512],
                         Tpack[:, 0:512], inc=VE)
                    R.wait("tensor", VE)
                emit_ln_stats(R, lt, yf[:], 1024, VE, SE)
                R.op("vector", "tensor_scalar", yf[:], yf[:], lt.negm[:],
                     lt.rstd[:], ALU.add, op1=ALU.mult, inc=VE)
                R.wait("vector", VE)
                R.wait("vector", OT)
                R.op("vector", "tensor_add", x2b[:], yf[:],
                     xc[:, ts * 1024:(ts + 1) * 1024], inc=VE)
                R.wait("gpsimd", VE)
                tt = qc * 4 + ts
                R.dma("gpsimd", x2out[tt * 128:(tt + 1) * 128, :], x2b[:], OT)
        R.wait("gpsimd", OT)
        R.flush(block)
    return nc


def stage_B(x1, inputs):
    wgT = np.ascontiguousarray(inputs["L0_ffn_w_gate"].T)
    wuT = np.ascontiguousarray(inputs["L0_ffn_w_up"].T)
    wdT = np.ascontiguousarray(inputs["L0_ffn_w_down"].T)
    ident = np.eye(128, dtype=np.float32)
    maps = []
    for c in range(NCORES):
        b, par = c // 2, c % 2
        maps.append({
            "x1i": np.ascontiguousarray(x1[b][own_rows(par)]),
            "wgT": wgT, "wuT": wuT, "wdT": wdT, "ident_in": ident,
        })
    return maps


def run_B(x1, inputs):
    if "B" not in _NC_CACHE:
        _NC_CACHE["B"] = build_launch_B()
    res = run_bass_kernel_spmd(_NC_CACHE["B"], stage_B(x1, inputs),
                               core_ids=list(range(NCORES)))
    return gather_own(res.results, "x2")



def build_launch_C(maxphase=3, fc=None):
    pre = "" if fc is None else "c_"
    if fc is None:
        nc = bass.Bass("TRN2", target_bir_lowering=False)
        R = Rec(nc)
        xfull = nc.dram_tensor("xfull", [T, D], F32, kind="ExternalInput").ap()
        xq = nc.dram_tensor("xq", [OWNT, D], F32, kind="ExternalInput").ap()
        wqkv = nc.dram_tensor("wqkv", [D, 3072], F32, kind="ExternalInput").ap()
        wf2 = nc.dram_tensor("wf2", [128, 128], F32, kind="ExternalInput").ap()
        bf_p = nc.dram_tensor("bf_p", [128, 1], F32, kind="ExternalInput").ap()
        bf_r = nc.dram_tensor("bf_r", [1, 128], F32, kind="ExternalInput").ap()
        wout = nc.dram_tensor("wout", [D, D], F32, kind="ExternalInput").ap()
        bout_r = nc.dram_tensor("bout_r", [1, D], F32, kind="ExternalInput").ap()
        ident_in = nc.dram_tensor("ident_in", [128, 128], F32, kind="ExternalInput").ap()
        x3out = nc.dram_tensor("x3", [OWNT, D], F32, kind="ExternalOutput").ap()
    else:
        nc, R = fc["nc"], fc["R"]
        io = fc["io"]
        xfull = fc["x2g"].ap()
        xq = fc["x2_d"].ap()
        wqkv, wf2, bf_p, bf_r = io["c_wqkv"], io["c_wf2"], io["c_bf_p"], io["c_bf_r"]
        wout, bout_r = io["c_wout"], io["c_bout_r"]
        ident_in = io["ident_in"]
        x3out = fc["x3_d"].ap()

    with ExitStack() as CTX:
        sb = lambda n, s, dt: CTX.enter_context(nc.sbuf_tensor(pre + n, s, dt))
        ps = lambda n, s, dt=F32: CTX.enter_context(nc.psum_tensor(pre + n, s, dt))
        if fc is None:
            sem = lambda n: Sem(CTX.enter_context(nc.semaphore(n)))
            LD = sem("LD"); GD = sem("GD")
            TE = sem("TE"); VE = sem("VE"); SE = sem("SE"); OT = sem("OT")
        else:
            sm = fc["sems"]
            LD, GD = sm["LD"], sm["GD"]
            TE, VE, SE, OT = sm["TE"], sm["VE"], sm["SE"], sm["OT"]

        wq_sb = sb("wq_sb", [128, 8 * 3072], BF16)
        wout_sb = sb("wout_sb", [128, 8 * 1024], BF16)
        wf_sb = sb("wf_sb", [128, 128], BF16)
        bfp_sb = sb("bfp_sb", [128, 1], F32)
        bfr_sb = sb("bfr_sb", [1, 128], BF16)
        bout_sb = sb("bout_sb", [1, D], BF16)
        ident = sb("ident", [128, 128], BF16)
        ones_b = sb("ones_b", [1, 128], BF16)
        ones_c = sb("ones_c", [128, 1], BF16)
        xb = [sb(f"xb{i}", [128, 1024], F32) for i in range(2)]
        xhat = [sb(f"xhat{i}", [128, 1024], BF16) for i in range(2)]
        xTc = sb("xTc", [128, 8 * 512], BF16)
        kTp = sb("kTp", [128, 8 * 512], BF16)
        Kp = sb("Kp", [128, 4 * 1024], BF16)
        Vp = sb("Vp", [128, 4 * 1024], BF16)
        kv_acc = sb("kv_acc", [128, 8 * 65], F32)
        kvb = sb("kvb", [128, 16 * 64], BF16)
        ksb = sb("ksb", [128, 16], BF16)
        qpT = sb("qpT", [128, 8 * 2048], BF16)
        nrm = sb("nrm", [128, 16], F32)
        obn = sb("obn", [128, 1024], BF16)
        oT = sb("oT", [128, 8 * 512], BF16)
        xqc = sb("xqc", [128, 4 * 1024], F32)
        af = sb("af", [128, 1024], F32)
        x3b = sb("x3b", [128, 1024], F32)
        lt = LnTiles(nc, CTX, tag=pre)
        pps = ps("pps", [128, 512])
        pp2 = ps("pp2", [128, 512])
        Tpack = ps("Tpack", [128, 1024], BF16)
        fps = ps("fps", [128, 128])
        kvps = ps("kvps", [64, 65])
        ops1 = ps("ops1", [128, 512])
        ops2 = ps("ops2", [128, 512])
        smalls = ps("smalls", [128, 512])

        block = fc["block"] if fc else CTX.enter_context(nc.Block())

        R.dma("gpsimd", ident[:], ident_in, GD)
        R.dma("gpsimd", wf_sb[:], wf2, GD)
        R.dma("gpsimd", bfr_sb[:], bf_r, GD)
        R.dma("gpsimd", bout_sb[:], bout_r, GD)
        R.dma("sync", bfp_sb[:], bf_p, LD)
        R.dma("gpsimd", wq_sb[:].rearrange("p (c n) -> p c n", n=3072),
              wqkv.rearrange("(c p) n -> p c n", p=128), GD)
        R.dma("gpsimd", wout_sb[:].rearrange("p (c n) -> p c n", n=1024),
              wout.rearrange("(c p) n -> p c n", p=128), GD)
        R.op("vector", "memset", ones_b[:], 1.0)
        R.op("vector", "memset", ones_c[:], 1.0)
        R.op("vector", "memset", kv_acc[:], 0.0, inc=VE)
        R.wait("tensor", GD)
        R.wait("tensor", LD)
        R.wait("tensor", VE)

        def ln_transpose_tile(src_f32, xhat_t, xT_t, t4):
            emit_ln_stats(R, lt, src_f32, 1024, VE, SE)
            R.wait("vector", TE)
            R.op("vector", "tensor_scalar", xhat_t[:], src_f32, lt.negm[:],
                 lt.rstd[:], ALU.add, op1=ALU.mult, inc=VE)
            R.wait("tensor", VE)
            xT3 = xT_t[:].rearrange("p (c n) -> p c n", n=512)
            for half in range(2):
                for c4 in range(4):
                    c = half * 4 + c4
                    R.op("tensor", "transpose", Tpack[:, c4 * 128:(c4 + 1) * 128],
                         xhat_t[:, c * 128:(c + 1) * 128], ident[:],
                         inc=(TE if c4 == 3 else None))
                R.wait("vector", TE)
                R.op("vector", "tensor_copy",
                     xT3[:, half * 4:half * 4 + 4, t4 * 128:(t4 + 1) * 128],
                     Tpack[:, 0:512].rearrange("p (k n) -> p k n", n=128), inc=VE)
                R.wait("tensor", VE)

        for tc in range(8):
            for t4 in range(4):
                tt = tc * 4 + t4
                R.wait("sync", VE)
                R.dma("sync", xb[tt % 2][:], xfull[tt * 128:(tt + 1) * 128, :], LD)
                R.wait("vector", LD)
                ln_transpose_tile(xb[tt % 2][:], xhat[tt % 2], xTc, t4)
            for ic in range(8):
                R.wait("tensor", SE)
                for kc in range(8):
                    R.op("tensor", "matmul", pps[:, 0:512],
                         wq_sb[:, kc * 3072 + 1024 + ic * 128: kc * 3072 + 1024 + ic * 128 + 128],
                         xTc[:, kc * 512:(kc + 1) * 512],
                         start=(kc == 0), stop=(kc == 7),
                         inc=(TE if kc == 7 else None))
                R.wait("vector", TE)
                R.op("vector", "tensor_copy", kTp[:, ic * 512:(ic + 1) * 512],
                     pps[:, 0:512], inc=VE)
                R.wait("tensor", VE)
            for t4 in range(4):
                for half in range(2):
                    bank = pps if half == 0 else pp2
                    for kc in range(8):
                        R.op("tensor", "matmul", bank[:, 0:512],
                             xTc[:, kc * 512 + t4 * 128: kc * 512 + t4 * 128 + 128],
                             wq_sb[:, kc * 3072 + 2048 + half * 512: kc * 3072 + 2048 + half * 512 + 512],
                             start=(kc == 0), stop=(kc == 7),
                             inc=(TE if kc == 7 else None))
                R.wait("vector", TE)
                R.op("vector", "tensor_copy", Vp[:, t4 * 1024: t4 * 1024 + 512], pps[:, 0:512])
                R.op("vector", "tensor_copy", Vp[:, t4 * 1024 + 512: t4 * 1024 + 1024],
                     pp2[:, 0:512], inc=VE)
                R.wait("tensor", VE)
            for ic in range(8):
                for t4 in range(4):
                    R.wait("tensor", SE)
                    R.op("tensor", "matmul", fps[:, 0:128],
                         kTp[:, ic * 512 + t4 * 128: ic * 512 + t4 * 128 + 128],
                         wf_sb[:, 0:128], start=True, stop=False)
                    R.op("tensor", "matmul", fps[:, 0:128],
                         ones_b[0:1, 0:128], bfr_sb[:],
                         start=False, stop=True, inc=TE)
                    R.wait("scalar", TE)
                    R.op("scalar", "activation",
                         Kp[:, t4 * 1024 + ic * 128: t4 * 1024 + ic * 128 + 128],
                         fps[:, 0:128], AF.Relu, inc=SE)
            R.wait("tensor", SE)
            for ic in range(8):
                for hh in range(2):
                    h = 2 * ic + hh
                    rows = slice(64 * hh, 64 * hh + 64)
                    for t4 in range(4):
                        R.op("tensor", "matmul", kvps[:, 0:64],
                             Kp[:, t4 * 1024 + h * 64: t4 * 1024 + h * 64 + 64],
                             Vp[:, t4 * 1024 + h * 64: t4 * 1024 + h * 64 + 64],
                             start=(t4 == 0), stop=False)
                        R.op("tensor", "matmul", kvps[:, 64:65],
                             Kp[:, t4 * 1024 + h * 64: t4 * 1024 + h * 64 + 64],
                             ones_c[:], start=False, stop=(t4 == 3),
                             inc=(TE if t4 == 3 else None))
                    R.wait("vector", TE)
                    R.op("vector", "tensor_tensor",
                         kv_acc[rows, ic * 65:(ic + 1) * 65],
                         kv_acc[rows, ic * 65:(ic + 1) * 65],
                         kvps[:, 0:65], ALU.add, inc=VE)
                    R.wait("tensor", VE)

        if maxphase < 2:
            R.wait("gpsimd", VE)
            R.flush(block)
            return nc
        R.op("vector", "memset", kvb[:], 0.0)
        R.op("vector", "memset", ksb[:], 0.0, inc=VE)
        R.wait("vector", VE)
        kva = kv_acc[:].rearrange("p (c n) -> p c n", n=65)
        kvb3 = kvb[:].rearrange("p (c n) -> p c n", n=64)
        for ic in range(8):
            for hh in range(2):
                h = 2 * ic + hh
                rows = slice(64 * hh, 64 * hh + 64)
                R.op("vector", "tensor_copy", kvb3[rows, h, :], kva[rows, ic, 0:64])
                R.op("vector", "tensor_copy", ksb[rows, h:h + 1], kva[rows, ic, 64:65],
                     inc=(VE if h == 15 else None))

        for qc in range(4):
            for t4 in range(4):
                tt = qc * 4 + t4
                R.wait("sync", VE)
                R.dma("sync", xb[tt % 2][:], xq[tt * 128:(tt + 1) * 128, :], LD)
                R.wait("vector", LD)
                ln_transpose_tile(xb[tt % 2][:], xhat[tt % 2], xTc, t4)
            for ic in range(8):
                R.wait("tensor", SE)
                for kc in range(8):
                    R.op("tensor", "matmul", pps[:, 0:512],
                         wq_sb[:, kc * 3072 + ic * 128: kc * 3072 + ic * 128 + 128],
                         xTc[:, kc * 512:(kc + 1) * 512],
                         start=(kc == 0), stop=(kc == 7),
                         inc=(TE if kc == 7 else None))
                R.wait("vector", TE)
                R.op("vector", "tensor_copy", kTp[:, ic * 512:(ic + 1) * 512],
                     pps[:, 0:512], inc=VE)
                R.wait("tensor", VE)
                R.op("tensor", "matmul", pp2[:, 0:512], wf_sb[:, 0:128],
                     kTp[:, ic * 512:(ic + 1) * 512],
                     start=True, stop=True, inc=TE)
                R.wait("scalar", TE)
                R.op("scalar", "activation",
                     qpT[:, ic * 2048 + qc * 512: ic * 2048 + (qc + 1) * 512],
                     pp2[:, 0:512], AF.Relu, bias=bfp_sb[:], inc=SE)
                R.wait("tensor", SE)

        if maxphase < 2.2:
            R.wait("gpsimd", VE)
            R.flush(block)
            return nc
        for qc in range(4):
            R.wait("sync", OT)
            R.wait("sync", VE)
            R.dma("sync", xqc[:].rearrange("p (t n) -> p t n", n=1024),
                  xq[qc * 512:(qc + 1) * 512, :].rearrange("(t p) n -> p t n", p=128), LD)
            for ts in range(4):
                R.wait("tensor", SE)
                R.wait("tensor", VE)
                for ic in range(8):
                    lhsT = qpT[:, ic * 2048 + qc * 512 + ts * 128:
                               ic * 2048 + qc * 512 + ts * 128 + 128]
                    for hh in range(2):
                        h = 2 * ic + hh
                        bank = ops1 if h < 8 else ops2
                        col = (h % 8) * 64
                        R.op("tensor", "matmul", bank[:, col:col + 64], lhsT,
                             kvb[:, h * 64:(h + 1) * 64], start=True, stop=True)
                        R.op("tensor", "matmul", smalls[:, h:h + 1], lhsT,
                             ksb[:, h:h + 1], start=True, stop=True,
                             inc=(TE if h == 15 else None))
                R.wait("vector", TE)
                if maxphase < 2.22:
                    continue
                R.op("vector", "tensor_scalar", nrm[:], smalls[:, 0:16], 1e-6, None,
                     ALU.add, inc=VE)
                R.wait("vector", VE)
                R.op("vector", "reciprocal", nrm[:], nrm[:], inc=VE)
                R.wait("vector", VE)
                if maxphase < 2.24:
                    continue
                for h in range(16):
                    bank = ops1 if h < 8 else ops2
                    col = (h % 8) * 64
                    R.op("vector", "tensor_scalar", obn[:, h * 64:(h + 1) * 64],
                         bank[:, col:col + 64], nrm[:, h:h + 1], None, ALU.mult,
                         inc=(VE if h == 15 else None))
                R.wait("tensor", VE)
                if maxphase < 2.5:
                    continue
                for half in range(2):
                    for c4 in range(4):
                        c = half * 4 + c4
                        R.op("tensor", "transpose", Tpack[:, c4 * 128:(c4 + 1) * 128],
                             obn[:, c * 128:(c + 1) * 128], ident[:],
                             inc=(TE if c4 == 3 else None))
                    R.wait("vector", TE)
                    R.op("vector", "tensor_copy",
                         oT[:].rearrange("p (c n) -> p c n", n=512)[
                             :, half * 4:half * 4 + 4, ts * 128:(ts + 1) * 128],
                         Tpack[:, 0:512].rearrange("p (k n) -> p k n", n=128), inc=VE)
                    R.wait("tensor", VE)
            if maxphase < 2.75:
                continue
            R.wait("vector", LD)
            for ts in range(4):
                tt = qc * 4 + ts
                R.wait("scalar", VE)
                for half in range(2):
                    for ic in range(8):
                        R.op("tensor", "matmul", pps[:, 0:512],
                             oT[:, ic * 512 + ts * 128: ic * 512 + ts * 128 + 128],
                             wout_sb[:, ic * 1024 + half * 512: ic * 1024 + half * 512 + 512],
                             start=(ic == 0), stop=False)
                    R.op("tensor", "matmul", pps[:, 0:512], ones_b[0:1, 0:128],
                         bout_sb[0:1, half * 512:(half + 1) * 512],
                         start=False, stop=True, inc=TE)
                    R.wait("scalar", TE)
                    R.op("scalar", "copy", af[:, half * 512:(half + 1) * 512],
                         pps[:, 0:512], inc=SE)
                    R.wait("tensor", SE)
                R.wait("vector", SE)
                emit_ln_stats(R, lt, af[:], 1024, VE, SE)
                R.op("vector", "tensor_scalar", af[:], af[:], lt.negm[:],
                     lt.rstd[:], ALU.add, op1=ALU.mult, inc=VE)
                R.wait("vector", VE)
                R.wait("vector", OT)
                R.op("vector", "tensor_add", x3b[:], af[:],
                     xqc[:, ts * 1024:(ts + 1) * 1024], inc=VE)
                R.wait("gpsimd", VE)
                R.dma("gpsimd", x3out[tt * 128:(tt + 1) * 128, :], x3b[:], OT)
        R.wait("gpsimd", OT)
        R.flush(block)
    return nc


def stage_C(x2, inputs):
    wf = np.ascontiguousarray(inputs["L1_attn_W_f"], np.float32)
    wf2 = np.zeros((128, 128), np.float32)
    wf2[0:64, 0:64] = wf
    wf2[64:128, 64:128] = wf
    bf = np.ascontiguousarray(inputs["L1_attn_b_f"], np.float32)
    bf_p = np.concatenate([bf, bf]).reshape(128, 1)
    bf_r = np.concatenate([bf, bf]).reshape(1, 128)
    bout_r = np.ascontiguousarray(inputs["L1_attn_b_out"], np.float32).reshape(1, D)
    ident = np.eye(128, dtype=np.float32)
    maps = []
    for c in range(NCORES):
        b, par = c // 2, c % 2
        maps.append({
            "xfull": x2[b],
            "xq": np.ascontiguousarray(x2[b][own_rows(par)]),
            "wqkv": np.ascontiguousarray(inputs["L1_attn_W_qkv"], np.float32),
            "wf2": wf2, "bf_p": bf_p, "bf_r": bf_r,
            "wout": np.ascontiguousarray(inputs["L1_attn_W_out"], np.float32),
            "bout_r": bout_r, "ident_in": ident,
        })
    return maps


def run_C(x2, inputs, maxphase=3):
    if ("C", maxphase) not in _NC_CACHE:
        _NC_CACHE[("C", maxphase)] = build_launch_C(maxphase)
    _NC_CACHE["C"] = _NC_CACHE[("C", maxphase)]
    res = run_bass_kernel_spmd(_NC_CACHE["C"], stage_C(x2, inputs),
                               core_ids=list(range(NCORES)))
    return gather_own(res.results, "x3")


def kernel(**inputs):
    x1 = run_A(inputs)
    x2 = run_B(x1, inputs)
    x3 = run_C(x2, inputs)
    return x3

def build_launch_D(ln_in, sandwich, debug=False, fc=None, li=0):
    pre = "" if fc is None else f"d{li}_"
    QKS = 8.0
    if fc is None:
        nc = bass.Bass("TRN2", target_bir_lowering=False)
        R = Rec(nc)
        xe = nc.dram_tensor("xe", [4, 1664, D], F32, kind="ExternalInput").ap()
        xq3 = nc.dram_tensor("xq3", [OWNT, D], F32, kind="ExternalInput").ap()
        wqkv = nc.dram_tensor("wqkv", [D, 1536], F32, kind="ExternalInput").ap()
        wo = nc.dram_tensor("wo", [512, D], F32, kind="ExternalInput").ap()
        w1r = nc.dram_tensor("w1r", [D, 5632], F32, kind="ExternalInput").ap()
        w2r = nc.dram_tensor("w2r", [2816, D], F32, kind="ExternalInput").ap()
        b1r = nc.dram_tensor("b1r", [1, 5632], F32, kind="ExternalInput").ap()
        b2r = nc.dram_tensor("b2r", [1, D], F32, kind="ExternalInput").ap()
        lnb_p = nc.dram_tensor("lnb_p", [128, 8], F32, kind="ExternalInput").ap()
        lng_p = nc.dram_tensor("lng_p", [128, 8], F32, kind="ExternalInput").ap()
        qs_r = nc.dram_tensor("qs_r", [1, 64], F32, kind="ExternalInput").ap()
        ks_r = nc.dram_tensor("ks_r", [1, 64], F32, kind="ExternalInput").ap()
        rotq = nc.dram_tensor("rotq", [4, 2, 1024, 512], F32, kind="ExternalInput").ap()
        rotk = nc.dram_tensor("rotk", [4, 2, 1536, 512], F32, kind="ExternalInput").ap()
        msk = nc.dram_tensor("msk", [4, 8, 128, 640], F32, kind="ExternalInput").ap()
        ident_in = nc.dram_tensor("ident_in", [128, 128], F32, kind="ExternalInput").ap()
        zout = nc.dram_tensor("zout", [OWNT, D], F32, kind="ExternalOutput").ap()
    else:
        nc, R = fc["nc"], fc["R"]
        io = fc["io"]
        xe = fc["xe_d"].ap()
        xq3 = fc["x3_d"].ap()
        wqkv, wo = io[pre + "wqkv"], io[pre + "wo"]
        w1r, w2r, b1r, b2r = io[pre + "w1r"], io[pre + "w2r"], io[pre + "b1r"], io[pre + "b2r"]
        lnb_p, lng_p = io[pre + "lnb_p"], io[pre + "lng_p"]
        qs_r, ks_r = io[pre + "qs_r"], io[pre + "ks_r"]
        rotq, rotk, msk = io["d_rotq"], io["d_rotk"], io["d_msk"]
        ident_in = io["ident_in"]
        zout = (fc["z1_d"].ap() if li == 0 else fc["out"])

    xm_d = nc.dram_tensor(pre + "xm_d", [128, 4 * 8 * 1024], F32)
    h2_d = nc.dram_tensor(pre + "h2_d", [128, 4 * 8 * 512], BF16)
    if debug:
        dbg_xm = nc.dram_tensor("dbg_xm", [128, 4 * 8 * 1024], F32, kind="ExternalOutput").ap()
        dbg_h2 = nc.dram_tensor("dbg_h2", [128, 4 * 8 * 512], BF16, kind="ExternalOutput").ap()
        dbg_q = nc.dram_tensor("dbg_q", [128, 4 * 1024], BF16, kind="ExternalOutput").ap()
        dbg_k = nc.dram_tensor("dbg_k", [128, 4 * 1536], BF16, kind="ExternalOutput").ap()
        dbg_v = nc.dram_tensor("dbg_v", [128, 12 * 512], BF16, kind="ExternalOutput").ap()
        dbg_o = nc.dram_tensor("dbg_o", [128, 4 * 1024], BF16, kind="ExternalOutput").ap()
        dbg_zT = nc.dram_tensor("dbg_zT", [128, 8 * 1664], BF16, kind="ExternalOutput").ap()

    with ExitStack() as CTX:
        sb = lambda n, s, dt: CTX.enter_context(nc.sbuf_tensor(pre + n, s, dt))
        ps = lambda stk, n, s, dt=F32: stk.enter_context(nc.psum_tensor(pre + n, s, dt))
        if fc is None:
            sem = lambda n: Sem(CTX.enter_context(nc.semaphore(n)))
            LD = sem("LD"); GD = sem("GD"); SP = sem("SP")
            TE = sem("TE"); VE = sem("VE"); SE = sem("SE"); OT = sem("OT")
        else:
            sm = fc["sems"]
            LD, GD, SP = sm["LD"], sm["GD"], sm["SP"]
            TE, VE, SE, OT = sm["TE"], sm["VE"], sm["SE"], sm["OT"]

        ident = sb("ident", [128, 128], BF16)
        ident_f = sb("ident_f", [128, 128], F32)
        ones_f = sb("ones_f", [1, 128], F32)
        ones_bf = sb("ones_bf", [1, 128], BF16)
        qst = sb("qst", [128, 512], F32)
        kst = sb("kst", [128, 512], F32)
        lng_sb = sb("lng_sb", [128, 8], F32)
        lt = LnTiles(nc, CTX, tag=pre)
        l2r = sb("l2r", [128, 8], F32)
        Tpack = ps(CTX, "Tpack", [128, 1024], BF16)
        smalls = ps(CTX, "smalls", [128, 512], F32)

        block = fc["block"] if fc else CTX.enter_context(nc.Block())

        R.dma("gpsimd", ident[:], ident_in, GD)
        R.dma("sync", ident_f[:], ident_in, LD)
        R.dma("sync", lng_sb[:], lng_p, LD)
        R.op("vector", "memset", ones_f[:], 1.0)
        R.op("vector", "memset", ones_bf[:], 1.0, inc=VE)
        R.wait("tensor", VE)
        for row_ap, dst in ((qs_r, qst), (ks_r, kst)):
            rsb = sb("row_" + dst.name, [1, 64], F32)
            R.dma("sync", rsb[:], row_ap, LD)
            R.wait("tensor", LD)
            R.op("tensor", "matmul", smalls[:, 0:64], ones_f[:], rsb[:],
                 start=True, stop=True, inc=TE)
            R.wait("vector", TE)
            for h in range(8):
                R.op("vector", "tensor_copy", dst[:, h * 64:(h + 1) * 64],
                     smalls[:, 0:64], inc=(VE if h == 7 else None))
            R.wait("tensor", VE)

        with ExitStack() as PH1:
            sb1 = lambda n, s, dt: PH1.enter_context(nc.sbuf_tensor(pre + n, s, dt))
            wq_sb = sb1("wq_sb", [128, 8 * 1536], BF16)
            wo_sb = sb1("wo_sb", [128, 4 * 1024], BF16)
            xb = [sb1(f"xb{i}", [128, 1024], F32) for i in range(2)]
            zf = sb1("zf", [128, 1024], F32)
            zhat = [sb1(f"zhat{i}", [128, 1024], BF16) for i in range(2)]
            zT = sb1("zT", [128, 8 * 1664], BF16)
            qev = sb1("qev", [128, 512], F32)
            q2s = sb1("q2s", [128, 512], F32)
            rotA = [sb1(f"rotA{i}", [128, 512], F32) for i in range(2)]
            rotB = [sb1(f"rotB{i}", [128, 512], F32) for i in range(2)]
            t1b = sb1("t1b", [128, 512], F32)
            t2b = sb1("t2b", [128, 512], F32)
            q3b = sb1("q3b", [128, 512], BF16)
            qfT = sb1("qfT", [128, 4 * 1024], BF16)
            kfT = sb1("kfT", [128, 4 * 1536], BF16)
            vtm = sb1("vtm", [128, 12 * 512], BF16)
            msk_sb = sb1("msk_sb", [128, 8 * 640], BF16)
            P = sb1("P", [128, 640], BF16)
            PT = sb1("PT", [128, 12 * 1024], BF16)
            oT = sb1("oT", [128, 4 * 1024], BF16)
            rg = sb1("rg", [128, 8], F32)
            row_a = sb1("row_a", [1, 1024], F32)
            rb_sb = sb1("rb_sb", [128, 1024], F32)
            xm = [sb1(f"xm{i}", [128, 1024], F32) for i in range(2)]
            h2t = sb1("h2t", [128, 1024], F32)
            h2n = sb1("h2n", [128, 1024], BF16)
            pps = ps(PH1, "pps", [128, 512])
            Sps = ps(PH1, "Sps", [128, 640])
            ybank = ps(PH1, "ybank", [128, 1024])
            bcb = ps(PH1, "bcb", [128, 512])

            R.dma("gpsimd", wq_sb[:].rearrange("p (c n) -> p c n", n=1536),
                  wqkv.rearrange("(c p) n -> p c n", p=128), GD)
            R.dma("gpsimd", wo_sb[:].rearrange("p (c n) -> p c n", n=1024),
                  wo.rearrange("(c p) n -> p c n", p=128), GD)
            R.wait("tensor", GD)

            def hT(kc, c0, w):
                off = -1 if kc >= 4 else 0
                return zT[:, kc * 1664 + c0 + off: kc * 1664 + c0 + off + w]

            for o in range(4):
                R.wait("gpsimd", VE)
                R.dma("gpsimd", msk_sb[:].rearrange("p (t n) -> p t n", n=640),
                      msk[o].rearrange("t p n -> p t n"), GD)
                R.wait("vector", GD)
                zT3 = zT[:].rearrange("p (c n) -> p c n", n=1664)
                for t in range(13):
                    buf = xb[t % 2]
                    R.wait("sync", VE)
                    R.dma("sync", buf[:], xe[o, t * 128:(t + 1) * 128, :], LD)
                    R.wait("vector", LD)
                    if ln_in:
                        emit_ln_stats(R, lt, buf[:], 1024, VE, SE)
                        R.op("vector", "tensor_scalar", zf[:], buf[:], lt.negm[:],
                             lt.rstd[:], ALU.add, op1=ALU.mult, inc=VE)
                        R.wait("vector", VE)
                        src_f = zf
                    else:
                        src_f = buf
                    R.wait("vector", TE)
                    R.op("vector", "tensor_copy", zhat[t % 2][:], src_f[:], inc=VE)
                    R.wait("tensor", VE)
                    for half in range(2):
                        for c4 in range(4):
                            c = half * 4 + c4
                            R.op("tensor", "transpose", Tpack[:, c4 * 128:(c4 + 1) * 128],
                                 zhat[t % 2][:, c * 128:(c + 1) * 128], ident[:],
                                 inc=(TE if c4 == 3 else None))
                        R.wait("vector", TE)
                        R.op("vector", "tensor_copy",
                             zT3[:, half * 4:half * 4 + 4, t * 128:(t + 1) * 128],
                             Tpack[:, 0:512].rearrange("p (k n) -> p k n", n=128), inc=VE)
                        R.wait("tensor", VE)


                def qk_pipe(kind, tiles, s0, dst_fT, rot_ap, sct):
                    inner0 = 0 if kind == "q" else 512
                    nn = 1024 if kind == "q" else 1536
                    for it in range(tiles):
                        s = s0 + it * 128
                        R.wait("sync", VE)
                        R.dma("sync", rotA[it % 2][:],
                              rot_ap[o, 0, it * 128:(it + 1) * 128, :], LD)
                        R.dma("sync", rotB[it % 2][:],
                              rot_ap[o, 1, it * 128:(it + 1) * 128, :], LD)
                        R.wait("tensor", SE)
                        R.wait("tensor", VE)
                        for kc in range(8):
                            R.op("tensor", "matmul", pps[:, 0:512],
                                 hT(kc, s, 128),
                                 wq_sb[:, kc * 1536 + inner0: kc * 1536 + inner0 + 512],
                                 start=(kc == 0), stop=(kc == 7),
                                 inc=(TE if kc == 7 else None))
                        R.wait("vector", TE)
                        R.op("vector", "tensor_copy", qev[:], pps[:, 0:512], inc=VE)
                        R.wait("scalar", VE)
                        R.op("scalar", "activation", t1b[:], qev[:], AF.Square, inc=SE)
                        R.wait("vector", SE)
                        R.op("vector", "tensor_reduce", l2r[:],
                             t1b[:].rearrange("p (h d) -> p h d", d=64),
                             axis=AX.X, op=ALU.add, inc=VE)
                        R.wait("scalar", VE)
                        R.op("scalar", "activation", l2r[:], l2r[:], AF.Sqrt, inc=SE)
                        R.wait("vector", SE)
                        R.op("vector", "tensor_scalar", l2r[:], l2r[:], 1e-12, None,
                             ALU.max, inc=VE)
                        R.wait("vector", VE)
                        R.op("vector", "reciprocal", l2r[:], l2r[:], inc=VE)
                        R.wait("vector", VE)
                        R.wait("vector", LD)
                        for h in range(8):
                            R.op("vector", "scalar_tensor_tensor",
                                 q2s[:, h * 64:(h + 1) * 64],
                                 qev[:, h * 64:(h + 1) * 64],
                                 l2r[:, h:h + 1], sct[:, h * 64:(h + 1) * 64],
                                 ALU.mult, ALU.mult,
                                 inc=(VE if h == 7 else None))
                        R.wait("vector", VE)
                        R.op("vector", "tensor_mul", t1b[:], q2s[:], rotA[it % 2][:])
                        q23 = q2s[:].rearrange("p (h two d) -> p h two d", two=2, d=32)
                        t23 = t2b[:].rearrange("p (h two d) -> p h two d", two=2, d=32)
                        B3 = rotB[it % 2][:].rearrange("p (h two d) -> p h two d",
                                                       two=2, d=32)
                        for h in range(8):
                            R.op("vector", "tensor_tensor", t23[:, h, 0, :],
                                 q23[:, h, 1, :], B3[:, h, 0, :], ALU.mult)
                            R.op("vector", "tensor_tensor", t23[:, h, 1, :],
                                 q23[:, h, 0, :], B3[:, h, 1, :], ALU.mult,
                                 inc=(VE if h == 7 else None))
                        R.wait("vector", VE)
                        R.op("vector", "tensor_add", q3b[:], t1b[:], t2b[:], inc=VE)
                        R.wait("tensor", VE)
                        for ic in range(4):
                            R.op("tensor", "transpose", Tpack[:, ic * 128:(ic + 1) * 128],
                                 q3b[:, ic * 128:(ic + 1) * 128], ident[:],
                                 inc=(TE if ic == 3 else None))
                        R.wait("vector", TE)
                        R.op("vector", "tensor_copy",
                             dst_fT[:].rearrange("p (c n) -> p c n", n=nn)[
                                 :, :, it * 128:(it + 1) * 128],
                             Tpack[:, 0:512].rearrange("p (k n) -> p k n", n=128), inc=VE)
                        R.wait("tensor", VE)

                qk_pipe("q", 8, 640, qfT, rotq, qst)
                qk_pipe("k", 12, 128, kfT, rotk, kst)
                for it in range(12):
                    s = 128 + it * 128
                    R.wait("tensor", VE)
                    for kc in range(8):
                        R.op("tensor", "matmul", pps[:, 0:512],
                             hT(kc, s, 128),
                             wq_sb[:, kc * 1536 + 1024: kc * 1536 + 1536],
                             start=(kc == 0), stop=(kc == 7),
                             inc=(TE if kc == 7 else None))
                    R.wait("vector", TE)
                    R.op("vector", "tensor_copy", vtm[:, it * 512:(it + 1) * 512],
                         pps[:, 0:512], inc=VE)
                    R.wait("tensor", VE)

                PT3 = PT[:].rearrange("p (k n) -> p k n", n=1024)
                for ic in range(4):
                    for hh in range(2):
                        h = 2 * ic + hh
                        rows = slice(64 * hh, 64 * hh + 64)
                        for tq in range(8):
                            R.wait("tensor", SE)
                            R.wait("tensor", VE)
                            kcol = ic * 1536 + tq * 128
                            R.op("tensor", "matmul", Sps[:, 0:512],
                                 qfT[rows, ic * 1024 + tq * 128: ic * 1024 + (tq + 1) * 128],
                                 kfT[rows, kcol: kcol + 512], start=True, stop=True)
                            R.op("tensor", "matmul", Sps[:, 512:640],
                                 qfT[rows, ic * 1024 + tq * 128: ic * 1024 + (tq + 1) * 128],
                                 kfT[rows, kcol + 512: kcol + 640], start=True, stop=True,
                                 inc=TE)
                            R.wait("vector", TE)
                            R.op("vector", "tensor_tensor", Sps[:, 0:640], Sps[:, 0:640],
                                 msk_sb[:, tq * 640:(tq + 1) * 640], ALU.add)
                            R.op("vector", "reduce_max", lt.negm[:], Sps[:, 0:640],
                                 axis=AX.X, inc=VE)
                            R.wait("vector", VE)
                            R.op("vector", "tensor_scalar", lt.va[:], lt.negm[:],
                                 -QKS, None, ALU.mult, inc=VE)
                            R.wait("scalar", VE)
                            R.wait("scalar", TE)
                            R.op("scalar", "activation", P[:, 0:640], Sps[:, 0:640],
                                 AF.Exp, bias=lt.va[:], scale=QKS,
                                 accum_out=rg[:, tq:tq + 1], inc=SE)
                            R.wait("tensor", SE)
                            for kt in range(5):
                                R.op("tensor", "transpose",
                                     Tpack[:, kt * 128:(kt + 1) * 128],
                                     P[:, kt * 128:(kt + 1) * 128], ident[:],
                                     inc=(TE if kt == 4 else None))
                            R.wait("vector", TE)
                            R.op("vector", "tensor_copy",
                                 PT3[:, tq:tq + 5, tq * 128:(tq + 1) * 128],
                                 Tpack[:, 0:640].rearrange("p (k n) -> p k n", n=128),
                                 inc=VE)
                        R.wait("tensor", VE)
                        pieces = []
                        for ktG in range(12):
                            lo, hi = max(0, ktG - 4), min(7, ktG)
                            c = lo * 128
                            wdt = (hi - lo + 1) * 128
                            while wdt > 0:
                                n = min(512 - (c % 512), wdt)
                                pieces.append((ktG, c, n))
                                c += n
                                wdt -= n
                        first_in_bank = {}
                        last_in_bank = {}
                        for i, (ktG, c, n) in enumerate(pieces):
                            bk = c // 512
                            first_in_bank.setdefault(bk, i)
                            last_in_bank[bk] = i
                        lasts = set(last_in_bank.values())
                        for i, (ktG, c, n) in enumerate(pieces):
                            bk = c // 512
                            R.op("tensor", "matmul", ybank[rows, c:c + n],
                                 vtm[:, ktG * 512 + h * 64: ktG * 512 + h * 64 + 64],
                                 PT[:, ktG * 1024 + c: ktG * 1024 + c + n],
                                 start=(first_in_bank[bk] == i), stop=(i in lasts),
                                 inc=(TE if i == len(pieces) - 1 else None))
                        R.wait("vector", TE)
                        R.wait("vector", SE)
                        R.op("vector", "tensor_scalar", rg[:], rg[:], 1e-30, None,
                             ALU.add, inc=VE)
                        R.wait("vector", VE)
                        R.op("vector", "reciprocal", rg[:], rg[:], inc=VE)
                        R.wait("tensor", VE)
                        for tq in range(8):
                            R.op("tensor", "transpose",
                                 smalls[0:1, (tq % 4) * 128:(tq % 4) * 128 + 128],
                                 rg[:, tq:tq + 1], ident_f[:],
                                 inc=(TE if tq % 4 == 3 else None))
                            if tq % 4 == 3:
                                R.wait("vector", TE)
                                R.op("vector", "tensor_copy",
                                     row_a[0:1, (tq // 4) * 512:(tq // 4) * 512 + 512],
                                     smalls[0:1, 0:512], inc=VE)
                                R.wait("tensor", VE)
                        for half in range(2):
                            R.op("tensor", "matmul", bcb[:, 0:512],
                                 ones_f[:], row_a[0:1, half * 512:(half + 1) * 512],
                                 start=True, stop=True, inc=TE)
                            R.wait("vector", TE)
                            R.op("vector", "tensor_copy",
                                 rb_sb[:, half * 512:(half + 1) * 512],
                                 bcb[:, 0:512], inc=VE)
                            R.wait("tensor", VE)
                        R.wait("vector", VE)
                        R.op("vector", "tensor_mul", oT[rows, ic * 1024:(ic + 1) * 1024],
                             ybank[rows, 0:1024], rb_sb[rows, :], inc=VE)
                        R.wait("tensor", VE)

                if debug and o == 3:
                    R.wait("gpsimd", VE)
                    R.dma("gpsimd", dbg_q, qfT[:], OT)
                    R.dma("gpsimd", dbg_k, kfT[:], OT)
                    R.dma("gpsimd", dbg_v, vtm[:], OT)
                    R.dma("gpsimd", dbg_o, oT[:], OT)
                    R.dma("gpsimd", dbg_zT, zT[:], OT)
                    R.wait("tensor", OT)
                for tq in range(8):
                    xmb = xm[tq % 2]
                    buf = xb[tq % 2]
                    R.wait("sync", VE)
                    R.dma("sync", buf[:], xe[o, (5 + tq) * 128:(6 + tq) * 128, :], LD)
                    R.wait("vector", LD)
                    if ln_in:
                        emit_ln_stats(R, lt, buf[:], 1024, VE, SE)
                        R.op("vector", "tensor_scalar", zf[:], buf[:], lt.negm[:],
                             lt.rstd[:], ALU.add, op1=ALU.mult, inc=VE)
                        resid = zf
                    else:
                        resid = buf
                    R.wait("vector", SP)
                    for half in range(2):
                        R.wait("tensor", VE)
                        for ic in range(4):
                            R.op("tensor", "matmul", pps[:, 0:512],
                                 oT[:, ic * 1024 + tq * 128: ic * 1024 + tq * 128 + 128],
                                 wo_sb[:, ic * 1024 + half * 512: ic * 1024 + half * 512 + 512],
                                 start=(ic == 0), stop=(ic == 3),
                                 inc=(TE if ic == 3 else None))
                        R.wait("vector", TE)
                        R.op("vector", "tensor_add", xmb[:, half * 512:(half + 1) * 512],
                             pps[:, 0:512],
                             resid[:, half * 512:(half + 1) * 512],
                             inc=VE)
                    R.wait("gpsimd", VE)
                    R.dma("gpsimd",
                          xm_d.ap()[:, o * 8192 + tq * 1024: o * 8192 + (tq + 1) * 1024],
                          xmb[:], SP)
                    if tq >= 4:
                        R.wait("gpsimd", VE)
                        R.dma("gpsimd", h2t[:, 0:512], xmb[:, 0:512], LD)
                        R.dma("gpsimd", h2t[1:128, 512:1024], xmb[0:127, 512:1024], LD)
                        R.dma("gpsimd", h2t[0:1, 512:1024],
                              xm[(tq - 1) % 2][127:128, 512:1024], LD)
                        R.wait("vector", LD)
                        emit_ln_stats(R, lt, h2t[:], 1024, VE, SE)
                        R.op("vector", "tensor_scalar", h2n[:], h2t[:], lt.negm[:],
                             lt.rstd[:], ALU.add, op1=ALU.mult, inc=VE)
                        R.wait("tensor", VE)
                        for half in range(2):
                            for c4 in range(4):
                                c = half * 4 + c4
                                R.op("tensor", "transpose",
                                     Tpack[:, c4 * 128:(c4 + 1) * 128],
                                     h2n[:, c * 128:(c + 1) * 128], ident[:],
                                     inc=(TE if c4 == 3 else None))
                            R.wait("vector", TE)
                            R.wait("vector", SP)
                            for c4 in range(4):
                                c = half * 4 + c4
                                R.op("vector", "tensor_scalar",
                                     qev[:, c4 * 128:(c4 + 1) * 128],
                                     Tpack[:, c4 * 128:(c4 + 1) * 128],
                                     lng_sb[:, c:c + 1], None, ALU.mult,
                                     inc=(VE if c4 == 3 else None))
                            R.wait("gpsimd", VE)
                            h2v = h2_d.ap().rearrange("p (s c n) -> p s c n", s=4, c=8)
                            R.dma("gpsimd",
                                  h2v[:, o, half * 4:half * 4 + 4,
                                      (tq - 4) * 128:(tq - 3) * 128],
                                  qev[:, 0:512].rearrange("p (c n) -> p c n", n=128), SP)
                            R.wait("tensor", SP)
            R.flush(block)

        with ExitStack() as PH2:
            sb2 = lambda n, s, dt: PH2.enter_context(nc.sbuf_tensor(pre + n, s, dt))
            w1_sb = sb2("w1_sb", [128, 8 * 5632], BF16)
            w2_sb = sb2("w2_sb", [128, 22 * 1024], BF16)
            h2s = sb2("h2s", [128, 8 * 512], BF16)
            hT_sb = sb2("hT_sb", [128, 22 * 512], BF16)
            asbf = sb2("asbf", [128, 512], F32)
            gbf = sb2("gbf", [128, 512], BF16)
            xms = sb2("xms", [128, 4 * 1024], F32)
            xq3b = sb2("xq3b", [128, 1024], F32)
            b1p = sb2("b1p", [1, 512], F32)
            b2_sb = sb2("b2_sb", [1, D], BF16)
            bcol = sb2("bcol", [128, 44], F32)
            lnbc = sb2("lnbc", [128, 8], BF16)
            zb = sb2("zb", [128, 1024], F32)
            pps = ps(PH2, "ppsb", [128, 512])
            pp2 = ps(PH2, "pp2b", [128, 512])
            y2ps = ps(PH2, "y2ps", [128, 1024])
            brow_ps = ps(PH2, "brow_ps", [1, 512])

            R.wait("gpsimd", TE)
            R.wait("gpsimd", VE)
            R.wait("gpsimd", SE)
            R.wait("sync", TE)
            R.wait("sync", VE)
            R.dma("gpsimd", w1_sb[:].rearrange("p (c n) -> p c n", n=5632),
                  w1r.rearrange("(c p) n -> p c n", p=128), GD)
            R.dma("gpsimd", w2_sb[:].rearrange("p (c n) -> p c n", n=1024),
                  w2r.rearrange("(c p) n -> p c n", p=128), GD)
            R.dma("gpsimd", lnbc[:], lnb_p, GD)
            R.dma("gpsimd", b2_sb[:], b2r, GD)
            R.wait("tensor", GD)
            R.wait("tensor", LD)
            R.wait("vector", LD)
            for piece in range(11):
                R.wait("sync", VE)
                R.dma("sync", b1p[:], b1r[0:1, piece * 512:(piece + 1) * 512], LD)
                for kc in range(8):
                    R.op("tensor", "matmul", brow_ps[:, 0:512],
                         lnbc[:, kc:kc + 1],
                         w1_sb[:, kc * 5632 + piece * 512: kc * 5632 + (piece + 1) * 512],
                         start=(kc == 0), stop=(kc == 7), inc=(TE if kc == 7 else None))
                R.wait("vector", TE)
                R.wait("vector", LD)
                R.op("vector", "tensor_add", b1p[:], b1p[:], brow_ps[:, 0:512], inc=VE)
                R.wait("tensor", VE)
                for c4 in range(4):
                    R.op("tensor", "transpose", smalls[:, c4:c4 + 1],
                         b1p[0:1, c4 * 128:(c4 + 1) * 128], ident_f[0:1, 0:1],
                         inc=(TE if c4 == 3 else None))
                R.wait("vector", TE)
                R.op("vector", "tensor_copy", bcol[:, piece * 4: piece * 4 + 4],
                     smalls[:, 0:4], inc=VE)
                R.wait("tensor", VE)

            for o in range(4):
                R.wait("sync", VE)
                R.dma("sync", h2s[:], h2_d.ap()[:, o * 4096:(o + 1) * 4096], LD)
                R.dma("sync", xms[:], xm_d.ap()[:, o * 8192 + 4096: o * 8192 + 8192], LD)
                R.wait("tensor", LD)
                R.wait("vector", LD)
                for hc2 in range(22):
                    R.wait("tensor", SE)
                    R.wait("tensor", VE)
                    for kc in range(8):
                        R.op("tensor", "matmul", pps[:, 0:512],
                             w1_sb[:, kc * 5632 + (2 * hc2) * 128: kc * 5632 + (2 * hc2) * 128 + 128],
                             h2s[:, kc * 512:(kc + 1) * 512],
                             start=(kc == 0), stop=(kc == 7))
                    for kc in range(8):
                        R.op("tensor", "matmul", pp2[:, 0:512],
                             w1_sb[:, kc * 5632 + (2 * hc2 + 1) * 128: kc * 5632 + (2 * hc2 + 1) * 128 + 128],
                             h2s[:, kc * 512:(kc + 1) * 512],
                             start=(kc == 0), stop=(kc == 7),
                             inc=(TE if kc == 7 else None))
                    R.wait("vector", TE)
                    R.wait("scalar", TE)
                    R.op("vector", "tensor_scalar", asbf[:], pps[:, 0:512],
                         bcol[:, 2 * hc2: 2 * hc2 + 1], None, ALU.add)
                    R.op("scalar", "activation", gbf[:], pp2[:, 0:512], AF.Gelu,
                         bias=bcol[:, 2 * hc2 + 1: 2 * hc2 + 2], inc=SE)
                    R.wait("vector", SE)
                    R.op("vector", "tensor_tensor", hT_sb[:, hc2 * 512:(hc2 + 1) * 512],
                         asbf[:], gbf[:], ALU.mult, inc=VE)
                R.wait("tensor", VE)
                for ts in range(4):
                    tt = o * 4 + ts
                    if sandwich:
                        R.wait("sync", VE)
                        R.dma("sync", xq3b[:], xq3[tt * 128:(tt + 1) * 128, :], LD)
                    for half in range(2):
                        for hc2 in range(22):
                            R.op("tensor", "matmul", y2ps[:, half * 512:(half + 1) * 512],
                                 hT_sb[:, hc2 * 512 + ts * 128: hc2 * 512 + ts * 128 + 128],
                                 w2_sb[:, hc2 * 1024 + half * 512: hc2 * 1024 + half * 512 + 512],
                                 start=(hc2 == 0), stop=False)
                        R.op("tensor", "matmul", y2ps[:, half * 512:(half + 1) * 512],
                             ones_bf[:], b2_sb[0:1, half * 512:(half + 1) * 512],
                             start=False, stop=True, inc=TE)
                    R.wait("vector", TE)
                    R.wait("vector", OT)
                    R.op("vector", "tensor_add", zb[:, 0:512], y2ps[:, 0:512],
                         xms[:, ts * 1024: ts * 1024 + 512])
                    R.op("vector", "tensor_add", zb[:, 512:1024], y2ps[:, 512:1024],
                         xms[:, ts * 1024 + 512: ts * 1024 + 1024], inc=VE)
                    R.wait("tensor", VE)
                    if sandwich:
                        R.wait("vector", LD)
                        emit_ln_stats(R, lt, zb[:], 1024, VE, SE)
                        R.op("vector", "tensor_scalar", zb[:], zb[:], lt.negm[:],
                             lt.rstd[:], ALU.add, op1=ALU.mult, inc=VE)
                        R.wait("vector", VE)
                        R.op("vector", "tensor_add", zb[:], zb[:], xq3b[:], inc=VE)
                    R.wait("gpsimd", VE)
                    R.dma("gpsimd", zout[tt * 128:(tt + 1) * 128, :], zb[:], OT)
            if debug:
                R.dma("gpsimd", dbg_xm, xm_d.ap(), OT)
                R.dma("gpsimd", dbg_h2, h2_d.ap(), OT)
            R.wait("gpsimd", OT)
            R.wait("gpsimd", SP)
            R.flush(block)
    return nc


def _rot_tables(abs_pos, xpos_sign):
    DH = 64
    half = np.arange(0, DH, 2, dtype=np.float64)
    inv = 1.0 / 10000.0 ** (half / DH)
    t = abs_pos.astype(np.float64)
    fr = t[:, None] * inv[None, :]
    freqs = np.concatenate([fr, fr], axis=1)
    base = (half + 0.4 * DH) / (1.4 * DH)
    power = (t - 4096 // 2) / 256.0
    sc = base[None, :] ** power[:, None]
    sc = np.concatenate([sc, sc], axis=1)
    if xpos_sign < 0:
        sc = 1.0 / sc
    A = np.cos(freqs) * sc
    Bs = np.sin(freqs) * sc
    Bs[:, 0:32] *= -1.0
    return A.astype(np.float32), Bs.astype(np.float32)


def stage_D(xin, x3, inputs, layer):
    wqkv = np.ascontiguousarray(inputs["L1_ffn_wqkv"][layer], np.float32)
    wo_ = np.ascontiguousarray(inputs["L1_ffn_wo"][layer], np.float32)
    w1 = np.asarray(inputs["L1_ffn_w1"][layer], np.float32)
    b1 = np.asarray(inputs["L1_ffn_b1"][layer], np.float32)
    w2 = np.asarray(inputs["L1_ffn_w2"][layer], np.float32)
    w1r = np.zeros((D, 2 * 2816), np.float32)
    b1r = np.zeros((1, 2 * 2816), np.float32)
    a_, g_ = w1[:, :2730], w1[:, 2730:]
    ba_, bg_ = b1[:2730], b1[2730:]
    for j in range(22):
        lo, hi = j * 128, min((j + 1) * 128, 2730)
        w1r[:, 2 * j * 128: 2 * j * 128 + (hi - lo)] = a_[:, lo:hi]
        w1r[:, (2 * j + 1) * 128: (2 * j + 1) * 128 + (hi - lo)] = g_[:, lo:hi]
        b1r[0, 2 * j * 128: 2 * j * 128 + (hi - lo)] = ba_[lo:hi]
        b1r[0, (2 * j + 1) * 128: (2 * j + 1) * 128 + (hi - lo)] = bg_[lo:hi]
    w2r = np.zeros((2816, D), np.float32)
    w2r[:2730] = w2
    b2r = np.ascontiguousarray(inputs["L1_ffn_b2"][layer], np.float32).reshape(1, D)
    lnb = np.asarray(inputs["L1_ffn_ln_b"][layer], np.float32)
    lng = np.asarray(inputs["L1_ffn_ln_g"][layer], np.float32)
    lnb_p = np.ascontiguousarray(lnb.reshape(8, 128).T)
    lng_p = np.ascontiguousarray(lng.reshape(8, 128).T)
    qs_r = np.ascontiguousarray(inputs["L1_ffn_q_scale"][layer], np.float32).reshape(1, 64)
    ks_r = np.ascontiguousarray(inputs["L1_ffn_k_scale"][layer], np.float32).reshape(1, 64)
    ident = np.eye(128, dtype=np.float32)
    maps = []
    for c in range(NCORES):
        b, par = c // 2, c % 2
        xe = np.zeros((4, 1664, D), np.float32)
        rq = np.zeros((4, 2, 1024, 512), np.float32)
        rk = np.zeros((4, 2, 1536, 512), np.float32)
        mk = np.full((4, 8, 128, 640), NEG, np.float32)
        for o, A in enumerate(OWN[par]):
            c0 = A * 512
            lo = c0 - 1152
            src_lo, dst_lo = max(0, lo), max(0, -lo)
            xe[o, dst_lo:] = xin[b][src_lo: lo + 1664]
            qpos = np.clip(np.arange(c0 - 512, c0 + 512), 0, None)
            kpos = np.clip(np.arange(c0 - 1024, c0 + 512), 0, None)
            for si, (sign, pos, dst) in enumerate(
                    (((1, qpos, rq)), ((-1, kpos, rk)))):
                Aq, Bq = _rot_tables(pos, sign)
                dst[o, 0] = np.tile(Aq, (1, 8))
                dst[o, 1] = np.tile(Bq, (1, 8))
            for tq in range(8):
                for r in range(128):
                    aq = c0 - 512 + tq * 128 + r
                    ks = c0 - 1024 + tq * 128
                    cc = np.arange(ks, ks + 640)
                    ok = (cc >= 0) & (cc <= aq) & (cc >= aq - 512) & (aq >= 0)
                    mk[o, tq, r, ok] = 0.0
        maps.append({
            "xe": xe, "xq3": np.ascontiguousarray(x3[b][own_rows(par)]),
            "wqkv": wqkv, "wo": wo_, "w1r": w1r, "w2r": w2r, "b1r": b1r,
            "b2r": b2r, "lnb_p": lnb_p, "lng_p": lng_p, "qs_r": qs_r,
            "ks_r": ks_r, "rotq": rq, "rotk": rk, "msk": mk, "ident_in": ident,
        })
    return maps


def run_D(x3, inputs):
    if "D1" not in _NC_CACHE:
        _NC_CACHE["D1"] = build_launch_D(True, False)
        _NC_CACHE["D2"] = build_launch_D(False, True)
    res = run_bass_kernel_spmd(_NC_CACHE["D1"], stage_D(x3, x3, inputs, 0),
                               core_ids=list(range(NCORES)))
    z1 = gather_own(res.results, "zout")
    res = run_bass_kernel_spmd(_NC_CACHE["D2"], stage_D(z1, x3, inputs, 1),
                               core_ids=list(range(NCORES)))
    return gather_own(res.results, "zout")




GROUPS = [[0, 1], [2, 3], [4, 5], [6, 7]]


def emit_barrier(R, sems):
    for eng in Rec.ENGINES:
        for s in sems.values():
            R.wait(eng, s)


def build_fused():
    nc = bass.Bass("TRN2", target_bir_lowering=False)
    R = Rec(nc)
    io = {}

    def inp(name, shape, dt=F32):
        io[name] = nc.dram_tensor(name, shape, dt, kind="ExternalInput").ap()

    inp("ident_in", [128, 128])
    inp("a_xfull", [T, D]); inp("a_xq", [OWNT, D])
    for w in ("a_wq", "a_wk", "a_wv", "a_wo"):
        inp(w, [D, D])
    inp("a_nullk_p", [128, 8]); inp("a_nullv_r", [1, D]); inp("a_lsc_r", [1, 16])
    inp("a_mtail", [16, 5, 128, 128])
    inp("b_wgT", [D, 3072], I32); inp("b_wuT", [D, 3072], I32)
    inp("b_wdT", [3072, D], I32)
    inp("c_wqkv", [D, 3072]); inp("c_wf2", [128, 128]); inp("c_bf_p", [128, 1])
    inp("c_bf_r", [1, 128]); inp("c_wout", [D, D]); inp("c_bout_r", [1, D])
    for li in range(2):
        p = f"d{li}_"
        inp(p + "wqkv", [D, 1536]); inp(p + "wo", [512, D])
        inp(p + "w1r", [D, 5632]); inp(p + "w2r", [2816, D])
        inp(p + "b1r", [1, 5632]); inp(p + "b2r", [1, D])
        inp(p + "lnb_p", [128, 8]); inp(p + "lng_p", [128, 8])
        inp(p + "qs_r", [1, 64]); inp(p + "ks_r", [1, 64])
    inp("d_rotq", [4, 2, 1024, 512]); inp("d_rotk", [4, 2, 1536, 512])
    inp("d_msk", [4, 8, 128, 640])
    inp("dmeta", [1, 128], I32)
    out = nc.dram_tensor("out", [OWNT, D], F32, kind="ExternalOutput").ap()

    x1_d = nc.dram_tensor("x1_d", [OWNT, D], F32)
    x2_d = nc.dram_tensor("x2_d", [OWNT, D], F32)
    x3_d = nc.dram_tensor("x3_d", [OWNT, D], F32)
    z1_d = nc.dram_tensor("z1_d", [OWNT, D], F32)
    x2g = nc.dram_tensor("x2g", [T, D], F32)
    xg_scr = nc.dram_tensor("xg_scr", [T, D], F32)
    xabs = nc.dram_tensor("xabs", [1152 + T, D], F32)
    xe_d = nc.dram_tensor("xe_d", [4, 1664, D], F32)

    with ExitStack() as CTX:
        sems = {n: Sem(CTX.enter_context(nc.semaphore(n)))
                for n in ("LD", "GD", "SP", "TE", "VE", "SE", "OT", "CC")}
        meta_sb = CTX.enter_context(nc.sbuf_tensor("meta_sb", [1, 128], I32))
        zero_sb = CTX.enter_context(nc.sbuf_tensor("zero_sb", [128, 1024], F32))
        dynr = CTX.enter_context(nc.gpsimd.register("dynr"))
        block = CTX.enter_context(nc.Block())

        fcb = {"nc": nc, "R": R, "io": io, "sems": sems, "block": block,
               "x1_d": x1_d, "x2_d": x2_d, "x3_d": x3_d, "z1_d": z1_d,
               "x2g": x2g, "out": out, "xe_d": xe_d}

        LD, GD, OT, CC, VE = (sems[k] for k in ("LD", "GD", "OT", "CC", "VE"))
        R.dma("sync", meta_sb[:], io["dmeta"], LD)
        R.op("vector", "memset", zero_sb[:], 0.0, inc=VE)
        R.wait("gpsimd", VE)
        for zi in range(9):
            R.dma("gpsimd", xabs.ap()[zi * 128:(zi + 1) * 128, :], zero_sb[:], GD)

        def unscramble_and_slot(src_d):
            R.op("gpsimd", "collective_compute", "AllGather", ALU.bypass,
                 replica_groups=GROUPS, ins=[src_d.ap().opt()],
                 outs=[xg_scr.ap().opt()], inc=sems["CC"])
            R.wait("gpsimd", sems["CC"])
            for c in range(8):
                g = gathered_pos(c)
                R.dma("gpsimd", xabs.ap()[1152 + c * 512: 1152 + (c + 1) * 512, :],
                      xg_scr.ap()[g * 512:(g + 1) * 512, :], GD)
            R.wait("gpsimd", GD)
            R.wait("gpsimd", LD)
            for o in range(4):
                R.op("gpsimd", "reg_load", dynr, meta_sb[0:1, o:o + 1])
                R.dma("gpsimd", xe_d.ap()[o],
                      bass.AP(xabs, dynr, [[1024, 1664], [1, 1024]]), GD)

        build_launch_A(fc=fcb)
        emit_barrier(R, sems)
        build_launch_B(fc=fcb)
        emit_barrier(R, sems)
        R.op("gpsimd", "collective_compute", "AllGather", ALU.bypass,
             replica_groups=GROUPS, ins=[x2_d.ap().opt()], outs=[x2g.ap().opt()],
             inc=sems["CC"])
        emit_barrier(R, sems)
        build_launch_C(fc=fcb)
        emit_barrier(R, sems)
        unscramble_and_slot(x3_d)
        emit_barrier(R, sems)
        build_launch_D(True, False, fc=fcb, li=0)
        emit_barrier(R, sems)
        unscramble_and_slot(z1_d)
        emit_barrier(R, sems)
        build_launch_D(False, True, fc=fcb, li=1)
        emit_barrier(R, sems)
        R.flush(block)
    return nc


def gathered_pos(c):
    if c in OWN[0]:
        return OWN[0].index(c)
    return 4 + OWN[1].index(c)


def make_dmeta(par):
    m = np.zeros((1, 128), np.int32)
    for o, A in enumerate(OWN[par]):
        m[0, o] = (A * 512) * D
    return m


def stage_fused(inputs):
    zero4 = np.zeros((4, T, D), np.float32)
    zero2 = np.zeros((OWNT, D), np.float32)
    aM = stage_A(inputs)
    bM = stage_B(zero4, inputs)
    cM = stage_C(zero4, inputs)
    d0M = stage_D(zero4, zero4, inputs, 0)
    d1M = stage_D(zero4, zero4, inputs, 1)
    maps = []
    for c in range(NCORES):
        par = c % 2
        m = {"ident_in": aM[c]["ident_in"], "dmeta": make_dmeta(par)}
        for k, v in aM[c].items():
            if k != "ident_in":
                m["a_" + k] = v
        for k in ("wgT", "wuT", "wdT"):
            m["b_" + k] = bM[c][k]
        for k in ("wqkv", "wf2", "bf_p", "bf_r", "wout", "bout_r"):
            m["c_" + k] = cM[c][k]
        for li, dM in ((0, d0M), (1, d1M)):
            for k in ("wqkv", "wo", "w1r", "w2r", "b1r", "b2r", "lnb_p",
                      "lng_p", "qs_r", "ks_r"):
                m[f"d{li}_" + k] = dM[c][k]
        m["d_rotq"] = d0M[c]["rotq"]
        m["d_rotk"] = d0M[c]["rotk"]
        m["d_msk"] = d0M[c]["msk"]
        maps.append(m)
    return maps


def kernel(**inputs):
    if "FUSED" not in _NC_CACHE:
        _NC_CACHE["FUSED"] = build_fused()
    res = run_bass_kernel_spmd(_NC_CACHE["FUSED"], stage_fused(inputs),
                               core_ids=list(range(NCORES)))
    return gather_own(res.results, "out")


# version 03
# speedup vs baseline: 1.0834x; 1.0834x over previous
import numpy as np
import ml_dtypes
import concourse.bass as bass
import concourse.mybir as mybir
from concourse.bass_utils import run_bass_kernel_spmd
from contextlib import ExitStack

F32 = mybir.dt.float32
BF16 = mybir.dt.bfloat16
I32 = mybir.dt.int32
AF = mybir.ActivationFunctionType
ALU = mybir.AluOpType
AX = mybir.AxisListType
BF = ml_dtypes.bfloat16

NCORES = 8
T = 4096
D = 1024
OWNT = 2048
NEG = -1e30

OWN = [[0, 3, 4, 7], [1, 2, 5, 6]]


def own_rows(par):
    return np.concatenate([np.arange(c * 512, (c + 1) * 512) for c in OWN[par]])


KK_TAB = [5, 6, 7, 8, 13, 14, 15, 16, 21, 22, 23, 24, 29, 30, 31, 32]


def KK(j):
    return KK_TAB[j]


class Sem:
    def __init__(self, h):
        self.h = h
        self.n = 0


class Rec:
    ENGINES = ("sync", "gpsimd", "tensor", "vector", "scalar")

    def __init__(self, nc):
        self.nc = nc
        self.ops = {e: [] for e in self.ENGINES}

    def on(self, eng, f):
        self.ops[eng].append(f)

    def wait(self, eng, sem, n=None):
        v = sem.n if n is None else n
        if v > 0:
            self.on(eng, lambda e, s=sem.h, v=v: e.wait_ge(s, v))

    def op(self, eng, meth, *a, inc=None, inck=1, **kw):
        if inc is not None:
            inc.n += inck

        import traceback
        site = traceback.extract_stack(limit=3)[0]

        def f(e, meth=meth, a=a, kw=kw, inc=inc, inck=inck, site=site):
            try:
                ins = getattr(e, meth)(*a, **kw)
            except Exception as ex:
                import traceback as _tb
                _tb.print_exc()
                raise RuntimeError(
                    f"op {meth} recorded at {site.filename}:{site.lineno} failed: {ex}"
                ) from ex
            if inc is not None:
                ins.then_inc(inc.h, inck)
        self.on(eng, f)

    def dma(self, eng, out, in_, inc, **kw):
        self.op(eng, "dma_start", inc=inc, inck=16, out=out, in_=in_, **kw)

    def flush(self, block):
        for eng in self.ENGINES:
            if self.ops[eng]:
                ops = self.ops[eng]
                getattr(block, eng)(lambda e, ops=ops: [f(e) for f in ops])
        self.ops = {e: [] for e in self.ENGINES}


class LnTiles:

    def __init__(self, nc, ctx, tag=""):
        sb = lambda n, s, dt=F32: ctx.enter_context(nc.sbuf_tensor(n + tag, s, dt))
        self.stats = sb("ln_stats", [128, 12])
        self.mv = sb("ln_mv", [128, 2])
        self.negm = sb("ln_negm", [128, 1])
        self.va = sb("ln_va", [128, 1])
        self.rstd = sb("ln_rstd", [128, 1])


def emit_ln_stats(R, lt, src, width, VE, SE, eps=1e-5):
    h = width // 2
    R.op("vector", "bn_stats", lt.stats[:, 0:6], src[:, 0:h])
    R.op("vector", "bn_stats", lt.stats[:, 6:12], src[:, h:width], inc=VE)
    R.wait("vector", VE)
    R.op("vector", "bn_aggr", lt.mv[:], lt.stats[:], inc=VE)
    R.wait("vector", VE)
    R.op("vector", "tensor_scalar", lt.negm[:], lt.mv[:, 0:1], -1.0, None, ALU.mult)
    R.op("vector", "tensor_scalar", lt.va[:], lt.mv[:, 1:2], eps, None, ALU.add, inc=VE)
    R.wait("scalar", VE)
    R.op("scalar", "activation", lt.rstd[:], lt.va[:], AF.Sqrt, inc=SE)
    R.wait("vector", SE)
    R.op("vector", "reciprocal", lt.rstd[:], lt.rstd[:], inc=VE)
    R.wait("vector", VE)


def build_launch_A(debug=False, fc=None):
    pre = "" if fc is None else "a_"
    if fc is None:
        nc = bass.Bass("TRN2", target_bir_lowering=False)
        R = Rec(nc)
        xfull = nc.dram_tensor("xfull", [T, D], F32, kind="ExternalInput").ap()
        xq = nc.dram_tensor("xq", [OWNT, D], F32, kind="ExternalInput").ap()
        wq = nc.dram_tensor("wq", [D, D], F32, kind="ExternalInput").ap()
        wk = nc.dram_tensor("wk", [D, D], F32, kind="ExternalInput").ap()
        wv = nc.dram_tensor("wv", [D, D], F32, kind="ExternalInput").ap()
        wo = nc.dram_tensor("wo", [D, D], F32, kind="ExternalInput").ap()
        nullk_p = nc.dram_tensor("nullk_p", [128, 8], F32, kind="ExternalInput").ap()
        nullv_r = nc.dram_tensor("nullv_r", [1, D], F32, kind="ExternalInput").ap()
        lsc_r = nc.dram_tensor("lsc_r", [1, 16], F32, kind="ExternalInput").ap()
        ident_in = nc.dram_tensor("ident_in", [128, 128], F32, kind="ExternalInput").ap()
        mtail = nc.dram_tensor("mtail", [16, 5, 128, 128], F32, kind="ExternalInput").ap()
        x1out = nc.dram_tensor("x1", [OWNT, D], F32, kind="ExternalOutput").ap()
    else:
        nc, R = fc["nc"], fc["R"]
        io = fc["io"]
        xfull, xq = io["a_xfull"], io["a_xq"]
        wq, wk, wv, wo = io["a_wq"], io["a_wk"], io["a_wv"], io["a_wo"]
        nullk_p, nullv_r, lsc_r = io["a_nullk_p"], io["a_nullv_r"], io["a_lsc_r"]
        ident_in, mtail = io["ident_in"], io["a_mtail"]
        x1out = fc["x1_d"].ap()

    kT_d = nc.dram_tensor(pre + "kT_d", [128, 8 * 4096], BF16)
    v_d = nc.dram_tensor(pre + "v_d", [128, 8 * 4096], BF16)
    if debug:
        dbg_k = nc.dram_tensor("dbg_k", [128, 8 * 4096], BF16, kind="ExternalOutput").ap()
        dbg_v = nc.dram_tensor("dbg_v", [128, 8 * 4096], BF16, kind="ExternalOutput").ap()
        dbg_q = nc.dram_tensor("dbg_q", [128, 8 * 2048], BF16, kind="ExternalOutput").ap()
        dbg_yn = nc.dram_tensor("dbg_yn", [4, 128, 8 * 512], BF16, kind="ExternalOutput").ap()
        dbg_xqt = nc.dram_tensor("dbg_xqt", [4, 128, 8 * 512], BF16, kind="ExternalOutput").ap()

    with ExitStack() as CTX:
        sb = lambda n, s, dt: CTX.enter_context(nc.sbuf_tensor(pre + n, s, dt))
        ps = lambda stk, n, s, dt=F32: stk.enter_context(nc.psum_tensor(pre + n, s, dt))
        if fc is None:
            sem = lambda n: Sem(CTX.enter_context(nc.semaphore(n)))
            LD = sem("LD"); GD = sem("GD"); SP = sem("SP")
            TE = sem("TE"); VE = sem("VE"); SE = sem("SE"); OT = sem("OT")
        else:
            sm = fc["sems"]
            LD, GD, SP = sm["LD"], sm["GD"], sm["SP"]
            TE, VE, SE, OT = sm["TE"], sm["VE"], sm["SE"], sm["OT"]

        xq_sb = sb("xq_sb", [128, 16 * 1024], F32)
        qT = sb("qT", [128, 8 * 2048], BF16)
        ident = sb("ident", [128, 128], BF16)
        ident_f = sb("ident_f", [128, 128], F32)
        ones_f = sb("ones_f", [1, 128], F32)
        nullk_sb = sb("nullk_sb", [128, 8], BF16)
        nullv_sb = sb("nullv_sb", [1, D], BF16)
        lsc_sb = sb("lsc_sb", [1, 16], F32)
        scale_b = sb("scale_b", [128, 16], F32)
        lt = LnTiles(nc, CTX, tag=pre)

        Tpack = ps(CTX, "Tpack", [128, 1024], BF16)
        smalls = ps(CTX, "smalls", [128, 512], F32)
        bc = ps(CTX, "bc", [128, 512], F32)
        yps = ps(CTX, "yps", [128, 512], F32)

        block = fc["block"] if fc else CTX.enter_context(nc.Block())

        R.dma("gpsimd", ident[:], ident_in, GD)
        R.dma("gpsimd", nullk_sb[:], nullk_p, GD)
        R.dma("gpsimd", nullv_sb[:], nullv_r, GD)
        R.dma("sync", ident_f[:], ident_in, LD)
        R.dma("sync", lsc_sb[:], lsc_r, LD)
        R.dma("sync", xq_sb[:].rearrange("p (t n) -> p t n", n=1024),
              xq.rearrange("(t p) n -> p t n", p=128), LD)
        R.op("vector", "memset", ones_f[:], 1.0, inc=VE)
        R.wait("scalar", LD)
        R.op("scalar", "activation", lsc_sb[:], lsc_sb[:], AF.Exp, inc=SE)
        R.wait("vector", SE)
        R.op("vector", "tensor_scalar", lsc_sb[:], lsc_sb[:], 0.125, None, ALU.mult, inc=VE)
        R.wait("tensor", VE)
        R.op("tensor", "matmul", bc[:, 0:16], ones_f[:], lsc_sb[:], start=True, stop=True, inc=TE)
        R.wait("vector", TE)
        R.op("vector", "tensor_copy", scale_b[:], bc[:, 0:16], inc=VE)

        with ExitStack() as PH1:
            sb1 = lambda n, s, dt: PH1.enter_context(nc.sbuf_tensor(pre + n, s, dt))
            xb = [sb1(f"xb{i}", [128, 1024], F32) for i in range(2)]
            xhat = [sb1(f"xhat{i}", [128, 1024], BF16) for i in range(2)]
            xTc = [sb1(f"xTc{i}", [128, 8 * 512], BF16) for i in range(2)]
            wk_sb = sb1("wk_sb", [128, 8 * 1024], BF16)
            wv_sb = sb1("wv_sb", [128, 8 * 1024], BF16)
            wq_sb = sb1("wq_sb", [128, 8 * 1024], BF16)
            kspill = sb1("kspill", [128, 8 * 512], BF16)
            vbuf = [sb1(f"vbuf{i}", [128, 1024], BF16) for i in range(2)]
            pps = ps(PH1, "pps", [128, 512], F32)
            pp2 = ps(PH1, "pp2", [128, 512], F32)

            for w_ap, w_sb in ((wk, wk_sb), (wv, wv_sb), (wq, wq_sb)):
                R.dma("gpsimd", w_sb[:].rearrange("p (c n) -> p c n", n=1024),
                      w_ap.rearrange("(c p) n -> p c n", p=128), GD)
            wready = GD.n

            def ln_transpose_tile(src_f32, xhat_t, xT_t, t4):
                emit_ln_stats(R, lt, src_f32, 1024, VE, SE)
                R.wait("vector", TE)
                R.op("vector", "tensor_scalar", xhat_t[:], src_f32, lt.negm[:],
                     lt.rstd[:], ALU.add, op1=ALU.mult, inc=VE)
                R.wait("tensor", VE)
                xT3 = xT_t[:].rearrange("p (c n) -> p c n", n=512)
                for half in range(2):
                    for c4 in range(4):
                        c = half * 4 + c4
                        R.op("tensor", "transpose", Tpack[:, c4 * 128:(c4 + 1) * 128],
                             xhat_t[:, c * 128:(c + 1) * 128], ident[:],
                             inc=(TE if c4 == 3 else None))
                    R.wait("vector", TE)
                    R.op("vector", "tensor_copy",
                         xT3[:, half * 4:half * 4 + 4, t4 * 128:(t4 + 1) * 128],
                         Tpack[:, 0:512].rearrange("p (k n) -> p k n", n=128), inc=VE)
                    R.wait("tensor", VE)

            R.wait("tensor", GD, wready)
            v3 = v_d.ap().rearrange("p (c n) -> p c n", n=4096)
            for tc in range(8):
                for t4 in range(4):
                    tt = tc * 4 + t4
                    buf = xb[tt % 2]
                    R.wait("sync", VE)
                    R.dma("sync", buf[:], xfull[tt * 128:(tt + 1) * 128, :], LD)
                    R.wait("vector", LD)
                    ln_transpose_tile(buf[:], xhat[tt % 2], xTc[tc % 2], t4)
                for ic in range(8):
                    for kc in range(8):
                        R.op("tensor", "matmul", pps[:, 0:512],
                             wk_sb[:, kc * 1024 + ic * 128: kc * 1024 + ic * 128 + 128],
                             xTc[tc % 2][:, kc * 512:(kc + 1) * 512],
                             start=(kc == 0), stop=(kc == 7),
                             inc=(TE if kc == 7 else None))
                    R.wait("vector", TE)
                    R.wait("vector", SP)
                    R.op("vector", "tensor_copy", kspill[:, ic * 512:(ic + 1) * 512],
                         pps[:, 0:512], inc=VE)
                    R.wait("tensor", VE)
                R.wait("gpsimd", VE)
                for ic in range(8):
                    R.dma("gpsimd",
                          kT_d.ap()[:, ic * 4096 + tc * 512: ic * 4096 + (tc + 1) * 512],
                          kspill[:, ic * 512:(ic + 1) * 512], SP)
                for t4 in range(4):
                    tt = tc * 4 + t4
                    for half in range(2):
                        bank = pps if half == 0 else pp2
                        for kc in range(8):
                            R.op("tensor", "matmul", bank[:, 0:512],
                                 xTc[tc % 2][:, kc * 512 + t4 * 128: kc * 512 + t4 * 128 + 128],
                                 wv_sb[:, kc * 1024 + half * 512: kc * 1024 + half * 512 + 512],
                                 start=(kc == 0), stop=(kc == 7),
                                 inc=(TE if kc == 7 else None))
                    R.wait("vector", TE)
                    R.wait("vector", SP)
                    R.op("vector", "tensor_copy", vbuf[tt % 2][:, 0:512], pps[:, 0:512])
                    R.op("vector", "tensor_copy", vbuf[tt % 2][:, 512:1024], pp2[:, 0:512],
                         inc=VE)
                    R.wait("tensor", VE)
                    R.wait("gpsimd", VE)
                    R.dma("gpsimd", v3[:, :, tt * 128:(tt + 1) * 128],
                          vbuf[tt % 2][:].rearrange("p (c n) -> p c n", n=128), SP)

            R.wait("vector", LD)
            for qc in range(4):
                for t4 in range(4):
                    tt = qc * 4 + t4
                    R.wait("vector", VE)
                    ln_transpose_tile(xq_sb[:, tt * 1024:(tt + 1) * 1024],
                                      xhat[tt % 2], xTc[qc % 2], t4)
                if debug:
                    R.wait("gpsimd", VE)
                    R.dma("gpsimd", dbg_xqt[qc], xTc[qc % 2][:], SP)
                    R.wait("tensor", SP)
                for ic in range(8):
                    for kc in range(8):
                        R.op("tensor", "matmul", pps[:, 0:512],
                             wq_sb[:, kc * 1024 + ic * 128: kc * 1024 + ic * 128 + 128],
                             xTc[qc % 2][:, kc * 512:(kc + 1) * 512],
                             start=(kc == 0), stop=(kc == 7),
                             inc=(TE if kc == 7 else None))
                    R.wait("vector", TE)
                    R.op("vector", "tensor_copy",
                         qT[:, ic * 2048 + qc * 512: ic * 2048 + (qc + 1) * 512],
                         pps[:, 0:512], inc=VE)
                    R.wait("tensor", VE)
            R.flush(block)

        with ExitStack() as PH2:
            sb2 = lambda n, s, dt: PH2.enter_context(nc.sbuf_tensor(pre + n, s, dt))
            Kg = sb2("Kg", [128, 2048], BF16)
            Vg = sb2("Vg", [128, 2048], BF16)
            P = sb2("P", [128, 2048], BF16)
            PT = sb2("PT", [128, 16 * 512], BF16)
            ynT = sb2("ynT", [128, 8 * 512], BF16)
            y_acc = sb2("y_acc", [128, 512], F32)
            tmp_y = sb2("tmp_y", [128, 512], F32)
            rb_sb = sb2("rb_sb", [128, 512], F32)
            row_a = sb2("row_a", [1, 512], F32)
            row_b = sb2("row_b", [1, 512], F32)
            mt_sb = sb2("mt_sb", [128, 4 * 5 * 128], BF16)
            rg = sb2("rg", [128, 4], F32)
            mg = sb2("mg", [128, 4], F32)
            r_acc = sb2("r_acc", [128, 4], F32)
            m_acc = sb2("m_acc", [128, 4], F32)
            sm1 = sb2("sm1", [128, 4], F32)
            sm2 = sb2("sm2", [128, 4], F32)
            pnull = sb2("pnull", [128, 4], BF16)
            pnullT = sb2("pnullT", [1, 512], BF16)
            wo_sb = sb2("wo_sb", [128, 8 * 1024], BF16)
            attn_f32 = sb2("attn_f32", [128, 1024], F32)
            x1b = sb2("x1b", [128, 1024], F32)
            Sps = ps(PH2, "Sps", [128, 2048], F32)
            PT3 = PT[:].rearrange("p (k n) -> p k n", n=512)

            R.wait("gpsimd", TE)
            R.wait("gpsimd", VE)
            R.wait("sync", TE)
            R.wait("sync", VE)
            R.dma("gpsimd", wo_sb[:].rearrange("p (c n) -> p c n", n=1024),
                  wo.rearrange("(c p) n -> p c n", p=128), GD)

            for qb in range(4):
                KKs = [KK(4 * qb + qs) for qs in range(4)]
                Wmax = 128 * KKs[3]
                ngroups = 1 if Wmax <= 2048 else 2
                R.wait("gpsimd", VE)
                R.dma("gpsimd", mt_sb[:].rearrange("p (j t n) -> p j t n", t=5, n=128),
                      mtail[4 * qb:4 * qb + 4, :, :, :].rearrange("j t p n -> p j t n"), GD)
                R.wait("vector", GD)
                for ic in range(8):
                    for hh in range(2):
                        h = 2 * ic + hh
                        rows = slice(64 * hh, 64 * hh + 64)
                        scol = scale_b[:, h:h + 1]
                        for g in range(ngroups):
                            g0 = g * 2048
                            gW = min(Wmax, g0 + 2048) - g0
                            if hh == 0 or ngroups > 1:
                                R.wait("sync", TE)
                                R.wait("sync", SE)
                                R.dma("sync", Kg[:, 0:gW],
                                      kT_d.ap()[:, ic * 4096 + g0: ic * 4096 + g0 + gW], LD)
                                R.dma("sync", Vg[:, 0:gW],
                                      v_d.ap()[:, ic * 4096 + g0: ic * 4096 + g0 + gW], LD)
                            R.wait("tensor", LD)
                            for qs in range(4):
                                j = 4 * qb + qs
                                Wj = 128 * KKs[qs]
                                lo, hi = g0, min(Wj, g0 + 2048)
                                if hi <= lo:
                                    continue
                                w = hi - lo
                                lhsT = qT[rows, ic * 2048 + j * 128: ic * 2048 + (j + 1) * 128]
                                R.wait("tensor", SE)
                                R.wait("tensor", VE)
                                mm_list = []
                                c = 0
                                while c < w:
                                    n = min(512, w - c)
                                    mm_list.append((c, n))
                                    c += n
                                for i, (c, n) in enumerate(mm_list):
                                    last = (i == len(mm_list) - 1) and g != 0
                                    R.op("tensor", "matmul", Sps[:, c:c + n], lhsT,
                                         Kg[rows, c:c + n], start=True, stop=True,
                                         inc=(TE if last else None))
                                if g == 0:
                                    R.op("tensor", "matmul", smalls[:, qs:qs + 1], lhsT,
                                         nullk_sb[rows, ic:ic + 1], start=True, stop=True,
                                         inc=TE)
                                R.wait("vector", TE)
                                tail0 = (KKs[qs] - 5) * 128
                                if lo <= tail0 and tail0 + 640 <= hi:
                                    R.op("vector", "tensor_tensor",
                                         Sps[:, tail0 - lo: tail0 - lo + 640],
                                         Sps[:, tail0 - lo: tail0 - lo + 640],
                                         mt_sb[:, qs * 640: qs * 640 + 640],
                                         ALU.add)
                                R.op("vector", "reduce_max", mg[:, qs:qs + 1], Sps[:, 0:w],
                                     axis=AX.X, inc=VE)
                                R.wait("vector", VE)
                                if g == 0:
                                    R.op("vector", "tensor_max", mg[:, qs:qs + 1],
                                         mg[:, qs:qs + 1], smalls[:, qs:qs + 1], inc=VE)
                                    R.wait("vector", VE)
                                R.op("vector", "tensor_scalar", sm1[:, qs:qs + 1],
                                     mg[:, qs:qs + 1], scol, -1.0, ALU.mult, op1=ALU.mult,
                                     inc=VE)
                                R.wait("scalar", VE)
                                R.wait("scalar", TE)
                                R.op("scalar", "activation", P[:, 0:w], Sps[:, 0:w], AF.Exp,
                                     bias=sm1[:, qs:qs + 1], scale=scol,
                                     accum_out=rg[:, qs:qs + 1], inc=SE)
                                if g == 0:
                                    R.op("scalar", "activation", pnull[:, qs:qs + 1],
                                         smalls[:, qs:qs + 1], AF.Exp,
                                         bias=sm1[:, qs:qs + 1], scale=scol, inc=SE)
                                    R.wait("vector", SE)
                                    R.op("vector", "tensor_add", rg[:, qs:qs + 1],
                                         rg[:, qs:qs + 1], pnull[:, qs:qs + 1], inc=VE)
                                R.wait("tensor", SE)
                                nk = (w + 127) // 128
                                for kb in range(0, nk, 8):
                                    ke = min(kb + 8, nk)
                                    R.wait("tensor", VE)
                                    for kt in range(kb, ke):
                                        R.op("tensor", "transpose",
                                             Tpack[:, (kt - kb) * 128:(kt - kb + 1) * 128],
                                             P[:, kt * 128: min((kt + 1) * 128, w)], ident[:],
                                             inc=(TE if kt == ke - 1 else None))
                                    R.wait("vector", TE)
                                    R.op("vector", "tensor_copy",
                                         PT3[:, kb:ke, qs * 128:(qs + 1) * 128],
                                         Tpack[:, 0:(ke - kb) * 128].rearrange(
                                             "p (k n) -> p k n", n=128), inc=VE)
                            if g == 0:
                                R.wait("tensor", VE)
                                for qs in range(4):
                                    R.op("tensor", "transpose",
                                         Tpack[0:1, qs * 128:(qs + 1) * 128],
                                         pnull[:, qs:qs + 1], ident[:],
                                         inc=(TE if qs == 3 else None))
                                R.wait("vector", TE)
                                R.op("vector", "tensor_copy", pnullT[:],
                                     Tpack[0:1, 0:512], inc=VE)
                            R.wait("tensor", VE)
                            nkg = gW // 128
                            av = []
                            for ktL in range(nkg):
                                ktA = g0 // 128 + ktL
                                qsmin = next((qs for qs in range(4) if KKs[qs] > ktA), None)
                                if qsmin is None:
                                    continue
                                av.append((ktL, qsmin * 128))
                            for i, (ktL, qoff) in enumerate(av):
                                last = (i == len(av) - 1) and g != 0
                                R.op("tensor", "matmul", yps[rows, qoff:512],
                                     Vg[:, ktL * 128 + hh * 64: ktL * 128 + hh * 64 + 64],
                                     PT3[:, ktL, qoff:512],
                                     start=(i == 0), stop=last,
                                     inc=(TE if last else None))
                            if g == 0:
                                R.op("tensor", "matmul", yps[rows, 0:512],
                                     nullv_sb[0:1, h * 64: h * 64 + 64], pnullT[:],
                                     start=False, stop=True, inc=TE)
                            R.wait("vector", TE)
                            R.wait("vector", SE)
                            if g == 0:
                                R.op("vector", "tensor_copy", y_acc[rows, :], yps[rows, :])
                                R.op("vector", "tensor_copy", r_acc[:], rg[:])
                                R.op("vector", "tensor_copy", m_acc[:], mg[:], inc=VE)
                                R.wait("tensor", VE)
                            else:
                                R.op("vector", "tensor_max", sm2[:], m_acc[:], mg[:], inc=VE)
                                R.wait("vector", VE)
                                R.op("vector", "tensor_sub", sm1[:], m_acc[:], sm2[:])
                                R.op("vector", "tensor_sub", mg[:], mg[:], sm2[:])
                                R.op("vector", "tensor_copy", m_acc[:], sm2[:])
                                R.op("vector", "tensor_scalar", sm1[:], sm1[:], scol, None,
                                     ALU.mult)
                                R.op("vector", "tensor_scalar", mg[:], mg[:], scol, None,
                                     ALU.mult, inc=VE)
                                R.wait("scalar", VE)
                                R.op("scalar", "activation", sm1[:], sm1[:], AF.Exp)
                                R.op("scalar", "activation", mg[:], mg[:], AF.Exp, inc=SE)
                                R.wait("vector", SE)
                                R.op("vector", "tensor_mul", r_acc[:], r_acc[:], sm1[:])
                                R.op("vector", "tensor_mul", rg[:], rg[:], mg[:], inc=VE)
                                R.wait("vector", VE)
                                R.op("vector", "tensor_add", r_acc[:], r_acc[:], rg[:],
                                     inc=VE)
                                R.wait("tensor", VE)
                                for rowt, srcontainter in ((row_a, sm1), (row_b, mg)):
                                    for qs in range(4):
                                        R.op("tensor", "transpose",
                                             smalls[0:1, qs * 128:(qs + 1) * 128],
                                             srcontainter[:, qs:qs + 1], ident_f[:],
                                             inc=(TE if qs == 3 else None))
                                    R.wait("vector", TE)
                                    R.op("vector", "tensor_copy",
                                         rowt[:], smalls[0:1, 0:512], inc=VE)
                                    R.wait("tensor", VE)
                                R.op("tensor", "matmul", bc[:, 0:512],
                                     ones_f[:], row_a[:],
                                     start=True, stop=True, inc=TE)
                                R.wait("vector", TE)
                                R.op("vector", "tensor_copy", rb_sb[:],
                                     bc[:, 0:512], inc=VE)
                                R.wait("tensor", VE)
                                R.op("tensor", "matmul", bc[:, 0:512],
                                     ones_f[:], row_b[:],
                                     start=True, stop=True, inc=TE)
                                R.wait("vector", TE)
                                R.op("vector", "tensor_copy", tmp_y[:], bc[:, 0:512])
                                R.op("vector", "tensor_mul", y_acc[rows, :], y_acc[rows, :],
                                     rb_sb[rows, :], inc=VE)
                                R.wait("vector", VE)
                                R.op("vector", "tensor_mul", tmp_y[rows, :], yps[rows, :],
                                     tmp_y[rows, :], inc=VE)
                                R.wait("vector", VE)
                                R.op("vector", "tensor_add", y_acc[rows, :], y_acc[rows, :],
                                     tmp_y[rows, :], inc=VE)
                                R.wait("tensor", VE)
                        R.op("vector", "reciprocal", r_acc[:], r_acc[:], inc=VE)
                        R.wait("tensor", VE)
                        for qs in range(4):
                            R.op("tensor", "transpose", smalls[0:1, qs * 128:(qs + 1) * 128],
                                 r_acc[:, qs:qs + 1], ident_f[:],
                                 inc=(TE if qs == 3 else None))
                        R.wait("vector", TE)
                        R.op("vector", "tensor_copy", row_a[:], smalls[0:1, 0:512],
                             inc=VE)
                        R.wait("tensor", VE)
                        R.op("tensor", "matmul", bc[:, 0:512], ones_f[:],
                             row_a[:], start=True, stop=True, inc=TE)
                        R.wait("vector", TE)
                        R.op("vector", "tensor_copy", rb_sb[:], bc[:, 0:512], inc=VE)
                        R.wait("vector", VE)
                        R.op("vector", "tensor_mul", ynT[rows, ic * 512:(ic + 1) * 512],
                             y_acc[rows, :], rb_sb[rows, :], inc=VE)
                        R.wait("tensor", VE)

                if debug:
                    R.wait("gpsimd", VE)
                    R.dma("gpsimd", dbg_yn[qb], ynT[:], OT)
                R.wait("tensor", GD)
                for ts in range(4):
                    tt = qb * 4 + ts
                    R.wait("scalar", VE)
                    for half in range(2):
                        for ic in range(8):
                            R.op("tensor", "matmul", bc[:, 0:512],
                                 ynT[:, ic * 512 + ts * 128: ic * 512 + ts * 128 + 128],
                                 wo_sb[:, ic * 1024 + half * 512: ic * 1024 + half * 512 + 512],
                                 start=(ic == 0), stop=(ic == 7),
                                 inc=(TE if ic == 7 else None))
                        R.wait("scalar", TE)
                        R.op("scalar", "copy", attn_f32[:, half * 512:(half + 1) * 512],
                             bc[:, 0:512], inc=SE)
                        R.wait("tensor", SE)
                    R.wait("vector", SE)
                    emit_ln_stats(R, lt, attn_f32[:], 1024, VE, SE)
                    R.op("vector", "tensor_scalar", attn_f32[:], attn_f32[:], lt.negm[:],
                         lt.rstd[:], ALU.add, op1=ALU.mult, inc=VE)
                    R.wait("vector", VE)
                    R.op("vector", "tensor_add", x1b[:], attn_f32[:],
                         xq_sb[:, tt * 1024:(tt + 1) * 1024], inc=VE)
                    R.wait("gpsimd", VE)
                    R.dma("gpsimd", x1out[tt * 128:(tt + 1) * 128, :], x1b[:], OT)
                    R.wait("vector", OT)

            if debug:
                R.dma("gpsimd", dbg_k, kT_d.ap(), OT)
                R.dma("gpsimd", dbg_v, v_d.ap(), OT)
                R.dma("gpsimd", dbg_q, qT[:], OT)
            R.wait("gpsimd", OT)
            R.wait("gpsimd", SP)
            R.flush(block)
    return nc


def make_mtail(par):
    tri = np.triu(np.full((128, 128), NEG, np.float32), 1)
    zero = np.zeros((128, 128), np.float32)
    allm = np.full((128, 128), NEG, np.float32)
    out = np.empty((16, 5, 128, 128), np.float32)
    for j in range(16):
        A = OWN[par][j // 4]
        V = 4 * A + (j % 4) + 1
        for s in range(5):
            kt = KK(j) - 5 + s
            out[j, s] = zero if kt < V - 1 else (tri if kt == V - 1 else allm)
    return out


def stage_A(inputs):
    x = np.ascontiguousarray(inputs["L0_attn_x"], np.float32)
    nk = np.ascontiguousarray(inputs["L0_attn_null_k"], np.float32).reshape(16, 64)
    nv = np.ascontiguousarray(inputs["L0_attn_null_v"], np.float32).reshape(16, 64)
    nullk_p = np.ascontiguousarray(
        nk.reshape(8, 2, 64).transpose(1, 2, 0).reshape(128, 8))
    nullv_r = np.ascontiguousarray(nv.reshape(1, 1024))
    lsc_r = np.ascontiguousarray(
        inputs["L0_attn_logit_scale"], np.float32).reshape(1, 16)
    ident = np.eye(128, dtype=np.float32)
    mt = [make_mtail(0), make_mtail(1)]
    maps = []
    for c in range(NCORES):
        b, par = c // 2, c % 2
        rows = own_rows(par)
        maps.append({
            "xfull": x[b],
            "xq": np.ascontiguousarray(x[b][rows]),
            "wq": np.ascontiguousarray(inputs["L0_attn_Wq"], np.float32),
            "wk": np.ascontiguousarray(inputs["L0_attn_Wk"], np.float32),
            "wv": np.ascontiguousarray(inputs["L0_attn_Wv"], np.float32),
            "wo": np.ascontiguousarray(inputs["L0_attn_Wo"], np.float32),
            "nullk_p": nullk_p, "nullv_r": nullv_r, "lsc_r": lsc_r,
            "ident_in": ident, "mtail": mt[par],
        })
    return maps


def gather_own(results, key):
    out = np.empty((4, T, D), np.float32)
    for c in range(NCORES):
        b, par = c // 2, c % 2
        out[b][own_rows(par)] = results[c][key]
    return out


_NC_CACHE = {}


def run_A(inputs, debug=False):
    key = ("A", debug)
    if key not in _NC_CACHE:
        _NC_CACHE[key] = build_launch_A(debug)
    maps = stage_A(inputs)
    res = run_bass_kernel_spmd(_NC_CACHE[key], maps, core_ids=list(range(NCORES)))
    if debug:
        return gather_own(res.results, "x1"), res.results
    return gather_own(res.results, "x1")



def build_launch_B(fc=None):
    pre = "" if fc is None else "b_"
    S127 = 1.0 / 127.0
    if fc is None:
        nc = bass.Bass("TRN2", target_bir_lowering=False)
        R = Rec(nc)
        x1i = nc.dram_tensor("x1i", [OWNT, D], F32, kind="ExternalInput").ap()
        wgT = nc.dram_tensor("wgT", [D, 3072], I32, kind="ExternalInput").ap()
        wuT = nc.dram_tensor("wuT", [D, 3072], I32, kind="ExternalInput").ap()
        wdT = nc.dram_tensor("wdT", [3072, D], I32, kind="ExternalInput").ap()
        ident_in = nc.dram_tensor("ident_in", [128, 128], F32, kind="ExternalInput").ap()
        x2out = nc.dram_tensor("x2", [OWNT, D], F32, kind="ExternalOutput").ap()
    else:
        nc, R = fc["nc"], fc["R"]
        io = fc["io"]
        x1i = fc["x1_d"].ap()
        wgT, wuT, wdT = io["b_wgT"], io["b_wuT"], io["b_wdT"]
        ident_in = io["ident_in"]
        x2out = fc["x2_d"].ap()

    with ExitStack() as CTX:
        sb = lambda n, s, dt: CTX.enter_context(nc.sbuf_tensor(pre + n, s, dt))
        ps = lambda n, s, dt=F32: CTX.enter_context(nc.psum_tensor(pre + n, s, dt))
        if fc is None:
            sem = lambda n: Sem(CTX.enter_context(nc.semaphore(n)))
            LD = sem("LD"); GD = sem("GD")
            TE = sem("TE"); VE = sem("VE"); SE = sem("SE"); OT = sem("OT")
        else:
            sm = fc["sems"]
            LD, GD = sm["LD"], sm["GD"]
            TE, VE, SE, OT = sm["TE"], sm["VE"], sm["SE"], sm["OT"]

        wg_sb = sb("wg_sb", [128, 8 * 3072], BF16)
        wu_sb = sb("wu_sb", [128, 8 * 3072], BF16)
        wd_sb = [sb(f"wd_sb{i}", [128, 24 * 128], BF16) for i in range(2)]
        xc = sb("xc", [128, 4 * 1024], F32)
        xhat = [sb(f"xhat{i}", [128, 1024], BF16) for i in range(2)]
        xTc = sb("xTc", [128, 8 * 512], BF16)
        Bt = sb("Bt", [128, 24 * 512], BF16)
        asb = sb("asb", [128, 512], BF16)
        ytb = sb("ytb", [128, 8 * 512], BF16)
        yf = sb("yf", [128, 1024], F32)
        x2b = sb("x2b", [128, 1024], F32)
        ident = sb("ident", [128, 128], BF16)
        lt = LnTiles(nc, CTX, tag=pre)
        pps = ps("pps", [128, 512])
        pp2 = ps("pp2", [128, 512])
        Tpack = ps("Tpack", [128, 1024], BF16)
        dps = ps("dps", [128, 512])

        block = fc["block"] if fc else CTX.enter_context(nc.Block())

        R.dma("gpsimd", ident[:], ident_in, GD)
        R.dma("gpsimd", wg_sb[:].rearrange("p (c n) -> p c n", n=3072),
              wgT.rearrange("(c p) n -> p c n", p=128), GD)
        R.dma("gpsimd", wu_sb[:].rearrange("p (c n) -> p c n", n=3072),
              wuT.rearrange("(c p) n -> p c n", p=128), GD)
        R.wait("tensor", GD)
        wdT3 = wdT.rearrange("(c p) n -> p c n", p=128)

        def ln_transpose_tile(src_f32, xhat_t, xT_t, t4):
            emit_ln_stats(R, lt, src_f32, 1024, VE, SE)
            R.wait("vector", TE)
            R.op("vector", "tensor_scalar", xhat_t[:], src_f32, lt.negm[:],
                 lt.rstd[:], ALU.add, op1=ALU.mult, inc=VE)
            R.wait("tensor", VE)
            xT3 = xT_t[:].rearrange("p (c n) -> p c n", n=512)
            for half in range(2):
                for c4 in range(4):
                    c = half * 4 + c4
                    R.op("tensor", "transpose", Tpack[:, c4 * 128:(c4 + 1) * 128],
                         xhat_t[:, c * 128:(c + 1) * 128], ident[:],
                         inc=(TE if c4 == 3 else None))
                R.wait("vector", TE)
                R.op("vector", "tensor_copy",
                     xT3[:, half * 4:half * 4 + 4, t4 * 128:(t4 + 1) * 128],
                     Tpack[:, 0:512].rearrange("p (k n) -> p k n", n=128), inc=VE)
                R.wait("tensor", VE)

        for qc in range(4):
            R.wait("sync", VE)
            R.wait("sync", OT)
            R.dma("sync", xc[:].rearrange("p (t n) -> p t n", n=1024),
                  x1i[qc * 512:(qc + 1) * 512, :].rearrange("(t p) n -> p t n", p=128), LD)
            R.wait("vector", LD)
            for t4 in range(4):
                ln_transpose_tile(xc[:, t4 * 1024:(t4 + 1) * 1024], xhat[t4 % 2], xTc, t4)
            for hc in range(24):
                R.wait("tensor", SE)
                R.wait("tensor", VE)
                for kc in range(8):
                    R.op("tensor", "matmul", pps[:, 0:512],
                         wg_sb[:, kc * 3072 + hc * 128: kc * 3072 + hc * 128 + 128],
                         xTc[:, kc * 512:(kc + 1) * 512],
                         start=(kc == 0), stop=(kc == 7))
                for kc in range(8):
                    R.op("tensor", "matmul", pp2[:, 0:512],
                         wu_sb[:, kc * 3072 + hc * 128: kc * 3072 + hc * 128 + 128],
                         xTc[:, kc * 512:(kc + 1) * 512],
                         start=(kc == 0), stop=(kc == 7),
                         inc=(TE if kc == 7 else None))
                R.wait("scalar", TE)
                R.op("scalar", "activation", asb[:], pps[:, 0:512], AF.Silu,
                     scale=S127, inc=SE)
                R.wait("vector", SE)
                R.op("vector", "tensor_tensor", Bt[:, hc * 512:(hc + 1) * 512],
                     pp2[:, 0:512], asb[:], ALU.mult, inc=VE)
            R.wait("tensor", VE)
            for oc in range(8):
                wdb = wd_sb[oc % 2]
                R.wait("gpsimd", TE)
                R.dma("gpsimd", wdb[:].rearrange("p (c n) -> p c n", n=128),
                      wdT3[:, :, oc * 128:(oc + 1) * 128], GD)
                R.wait("tensor", GD)
                R.wait("tensor", SE)
                for hc in range(24):
                    R.op("tensor", "matmul", dps[:, 0:512],
                         wdb[:, hc * 128:(hc + 1) * 128],
                         Bt[:, hc * 512:(hc + 1) * 512],
                         start=(hc == 0), stop=(hc == 23),
                         inc=(TE if hc == 23 else None))
                R.wait("scalar", TE)
                R.op("scalar", "mul", ytb[:, oc * 512:(oc + 1) * 512], dps[:, 0:512],
                     S127 * S127, inc=SE)
            for ts in range(4):
                R.wait("tensor", SE)
                R.wait("tensor", VE)
                for half in range(2):
                    for c4 in range(4):
                        oc = half * 4 + c4
                        R.op("tensor", "transpose", Tpack[:, c4 * 128:(c4 + 1) * 128],
                             ytb[:, oc * 512 + ts * 128: oc * 512 + ts * 128 + 128],
                             ident[:], inc=(TE if c4 == 3 else None))
                    R.wait("vector", TE)
                    R.op("vector", "tensor_copy",
                         yf[:, half * 512:(half + 1) * 512],
                         Tpack[:, 0:512], inc=VE)
                    R.wait("tensor", VE)
                emit_ln_stats(R, lt, yf[:], 1024, VE, SE)
                R.op("vector", "tensor_scalar", yf[:], yf[:], lt.negm[:],
                     lt.rstd[:], ALU.add, op1=ALU.mult, inc=VE)
                R.wait("vector", VE)
                R.wait("vector", OT)
                R.op("vector", "tensor_add", x2b[:], yf[:],
                     xc[:, ts * 1024:(ts + 1) * 1024], inc=VE)
                R.wait("gpsimd", VE)
                tt = qc * 4 + ts
                R.dma("gpsimd", x2out[tt * 128:(tt + 1) * 128, :], x2b[:], OT)
        R.wait("gpsimd", OT)
        R.flush(block)
    return nc


def stage_B(x1, inputs):
    wgT = np.ascontiguousarray(inputs["L0_ffn_w_gate"].T)
    wuT = np.ascontiguousarray(inputs["L0_ffn_w_up"].T)
    wdT = np.ascontiguousarray(inputs["L0_ffn_w_down"].T)
    ident = np.eye(128, dtype=np.float32)
    maps = []
    for c in range(NCORES):
        b, par = c // 2, c % 2
        maps.append({
            "x1i": np.ascontiguousarray(x1[b][own_rows(par)]),
            "wgT": wgT, "wuT": wuT, "wdT": wdT, "ident_in": ident,
        })
    return maps


def run_B(x1, inputs):
    if "B" not in _NC_CACHE:
        _NC_CACHE["B"] = build_launch_B()
    res = run_bass_kernel_spmd(_NC_CACHE["B"], stage_B(x1, inputs),
                               core_ids=list(range(NCORES)))
    return gather_own(res.results, "x2")



def build_launch_C(maxphase=3, fc=None):
    pre = "" if fc is None else "c_"
    if fc is None:
        nc = bass.Bass("TRN2", target_bir_lowering=False)
        R = Rec(nc)
        xfull = nc.dram_tensor("xfull", [T, D], F32, kind="ExternalInput").ap()
        xq = nc.dram_tensor("xq", [OWNT, D], F32, kind="ExternalInput").ap()
        wqkv = nc.dram_tensor("wqkv", [D, 3072], F32, kind="ExternalInput").ap()
        wf2 = nc.dram_tensor("wf2", [128, 128], F32, kind="ExternalInput").ap()
        bf_p = nc.dram_tensor("bf_p", [128, 1], F32, kind="ExternalInput").ap()
        bf_r = nc.dram_tensor("bf_r", [1, 128], F32, kind="ExternalInput").ap()
        wout = nc.dram_tensor("wout", [D, D], F32, kind="ExternalInput").ap()
        bout_r = nc.dram_tensor("bout_r", [1, D], F32, kind="ExternalInput").ap()
        ident_in = nc.dram_tensor("ident_in", [128, 128], F32, kind="ExternalInput").ap()
        x3out = nc.dram_tensor("x3", [OWNT, D], F32, kind="ExternalOutput").ap()
    else:
        nc, R = fc["nc"], fc["R"]
        io = fc["io"]
        xfull = fc["x2g"].ap()
        xq = fc["x2_d"].ap()
        wqkv, wf2, bf_p, bf_r = io["c_wqkv"], io["c_wf2"], io["c_bf_p"], io["c_bf_r"]
        wout, bout_r = io["c_wout"], io["c_bout_r"]
        ident_in = io["ident_in"]
        x3out = fc["x3_d"].ap()

    with ExitStack() as CTX:
        sb = lambda n, s, dt: CTX.enter_context(nc.sbuf_tensor(pre + n, s, dt))
        ps = lambda n, s, dt=F32: CTX.enter_context(nc.psum_tensor(pre + n, s, dt))
        if fc is None:
            sem = lambda n: Sem(CTX.enter_context(nc.semaphore(n)))
            LD = sem("LD"); GD = sem("GD")
            TE = sem("TE"); VE = sem("VE"); SE = sem("SE"); OT = sem("OT")
        else:
            sm = fc["sems"]
            LD, GD = sm["LD"], sm["GD"]
            TE, VE, SE, OT = sm["TE"], sm["VE"], sm["SE"], sm["OT"]

        wq_sb = sb("wq_sb", [128, 8 * 3072], BF16)
        wout_sb = sb("wout_sb", [128, 8 * 1024], BF16)
        wf_sb = sb("wf_sb", [128, 128], BF16)
        bfp_sb = sb("bfp_sb", [128, 1], F32)
        bfr_sb = sb("bfr_sb", [1, 128], BF16)
        bout_sb = sb("bout_sb", [1, D], BF16)
        ident = sb("ident", [128, 128], BF16)
        ones_b = sb("ones_b", [1, 128], BF16)
        ones_c = sb("ones_c", [128, 1], BF16)
        xb = [sb(f"xb{i}", [128, 1024], F32) for i in range(2)]
        xhat = [sb(f"xhat{i}", [128, 1024], BF16) for i in range(2)]
        xTc = sb("xTc", [128, 8 * 512], BF16)
        kTp = sb("kTp", [128, 8 * 512], BF16)
        Kp = sb("Kp", [128, 4 * 1024], BF16)
        Vp = sb("Vp", [128, 4 * 1024], BF16)
        kv_acc = sb("kv_acc", [128, 8 * 65], F32)
        kvb = sb("kvb", [128, 16 * 64], BF16)
        ksb = sb("ksb", [128, 16], BF16)
        qpT = sb("qpT", [128, 8 * 2048], BF16)
        nrm = sb("nrm", [128, 16], F32)
        obn = sb("obn", [128, 1024], BF16)
        oT = sb("oT", [128, 8 * 512], BF16)
        xqc = sb("xqc", [128, 4 * 1024], F32)
        af = sb("af", [128, 1024], F32)
        x3b = sb("x3b", [128, 1024], F32)
        lt = LnTiles(nc, CTX, tag=pre)
        pps = ps("pps", [128, 512])
        pp2 = ps("pp2", [128, 512])
        Tpack = ps("Tpack", [128, 1024], BF16)
        fps = ps("fps", [128, 128])
        kvps = ps("kvps", [64, 65])
        ops1 = ps("ops1", [128, 512])
        ops2 = ps("ops2", [128, 512])
        smalls = ps("smalls", [128, 512])

        block = fc["block"] if fc else CTX.enter_context(nc.Block())

        R.dma("gpsimd", ident[:], ident_in, GD)
        R.dma("gpsimd", wf_sb[:], wf2, GD)
        R.dma("gpsimd", bfr_sb[:], bf_r, GD)
        R.dma("gpsimd", bout_sb[:], bout_r, GD)
        R.dma("sync", bfp_sb[:], bf_p, LD)
        R.dma("gpsimd", wq_sb[:].rearrange("p (c n) -> p c n", n=3072),
              wqkv.rearrange("(c p) n -> p c n", p=128), GD)
        R.dma("gpsimd", wout_sb[:].rearrange("p (c n) -> p c n", n=1024),
              wout.rearrange("(c p) n -> p c n", p=128), GD)
        R.op("vector", "memset", ones_b[:], 1.0)
        R.op("vector", "memset", ones_c[:], 1.0)
        R.op("vector", "memset", kv_acc[:], 0.0, inc=VE)
        R.wait("tensor", GD)
        R.wait("tensor", LD)
        R.wait("tensor", VE)

        def ln_transpose_tile(src_f32, xhat_t, xT_t, t4):
            emit_ln_stats(R, lt, src_f32, 1024, VE, SE)
            R.wait("vector", TE)
            R.op("vector", "tensor_scalar", xhat_t[:], src_f32, lt.negm[:],
                 lt.rstd[:], ALU.add, op1=ALU.mult, inc=VE)
            R.wait("tensor", VE)
            xT3 = xT_t[:].rearrange("p (c n) -> p c n", n=512)
            for half in range(2):
                for c4 in range(4):
                    c = half * 4 + c4
                    R.op("tensor", "transpose", Tpack[:, c4 * 128:(c4 + 1) * 128],
                         xhat_t[:, c * 128:(c + 1) * 128], ident[:],
                         inc=(TE if c4 == 3 else None))
                R.wait("vector", TE)
                R.op("vector", "tensor_copy",
                     xT3[:, half * 4:half * 4 + 4, t4 * 128:(t4 + 1) * 128],
                     Tpack[:, 0:512].rearrange("p (k n) -> p k n", n=128), inc=VE)
                R.wait("tensor", VE)

        for tc in range(8):
            for t4 in range(4):
                tt = tc * 4 + t4
                R.wait("sync", VE)
                R.dma("sync", xb[tt % 2][:], xfull[tt * 128:(tt + 1) * 128, :], LD)
                R.wait("vector", LD)
                ln_transpose_tile(xb[tt % 2][:], xhat[tt % 2], xTc, t4)
            for ic in range(8):
                R.wait("tensor", SE)
                for kc in range(8):
                    R.op("tensor", "matmul", pps[:, 0:512],
                         wq_sb[:, kc * 3072 + 1024 + ic * 128: kc * 3072 + 1024 + ic * 128 + 128],
                         xTc[:, kc * 512:(kc + 1) * 512],
                         start=(kc == 0), stop=(kc == 7),
                         inc=(TE if kc == 7 else None))
                R.wait("vector", TE)
                R.op("vector", "tensor_copy", kTp[:, ic * 512:(ic + 1) * 512],
                     pps[:, 0:512], inc=VE)
                R.wait("tensor", VE)
            for t4 in range(4):
                for half in range(2):
                    bank = pps if half == 0 else pp2
                    for kc in range(8):
                        R.op("tensor", "matmul", bank[:, 0:512],
                             xTc[:, kc * 512 + t4 * 128: kc * 512 + t4 * 128 + 128],
                             wq_sb[:, kc * 3072 + 2048 + half * 512: kc * 3072 + 2048 + half * 512 + 512],
                             start=(kc == 0), stop=(kc == 7),
                             inc=(TE if kc == 7 else None))
                R.wait("vector", TE)
                R.op("vector", "tensor_copy", Vp[:, t4 * 1024: t4 * 1024 + 512], pps[:, 0:512])
                R.op("vector", "tensor_copy", Vp[:, t4 * 1024 + 512: t4 * 1024 + 1024],
                     pp2[:, 0:512], inc=VE)
                R.wait("tensor", VE)
            for ic in range(8):
                for t4 in range(4):
                    R.wait("tensor", SE)
                    R.op("tensor", "matmul", fps[:, 0:128],
                         kTp[:, ic * 512 + t4 * 128: ic * 512 + t4 * 128 + 128],
                         wf_sb[:, 0:128], start=True, stop=False)
                    R.op("tensor", "matmul", fps[:, 0:128],
                         ones_b[0:1, 0:128], bfr_sb[:],
                         start=False, stop=True, inc=TE)
                    R.wait("scalar", TE)
                    R.op("scalar", "activation",
                         Kp[:, t4 * 1024 + ic * 128: t4 * 1024 + ic * 128 + 128],
                         fps[:, 0:128], AF.Relu, inc=SE)
            R.wait("tensor", SE)
            for ic in range(8):
                for hh in range(2):
                    h = 2 * ic + hh
                    rows = slice(64 * hh, 64 * hh + 64)
                    for t4 in range(4):
                        R.op("tensor", "matmul", kvps[:, 0:64],
                             Kp[:, t4 * 1024 + h * 64: t4 * 1024 + h * 64 + 64],
                             Vp[:, t4 * 1024 + h * 64: t4 * 1024 + h * 64 + 64],
                             start=(t4 == 0), stop=False)
                        R.op("tensor", "matmul", kvps[:, 64:65],
                             Kp[:, t4 * 1024 + h * 64: t4 * 1024 + h * 64 + 64],
                             ones_c[:], start=False, stop=(t4 == 3),
                             inc=(TE if t4 == 3 else None))
                    R.wait("vector", TE)
                    R.op("vector", "tensor_tensor",
                         kv_acc[rows, ic * 65:(ic + 1) * 65],
                         kv_acc[rows, ic * 65:(ic + 1) * 65],
                         kvps[:, 0:65], ALU.add, inc=VE)
                    R.wait("tensor", VE)

        if maxphase < 2:
            R.wait("gpsimd", VE)
            R.flush(block)
            return nc
        R.op("vector", "memset", kvb[:], 0.0)
        R.op("vector", "memset", ksb[:], 0.0, inc=VE)
        R.wait("vector", VE)
        kva = kv_acc[:].rearrange("p (c n) -> p c n", n=65)
        kvb3 = kvb[:].rearrange("p (c n) -> p c n", n=64)
        for ic in range(8):
            for hh in range(2):
                h = 2 * ic + hh
                rows = slice(64 * hh, 64 * hh + 64)
                R.op("vector", "tensor_copy", kvb3[rows, h, :], kva[rows, ic, 0:64])
                R.op("vector", "tensor_copy", ksb[rows, h:h + 1], kva[rows, ic, 64:65],
                     inc=(VE if h == 15 else None))

        for qc in range(4):
            for t4 in range(4):
                tt = qc * 4 + t4
                R.wait("sync", VE)
                R.dma("sync", xb[tt % 2][:], xq[tt * 128:(tt + 1) * 128, :], LD)
                R.wait("vector", LD)
                ln_transpose_tile(xb[tt % 2][:], xhat[tt % 2], xTc, t4)
            for ic in range(8):
                R.wait("tensor", SE)
                for kc in range(8):
                    R.op("tensor", "matmul", pps[:, 0:512],
                         wq_sb[:, kc * 3072 + ic * 128: kc * 3072 + ic * 128 + 128],
                         xTc[:, kc * 512:(kc + 1) * 512],
                         start=(kc == 0), stop=(kc == 7),
                         inc=(TE if kc == 7 else None))
                R.wait("vector", TE)
                R.op("vector", "tensor_copy", kTp[:, ic * 512:(ic + 1) * 512],
                     pps[:, 0:512], inc=VE)
                R.wait("tensor", VE)
                R.op("tensor", "matmul", pp2[:, 0:512], wf_sb[:, 0:128],
                     kTp[:, ic * 512:(ic + 1) * 512],
                     start=True, stop=True, inc=TE)
                R.wait("scalar", TE)
                R.op("scalar", "activation",
                     qpT[:, ic * 2048 + qc * 512: ic * 2048 + (qc + 1) * 512],
                     pp2[:, 0:512], AF.Relu, bias=bfp_sb[:], inc=SE)
                R.wait("tensor", SE)

        if maxphase < 2.2:
            R.wait("gpsimd", VE)
            R.flush(block)
            return nc
        for qc in range(4):
            R.wait("sync", OT)
            R.wait("sync", VE)
            R.dma("sync", xqc[:].rearrange("p (t n) -> p t n", n=1024),
                  xq[qc * 512:(qc + 1) * 512, :].rearrange("(t p) n -> p t n", p=128), LD)
            for ts in range(4):
                R.wait("tensor", SE)
                R.wait("tensor", VE)
                for ic in range(8):
                    lhsT = qpT[:, ic * 2048 + qc * 512 + ts * 128:
                               ic * 2048 + qc * 512 + ts * 128 + 128]
                    for hh in range(2):
                        h = 2 * ic + hh
                        bank = ops1 if h < 8 else ops2
                        col = (h % 8) * 64
                        R.op("tensor", "matmul", bank[:, col:col + 64], lhsT,
                             kvb[:, h * 64:(h + 1) * 64], start=True, stop=True)
                        R.op("tensor", "matmul", smalls[:, h:h + 1], lhsT,
                             ksb[:, h:h + 1], start=True, stop=True,
                             inc=(TE if h == 15 else None))
                R.wait("vector", TE)
                if maxphase < 2.22:
                    continue
                R.op("vector", "tensor_scalar", nrm[:], smalls[:, 0:16], 1e-6, None,
                     ALU.add, inc=VE)
                R.wait("vector", VE)
                R.op("vector", "reciprocal", nrm[:], nrm[:], inc=VE)
                R.wait("vector", VE)
                if maxphase < 2.24:
                    continue
                for h in range(16):
                    bank = ops1 if h < 8 else ops2
                    col = (h % 8) * 64
                    R.op("vector", "tensor_scalar", obn[:, h * 64:(h + 1) * 64],
                         bank[:, col:col + 64], nrm[:, h:h + 1], None, ALU.mult,
                         inc=(VE if h == 15 else None))
                R.wait("tensor", VE)
                if maxphase < 2.5:
                    continue
                for half in range(2):
                    for c4 in range(4):
                        c = half * 4 + c4
                        R.op("tensor", "transpose", Tpack[:, c4 * 128:(c4 + 1) * 128],
                             obn[:, c * 128:(c + 1) * 128], ident[:],
                             inc=(TE if c4 == 3 else None))
                    R.wait("vector", TE)
                    R.op("vector", "tensor_copy",
                         oT[:].rearrange("p (c n) -> p c n", n=512)[
                             :, half * 4:half * 4 + 4, ts * 128:(ts + 1) * 128],
                         Tpack[:, 0:512].rearrange("p (k n) -> p k n", n=128), inc=VE)
                    R.wait("tensor", VE)
            if maxphase < 2.75:
                continue
            R.wait("vector", LD)
            for ts in range(4):
                tt = qc * 4 + ts
                R.wait("scalar", VE)
                for half in range(2):
                    for ic in range(8):
                        R.op("tensor", "matmul", pps[:, 0:512],
                             oT[:, ic * 512 + ts * 128: ic * 512 + ts * 128 + 128],
                             wout_sb[:, ic * 1024 + half * 512: ic * 1024 + half * 512 + 512],
                             start=(ic == 0), stop=False)
                    R.op("tensor", "matmul", pps[:, 0:512], ones_b[0:1, 0:128],
                         bout_sb[0:1, half * 512:(half + 1) * 512],
                         start=False, stop=True, inc=TE)
                    R.wait("scalar", TE)
                    R.op("scalar", "copy", af[:, half * 512:(half + 1) * 512],
                         pps[:, 0:512], inc=SE)
                    R.wait("tensor", SE)
                R.wait("vector", SE)
                emit_ln_stats(R, lt, af[:], 1024, VE, SE)
                R.op("vector", "tensor_scalar", af[:], af[:], lt.negm[:],
                     lt.rstd[:], ALU.add, op1=ALU.mult, inc=VE)
                R.wait("vector", VE)
                R.wait("vector", OT)
                R.op("vector", "tensor_add", x3b[:], af[:],
                     xqc[:, ts * 1024:(ts + 1) * 1024], inc=VE)
                R.wait("gpsimd", VE)
                R.dma("gpsimd", x3out[tt * 128:(tt + 1) * 128, :], x3b[:], OT)
        R.wait("gpsimd", OT)
        R.flush(block)
    return nc


def stage_C(x2, inputs):
    wf = np.ascontiguousarray(inputs["L1_attn_W_f"], np.float32)
    wf2 = np.zeros((128, 128), np.float32)
    wf2[0:64, 0:64] = wf
    wf2[64:128, 64:128] = wf
    bf = np.ascontiguousarray(inputs["L1_attn_b_f"], np.float32)
    bf_p = np.concatenate([bf, bf]).reshape(128, 1)
    bf_r = np.concatenate([bf, bf]).reshape(1, 128)
    bout_r = np.ascontiguousarray(inputs["L1_attn_b_out"], np.float32).reshape(1, D)
    ident = np.eye(128, dtype=np.float32)
    maps = []
    for c in range(NCORES):
        b, par = c // 2, c % 2
        maps.append({
            "xfull": x2[b],
            "xq": np.ascontiguousarray(x2[b][own_rows(par)]),
            "wqkv": np.ascontiguousarray(inputs["L1_attn_W_qkv"], np.float32),
            "wf2": wf2, "bf_p": bf_p, "bf_r": bf_r,
            "wout": np.ascontiguousarray(inputs["L1_attn_W_out"], np.float32),
            "bout_r": bout_r, "ident_in": ident,
        })
    return maps


def run_C(x2, inputs, maxphase=3):
    if ("C", maxphase) not in _NC_CACHE:
        _NC_CACHE[("C", maxphase)] = build_launch_C(maxphase)
    _NC_CACHE["C"] = _NC_CACHE[("C", maxphase)]
    res = run_bass_kernel_spmd(_NC_CACHE["C"], stage_C(x2, inputs),
                               core_ids=list(range(NCORES)))
    return gather_own(res.results, "x3")


def kernel(**inputs):
    x1 = run_A(inputs)
    x2 = run_B(x1, inputs)
    x3 = run_C(x2, inputs)
    return x3

def build_launch_D(ln_in, sandwich, debug=False, fc=None, li=0):
    pre = "" if fc is None else f"d{li}_"
    QKS = 8.0
    if fc is None:
        nc = bass.Bass("TRN2", target_bir_lowering=False)
        R = Rec(nc)
        xe = nc.dram_tensor("xe", [4, 1664, D], F32, kind="ExternalInput").ap()
        xq3 = nc.dram_tensor("xq3", [OWNT, D], F32, kind="ExternalInput").ap()
        wqkv = nc.dram_tensor("wqkv", [D, 1536], F32, kind="ExternalInput").ap()
        wo = nc.dram_tensor("wo", [512, D], F32, kind="ExternalInput").ap()
        w1r = nc.dram_tensor("w1r", [D, 5632], F32, kind="ExternalInput").ap()
        w2r = nc.dram_tensor("w2r", [2816, D], F32, kind="ExternalInput").ap()
        b1r = nc.dram_tensor("b1r", [1, 5632], F32, kind="ExternalInput").ap()
        b2r = nc.dram_tensor("b2r", [1, D], F32, kind="ExternalInput").ap()
        lnb_p = nc.dram_tensor("lnb_p", [128, 8], F32, kind="ExternalInput").ap()
        lng_p = nc.dram_tensor("lng_p", [128, 8], F32, kind="ExternalInput").ap()
        qs_r = nc.dram_tensor("qs_r", [1, 64], F32, kind="ExternalInput").ap()
        ks_r = nc.dram_tensor("ks_r", [1, 64], F32, kind="ExternalInput").ap()
        rotq = nc.dram_tensor("rotq", [4, 2, 1024, 512], F32, kind="ExternalInput").ap()
        rotk = nc.dram_tensor("rotk", [4, 2, 1536, 512], F32, kind="ExternalInput").ap()
        msk = nc.dram_tensor("msk", [4, 8, 128, 640], F32, kind="ExternalInput").ap()
        ident_in = nc.dram_tensor("ident_in", [128, 128], F32, kind="ExternalInput").ap()
        zout = nc.dram_tensor("zout", [OWNT, D], F32, kind="ExternalOutput").ap()
    else:
        nc, R = fc["nc"], fc["R"]
        io = fc["io"]
        xe = fc["xe_d"].ap()
        xq3 = fc["x3_d"].ap()
        wqkv, wo = io[pre + "wqkv"], io[pre + "wo"]
        w1r, w2r, b1r, b2r = io[pre + "w1r"], io[pre + "w2r"], io[pre + "b1r"], io[pre + "b2r"]
        lnb_p, lng_p = io[pre + "lnb_p"], io[pre + "lng_p"]
        qs_r, ks_r = io[pre + "qs_r"], io[pre + "ks_r"]
        rotq, rotk, msk = io["d_rotq"], io["d_rotk"], io["d_msk"]
        ident_in = io["ident_in"]
        zout = (fc["z1_d"].ap() if li == 0 else fc["out"])

    xm_d = nc.dram_tensor(pre + "xm_d", [128, 4 * 8 * 1024], F32)
    h2_d = nc.dram_tensor(pre + "h2_d", [128, 4 * 8 * 512], BF16)
    if debug:
        dbg_xm = nc.dram_tensor("dbg_xm", [128, 4 * 8 * 1024], F32, kind="ExternalOutput").ap()
        dbg_h2 = nc.dram_tensor("dbg_h2", [128, 4 * 8 * 512], BF16, kind="ExternalOutput").ap()
        dbg_q = nc.dram_tensor("dbg_q", [128, 4 * 1024], BF16, kind="ExternalOutput").ap()
        dbg_k = nc.dram_tensor("dbg_k", [128, 4 * 1536], BF16, kind="ExternalOutput").ap()
        dbg_v = nc.dram_tensor("dbg_v", [128, 12 * 512], BF16, kind="ExternalOutput").ap()
        dbg_o = nc.dram_tensor("dbg_o", [128, 4 * 1024], BF16, kind="ExternalOutput").ap()
        dbg_zT = nc.dram_tensor("dbg_zT", [128, 8 * 1664], BF16, kind="ExternalOutput").ap()

    with ExitStack() as CTX:
        sb = lambda n, s, dt: CTX.enter_context(nc.sbuf_tensor(pre + n, s, dt))
        ps = lambda stk, n, s, dt=F32: stk.enter_context(nc.psum_tensor(pre + n, s, dt))
        if fc is None:
            sem = lambda n: Sem(CTX.enter_context(nc.semaphore(n)))
            LD = sem("LD"); GD = sem("GD"); SP = sem("SP")
            TE = sem("TE"); VE = sem("VE"); SE = sem("SE"); OT = sem("OT")
        else:
            sm = fc["sems"]
            LD, GD, SP = sm["LD"], sm["GD"], sm["SP"]
            TE, VE, SE, OT = sm["TE"], sm["VE"], sm["SE"], sm["OT"]

        ident = sb("ident", [128, 128], BF16)
        ident_f = sb("ident_f", [128, 128], F32)
        ones_f = sb("ones_f", [1, 128], F32)
        ones_bf = sb("ones_bf", [1, 128], BF16)
        qst = sb("qst", [128, 512], F32)
        kst = sb("kst", [128, 512], F32)
        lng_sb = sb("lng_sb", [128, 8], F32)
        lt = LnTiles(nc, CTX, tag=pre)
        l2r = sb("l2r", [128, 8], F32)
        Tpack = ps(CTX, "Tpack", [128, 1024], BF16)
        smalls = ps(CTX, "smalls", [128, 512], F32)

        block = fc["block"] if fc else CTX.enter_context(nc.Block())

        R.dma("gpsimd", ident[:], ident_in, GD)
        R.dma("sync", ident_f[:], ident_in, LD)
        R.dma("sync", lng_sb[:], lng_p, LD)
        R.op("vector", "memset", ones_f[:], 1.0)
        R.op("vector", "memset", ones_bf[:], 1.0, inc=VE)
        R.wait("tensor", VE)
        for row_ap, dst in ((qs_r, qst), (ks_r, kst)):
            rsb = sb("row_" + dst.name, [1, 64], F32)
            R.dma("sync", rsb[:], row_ap, LD)
            R.wait("tensor", LD)
            R.op("tensor", "matmul", smalls[:, 0:64], ones_f[:], rsb[:],
                 start=True, stop=True, inc=TE)
            R.wait("vector", TE)
            for h in range(8):
                R.op("vector", "tensor_copy", dst[:, h * 64:(h + 1) * 64],
                     smalls[:, 0:64], inc=(VE if h == 7 else None))
            R.wait("tensor", VE)

        with ExitStack() as PH1:
            sb1 = lambda n, s, dt: PH1.enter_context(nc.sbuf_tensor(pre + n, s, dt))
            wq_sb = sb1("wq_sb", [128, 8 * 1536], BF16)
            wo_sb = sb1("wo_sb", [128, 4 * 1024], BF16)
            xb = [sb1(f"xb{i}", [128, 1024], F32) for i in range(2)]
            zf = sb1("zf", [128, 1024], F32)
            zhat = [sb1(f"zhat{i}", [128, 1024], BF16) for i in range(2)]
            zT = sb1("zT", [128, 8 * 1664], BF16)
            qev = sb1("qev", [128, 512], F32)
            q2s = sb1("q2s", [128, 512], F32)
            rotA = [sb1(f"rotA{i}", [128, 512], F32) for i in range(2)]
            rotB = [sb1(f"rotB{i}", [128, 512], F32) for i in range(2)]
            t1b = sb1("t1b", [128, 512], F32)
            t2b = sb1("t2b", [128, 512], F32)
            q3b = sb1("q3b", [128, 512], BF16)
            qfT = sb1("qfT", [128, 4 * 1024], BF16)
            kfT = sb1("kfT", [128, 4 * 1536], BF16)
            vtm = sb1("vtm", [128, 12 * 512], BF16)
            msk_sb = sb1("msk_sb", [128, 8 * 640], BF16)
            P = sb1("P", [128, 640], BF16)
            PT = sb1("PT", [128, 12 * 1024], BF16)
            oT = sb1("oT", [128, 4 * 1024], BF16)
            rg = sb1("rg", [128, 8], F32)
            row_a = sb1("row_a", [1, 1024], F32)
            rb_sb = sb1("rb_sb", [128, 1024], F32)
            xm = [sb1(f"xm{i}", [128, 1024], F32) for i in range(2)]
            h2t = sb1("h2t", [128, 1024], F32)
            h2n = sb1("h2n", [128, 1024], BF16)
            pps = ps(PH1, "pps", [128, 512])
            Sps = ps(PH1, "Sps", [128, 640])
            ybank = ps(PH1, "ybank", [128, 1024])
            bcb = ps(PH1, "bcb", [128, 512])

            R.dma("gpsimd", wq_sb[:].rearrange("p (c n) -> p c n", n=1536),
                  wqkv.rearrange("(c p) n -> p c n", p=128), GD)
            R.dma("gpsimd", wo_sb[:].rearrange("p (c n) -> p c n", n=1024),
                  wo.rearrange("(c p) n -> p c n", p=128), GD)
            R.wait("tensor", GD)

            def hT(kc, c0, w):
                off = -1 if kc >= 4 else 0
                return zT[:, kc * 1664 + c0 + off: kc * 1664 + c0 + off + w]

            for o in range(4):
                R.wait("gpsimd", VE)
                R.dma("gpsimd", msk_sb[:].rearrange("p (t n) -> p t n", n=640),
                      msk[o].rearrange("t p n -> p t n"), GD)
                R.wait("vector", GD)
                zT3 = zT[:].rearrange("p (c n) -> p c n", n=1664)
                for t in range(13):
                    buf = xb[t % 2]
                    R.wait("sync", VE)
                    R.dma("sync", buf[:], xe[o, t * 128:(t + 1) * 128, :], LD)
                    R.wait("vector", LD)
                    if ln_in:
                        emit_ln_stats(R, lt, buf[:], 1024, VE, SE)
                        R.op("vector", "tensor_scalar", zf[:], buf[:], lt.negm[:],
                             lt.rstd[:], ALU.add, op1=ALU.mult, inc=VE)
                        R.wait("vector", VE)
                        src_f = zf
                    else:
                        src_f = buf
                    R.wait("vector", TE)
                    R.op("vector", "tensor_copy", zhat[t % 2][:], src_f[:], inc=VE)
                    R.wait("tensor", VE)
                    for half in range(2):
                        for c4 in range(4):
                            c = half * 4 + c4
                            R.op("tensor", "transpose", Tpack[:, c4 * 128:(c4 + 1) * 128],
                                 zhat[t % 2][:, c * 128:(c + 1) * 128], ident[:],
                                 inc=(TE if c4 == 3 else None))
                        R.wait("vector", TE)
                        R.op("vector", "tensor_copy",
                             zT3[:, half * 4:half * 4 + 4, t * 128:(t + 1) * 128],
                             Tpack[:, 0:512].rearrange("p (k n) -> p k n", n=128), inc=VE)
                        R.wait("tensor", VE)


                def qk_pipe(kind, tiles, s0, dst_fT, rot_ap, sct, it0=0):
                    inner0 = 0 if kind == "q" else 512
                    nn = 1024 if kind == "q" else 1536
                    for it in range(it0, tiles):
                        s = s0 + it * 128
                        R.wait("sync", VE)
                        R.dma("sync", rotA[it % 2][:],
                              rot_ap[o, 0, it * 128:(it + 1) * 128, :], LD)
                        R.dma("sync", rotB[it % 2][:],
                              rot_ap[o, 1, it * 128:(it + 1) * 128, :], LD)
                        R.wait("tensor", SE)
                        R.wait("tensor", VE)
                        for kc in range(8):
                            R.op("tensor", "matmul", pps[:, 0:512],
                                 hT(kc, s, 128),
                                 wq_sb[:, kc * 1536 + inner0: kc * 1536 + inner0 + 512],
                                 start=(kc == 0), stop=(kc == 7),
                                 inc=(TE if kc == 7 else None))
                        R.wait("vector", TE)
                        R.op("vector", "tensor_copy", qev[:], pps[:, 0:512], inc=VE)
                        R.wait("scalar", VE)
                        R.op("scalar", "activation", t1b[:], qev[:], AF.Square, inc=SE)
                        R.wait("vector", SE)
                        R.op("vector", "tensor_reduce", l2r[:],
                             t1b[:].rearrange("p (h d) -> p h d", d=64),
                             axis=AX.X, op=ALU.add, inc=VE)
                        R.wait("scalar", VE)
                        R.op("scalar", "activation", l2r[:], l2r[:], AF.Sqrt, inc=SE)
                        R.wait("vector", SE)
                        R.op("vector", "tensor_scalar", l2r[:], l2r[:], 1e-12, None,
                             ALU.max, inc=VE)
                        R.wait("vector", VE)
                        R.op("vector", "reciprocal", l2r[:], l2r[:], inc=VE)
                        R.wait("vector", VE)
                        R.wait("vector", LD)
                        for h in range(8):
                            R.op("vector", "scalar_tensor_tensor",
                                 q2s[:, h * 64:(h + 1) * 64],
                                 qev[:, h * 64:(h + 1) * 64],
                                 l2r[:, h:h + 1], sct[:, h * 64:(h + 1) * 64],
                                 ALU.mult, ALU.mult,
                                 inc=(VE if h == 7 else None))
                        R.wait("vector", VE)
                        R.op("vector", "tensor_mul", t1b[:], q2s[:], rotA[it % 2][:])
                        q23 = q2s[:].rearrange("p (h two d) -> p h two d", two=2, d=32)
                        t23 = t2b[:].rearrange("p (h two d) -> p h two d", two=2, d=32)
                        B3 = rotB[it % 2][:].rearrange("p (h two d) -> p h two d",
                                                       two=2, d=32)
                        for h in range(8):
                            R.op("vector", "tensor_tensor", t23[:, h, 0, :],
                                 q23[:, h, 1, :], B3[:, h, 0, :], ALU.mult)
                            R.op("vector", "tensor_tensor", t23[:, h, 1, :],
                                 q23[:, h, 0, :], B3[:, h, 1, :], ALU.mult,
                                 inc=(VE if h == 7 else None))
                        R.wait("vector", VE)
                        R.op("vector", "tensor_add", q3b[:], t1b[:], t2b[:], inc=VE)
                        R.wait("tensor", VE)
                        for ic in range(4):
                            R.op("tensor", "transpose", Tpack[:, ic * 128:(ic + 1) * 128],
                                 q3b[:, ic * 128:(ic + 1) * 128], ident[:],
                                 inc=(TE if ic == 3 else None))
                        R.wait("vector", TE)
                        R.op("vector", "tensor_copy",
                             dst_fT[:].rearrange("p (c n) -> p c n", n=nn)[
                                 :, :, it * 128:(it + 1) * 128],
                             Tpack[:, 0:512].rearrange("p (k n) -> p k n", n=128), inc=VE)
                        R.wait("tensor", VE)

                qk_pipe("q", 8, 640, qfT, rotq, qst, it0=3)
                qk_pipe("k", 12, 128, kfT, rotk, kst)
                for it in range(12):
                    s = 128 + it * 128
                    R.wait("tensor", VE)
                    for kc in range(8):
                        R.op("tensor", "matmul", pps[:, 0:512],
                             hT(kc, s, 128),
                             wq_sb[:, kc * 1536 + 1024: kc * 1536 + 1536],
                             start=(kc == 0), stop=(kc == 7),
                             inc=(TE if kc == 7 else None))
                    R.wait("vector", TE)
                    R.op("vector", "tensor_copy", vtm[:, it * 512:(it + 1) * 512],
                         pps[:, 0:512], inc=VE)
                    R.wait("tensor", VE)

                PT3 = PT[:].rearrange("p (k n) -> p k n", n=1024)
                for ic in range(4):
                    for hh in range(2):
                        h = 2 * ic + hh
                        rows = slice(64 * hh, 64 * hh + 64)
                        for tq in range(3, 8):
                            R.wait("tensor", SE)
                            R.wait("tensor", VE)
                            kcol = ic * 1536 + tq * 128
                            R.op("tensor", "matmul", Sps[:, 0:512],
                                 qfT[rows, ic * 1024 + tq * 128: ic * 1024 + (tq + 1) * 128],
                                 kfT[rows, kcol: kcol + 512], start=True, stop=True)
                            R.op("tensor", "matmul", Sps[:, 512:640],
                                 qfT[rows, ic * 1024 + tq * 128: ic * 1024 + (tq + 1) * 128],
                                 kfT[rows, kcol + 512: kcol + 640], start=True, stop=True,
                                 inc=TE)
                            R.wait("vector", TE)
                            R.op("vector", "tensor_tensor", Sps[:, 0:640], Sps[:, 0:640],
                                 msk_sb[:, tq * 640:(tq + 1) * 640], ALU.add)
                            R.op("vector", "reduce_max", lt.negm[:], Sps[:, 0:640],
                                 axis=AX.X, inc=VE)
                            R.wait("vector", VE)
                            R.op("vector", "tensor_scalar", lt.va[:], lt.negm[:],
                                 -QKS, None, ALU.mult, inc=VE)
                            R.wait("scalar", VE)
                            R.wait("scalar", TE)
                            R.op("scalar", "activation", P[:, 0:640], Sps[:, 0:640],
                                 AF.Exp, bias=lt.va[:], scale=QKS,
                                 accum_out=rg[:, tq:tq + 1], inc=SE)
                            R.wait("tensor", SE)
                            for kt in range(5):
                                R.op("tensor", "transpose",
                                     Tpack[:, kt * 128:(kt + 1) * 128],
                                     P[:, kt * 128:(kt + 1) * 128], ident[:],
                                     inc=(TE if kt == 4 else None))
                            R.wait("vector", TE)
                            R.op("vector", "tensor_copy",
                                 PT3[:, tq:tq + 5, tq * 128:(tq + 1) * 128],
                                 Tpack[:, 0:640].rearrange("p (k n) -> p k n", n=128),
                                 inc=VE)
                        R.wait("tensor", VE)
                        pieces = []
                        for ktG in range(12):
                            lo, hi = max(3, ktG - 4), min(7, ktG)
                            if lo > hi:
                                continue
                            c = lo * 128
                            wdt = (hi - lo + 1) * 128
                            while wdt > 0:
                                n = min(512 - (c % 512), wdt)
                                pieces.append((ktG, c, n))
                                c += n
                                wdt -= n
                        first_in_bank = {}
                        last_in_bank = {}
                        for i, (ktG, c, n) in enumerate(pieces):
                            bk = c // 512
                            first_in_bank.setdefault(bk, i)
                            last_in_bank[bk] = i
                        lasts = set(last_in_bank.values())
                        for i, (ktG, c, n) in enumerate(pieces):
                            bk = c // 512
                            R.op("tensor", "matmul", ybank[rows, c:c + n],
                                 vtm[:, ktG * 512 + h * 64: ktG * 512 + h * 64 + 64],
                                 PT[:, ktG * 1024 + c: ktG * 1024 + c + n],
                                 start=(first_in_bank[bk] == i), stop=(i in lasts),
                                 inc=(TE if i == len(pieces) - 1 else None))
                        R.wait("vector", TE)
                        R.wait("vector", SE)
                        R.op("vector", "tensor_scalar", rg[:], rg[:], 1e-30, None,
                             ALU.add, inc=VE)
                        R.wait("vector", VE)
                        R.op("vector", "reciprocal", rg[:], rg[:], inc=VE)
                        R.wait("tensor", VE)
                        for tq in range(8):
                            R.op("tensor", "transpose",
                                 smalls[0:1, (tq % 4) * 128:(tq % 4) * 128 + 128],
                                 rg[:, tq:tq + 1], ident_f[:],
                                 inc=(TE if tq % 4 == 3 else None))
                            if tq % 4 == 3:
                                R.wait("vector", TE)
                                R.op("vector", "tensor_copy",
                                     row_a[0:1, (tq // 4) * 512:(tq // 4) * 512 + 512],
                                     smalls[0:1, 0:512], inc=VE)
                                R.wait("tensor", VE)
                        for half in range(2):
                            R.op("tensor", "matmul", bcb[:, 0:512],
                                 ones_f[:], row_a[0:1, half * 512:(half + 1) * 512],
                                 start=True, stop=True, inc=TE)
                            R.wait("vector", TE)
                            R.op("vector", "tensor_copy",
                                 rb_sb[:, half * 512:(half + 1) * 512],
                                 bcb[:, 0:512], inc=VE)
                            R.wait("tensor", VE)
                        R.wait("vector", VE)
                        R.op("vector", "tensor_mul", oT[rows, ic * 1024:(ic + 1) * 1024],
                             ybank[rows, 0:1024], rb_sb[rows, :], inc=VE)
                        R.wait("tensor", VE)

                if debug and o == 3:
                    R.wait("gpsimd", VE)
                    R.dma("gpsimd", dbg_q, qfT[:], OT)
                    R.dma("gpsimd", dbg_k, kfT[:], OT)
                    R.dma("gpsimd", dbg_v, vtm[:], OT)
                    R.dma("gpsimd", dbg_o, oT[:], OT)
                    R.dma("gpsimd", dbg_zT, zT[:], OT)
                    R.wait("tensor", OT)
                for tq in range(3, 8):
                    xmb = xm[tq % 2]
                    buf = xb[tq % 2]
                    R.wait("sync", VE)
                    R.dma("sync", buf[:], xe[o, (5 + tq) * 128:(6 + tq) * 128, :], LD)
                    R.wait("vector", LD)
                    if ln_in:
                        emit_ln_stats(R, lt, buf[:], 1024, VE, SE)
                        R.op("vector", "tensor_scalar", zf[:], buf[:], lt.negm[:],
                             lt.rstd[:], ALU.add, op1=ALU.mult, inc=VE)
                        resid = zf
                    else:
                        resid = buf
                    R.wait("vector", SP)
                    for half in range(2):
                        R.wait("tensor", VE)
                        for ic in range(4):
                            R.op("tensor", "matmul", pps[:, 0:512],
                                 oT[:, ic * 1024 + tq * 128: ic * 1024 + tq * 128 + 128],
                                 wo_sb[:, ic * 1024 + half * 512: ic * 1024 + half * 512 + 512],
                                 start=(ic == 0), stop=(ic == 3),
                                 inc=(TE if ic == 3 else None))
                        R.wait("vector", TE)
                        R.op("vector", "tensor_add", xmb[:, half * 512:(half + 1) * 512],
                             pps[:, 0:512],
                             resid[:, half * 512:(half + 1) * 512],
                             inc=VE)
                    R.wait("gpsimd", VE)
                    R.dma("gpsimd",
                          xm_d.ap()[:, o * 8192 + tq * 1024: o * 8192 + (tq + 1) * 1024],
                          xmb[:], SP)
                    if tq >= 4:
                        R.wait("gpsimd", VE)
                        R.dma("gpsimd", h2t[:, 0:512], xmb[:, 0:512], LD)
                        R.dma("gpsimd", h2t[1:128, 512:1024], xmb[0:127, 512:1024], LD)
                        R.dma("gpsimd", h2t[0:1, 512:1024],
                              xm[(tq - 1) % 2][127:128, 512:1024], LD)
                        R.wait("vector", LD)
                        emit_ln_stats(R, lt, h2t[:], 1024, VE, SE)
                        R.op("vector", "tensor_scalar", h2n[:], h2t[:], lt.negm[:],
                             lt.rstd[:], ALU.add, op1=ALU.mult, inc=VE)
                        R.wait("tensor", VE)
                        for half in range(2):
                            for c4 in range(4):
                                c = half * 4 + c4
                                R.op("tensor", "transpose",
                                     Tpack[:, c4 * 128:(c4 + 1) * 128],
                                     h2n[:, c * 128:(c + 1) * 128], ident[:],
                                     inc=(TE if c4 == 3 else None))
                            R.wait("vector", TE)
                            R.wait("vector", SP)
                            for c4 in range(4):
                                c = half * 4 + c4
                                R.op("vector", "tensor_scalar",
                                     qev[:, c4 * 128:(c4 + 1) * 128],
                                     Tpack[:, c4 * 128:(c4 + 1) * 128],
                                     lng_sb[:, c:c + 1], None, ALU.mult,
                                     inc=(VE if c4 == 3 else None))
                            R.wait("gpsimd", VE)
                            h2v = h2_d.ap().rearrange("p (s c n) -> p s c n", s=4, c=8)
                            R.dma("gpsimd",
                                  h2v[:, o, half * 4:half * 4 + 4,
                                      (tq - 4) * 128:(tq - 3) * 128],
                                  qev[:, 0:512].rearrange("p (c n) -> p c n", n=128), SP)
                            R.wait("tensor", SP)
            R.flush(block)

        with ExitStack() as PH2:
            sb2 = lambda n, s, dt: PH2.enter_context(nc.sbuf_tensor(pre + n, s, dt))
            w1_sb = sb2("w1_sb", [128, 8 * 5632], BF16)
            w2_sb = sb2("w2_sb", [128, 22 * 1024], BF16)
            h2s = sb2("h2s", [128, 8 * 512], BF16)
            hT_sb = sb2("hT_sb", [128, 22 * 512], BF16)
            asbf = sb2("asbf", [128, 512], F32)
            gbf = sb2("gbf", [128, 512], BF16)
            xms = sb2("xms", [128, 4 * 1024], F32)
            xq3b = sb2("xq3b", [128, 1024], F32)
            b1p = sb2("b1p", [1, 512], F32)
            b2_sb = sb2("b2_sb", [1, D], BF16)
            bcol = sb2("bcol", [128, 44], F32)
            lnbc = sb2("lnbc", [128, 8], BF16)
            zb = sb2("zb", [128, 1024], F32)
            pps = ps(PH2, "ppsb", [128, 512])
            pp2 = ps(PH2, "pp2b", [128, 512])
            y2ps = ps(PH2, "y2ps", [128, 1024])
            brow_ps = ps(PH2, "brow_ps", [1, 512])

            R.wait("gpsimd", TE)
            R.wait("gpsimd", VE)
            R.wait("gpsimd", SE)
            R.wait("sync", TE)
            R.wait("sync", VE)
            R.dma("gpsimd", w1_sb[:].rearrange("p (c n) -> p c n", n=5632),
                  w1r.rearrange("(c p) n -> p c n", p=128), GD)
            R.dma("gpsimd", w2_sb[:].rearrange("p (c n) -> p c n", n=1024),
                  w2r.rearrange("(c p) n -> p c n", p=128), GD)
            R.dma("gpsimd", lnbc[:], lnb_p, GD)
            R.dma("gpsimd", b2_sb[:], b2r, GD)
            R.wait("tensor", GD)
            R.wait("tensor", LD)
            R.wait("vector", LD)
            for piece in range(11):
                R.wait("sync", VE)
                R.dma("sync", b1p[:], b1r[0:1, piece * 512:(piece + 1) * 512], LD)
                for kc in range(8):
                    R.op("tensor", "matmul", brow_ps[:, 0:512],
                         lnbc[:, kc:kc + 1],
                         w1_sb[:, kc * 5632 + piece * 512: kc * 5632 + (piece + 1) * 512],
                         start=(kc == 0), stop=(kc == 7), inc=(TE if kc == 7 else None))
                R.wait("vector", TE)
                R.wait("vector", LD)
                R.op("vector", "tensor_add", b1p[:], b1p[:], brow_ps[:, 0:512], inc=VE)
                R.wait("tensor", VE)
                for c4 in range(4):
                    R.op("tensor", "transpose", smalls[:, c4:c4 + 1],
                         b1p[0:1, c4 * 128:(c4 + 1) * 128], ident_f[0:1, 0:1],
                         inc=(TE if c4 == 3 else None))
                R.wait("vector", TE)
                R.op("vector", "tensor_copy", bcol[:, piece * 4: piece * 4 + 4],
                     smalls[:, 0:4], inc=VE)
                R.wait("tensor", VE)

            for o in range(4):
                R.wait("sync", VE)
                R.dma("sync", h2s[:], h2_d.ap()[:, o * 4096:(o + 1) * 4096], LD)
                R.dma("sync", xms[:], xm_d.ap()[:, o * 8192 + 4096: o * 8192 + 8192], LD)
                R.wait("tensor", LD)
                R.wait("vector", LD)
                for hc2 in range(22):
                    R.wait("tensor", SE)
                    R.wait("tensor", VE)
                    for kc in range(8):
                        R.op("tensor", "matmul", pps[:, 0:512],
                             w1_sb[:, kc * 5632 + (2 * hc2) * 128: kc * 5632 + (2 * hc2) * 128 + 128],
                             h2s[:, kc * 512:(kc + 1) * 512],
                             start=(kc == 0), stop=(kc == 7))
                    for kc in range(8):
                        R.op("tensor", "matmul", pp2[:, 0:512],
                             w1_sb[:, kc * 5632 + (2 * hc2 + 1) * 128: kc * 5632 + (2 * hc2 + 1) * 128 + 128],
                             h2s[:, kc * 512:(kc + 1) * 512],
                             start=(kc == 0), stop=(kc == 7),
                             inc=(TE if kc == 7 else None))
                    R.wait("vector", TE)
                    R.wait("scalar", TE)
                    R.op("vector", "tensor_scalar", asbf[:], pps[:, 0:512],
                         bcol[:, 2 * hc2: 2 * hc2 + 1], None, ALU.add)
                    R.op("scalar", "activation", gbf[:], pp2[:, 0:512], AF.Gelu,
                         bias=bcol[:, 2 * hc2 + 1: 2 * hc2 + 2], inc=SE)
                    R.wait("vector", SE)
                    R.op("vector", "tensor_tensor", hT_sb[:, hc2 * 512:(hc2 + 1) * 512],
                         asbf[:], gbf[:], ALU.mult, inc=VE)
                R.wait("tensor", VE)
                for ts in range(4):
                    tt = o * 4 + ts
                    if sandwich:
                        R.wait("sync", VE)
                        R.dma("sync", xq3b[:], xq3[tt * 128:(tt + 1) * 128, :], LD)
                    for half in range(2):
                        for hc2 in range(22):
                            R.op("tensor", "matmul", y2ps[:, half * 512:(half + 1) * 512],
                                 hT_sb[:, hc2 * 512 + ts * 128: hc2 * 512 + ts * 128 + 128],
                                 w2_sb[:, hc2 * 1024 + half * 512: hc2 * 1024 + half * 512 + 512],
                                 start=(hc2 == 0), stop=False)
                        R.op("tensor", "matmul", y2ps[:, half * 512:(half + 1) * 512],
                             ones_bf[:], b2_sb[0:1, half * 512:(half + 1) * 512],
                             start=False, stop=True, inc=TE)
                    R.wait("vector", TE)
                    R.wait("vector", OT)
                    R.op("vector", "tensor_add", zb[:, 0:512], y2ps[:, 0:512],
                         xms[:, ts * 1024: ts * 1024 + 512])
                    R.op("vector", "tensor_add", zb[:, 512:1024], y2ps[:, 512:1024],
                         xms[:, ts * 1024 + 512: ts * 1024 + 1024], inc=VE)
                    R.wait("tensor", VE)
                    if sandwich:
                        R.wait("vector", LD)
                        emit_ln_stats(R, lt, zb[:], 1024, VE, SE)
                        R.op("vector", "tensor_scalar", zb[:], zb[:], lt.negm[:],
                             lt.rstd[:], ALU.add, op1=ALU.mult, inc=VE)
                        R.wait("vector", VE)
                        R.op("vector", "tensor_add", zb[:], zb[:], xq3b[:], inc=VE)
                    R.wait("gpsimd", VE)
                    R.dma("gpsimd", zout[tt * 128:(tt + 1) * 128, :], zb[:], OT)
            if debug:
                R.dma("gpsimd", dbg_xm, xm_d.ap(), OT)
                R.dma("gpsimd", dbg_h2, h2_d.ap(), OT)
            R.wait("gpsimd", OT)
            R.wait("gpsimd", SP)
            R.flush(block)
    return nc


def _rot_tables(abs_pos, xpos_sign):
    DH = 64
    half = np.arange(0, DH, 2, dtype=np.float64)
    inv = 1.0 / 10000.0 ** (half / DH)
    t = abs_pos.astype(np.float64)
    fr = t[:, None] * inv[None, :]
    freqs = np.concatenate([fr, fr], axis=1)
    base = (half + 0.4 * DH) / (1.4 * DH)
    power = (t - 4096 // 2) / 256.0
    sc = base[None, :] ** power[:, None]
    sc = np.concatenate([sc, sc], axis=1)
    if xpos_sign < 0:
        sc = 1.0 / sc
    A = np.cos(freqs) * sc
    Bs = np.sin(freqs) * sc
    Bs[:, 0:32] *= -1.0
    return A.astype(np.float32), Bs.astype(np.float32)


def stage_D(xin, x3, inputs, layer):
    wqkv = np.ascontiguousarray(inputs["L1_ffn_wqkv"][layer], np.float32)
    wo_ = np.ascontiguousarray(inputs["L1_ffn_wo"][layer], np.float32)
    w1 = np.asarray(inputs["L1_ffn_w1"][layer], np.float32)
    b1 = np.asarray(inputs["L1_ffn_b1"][layer], np.float32)
    w2 = np.asarray(inputs["L1_ffn_w2"][layer], np.float32)
    w1r = np.zeros((D, 2 * 2816), np.float32)
    b1r = np.zeros((1, 2 * 2816), np.float32)
    a_, g_ = w1[:, :2730], w1[:, 2730:]
    ba_, bg_ = b1[:2730], b1[2730:]
    for j in range(22):
        lo, hi = j * 128, min((j + 1) * 128, 2730)
        w1r[:, 2 * j * 128: 2 * j * 128 + (hi - lo)] = a_[:, lo:hi]
        w1r[:, (2 * j + 1) * 128: (2 * j + 1) * 128 + (hi - lo)] = g_[:, lo:hi]
        b1r[0, 2 * j * 128: 2 * j * 128 + (hi - lo)] = ba_[lo:hi]
        b1r[0, (2 * j + 1) * 128: (2 * j + 1) * 128 + (hi - lo)] = bg_[lo:hi]
    w2r = np.zeros((2816, D), np.float32)
    w2r[:2730] = w2
    b2r = np.ascontiguousarray(inputs["L1_ffn_b2"][layer], np.float32).reshape(1, D)
    lnb = np.asarray(inputs["L1_ffn_ln_b"][layer], np.float32)
    lng = np.asarray(inputs["L1_ffn_ln_g"][layer], np.float32)
    lnb_p = np.ascontiguousarray(lnb.reshape(8, 128).T)
    lng_p = np.ascontiguousarray(lng.reshape(8, 128).T)
    qs_r = np.ascontiguousarray(inputs["L1_ffn_q_scale"][layer], np.float32).reshape(1, 64)
    ks_r = np.ascontiguousarray(inputs["L1_ffn_k_scale"][layer], np.float32).reshape(1, 64)
    ident = np.eye(128, dtype=np.float32)
    maps = []
    for c in range(NCORES):
        b, par = c // 2, c % 2
        xe = np.zeros((4, 1664, D), np.float32)
        rq = np.zeros((4, 2, 1024, 512), np.float32)
        rk = np.zeros((4, 2, 1536, 512), np.float32)
        mk = np.full((4, 8, 128, 640), NEG, np.float32)
        for o, A in enumerate(OWN[par]):
            c0 = A * 512
            lo = c0 - 1152
            src_lo, dst_lo = max(0, lo), max(0, -lo)
            xe[o, dst_lo:] = xin[b][src_lo: lo + 1664]
            qpos = np.clip(np.arange(c0 - 512, c0 + 512), 0, None)
            kpos = np.clip(np.arange(c0 - 1024, c0 + 512), 0, None)
            for si, (sign, pos, dst) in enumerate(
                    (((1, qpos, rq)), ((-1, kpos, rk)))):
                Aq, Bq = _rot_tables(pos, sign)
                dst[o, 0] = np.tile(Aq, (1, 8))
                dst[o, 1] = np.tile(Bq, (1, 8))
            for tq in range(8):
                for r in range(128):
                    aq = c0 - 512 + tq * 128 + r
                    ks = c0 - 1024 + tq * 128
                    cc = np.arange(ks, ks + 640)
                    ok = (cc >= 0) & (cc <= aq) & (cc >= aq - 512) & (aq >= 0)
                    mk[o, tq, r, ok] = 0.0
        maps.append({
            "xe": xe, "xq3": np.ascontiguousarray(x3[b][own_rows(par)]),
            "wqkv": wqkv, "wo": wo_, "w1r": w1r, "w2r": w2r, "b1r": b1r,
            "b2r": b2r, "lnb_p": lnb_p, "lng_p": lng_p, "qs_r": qs_r,
            "ks_r": ks_r, "rotq": rq, "rotk": rk, "msk": mk, "ident_in": ident,
        })
    return maps


def run_D(x3, inputs):
    if "D1" not in _NC_CACHE:
        _NC_CACHE["D1"] = build_launch_D(True, False)
        _NC_CACHE["D2"] = build_launch_D(False, True)
    res = run_bass_kernel_spmd(_NC_CACHE["D1"], stage_D(x3, x3, inputs, 0),
                               core_ids=list(range(NCORES)))
    z1 = gather_own(res.results, "zout")
    res = run_bass_kernel_spmd(_NC_CACHE["D2"], stage_D(z1, x3, inputs, 1),
                               core_ids=list(range(NCORES)))
    return gather_own(res.results, "zout")




GROUPS = [[0, 1], [2, 3], [4, 5], [6, 7]]


def emit_barrier(R, sems):
    for eng in Rec.ENGINES:
        for s in sems.values():
            R.wait(eng, s)


def build_fused():
    nc = bass.Bass("TRN2", target_bir_lowering=False)
    R = Rec(nc)
    io = {}

    def inp(name, shape, dt=F32):
        io[name] = nc.dram_tensor(name, shape, dt, kind="ExternalInput").ap()

    inp("ident_in", [128, 128])
    inp("a_xfull", [T, D]); inp("a_xq", [OWNT, D])
    for w in ("a_wq", "a_wk", "a_wv", "a_wo"):
        inp(w, [D, D])
    inp("a_nullk_p", [128, 8]); inp("a_nullv_r", [1, D]); inp("a_lsc_r", [1, 16])
    inp("a_mtail", [16, 5, 128, 128])
    inp("b_wgT", [D, 3072], I32); inp("b_wuT", [D, 3072], I32)
    inp("b_wdT", [3072, D], I32)
    inp("c_wqkv", [D, 3072]); inp("c_wf2", [128, 128]); inp("c_bf_p", [128, 1])
    inp("c_bf_r", [1, 128]); inp("c_wout", [D, D]); inp("c_bout_r", [1, D])
    for li in range(2):
        p = f"d{li}_"
        inp(p + "wqkv", [D, 1536]); inp(p + "wo", [512, D])
        inp(p + "w1r", [D, 5632]); inp(p + "w2r", [2816, D])
        inp(p + "b1r", [1, 5632]); inp(p + "b2r", [1, D])
        inp(p + "lnb_p", [128, 8]); inp(p + "lng_p", [128, 8])
        inp(p + "qs_r", [1, 64]); inp(p + "ks_r", [1, 64])
    inp("d_rotq", [4, 2, 1024, 512]); inp("d_rotk", [4, 2, 1536, 512])
    inp("d_msk", [4, 8, 128, 640])
    inp("dmeta", [1, 128], I32)
    out = nc.dram_tensor("out", [OWNT, D], F32, kind="ExternalOutput").ap()

    x1_d = nc.dram_tensor("x1_d", [OWNT, D], F32)
    x2_d = nc.dram_tensor("x2_d", [OWNT, D], F32)
    x3_d = nc.dram_tensor("x3_d", [OWNT, D], F32)
    z1_d = nc.dram_tensor("z1_d", [OWNT, D], F32)
    x2g = nc.dram_tensor("x2g", [T, D], F32)
    xg_scr = nc.dram_tensor("xg_scr", [T, D], F32)
    xabs = nc.dram_tensor("xabs", [1152 + T, D], F32)
    xe_d = nc.dram_tensor("xe_d", [4, 1664, D], F32)

    with ExitStack() as CTX:
        sems = {n: Sem(CTX.enter_context(nc.semaphore(n)))
                for n in ("LD", "GD", "SP", "TE", "VE", "SE", "OT", "CC")}
        meta_sb = CTX.enter_context(nc.sbuf_tensor("meta_sb", [1, 128], I32))
        zero_sb = CTX.enter_context(nc.sbuf_tensor("zero_sb", [128, 1024], F32))
        dynr = CTX.enter_context(nc.gpsimd.register("dynr"))
        block = CTX.enter_context(nc.Block())

        fcb = {"nc": nc, "R": R, "io": io, "sems": sems, "block": block,
               "x1_d": x1_d, "x2_d": x2_d, "x3_d": x3_d, "z1_d": z1_d,
               "x2g": x2g, "out": out, "xe_d": xe_d}

        LD, GD, OT, CC, VE = (sems[k] for k in ("LD", "GD", "OT", "CC", "VE"))
        R.dma("sync", meta_sb[:], io["dmeta"], LD)
        R.op("vector", "memset", zero_sb[:], 0.0, inc=VE)
        R.wait("gpsimd", VE)
        for zi in range(9):
            R.dma("gpsimd", xabs.ap()[zi * 128:(zi + 1) * 128, :], zero_sb[:], GD)

        def unscramble_and_slot(src_d):
            R.op("gpsimd", "collective_compute", "AllGather", ALU.bypass,
                 replica_groups=GROUPS, ins=[src_d.ap().opt()],
                 outs=[xg_scr.ap().opt()], inc=sems["CC"])
            R.wait("gpsimd", sems["CC"])
            for c in range(8):
                g = gathered_pos(c)
                R.dma("gpsimd", xabs.ap()[1152 + c * 512: 1152 + (c + 1) * 512, :],
                      xg_scr.ap()[g * 512:(g + 1) * 512, :], GD)
            R.wait("gpsimd", GD)
            R.wait("gpsimd", LD)
            for o in range(4):
                R.op("gpsimd", "reg_load", dynr, meta_sb[0:1, o:o + 1])
                R.dma("gpsimd", xe_d.ap()[o],
                      bass.AP(xabs, dynr, [[1024, 1664], [1, 1024]]), GD)

        build_launch_A(fc=fcb)
        emit_barrier(R, sems)
        build_launch_B(fc=fcb)
        emit_barrier(R, sems)
        R.op("gpsimd", "collective_compute", "AllGather", ALU.bypass,
             replica_groups=GROUPS, ins=[x2_d.ap().opt()], outs=[x2g.ap().opt()],
             inc=sems["CC"])
        emit_barrier(R, sems)
        build_launch_C(fc=fcb)
        emit_barrier(R, sems)
        unscramble_and_slot(x3_d)
        emit_barrier(R, sems)
        build_launch_D(True, False, fc=fcb, li=0)
        emit_barrier(R, sems)
        unscramble_and_slot(z1_d)
        emit_barrier(R, sems)
        build_launch_D(False, True, fc=fcb, li=1)
        emit_barrier(R, sems)
        R.flush(block)
    return nc


def gathered_pos(c):
    if c in OWN[0]:
        return OWN[0].index(c)
    return 4 + OWN[1].index(c)


def make_dmeta(par):
    m = np.zeros((1, 128), np.int32)
    for o, A in enumerate(OWN[par]):
        m[0, o] = (A * 512) * D
    return m


def stage_fused(inputs):
    zero4 = np.zeros((4, T, D), np.float32)
    zero2 = np.zeros((OWNT, D), np.float32)
    aM = stage_A(inputs)
    bM = stage_B(zero4, inputs)
    cM = stage_C(zero4, inputs)
    d0M = stage_D(zero4, zero4, inputs, 0)
    d1M = stage_D(zero4, zero4, inputs, 1)
    maps = []
    for c in range(NCORES):
        par = c % 2
        m = {"ident_in": aM[c]["ident_in"], "dmeta": make_dmeta(par)}
        for k, v in aM[c].items():
            if k != "ident_in":
                m["a_" + k] = v
        for k in ("wgT", "wuT", "wdT"):
            m["b_" + k] = bM[c][k]
        for k in ("wqkv", "wf2", "bf_p", "bf_r", "wout", "bout_r"):
            m["c_" + k] = cM[c][k]
        for li, dM in ((0, d0M), (1, d1M)):
            for k in ("wqkv", "wo", "w1r", "w2r", "b1r", "b2r", "lnb_p",
                      "lng_p", "qs_r", "ks_r"):
                m[f"d{li}_" + k] = dM[c][k]
        m["d_rotq"] = d0M[c]["rotq"]
        m["d_rotk"] = d0M[c]["rotk"]
        m["d_msk"] = d0M[c]["msk"]
        maps.append(m)
    return maps


def kernel(**inputs):
    if "FUSED" not in _NC_CACHE:
        _NC_CACHE["FUSED"] = build_fused()
    res = run_bass_kernel_spmd(_NC_CACHE["FUSED"], stage_fused(inputs),
                               core_ids=list(range(NCORES)))
    return gather_own(res.results, "out")


# version 04
# speedup vs baseline: 1.1326x; 1.0454x over previous
import numpy as np
import ml_dtypes
import concourse.bass as bass
import concourse.mybir as mybir
from concourse.bass_utils import run_bass_kernel_spmd
from contextlib import ExitStack

F32 = mybir.dt.float32
BF16 = mybir.dt.bfloat16
I32 = mybir.dt.int32
AF = mybir.ActivationFunctionType
ALU = mybir.AluOpType
AX = mybir.AxisListType
BF = ml_dtypes.bfloat16

NCORES = 8
T = 4096
D = 1024
OWNT = 2048
NEG = -1e30

OWN = [[0, 3, 4, 7], [1, 2, 5, 6]]


def own_rows(par):
    return np.concatenate([np.arange(c * 512, (c + 1) * 512) for c in OWN[par]])


KK_TAB = [5, 6, 7, 8, 13, 14, 15, 16, 21, 22, 23, 24, 29, 30, 31, 32]


def KK(j):
    return KK_TAB[j]


class Sem:
    def __init__(self, h):
        self.h = h
        self.n = 0


class Rec:
    ENGINES = ("sync", "gpsimd", "tensor", "vector", "scalar")

    def __init__(self, nc):
        self.nc = nc
        self.ops = {e: [] for e in self.ENGINES}

    def on(self, eng, f):
        self.ops[eng].append(f)

    def wait(self, eng, sem, n=None):
        v = sem.n if n is None else n
        if v > 0:
            self.on(eng, lambda e, s=sem.h, v=v: e.wait_ge(s, v))

    def op(self, eng, meth, *a, inc=None, inck=1, **kw):
        if inc is not None:
            inc.n += inck

        import traceback
        site = traceback.extract_stack(limit=3)[0]

        def f(e, meth=meth, a=a, kw=kw, inc=inc, inck=inck, site=site):
            try:
                ins = getattr(e, meth)(*a, **kw)
            except Exception as ex:
                import traceback as _tb
                _tb.print_exc()
                raise RuntimeError(
                    f"op {meth} recorded at {site.filename}:{site.lineno} failed: {ex}"
                ) from ex
            if inc is not None:
                ins.then_inc(inc.h, inck)
        self.on(eng, f)

    def dma(self, eng, out, in_, inc, **kw):
        self.op(eng, "dma_start", inc=inc, inck=16, out=out, in_=in_, **kw)

    def flush(self, block):
        for eng in self.ENGINES:
            if self.ops[eng]:
                ops = self.ops[eng]
                getattr(block, eng)(lambda e, ops=ops: [f(e) for f in ops])
        self.ops = {e: [] for e in self.ENGINES}


class LnTiles:

    def __init__(self, nc, ctx, tag=""):
        sb = lambda n, s, dt=F32: ctx.enter_context(nc.sbuf_tensor(n + tag, s, dt))
        self.stats = sb("ln_stats", [128, 12])
        self.mv = sb("ln_mv", [128, 2])
        self.negm = sb("ln_negm", [128, 1])
        self.va = sb("ln_va", [128, 1])
        self.rstd = sb("ln_rstd", [128, 1])


def emit_ln_stats(R, lt, src, width, VE, SE, eps=1e-5):
    h = width // 2
    R.op("vector", "bn_stats", lt.stats[:, 0:6], src[:, 0:h])
    R.op("vector", "bn_stats", lt.stats[:, 6:12], src[:, h:width], inc=VE)
    R.wait("vector", VE)
    R.op("vector", "bn_aggr", lt.mv[:], lt.stats[:], inc=VE)
    R.wait("vector", VE)
    R.op("vector", "tensor_scalar", lt.negm[:], lt.mv[:, 0:1], -1.0, None, ALU.mult)
    R.op("vector", "tensor_scalar", lt.va[:], lt.mv[:, 1:2], eps, None, ALU.add, inc=VE)
    R.wait("scalar", VE)
    R.op("scalar", "activation", lt.rstd[:], lt.va[:], AF.Sqrt, inc=SE)
    R.wait("vector", SE)
    R.op("vector", "reciprocal", lt.rstd[:], lt.rstd[:], inc=VE)
    R.wait("vector", VE)


def build_launch_A(debug=False, fc=None):
    pre = "" if fc is None else "a_"
    if fc is None:
        nc = bass.Bass("TRN2", target_bir_lowering=False)
        R = Rec(nc)
        xfull = nc.dram_tensor("xfull", [T, D], F32, kind="ExternalInput").ap()
        xq = nc.dram_tensor("xq", [OWNT, D], F32, kind="ExternalInput").ap()
        wq = nc.dram_tensor("wq", [D, D], F32, kind="ExternalInput").ap()
        wk = nc.dram_tensor("wk", [D, D], F32, kind="ExternalInput").ap()
        wv = nc.dram_tensor("wv", [D, D], F32, kind="ExternalInput").ap()
        wo = nc.dram_tensor("wo", [D, D], F32, kind="ExternalInput").ap()
        nullk_p = nc.dram_tensor("nullk_p", [128, 8], F32, kind="ExternalInput").ap()
        nullv_r = nc.dram_tensor("nullv_r", [1, D], F32, kind="ExternalInput").ap()
        lsc_r = nc.dram_tensor("lsc_r", [1, 16], F32, kind="ExternalInput").ap()
        ident_in = nc.dram_tensor("ident_in", [128, 128], F32, kind="ExternalInput").ap()
        mtail = nc.dram_tensor("mtail", [16, 5, 128, 128], F32, kind="ExternalInput").ap()
        x1out = nc.dram_tensor("x1", [OWNT, D], F32, kind="ExternalOutput").ap()
    else:
        nc, R = fc["nc"], fc["R"]
        io = fc["io"]
        xfull, xq = io["a_xfull"], io["a_xq"]
        wq, wk, wv, wo = io["a_wq"], io["a_wk"], io["a_wv"], io["a_wo"]
        nullk_p, nullv_r, lsc_r = io["a_nullk_p"], io["a_nullv_r"], io["a_lsc_r"]
        ident_in, mtail = io["ident_in"], io["a_mtail"]
        x1out = fc["x1_d"].ap()

    kT_d = nc.dram_tensor(pre + "kT_d", [128, 8 * 4096], BF16)
    v_d = nc.dram_tensor(pre + "v_d", [128, 8 * 4096], BF16)
    if debug:
        dbg_k = nc.dram_tensor("dbg_k", [128, 8 * 4096], BF16, kind="ExternalOutput").ap()
        dbg_v = nc.dram_tensor("dbg_v", [128, 8 * 4096], BF16, kind="ExternalOutput").ap()
        dbg_q = nc.dram_tensor("dbg_q", [128, 8 * 2048], BF16, kind="ExternalOutput").ap()
        dbg_yn = nc.dram_tensor("dbg_yn", [4, 128, 8 * 512], BF16, kind="ExternalOutput").ap()
        dbg_xqt = nc.dram_tensor("dbg_xqt", [4, 128, 8 * 512], BF16, kind="ExternalOutput").ap()

    with ExitStack() as CTX:
        sb = lambda n, s, dt: CTX.enter_context(nc.sbuf_tensor(pre + n, s, dt))
        ps = lambda stk, n, s, dt=F32: stk.enter_context(nc.psum_tensor(pre + n, s, dt))
        if fc is None:
            sem = lambda n: Sem(CTX.enter_context(nc.semaphore(n)))
            LD = sem("LD"); GD = sem("GD"); SP = sem("SP")
            TE = sem("TE"); VE = sem("VE"); SE = sem("SE"); OT = sem("OT")
        else:
            sm = fc["sems"]
            LD, GD, SP = sm["LD"], sm["GD"], sm["SP"]
            TE, VE, SE, OT = sm["TE"], sm["VE"], sm["SE"], sm["OT"]

        xq_sb = sb("xq_sb", [128, 16 * 1024], F32)
        qT = sb("qT", [128, 8 * 2048], BF16)
        ident = sb("ident", [128, 128], BF16)
        ident_f = sb("ident_f", [128, 128], F32)
        ones_f = sb("ones_f", [1, 128], F32)
        nullk_sb = sb("nullk_sb", [128, 8], BF16)
        nullv_sb = sb("nullv_sb", [1, D], BF16)
        lsc_sb = sb("lsc_sb", [1, 16], F32)
        scale_b = sb("scale_b", [128, 16], F32)
        scol2 = sb("scol2", [128, 8], F32)
        lt = LnTiles(nc, CTX, tag=pre)

        Tpack = ps(CTX, "Tpack", [128, 1024], BF16)
        smalls = ps(CTX, "smalls", [128, 512], F32)
        bc = ps(CTX, "bc", [128, 512], F32)
        yps = ps(CTX, "yps", [128, 512], F32)

        block = fc["block"] if fc else CTX.enter_context(nc.Block())

        R.dma("gpsimd", ident[:], ident_in, GD)
        R.dma("gpsimd", nullk_sb[:], nullk_p, GD)
        R.dma("gpsimd", nullv_sb[:], nullv_r, GD)
        R.dma("sync", ident_f[:], ident_in, LD)
        R.dma("sync", lsc_sb[:], lsc_r, LD)
        R.dma("sync", xq_sb[:].rearrange("p (t n) -> p t n", n=1024),
              xq.rearrange("(t p) n -> p t n", p=128), LD)
        R.op("vector", "memset", ones_f[:], 1.0, inc=VE)
        R.wait("scalar", LD)
        R.op("scalar", "activation", lsc_sb[:], lsc_sb[:], AF.Exp, inc=SE)
        R.wait("vector", SE)
        R.op("vector", "tensor_scalar", lsc_sb[:], lsc_sb[:], 0.125, None, ALU.mult, inc=VE)
        R.wait("tensor", VE)
        R.op("tensor", "matmul", bc[:, 0:16], ones_f[:], lsc_sb[:], start=True, stop=True, inc=TE)
        R.wait("vector", TE)
        R.op("vector", "tensor_copy", scale_b[:], bc[:, 0:16])
        for ic in range(8):
            R.op("vector", "tensor_copy", scol2[0:64, ic:ic + 1],
                 bc[0:64, 2 * ic:2 * ic + 1])
            R.op("vector", "tensor_copy", scol2[64:128, ic:ic + 1],
                 bc[64:128, 2 * ic + 1:2 * ic + 2], inc=(VE if ic == 7 else None))
        R.op("vector", "engine_nop", inc=VE) if False else None
        VE.n += 0

        with ExitStack() as PH1:
            sb1 = lambda n, s, dt: PH1.enter_context(nc.sbuf_tensor(pre + n, s, dt))
            xb = [sb1(f"xb{i}", [128, 1024], F32) for i in range(2)]
            xhat = [sb1(f"xhat{i}", [128, 1024], BF16) for i in range(2)]
            xTc = [sb1(f"xTc{i}", [128, 8 * 512], BF16) for i in range(2)]
            wk_sb = sb1("wk_sb", [128, 8 * 1024], BF16)
            wv_sb = sb1("wv_sb", [128, 8 * 1024], BF16)
            wq_sb = sb1("wq_sb", [128, 8 * 1024], BF16)
            kspill = sb1("kspill", [128, 8 * 512], BF16)
            vbuf = [sb1(f"vbuf{i}", [128, 1024], BF16) for i in range(2)]
            pps = ps(PH1, "pps", [128, 512], F32)
            pp2 = ps(PH1, "pp2", [128, 512], F32)

            for w_ap, w_sb in ((wk, wk_sb), (wv, wv_sb), (wq, wq_sb)):
                R.dma("gpsimd", w_sb[:].rearrange("p (c n) -> p c n", n=1024),
                      w_ap.rearrange("(c p) n -> p c n", p=128), GD)
            wready = GD.n

            def ln_transpose_tile(src_f32, xhat_t, xT_t, t4):
                emit_ln_stats(R, lt, src_f32, 1024, VE, SE)
                R.wait("vector", TE)
                R.op("vector", "tensor_scalar", xhat_t[:], src_f32, lt.negm[:],
                     lt.rstd[:], ALU.add, op1=ALU.mult, inc=VE)
                R.wait("tensor", VE)
                xT3 = xT_t[:].rearrange("p (c n) -> p c n", n=512)
                for half in range(2):
                    for c4 in range(4):
                        c = half * 4 + c4
                        R.op("tensor", "transpose", Tpack[:, c4 * 128:(c4 + 1) * 128],
                             xhat_t[:, c * 128:(c + 1) * 128], ident[:],
                             inc=(TE if c4 == 3 else None))
                    R.wait("vector", TE)
                    R.op("vector", "tensor_copy",
                         xT3[:, half * 4:half * 4 + 4, t4 * 128:(t4 + 1) * 128],
                         Tpack[:, 0:512].rearrange("p (k n) -> p k n", n=128), inc=VE)
                    R.wait("tensor", VE)

            R.wait("tensor", GD, wready)
            v3 = v_d.ap().rearrange("p (c n) -> p c n", n=4096)
            for tc in range(8):
                for t4 in range(4):
                    tt = tc * 4 + t4
                    buf = xb[tt % 2]
                    R.wait("sync", VE)
                    R.dma("sync", buf[:], xfull[tt * 128:(tt + 1) * 128, :], LD)
                    R.wait("vector", LD)
                    ln_transpose_tile(buf[:], xhat[tt % 2], xTc[tc % 2], t4)
                for ic in range(8):
                    for kc in range(8):
                        R.op("tensor", "matmul", pps[:, 0:512],
                             wk_sb[:, kc * 1024 + ic * 128: kc * 1024 + ic * 128 + 128],
                             xTc[tc % 2][:, kc * 512:(kc + 1) * 512],
                             start=(kc == 0), stop=(kc == 7),
                             inc=(TE if kc == 7 else None))
                    R.wait("vector", TE)
                    R.wait("vector", SP)
                    R.op("vector", "tensor_copy", kspill[:, ic * 512:(ic + 1) * 512],
                         pps[:, 0:512], inc=VE)
                    R.wait("tensor", VE)
                R.wait("gpsimd", VE)
                for ic in range(8):
                    R.dma("gpsimd",
                          kT_d.ap()[:, ic * 4096 + tc * 512: ic * 4096 + (tc + 1) * 512],
                          kspill[:, ic * 512:(ic + 1) * 512], SP)
                for t4 in range(4):
                    tt = tc * 4 + t4
                    for half in range(2):
                        bank = pps if half == 0 else pp2
                        for kc in range(8):
                            R.op("tensor", "matmul", bank[:, 0:512],
                                 xTc[tc % 2][:, kc * 512 + t4 * 128: kc * 512 + t4 * 128 + 128],
                                 wv_sb[:, kc * 1024 + half * 512: kc * 1024 + half * 512 + 512],
                                 start=(kc == 0), stop=(kc == 7),
                                 inc=(TE if kc == 7 else None))
                    R.wait("vector", TE)
                    R.wait("vector", SP)
                    R.op("vector", "tensor_copy", vbuf[tt % 2][:, 0:512], pps[:, 0:512])
                    R.op("vector", "tensor_copy", vbuf[tt % 2][:, 512:1024], pp2[:, 0:512],
                         inc=VE)
                    R.wait("tensor", VE)
                    R.wait("gpsimd", VE)
                    R.dma("gpsimd", v3[:, :, tt * 128:(tt + 1) * 128],
                          vbuf[tt % 2][:].rearrange("p (c n) -> p c n", n=128), SP)

            R.wait("vector", LD)
            for qc in range(4):
                for t4 in range(4):
                    tt = qc * 4 + t4
                    R.wait("vector", VE)
                    ln_transpose_tile(xq_sb[:, tt * 1024:(tt + 1) * 1024],
                                      xhat[tt % 2], xTc[qc % 2], t4)
                if debug:
                    R.wait("gpsimd", VE)
                    R.dma("gpsimd", dbg_xqt[qc], xTc[qc % 2][:], SP)
                    R.wait("tensor", SP)
                for ic in range(8):
                    for kc in range(8):
                        R.op("tensor", "matmul", pps[:, 0:512],
                             wq_sb[:, kc * 1024 + ic * 128: kc * 1024 + ic * 128 + 128],
                             xTc[qc % 2][:, kc * 512:(kc + 1) * 512],
                             start=(kc == 0), stop=(kc == 7),
                             inc=(TE if kc == 7 else None))
                    R.wait("vector", TE)
                    R.op("vector", "tensor_scalar",
                         qT[:, ic * 2048 + qc * 512: ic * 2048 + (qc + 1) * 512],
                         pps[:, 0:512], scol2[:, ic:ic + 1], None, ALU.mult, inc=VE)
                    R.wait("tensor", VE)
            R.flush(block)

        with ExitStack() as PH2:
            sb2 = lambda n, s, dt: PH2.enter_context(nc.sbuf_tensor(pre + n, s, dt))
            Kg = [sb2(f"Kg{i}", [128, 2048], BF16) for i in range(2)]
            Vg = [sb2(f"Vg{i}", [128, 2048], BF16) for i in range(2)]
            P = sb2("P", [128, 2048], BF16)
            PT = sb2("PT", [128, 16 * 512], BF16)
            ynT = sb2("ynT", [128, 8 * 512], BF16)
            y_acc = sb2("y_acc", [128, 512], F32)
            tmp_y = sb2("tmp_y", [128, 512], F32)
            rb_sb = sb2("rb_sb", [128, 512], F32)
            row_a = sb2("row_a", [1, 512], F32)
            row_b = sb2("row_b", [1, 512], F32)
            mt_sb = sb2("mt_sb", [128, 4 * 5 * 128], BF16)
            rg = sb2("rg", [128, 4], F32)
            mg = sb2("mg", [128, 4], F32)
            r_acc = sb2("r_acc", [128, 4], F32)
            m_acc = sb2("m_acc", [128, 4], F32)
            sm1 = sb2("sm1", [128, 4], F32)
            sm2 = sb2("sm2", [128, 4], F32)
            pnull = sb2("pnull", [128, 4], BF16)
            pnullT = sb2("pnullT", [1, 512], BF16)
            wo_sb = sb2("wo_sb", [128, 8 * 1024], BF16)
            attn_f32 = sb2("attn_f32", [128, 1024], F32)
            x1b = sb2("x1b", [128, 1024], F32)
            Sps = ps(PH2, "Sps", [128, 2048], F32)
            PT3 = PT[:].rearrange("p (k n) -> p k n", n=512)

            R.wait("gpsimd", TE)
            R.wait("gpsimd", VE)
            R.wait("sync", TE)
            R.wait("sync", VE)
            R.dma("gpsimd", wo_sb[:].rearrange("p (c n) -> p c n", n=1024),
                  wo.rearrange("(c p) n -> p c n", p=128), GD)

            ld_hist = [0, 0]
            ldi = 0
            kbuf, vbuf = Kg[0], Vg[0]
            for qb in range(4):
                KKs = [KK(4 * qb + qs) for qs in range(4)]
                Wmax = 128 * KKs[3]
                ngroups = 1 if Wmax <= 2048 else 2
                R.wait("gpsimd", VE)
                R.dma("gpsimd", mt_sb[:].rearrange("p (j t n) -> p j t n", t=5, n=128),
                      mtail[4 * qb:4 * qb + 4, :, :, :].rearrange("j t p n -> p j t n"), GD)
                R.wait("vector", GD)
                for ic in range(8):
                    for hh in range(2):
                        h = 2 * ic + hh
                        rows = slice(64 * hh, 64 * hh + 64)
                        scol = scale_b[:, h:h + 1]
                        for g in range(ngroups):
                            g0 = g * 2048
                            gW = min(Wmax, g0 + 2048) - g0
                            if hh == 0 or ngroups > 1:
                                kbuf, vbuf = Kg[ldi % 2], Vg[ldi % 2]
                                R.wait("sync", TE, ld_hist[ldi % 2])
                                R.dma("sync", kbuf[:, 0:gW],
                                      kT_d.ap()[:, ic * 4096 + g0: ic * 4096 + g0 + gW], LD)
                                R.dma("sync", vbuf[:, 0:gW],
                                      v_d.ap()[:, ic * 4096 + g0: ic * 4096 + g0 + gW], LD)
                                ldi += 1
                            R.wait("tensor", LD)
                            for qs in range(4):
                                j = 4 * qb + qs
                                Wj = 128 * KKs[qs]
                                lo, hi = g0, min(Wj, g0 + 2048)
                                if hi <= lo:
                                    continue
                                w = hi - lo
                                lhsT = qT[rows, ic * 2048 + j * 128: ic * 2048 + (j + 1) * 128]
                                R.wait("tensor", SE)
                                R.wait("tensor", VE)
                                mm_list = []
                                c = 0
                                while c < w:
                                    n = min(512, w - c)
                                    mm_list.append((c, n))
                                    c += n
                                for i, (c, n) in enumerate(mm_list):
                                    last = (i == len(mm_list) - 1) and g != 0
                                    R.op("tensor", "matmul", Sps[:, c:c + n], lhsT,
                                         kbuf[rows, c:c + n], start=True, stop=True,
                                         inc=(TE if last else None))
                                if g == 0:
                                    R.op("tensor", "matmul", smalls[:, qs:qs + 1], lhsT,
                                         nullk_sb[rows, ic:ic + 1], start=True, stop=True,
                                         inc=TE)
                                R.wait("vector", TE)
                                tail0 = (KKs[qs] - 5) * 128
                                if lo <= tail0 and tail0 + 640 <= hi:
                                    R.op("vector", "tensor_tensor",
                                         Sps[:, tail0 - lo: tail0 - lo + 640],
                                         Sps[:, tail0 - lo: tail0 - lo + 640],
                                         mt_sb[:, qs * 640: qs * 640 + 640],
                                         ALU.add)
                                R.op("vector", "reduce_max", mg[:, qs:qs + 1], Sps[:, 0:w],
                                     axis=AX.X, inc=VE)
                                R.wait("vector", VE)
                                if g == 0:
                                    R.op("vector", "tensor_max", mg[:, qs:qs + 1],
                                         mg[:, qs:qs + 1], smalls[:, qs:qs + 1], inc=VE)
                                    R.wait("vector", VE)
                                R.op("vector", "tensor_scalar", sm1[:, qs:qs + 1],
                                     mg[:, qs:qs + 1], -1.0, None, ALU.mult, inc=VE)
                                R.wait("scalar", VE)
                                R.wait("scalar", TE)
                                R.op("scalar", "activation", P[:, 0:w], Sps[:, 0:w], AF.Exp,
                                     bias=sm1[:, qs:qs + 1],
                                     accum_out=rg[:, qs:qs + 1], inc=SE)
                                if g == 0:
                                    R.op("scalar", "activation", pnull[:, qs:qs + 1],
                                         smalls[:, qs:qs + 1], AF.Exp,
                                         bias=sm1[:, qs:qs + 1], inc=SE)
                                    R.wait("vector", SE)
                                    R.op("vector", "tensor_add", rg[:, qs:qs + 1],
                                         rg[:, qs:qs + 1], pnull[:, qs:qs + 1], inc=VE)
                                R.wait("tensor", SE)
                                nk = (w + 127) // 128
                                for kb in range(0, nk, 8):
                                    ke = min(kb + 8, nk)
                                    R.wait("tensor", VE)
                                    for kt in range(kb, ke):
                                        R.op("tensor", "transpose",
                                             Tpack[:, (kt - kb) * 128:(kt - kb + 1) * 128],
                                             P[:, kt * 128: min((kt + 1) * 128, w)], ident[:],
                                             inc=(TE if kt == ke - 1 else None))
                                    R.wait("vector", TE)
                                    R.op("vector", "tensor_copy",
                                         PT3[:, kb:ke, qs * 128:(qs + 1) * 128],
                                         Tpack[:, 0:(ke - kb) * 128].rearrange(
                                             "p (k n) -> p k n", n=128), inc=VE)
                            if g == 0:
                                R.wait("tensor", VE)
                                for qs in range(4):
                                    R.op("tensor", "transpose",
                                         Tpack[0:1, qs * 128:(qs + 1) * 128],
                                         pnull[:, qs:qs + 1], ident[:],
                                         inc=(TE if qs == 3 else None))
                                R.wait("vector", TE)
                                R.op("vector", "tensor_copy", pnullT[:],
                                     Tpack[0:1, 0:512], inc=VE)
                            R.wait("tensor", VE)
                            nkg = gW // 128
                            av = []
                            for ktL in range(nkg):
                                ktA = g0 // 128 + ktL
                                qsmin = next((qs for qs in range(4) if KKs[qs] > ktA), None)
                                if qsmin is None:
                                    continue
                                av.append((ktL, qsmin * 128))
                            for i, (ktL, qoff) in enumerate(av):
                                last = (i == len(av) - 1) and g != 0
                                R.op("tensor", "matmul", yps[rows, qoff:512],
                                     vbuf[:, ktL * 128 + hh * 64: ktL * 128 + hh * 64 + 64],
                                     PT3[:, ktL, qoff:512],
                                     start=(i == 0), stop=last,
                                     inc=(TE if last else None))
                            if g == 0:
                                R.op("tensor", "matmul", yps[rows, 0:512],
                                     nullv_sb[0:1, h * 64: h * 64 + 64], pnullT[:],
                                     start=False, stop=True, inc=TE)
                            ld_hist[(ldi - 1) % 2] = TE.n
                            R.wait("vector", TE)
                            R.wait("vector", SE)
                            if g == 0:
                                R.op("vector", "tensor_copy", y_acc[rows, :], yps[rows, :])
                                R.op("vector", "tensor_copy", r_acc[:], rg[:])
                                R.op("vector", "tensor_copy", m_acc[:], mg[:], inc=VE)
                                R.wait("tensor", VE)
                            else:
                                R.op("vector", "tensor_max", sm2[:], m_acc[:], mg[:], inc=VE)
                                R.wait("vector", VE)
                                R.op("vector", "tensor_sub", sm1[:], m_acc[:], sm2[:])
                                R.op("vector", "tensor_sub", mg[:], mg[:], sm2[:])
                                R.op("vector", "tensor_copy", m_acc[:], sm2[:], inc=VE)
                                R.wait("scalar", VE)
                                R.op("scalar", "activation", sm1[:], sm1[:], AF.Exp)
                                R.op("scalar", "activation", mg[:], mg[:], AF.Exp, inc=SE)
                                R.wait("vector", SE)
                                R.op("vector", "tensor_mul", r_acc[:], r_acc[:], sm1[:])
                                R.op("vector", "tensor_mul", rg[:], rg[:], mg[:], inc=VE)
                                R.wait("vector", VE)
                                R.op("vector", "tensor_add", r_acc[:], r_acc[:], rg[:],
                                     inc=VE)
                                R.wait("tensor", VE)
                                for rowt, srcontainter in ((row_a, sm1), (row_b, mg)):
                                    for qs in range(4):
                                        R.op("tensor", "transpose",
                                             smalls[0:1, qs * 128:(qs + 1) * 128],
                                             srcontainter[:, qs:qs + 1], ident_f[:],
                                             inc=(TE if qs == 3 else None))
                                    R.wait("vector", TE)
                                    R.op("vector", "tensor_copy",
                                         rowt[:], smalls[0:1, 0:512], inc=VE)
                                    R.wait("tensor", VE)
                                R.op("tensor", "matmul", bc[:, 0:512],
                                     ones_f[:], row_a[:],
                                     start=True, stop=True, inc=TE)
                                R.wait("vector", TE)
                                R.op("vector", "tensor_copy", rb_sb[:],
                                     bc[:, 0:512], inc=VE)
                                R.wait("tensor", VE)
                                R.op("tensor", "matmul", bc[:, 0:512],
                                     ones_f[:], row_b[:],
                                     start=True, stop=True, inc=TE)
                                R.wait("vector", TE)
                                R.op("vector", "tensor_copy", tmp_y[:], bc[:, 0:512])
                                R.op("vector", "tensor_mul", y_acc[rows, :], y_acc[rows, :],
                                     rb_sb[rows, :], inc=VE)
                                R.wait("vector", VE)
                                R.op("vector", "tensor_mul", tmp_y[rows, :], yps[rows, :],
                                     tmp_y[rows, :], inc=VE)
                                R.wait("vector", VE)
                                R.op("vector", "tensor_add", y_acc[rows, :], y_acc[rows, :],
                                     tmp_y[rows, :], inc=VE)
                                R.wait("tensor", VE)
                        R.op("vector", "reciprocal", r_acc[:], r_acc[:], inc=VE)
                        R.wait("tensor", VE)
                        for qs in range(4):
                            R.op("tensor", "transpose", smalls[0:1, qs * 128:(qs + 1) * 128],
                                 r_acc[:, qs:qs + 1], ident_f[:],
                                 inc=(TE if qs == 3 else None))
                        R.wait("vector", TE)
                        R.op("vector", "tensor_copy", row_a[:], smalls[0:1, 0:512],
                             inc=VE)
                        R.wait("tensor", VE)
                        R.op("tensor", "matmul", bc[:, 0:512], ones_f[:],
                             row_a[:], start=True, stop=True, inc=TE)
                        R.wait("vector", TE)
                        R.op("vector", "tensor_copy", rb_sb[:], bc[:, 0:512], inc=VE)
                        R.wait("vector", VE)
                        R.op("vector", "tensor_mul", ynT[rows, ic * 512:(ic + 1) * 512],
                             y_acc[rows, :], rb_sb[rows, :], inc=VE)
                        R.wait("tensor", VE)

                if debug:
                    R.wait("gpsimd", VE)
                    R.dma("gpsimd", dbg_yn[qb], ynT[:], OT)
                R.wait("tensor", GD)
                for ts in range(4):
                    tt = qb * 4 + ts
                    R.wait("scalar", VE)
                    for half in range(2):
                        for ic in range(8):
                            R.op("tensor", "matmul", bc[:, 0:512],
                                 ynT[:, ic * 512 + ts * 128: ic * 512 + ts * 128 + 128],
                                 wo_sb[:, ic * 1024 + half * 512: ic * 1024 + half * 512 + 512],
                                 start=(ic == 0), stop=(ic == 7),
                                 inc=(TE if ic == 7 else None))
                        R.wait("scalar", TE)
                        R.op("scalar", "copy", attn_f32[:, half * 512:(half + 1) * 512],
                             bc[:, 0:512], inc=SE)
                        R.wait("tensor", SE)
                    R.wait("vector", SE)
                    emit_ln_stats(R, lt, attn_f32[:], 1024, VE, SE)
                    R.op("vector", "tensor_scalar", attn_f32[:], attn_f32[:], lt.negm[:],
                         lt.rstd[:], ALU.add, op1=ALU.mult, inc=VE)
                    R.wait("vector", VE)
                    R.op("vector", "tensor_add", x1b[:], attn_f32[:],
                         xq_sb[:, tt * 1024:(tt + 1) * 1024], inc=VE)
                    R.wait("gpsimd", VE)
                    R.dma("gpsimd", x1out[tt * 128:(tt + 1) * 128, :], x1b[:], OT)
                    R.wait("vector", OT)

            if debug:
                R.dma("gpsimd", dbg_k, kT_d.ap(), OT)
                R.dma("gpsimd", dbg_v, v_d.ap(), OT)
                R.dma("gpsimd", dbg_q, qT[:], OT)
            R.wait("gpsimd", OT)
            R.wait("gpsimd", SP)
            R.flush(block)
    return nc


def make_mtail(par):
    tri = np.triu(np.full((128, 128), NEG, np.float32), 1)
    zero = np.zeros((128, 128), np.float32)
    allm = np.full((128, 128), NEG, np.float32)
    out = np.empty((16, 5, 128, 128), np.float32)
    for j in range(16):
        A = OWN[par][j // 4]
        V = 4 * A + (j % 4) + 1
        for s in range(5):
            kt = KK(j) - 5 + s
            out[j, s] = zero if kt < V - 1 else (tri if kt == V - 1 else allm)
    return out


def stage_A(inputs):
    x = np.ascontiguousarray(inputs["L0_attn_x"], np.float32)
    nk = np.ascontiguousarray(inputs["L0_attn_null_k"], np.float32).reshape(16, 64)
    nv = np.ascontiguousarray(inputs["L0_attn_null_v"], np.float32).reshape(16, 64)
    nullk_p = np.ascontiguousarray(
        nk.reshape(8, 2, 64).transpose(1, 2, 0).reshape(128, 8))
    nullv_r = np.ascontiguousarray(nv.reshape(1, 1024))
    lsc_r = np.ascontiguousarray(
        inputs["L0_attn_logit_scale"], np.float32).reshape(1, 16)
    ident = np.eye(128, dtype=np.float32)
    mt = [make_mtail(0), make_mtail(1)]
    maps = []
    for c in range(NCORES):
        b, par = c // 2, c % 2
        rows = own_rows(par)
        maps.append({
            "xfull": x[b],
            "xq": np.ascontiguousarray(x[b][rows]),
            "wq": np.ascontiguousarray(inputs["L0_attn_Wq"], np.float32),
            "wk": np.ascontiguousarray(inputs["L0_attn_Wk"], np.float32),
            "wv": np.ascontiguousarray(inputs["L0_attn_Wv"], np.float32),
            "wo": np.ascontiguousarray(inputs["L0_attn_Wo"], np.float32),
            "nullk_p": nullk_p, "nullv_r": nullv_r, "lsc_r": lsc_r,
            "ident_in": ident, "mtail": mt[par],
        })
    return maps


def gather_own(results, key):
    out = np.empty((4, T, D), np.float32)
    for c in range(NCORES):
        b, par = c // 2, c % 2
        out[b][own_rows(par)] = results[c][key]
    return out


_NC_CACHE = {}


def run_A(inputs, debug=False):
    key = ("A", debug)
    if key not in _NC_CACHE:
        _NC_CACHE[key] = build_launch_A(debug)
    maps = stage_A(inputs)
    res = run_bass_kernel_spmd(_NC_CACHE[key], maps, core_ids=list(range(NCORES)))
    if debug:
        return gather_own(res.results, "x1"), res.results
    return gather_own(res.results, "x1")



def build_launch_B(fc=None):
    pre = "" if fc is None else "b_"
    S127 = 1.0 / 127.0
    if fc is None:
        nc = bass.Bass("TRN2", target_bir_lowering=False)
        R = Rec(nc)
        x1i = nc.dram_tensor("x1i", [OWNT, D], F32, kind="ExternalInput").ap()
        wgT = nc.dram_tensor("wgT", [D, 3072], I32, kind="ExternalInput").ap()
        wuT = nc.dram_tensor("wuT", [D, 3072], I32, kind="ExternalInput").ap()
        wdT = nc.dram_tensor("wdT", [3072, D], I32, kind="ExternalInput").ap()
        ident_in = nc.dram_tensor("ident_in", [128, 128], F32, kind="ExternalInput").ap()
        x2out = nc.dram_tensor("x2", [OWNT, D], F32, kind="ExternalOutput").ap()
    else:
        nc, R = fc["nc"], fc["R"]
        io = fc["io"]
        x1i = fc["x1_d"].ap()
        wgT, wuT, wdT = io["b_wgT"], io["b_wuT"], io["b_wdT"]
        ident_in = io["ident_in"]
        x2out = fc["x2_d"].ap()

    with ExitStack() as CTX:
        sb = lambda n, s, dt: CTX.enter_context(nc.sbuf_tensor(pre + n, s, dt))
        ps = lambda n, s, dt=F32: CTX.enter_context(nc.psum_tensor(pre + n, s, dt))
        if fc is None:
            sem = lambda n: Sem(CTX.enter_context(nc.semaphore(n)))
            LD = sem("LD"); GD = sem("GD")
            TE = sem("TE"); VE = sem("VE"); SE = sem("SE"); OT = sem("OT")
        else:
            sm = fc["sems"]
            LD, GD = sm["LD"], sm["GD"]
            TE, VE, SE, OT = sm["TE"], sm["VE"], sm["SE"], sm["OT"]

        wg_sb = sb("wg_sb", [128, 8 * 3072], BF16)
        wu_sb = sb("wu_sb", [128, 8 * 3072], BF16)
        wd_sb = [sb(f"wd_sb{i}", [128, 24 * 128], BF16) for i in range(2)]
        xc = sb("xc", [128, 4 * 1024], F32)
        xhat = [sb(f"xhat{i}", [128, 1024], BF16) for i in range(2)]
        xTc = sb("xTc", [128, 8 * 512], BF16)
        Bt = sb("Bt", [128, 24 * 512], BF16)
        asb = [sb(f"asb{i}", [128, 512], BF16) for i in range(2)]
        ytb = sb("ytb", [128, 8 * 512], BF16)
        yf = sb("yf", [128, 1024], F32)
        x2b = sb("x2b", [128, 1024], F32)
        ident = sb("ident", [128, 128], BF16)
        lt = LnTiles(nc, CTX, tag=pre)
        pps = [ps("pps0", [128, 512]), ps("pps1", [128, 512])]
        pp2 = [ps("pp20", [128, 512]), ps("pp21", [128, 512])]
        Tpack = ps("Tpack", [128, 1024], BF16)
        dps = ps("dps", [128, 512])

        block = fc["block"] if fc else CTX.enter_context(nc.Block())

        R.dma("gpsimd", ident[:], ident_in, GD)
        R.dma("gpsimd", wg_sb[:].rearrange("p (c n) -> p c n", n=3072),
              wgT.rearrange("(c p) n -> p c n", p=128), GD)
        R.dma("gpsimd", wu_sb[:].rearrange("p (c n) -> p c n", n=3072),
              wuT.rearrange("(c p) n -> p c n", p=128), GD)
        R.wait("tensor", GD)
        wdT3 = wdT.rearrange("(c p) n -> p c n", p=128)

        def ln_transpose_tile(src_f32, xhat_t, xT_t, t4):
            emit_ln_stats(R, lt, src_f32, 1024, VE, SE)
            R.wait("vector", TE)
            R.op("vector", "tensor_scalar", xhat_t[:], src_f32, lt.negm[:],
                 lt.rstd[:], ALU.add, op1=ALU.mult, inc=VE)
            R.wait("tensor", VE)
            xT3 = xT_t[:].rearrange("p (c n) -> p c n", n=512)
            for half in range(2):
                for c4 in range(4):
                    c = half * 4 + c4
                    R.op("tensor", "transpose", Tpack[:, c4 * 128:(c4 + 1) * 128],
                         xhat_t[:, c * 128:(c + 1) * 128], ident[:],
                         inc=(TE if c4 == 3 else None))
                R.wait("vector", TE)
                R.op("vector", "tensor_copy",
                     xT3[:, half * 4:half * 4 + 4, t4 * 128:(t4 + 1) * 128],
                     Tpack[:, 0:512].rearrange("p (k n) -> p k n", n=128), inc=VE)
                R.wait("tensor", VE)

        for qc in range(4):
            R.wait("sync", VE)
            R.wait("sync", OT)
            R.dma("sync", xc[:].rearrange("p (t n) -> p t n", n=1024),
                  x1i[qc * 512:(qc + 1) * 512, :].rearrange("(t p) n -> p t n", p=128), LD)
            R.wait("vector", LD)
            for t4 in range(4):
                ln_transpose_tile(xc[:, t4 * 1024:(t4 + 1) * 1024], xhat[t4 % 2], xTc, t4)
            se_hist = [0, 0]
            ve_hist = [0, 0]
            for hc in range(24):
                bidx = hc % 2
                R.wait("tensor", SE, se_hist[bidx])
                R.wait("tensor", VE, ve_hist[bidx])
                for kc in range(8):
                    R.op("tensor", "matmul", pps[bidx][:, 0:512],
                         wg_sb[:, kc * 3072 + hc * 128: kc * 3072 + hc * 128 + 128],
                         xTc[:, kc * 512:(kc + 1) * 512],
                         start=(kc == 0), stop=(kc == 7))
                for kc in range(8):
                    R.op("tensor", "matmul", pp2[bidx][:, 0:512],
                         wu_sb[:, kc * 3072 + hc * 128: kc * 3072 + hc * 128 + 128],
                         xTc[:, kc * 512:(kc + 1) * 512],
                         start=(kc == 0), stop=(kc == 7),
                         inc=(TE if kc == 7 else None))
                R.wait("scalar", TE)
                R.op("scalar", "activation", asb[bidx][:], pps[bidx][:, 0:512], AF.Silu,
                     scale=S127, inc=SE)
                R.wait("vector", SE)
                R.op("vector", "tensor_tensor", Bt[:, hc * 512:(hc + 1) * 512],
                     pp2[bidx][:, 0:512], asb[bidx][:], ALU.mult, inc=VE)
                se_hist[bidx] = SE.n
                ve_hist[bidx] = VE.n
            R.wait("tensor", VE)
            for oc in range(8):
                wdb = wd_sb[oc % 2]
                R.wait("gpsimd", TE)
                R.dma("gpsimd", wdb[:].rearrange("p (c n) -> p c n", n=128),
                      wdT3[:, :, oc * 128:(oc + 1) * 128], GD)
                R.wait("tensor", GD)
                R.wait("tensor", SE)
                for hc in range(24):
                    R.op("tensor", "matmul", dps[:, 0:512],
                         wdb[:, hc * 128:(hc + 1) * 128],
                         Bt[:, hc * 512:(hc + 1) * 512],
                         start=(hc == 0), stop=(hc == 23),
                         inc=(TE if hc == 23 else None))
                R.wait("scalar", TE)
                R.op("scalar", "mul", ytb[:, oc * 512:(oc + 1) * 512], dps[:, 0:512],
                     S127 * S127, inc=SE)
            for ts in range(4):
                R.wait("tensor", SE)
                R.wait("tensor", VE)
                for half in range(2):
                    for c4 in range(4):
                        oc = half * 4 + c4
                        R.op("tensor", "transpose", Tpack[:, c4 * 128:(c4 + 1) * 128],
                             ytb[:, oc * 512 + ts * 128: oc * 512 + ts * 128 + 128],
                             ident[:], inc=(TE if c4 == 3 else None))
                    R.wait("vector", TE)
                    R.op("vector", "tensor_copy",
                         yf[:, half * 512:(half + 1) * 512],
                         Tpack[:, 0:512], inc=VE)
                    R.wait("tensor", VE)
                emit_ln_stats(R, lt, yf[:], 1024, VE, SE)
                R.op("vector", "tensor_scalar", yf[:], yf[:], lt.negm[:],
                     lt.rstd[:], ALU.add, op1=ALU.mult, inc=VE)
                R.wait("vector", VE)
                R.wait("vector", OT)
                R.op("vector", "tensor_add", x2b[:], yf[:],
                     xc[:, ts * 1024:(ts + 1) * 1024], inc=VE)
                R.wait("gpsimd", VE)
                tt = qc * 4 + ts
                R.dma("gpsimd", x2out[tt * 128:(tt + 1) * 128, :], x2b[:], OT)
        R.wait("gpsimd", OT)
        R.flush(block)
    return nc


def stage_B(x1, inputs):
    wgT = np.ascontiguousarray(inputs["L0_ffn_w_gate"].T)
    wuT = np.ascontiguousarray(inputs["L0_ffn_w_up"].T)
    wdT = np.ascontiguousarray(inputs["L0_ffn_w_down"].T)
    ident = np.eye(128, dtype=np.float32)
    maps = []
    for c in range(NCORES):
        b, par = c // 2, c % 2
        maps.append({
            "x1i": np.ascontiguousarray(x1[b][own_rows(par)]),
            "wgT": wgT, "wuT": wuT, "wdT": wdT, "ident_in": ident,
        })
    return maps


def run_B(x1, inputs):
    if "B" not in _NC_CACHE:
        _NC_CACHE["B"] = build_launch_B()
    res = run_bass_kernel_spmd(_NC_CACHE["B"], stage_B(x1, inputs),
                               core_ids=list(range(NCORES)))
    return gather_own(res.results, "x2")



def build_launch_C(maxphase=3, fc=None):
    pre = "" if fc is None else "c_"
    if fc is None:
        nc = bass.Bass("TRN2", target_bir_lowering=False)
        R = Rec(nc)
        xfull = nc.dram_tensor("xfull", [T, D], F32, kind="ExternalInput").ap()
        xq = nc.dram_tensor("xq", [OWNT, D], F32, kind="ExternalInput").ap()
        wqkv = nc.dram_tensor("wqkv", [D, 3072], F32, kind="ExternalInput").ap()
        wf2 = nc.dram_tensor("wf2", [128, 128], F32, kind="ExternalInput").ap()
        bf_p = nc.dram_tensor("bf_p", [128, 1], F32, kind="ExternalInput").ap()
        bf_r = nc.dram_tensor("bf_r", [1, 128], F32, kind="ExternalInput").ap()
        wout = nc.dram_tensor("wout", [D, D], F32, kind="ExternalInput").ap()
        bout_r = nc.dram_tensor("bout_r", [1, D], F32, kind="ExternalInput").ap()
        ident_in = nc.dram_tensor("ident_in", [128, 128], F32, kind="ExternalInput").ap()
        x3out = nc.dram_tensor("x3", [OWNT, D], F32, kind="ExternalOutput").ap()
    else:
        nc, R = fc["nc"], fc["R"]
        io = fc["io"]
        xfull = fc["x2g"].ap()
        xq = fc["x2_d"].ap()
        wqkv, wf2, bf_p, bf_r = io["c_wqkv"], io["c_wf2"], io["c_bf_p"], io["c_bf_r"]
        wout, bout_r = io["c_wout"], io["c_bout_r"]
        ident_in = io["ident_in"]
        x3out = fc["x3_d"].ap()

    with ExitStack() as CTX:
        sb = lambda n, s, dt: CTX.enter_context(nc.sbuf_tensor(pre + n, s, dt))
        ps = lambda n, s, dt=F32: CTX.enter_context(nc.psum_tensor(pre + n, s, dt))
        if fc is None:
            sem = lambda n: Sem(CTX.enter_context(nc.semaphore(n)))
            LD = sem("LD"); GD = sem("GD")
            TE = sem("TE"); VE = sem("VE"); SE = sem("SE"); OT = sem("OT")
        else:
            sm = fc["sems"]
            LD, GD = sm["LD"], sm["GD"]
            TE, VE, SE, OT = sm["TE"], sm["VE"], sm["SE"], sm["OT"]

        wq_sb = sb("wq_sb", [128, 8 * 3072], BF16)
        wout_sb = sb("wout_sb", [128, 8 * 1024], BF16)
        wf_sb = sb("wf_sb", [128, 128], BF16)
        bfp_sb = sb("bfp_sb", [128, 1], F32)
        bfr_sb = sb("bfr_sb", [1, 128], BF16)
        bout_sb = sb("bout_sb", [1, D], BF16)
        ident = sb("ident", [128, 128], BF16)
        ones_b = sb("ones_b", [1, 128], BF16)
        ones_c = sb("ones_c", [128, 1], BF16)
        xb = [sb(f"xb{i}", [128, 1024], F32) for i in range(2)]
        xhat = [sb(f"xhat{i}", [128, 1024], BF16) for i in range(2)]
        xTc = sb("xTc", [128, 8 * 512], BF16)
        kTp = sb("kTp", [128, 8 * 512], BF16)
        Kp = sb("Kp", [128, 4 * 1024], BF16)
        Vp = sb("Vp", [128, 4 * 1024], BF16)
        kv_acc = sb("kv_acc", [128, 8 * 65], F32)
        kvb = sb("kvb", [128, 16 * 64], BF16)
        ksb = sb("ksb", [128, 16], BF16)
        qpT = sb("qpT", [128, 8 * 2048], BF16)
        nrm = sb("nrm", [128, 16], F32)
        obn = sb("obn", [128, 1024], BF16)
        oT = sb("oT", [128, 8 * 512], BF16)
        xqc = sb("xqc", [128, 4 * 1024], F32)
        af = sb("af", [128, 1024], F32)
        x3b = sb("x3b", [128, 1024], F32)
        lt = LnTiles(nc, CTX, tag=pre)
        pps = ps("pps", [128, 512])
        pp2 = ps("pp2", [128, 512])
        Tpack = ps("Tpack", [128, 1024], BF16)
        fps = ps("fps", [128, 128])
        kvps = ps("kvps", [64, 65])
        ops1 = ps("ops1", [128, 512])
        ops2 = ps("ops2", [128, 512])
        smalls = ps("smalls", [128, 512])

        block = fc["block"] if fc else CTX.enter_context(nc.Block())

        R.dma("gpsimd", ident[:], ident_in, GD)
        R.dma("gpsimd", wf_sb[:], wf2, GD)
        R.dma("gpsimd", bfr_sb[:], bf_r, GD)
        R.dma("gpsimd", bout_sb[:], bout_r, GD)
        R.dma("sync", bfp_sb[:], bf_p, LD)
        R.dma("gpsimd", wq_sb[:].rearrange("p (c n) -> p c n", n=3072),
              wqkv.rearrange("(c p) n -> p c n", p=128), GD)
        R.dma("gpsimd", wout_sb[:].rearrange("p (c n) -> p c n", n=1024),
              wout.rearrange("(c p) n -> p c n", p=128), GD)
        R.op("vector", "memset", ones_b[:], 1.0)
        R.op("vector", "memset", ones_c[:], 1.0)
        R.op("vector", "memset", kv_acc[:], 0.0, inc=VE)
        R.wait("tensor", GD)
        R.wait("tensor", LD)
        R.wait("tensor", VE)

        def ln_transpose_tile(src_f32, xhat_t, xT_t, t4):
            emit_ln_stats(R, lt, src_f32, 1024, VE, SE)
            R.wait("vector", TE)
            R.op("vector", "tensor_scalar", xhat_t[:], src_f32, lt.negm[:],
                 lt.rstd[:], ALU.add, op1=ALU.mult, inc=VE)
            R.wait("tensor", VE)
            xT3 = xT_t[:].rearrange("p (c n) -> p c n", n=512)
            for half in range(2):
                for c4 in range(4):
                    c = half * 4 + c4
                    R.op("tensor", "transpose", Tpack[:, c4 * 128:(c4 + 1) * 128],
                         xhat_t[:, c * 128:(c + 1) * 128], ident[:],
                         inc=(TE if c4 == 3 else None))
                R.wait("vector", TE)
                R.op("vector", "tensor_copy",
                     xT3[:, half * 4:half * 4 + 4, t4 * 128:(t4 + 1) * 128],
                     Tpack[:, 0:512].rearrange("p (k n) -> p k n", n=128), inc=VE)
                R.wait("tensor", VE)

        for tc in range(8):
            for t4 in range(4):
                tt = tc * 4 + t4
                R.wait("sync", VE)
                R.dma("sync", xb[tt % 2][:], xfull[tt * 128:(tt + 1) * 128, :], LD)
                R.wait("vector", LD)
                ln_transpose_tile(xb[tt % 2][:], xhat[tt % 2], xTc, t4)
            for ic in range(8):
                R.wait("tensor", SE)
                for kc in range(8):
                    R.op("tensor", "matmul", pps[:, 0:512],
                         wq_sb[:, kc * 3072 + 1024 + ic * 128: kc * 3072 + 1024 + ic * 128 + 128],
                         xTc[:, kc * 512:(kc + 1) * 512],
                         start=(kc == 0), stop=(kc == 7),
                         inc=(TE if kc == 7 else None))
                R.wait("vector", TE)
                R.op("vector", "tensor_copy", kTp[:, ic * 512:(ic + 1) * 512],
                     pps[:, 0:512], inc=VE)
                R.wait("tensor", VE)
            for t4 in range(4):
                for half in range(2):
                    bank = pps if half == 0 else pp2
                    for kc in range(8):
                        R.op("tensor", "matmul", bank[:, 0:512],
                             xTc[:, kc * 512 + t4 * 128: kc * 512 + t4 * 128 + 128],
                             wq_sb[:, kc * 3072 + 2048 + half * 512: kc * 3072 + 2048 + half * 512 + 512],
                             start=(kc == 0), stop=(kc == 7),
                             inc=(TE if kc == 7 else None))
                R.wait("vector", TE)
                R.op("vector", "tensor_copy", Vp[:, t4 * 1024: t4 * 1024 + 512], pps[:, 0:512])
                R.op("vector", "tensor_copy", Vp[:, t4 * 1024 + 512: t4 * 1024 + 1024],
                     pp2[:, 0:512], inc=VE)
                R.wait("tensor", VE)
            for ic in range(8):
                for t4 in range(4):
                    R.wait("tensor", SE)
                    R.op("tensor", "matmul", fps[:, 0:128],
                         kTp[:, ic * 512 + t4 * 128: ic * 512 + t4 * 128 + 128],
                         wf_sb[:, 0:128], start=True, stop=False)
                    R.op("tensor", "matmul", fps[:, 0:128],
                         ones_b[0:1, 0:128], bfr_sb[:],
                         start=False, stop=True, inc=TE)
                    R.wait("scalar", TE)
                    R.op("scalar", "activation",
                         Kp[:, t4 * 1024 + ic * 128: t4 * 1024 + ic * 128 + 128],
                         fps[:, 0:128], AF.Relu, inc=SE)
            R.wait("tensor", SE)
            for ic in range(8):
                for hh in range(2):
                    h = 2 * ic + hh
                    rows = slice(64 * hh, 64 * hh + 64)
                    for t4 in range(4):
                        R.op("tensor", "matmul", kvps[:, 0:64],
                             Kp[:, t4 * 1024 + h * 64: t4 * 1024 + h * 64 + 64],
                             Vp[:, t4 * 1024 + h * 64: t4 * 1024 + h * 64 + 64],
                             start=(t4 == 0), stop=False)
                        R.op("tensor", "matmul", kvps[:, 64:65],
                             Kp[:, t4 * 1024 + h * 64: t4 * 1024 + h * 64 + 64],
                             ones_c[:], start=False, stop=(t4 == 3),
                             inc=(TE if t4 == 3 else None))
                    R.wait("vector", TE)
                    R.op("vector", "tensor_tensor",
                         kv_acc[rows, ic * 65:(ic + 1) * 65],
                         kv_acc[rows, ic * 65:(ic + 1) * 65],
                         kvps[:, 0:65], ALU.add, inc=VE)
                    R.wait("tensor", VE)

        if maxphase < 2:
            R.wait("gpsimd", VE)
            R.flush(block)
            return nc
        R.op("vector", "memset", kvb[:], 0.0)
        R.op("vector", "memset", ksb[:], 0.0, inc=VE)
        R.wait("vector", VE)
        kva = kv_acc[:].rearrange("p (c n) -> p c n", n=65)
        kvb3 = kvb[:].rearrange("p (c n) -> p c n", n=64)
        for ic in range(8):
            for hh in range(2):
                h = 2 * ic + hh
                rows = slice(64 * hh, 64 * hh + 64)
                R.op("vector", "tensor_copy", kvb3[rows, h, :], kva[rows, ic, 0:64])
                R.op("vector", "tensor_copy", ksb[rows, h:h + 1], kva[rows, ic, 64:65],
                     inc=(VE if h == 15 else None))

        for qc in range(4):
            for t4 in range(4):
                tt = qc * 4 + t4
                R.wait("sync", VE)
                R.dma("sync", xb[tt % 2][:], xq[tt * 128:(tt + 1) * 128, :], LD)
                R.wait("vector", LD)
                ln_transpose_tile(xb[tt % 2][:], xhat[tt % 2], xTc, t4)
            for ic in range(8):
                R.wait("tensor", SE)
                for kc in range(8):
                    R.op("tensor", "matmul", pps[:, 0:512],
                         wq_sb[:, kc * 3072 + ic * 128: kc * 3072 + ic * 128 + 128],
                         xTc[:, kc * 512:(kc + 1) * 512],
                         start=(kc == 0), stop=(kc == 7),
                         inc=(TE if kc == 7 else None))
                R.wait("vector", TE)
                R.op("vector", "tensor_copy", kTp[:, ic * 512:(ic + 1) * 512],
                     pps[:, 0:512], inc=VE)
                R.wait("tensor", VE)
                R.op("tensor", "matmul", pp2[:, 0:512], wf_sb[:, 0:128],
                     kTp[:, ic * 512:(ic + 1) * 512],
                     start=True, stop=True, inc=TE)
                R.wait("scalar", TE)
                R.op("scalar", "activation",
                     qpT[:, ic * 2048 + qc * 512: ic * 2048 + (qc + 1) * 512],
                     pp2[:, 0:512], AF.Relu, bias=bfp_sb[:], inc=SE)
                R.wait("tensor", SE)

        if maxphase < 2.2:
            R.wait("gpsimd", VE)
            R.flush(block)
            return nc
        for qc in range(4):
            R.wait("sync", OT)
            R.wait("sync", VE)
            R.dma("sync", xqc[:].rearrange("p (t n) -> p t n", n=1024),
                  xq[qc * 512:(qc + 1) * 512, :].rearrange("(t p) n -> p t n", p=128), LD)
            for ts in range(4):
                R.wait("tensor", SE)
                R.wait("tensor", VE)
                for ic in range(8):
                    lhsT = qpT[:, ic * 2048 + qc * 512 + ts * 128:
                               ic * 2048 + qc * 512 + ts * 128 + 128]
                    for hh in range(2):
                        h = 2 * ic + hh
                        bank = ops1 if h < 8 else ops2
                        col = (h % 8) * 64
                        R.op("tensor", "matmul", bank[:, col:col + 64], lhsT,
                             kvb[:, h * 64:(h + 1) * 64], start=True, stop=True)
                        R.op("tensor", "matmul", smalls[:, h:h + 1], lhsT,
                             ksb[:, h:h + 1], start=True, stop=True,
                             inc=(TE if h == 15 else None))
                R.wait("vector", TE)
                if maxphase < 2.22:
                    continue
                R.op("vector", "tensor_scalar", nrm[:], smalls[:, 0:16], 1e-6, None,
                     ALU.add, inc=VE)
                R.wait("vector", VE)
                R.op("vector", "reciprocal", nrm[:], nrm[:], inc=VE)
                R.wait("vector", VE)
                if maxphase < 2.24:
                    continue
                for h in range(16):
                    bank = ops1 if h < 8 else ops2
                    col = (h % 8) * 64
                    R.op("vector", "tensor_scalar", obn[:, h * 64:(h + 1) * 64],
                         bank[:, col:col + 64], nrm[:, h:h + 1], None, ALU.mult,
                         inc=(VE if h == 15 else None))
                R.wait("tensor", VE)
                if maxphase < 2.5:
                    continue
                for half in range(2):
                    for c4 in range(4):
                        c = half * 4 + c4
                        R.op("tensor", "transpose", Tpack[:, c4 * 128:(c4 + 1) * 128],
                             obn[:, c * 128:(c + 1) * 128], ident[:],
                             inc=(TE if c4 == 3 else None))
                    R.wait("vector", TE)
                    R.op("vector", "tensor_copy",
                         oT[:].rearrange("p (c n) -> p c n", n=512)[
                             :, half * 4:half * 4 + 4, ts * 128:(ts + 1) * 128],
                         Tpack[:, 0:512].rearrange("p (k n) -> p k n", n=128), inc=VE)
                    R.wait("tensor", VE)
            if maxphase < 2.75:
                continue
            R.wait("vector", LD)
            for ts in range(4):
                tt = qc * 4 + ts
                R.wait("scalar", VE)
                for half in range(2):
                    for ic in range(8):
                        R.op("tensor", "matmul", pps[:, 0:512],
                             oT[:, ic * 512 + ts * 128: ic * 512 + ts * 128 + 128],
                             wout_sb[:, ic * 1024 + half * 512: ic * 1024 + half * 512 + 512],
                             start=(ic == 0), stop=False)
                    R.op("tensor", "matmul", pps[:, 0:512], ones_b[0:1, 0:128],
                         bout_sb[0:1, half * 512:(half + 1) * 512],
                         start=False, stop=True, inc=TE)
                    R.wait("scalar", TE)
                    R.op("scalar", "copy", af[:, half * 512:(half + 1) * 512],
                         pps[:, 0:512], inc=SE)
                    R.wait("tensor", SE)
                R.wait("vector", SE)
                emit_ln_stats(R, lt, af[:], 1024, VE, SE)
                R.op("vector", "tensor_scalar", af[:], af[:], lt.negm[:],
                     lt.rstd[:], ALU.add, op1=ALU.mult, inc=VE)
                R.wait("vector", VE)
                R.wait("vector", OT)
                R.op("vector", "tensor_add", x3b[:], af[:],
                     xqc[:, ts * 1024:(ts + 1) * 1024], inc=VE)
                R.wait("gpsimd", VE)
                R.dma("gpsimd", x3out[tt * 128:(tt + 1) * 128, :], x3b[:], OT)
        R.wait("gpsimd", OT)
        R.flush(block)
    return nc


def stage_C(x2, inputs):
    wf = np.ascontiguousarray(inputs["L1_attn_W_f"], np.float32)
    wf2 = np.zeros((128, 128), np.float32)
    wf2[0:64, 0:64] = wf
    wf2[64:128, 64:128] = wf
    bf = np.ascontiguousarray(inputs["L1_attn_b_f"], np.float32)
    bf_p = np.concatenate([bf, bf]).reshape(128, 1)
    bf_r = np.concatenate([bf, bf]).reshape(1, 128)
    bout_r = np.ascontiguousarray(inputs["L1_attn_b_out"], np.float32).reshape(1, D)
    ident = np.eye(128, dtype=np.float32)
    maps = []
    for c in range(NCORES):
        b, par = c // 2, c % 2
        maps.append({
            "xfull": x2[b],
            "xq": np.ascontiguousarray(x2[b][own_rows(par)]),
            "wqkv": np.ascontiguousarray(inputs["L1_attn_W_qkv"], np.float32),
            "wf2": wf2, "bf_p": bf_p, "bf_r": bf_r,
            "wout": np.ascontiguousarray(inputs["L1_attn_W_out"], np.float32),
            "bout_r": bout_r, "ident_in": ident,
        })
    return maps


def run_C(x2, inputs, maxphase=3):
    if ("C", maxphase) not in _NC_CACHE:
        _NC_CACHE[("C", maxphase)] = build_launch_C(maxphase)
    _NC_CACHE["C"] = _NC_CACHE[("C", maxphase)]
    res = run_bass_kernel_spmd(_NC_CACHE["C"], stage_C(x2, inputs),
                               core_ids=list(range(NCORES)))
    return gather_own(res.results, "x3")


def kernel(**inputs):
    x1 = run_A(inputs)
    x2 = run_B(x1, inputs)
    x3 = run_C(x2, inputs)
    return x3

def build_launch_D(ln_in, sandwich, debug=False, fc=None, li=0):
    pre = "" if fc is None else f"d{li}_"
    QKS = 8.0
    if fc is None:
        nc = bass.Bass("TRN2", target_bir_lowering=False)
        R = Rec(nc)
        xe = nc.dram_tensor("xe", [4, 1664, D], F32, kind="ExternalInput").ap()
        xq3 = nc.dram_tensor("xq3", [OWNT, D], F32, kind="ExternalInput").ap()
        wqkv = nc.dram_tensor("wqkv", [D, 1536], F32, kind="ExternalInput").ap()
        wo = nc.dram_tensor("wo", [512, D], F32, kind="ExternalInput").ap()
        w1r = nc.dram_tensor("w1r", [D, 5632], F32, kind="ExternalInput").ap()
        w2r = nc.dram_tensor("w2r", [2816, D], F32, kind="ExternalInput").ap()
        b1r = nc.dram_tensor("b1r", [1, 5632], F32, kind="ExternalInput").ap()
        b2r = nc.dram_tensor("b2r", [1, D], F32, kind="ExternalInput").ap()
        lnb_p = nc.dram_tensor("lnb_p", [128, 8], F32, kind="ExternalInput").ap()
        lng_p = nc.dram_tensor("lng_p", [128, 8], F32, kind="ExternalInput").ap()
        qs_r = nc.dram_tensor("qs_r", [1, 64], F32, kind="ExternalInput").ap()
        ks_r = nc.dram_tensor("ks_r", [1, 64], F32, kind="ExternalInput").ap()
        rotq = nc.dram_tensor("rotq", [4, 2, 1024, 512], F32, kind="ExternalInput").ap()
        rotk = nc.dram_tensor("rotk", [4, 2, 1536, 512], F32, kind="ExternalInput").ap()
        msk = nc.dram_tensor("msk", [4, 8, 128, 640], F32, kind="ExternalInput").ap()
        ident_in = nc.dram_tensor("ident_in", [128, 128], F32, kind="ExternalInput").ap()
        zout = nc.dram_tensor("zout", [OWNT, D], F32, kind="ExternalOutput").ap()
    else:
        nc, R = fc["nc"], fc["R"]
        io = fc["io"]
        xe = fc["xe_d"].ap()
        xq3 = fc["x3_d"].ap()
        wqkv, wo = io[pre + "wqkv"], io[pre + "wo"]
        w1r, w2r, b1r, b2r = io[pre + "w1r"], io[pre + "w2r"], io[pre + "b1r"], io[pre + "b2r"]
        lnb_p, lng_p = io[pre + "lnb_p"], io[pre + "lng_p"]
        qs_r, ks_r = io[pre + "qs_r"], io[pre + "ks_r"]
        rotq, rotk, msk = io["d_rotq"], io["d_rotk"], io["d_msk"]
        ident_in = io["ident_in"]
        zout = (fc["z1_d"].ap() if li == 0 else fc["out"])

    xm_d = nc.dram_tensor(pre + "xm_d", [128, 4 * 8 * 1024], F32)
    h2_d = nc.dram_tensor(pre + "h2_d", [128, 4 * 8 * 512], BF16)
    if debug:
        dbg_xm = nc.dram_tensor("dbg_xm", [128, 4 * 8 * 1024], F32, kind="ExternalOutput").ap()
        dbg_h2 = nc.dram_tensor("dbg_h2", [128, 4 * 8 * 512], BF16, kind="ExternalOutput").ap()
        dbg_q = nc.dram_tensor("dbg_q", [128, 4 * 1024], BF16, kind="ExternalOutput").ap()
        dbg_k = nc.dram_tensor("dbg_k", [128, 4 * 1536], BF16, kind="ExternalOutput").ap()
        dbg_v = nc.dram_tensor("dbg_v", [128, 12 * 512], BF16, kind="ExternalOutput").ap()
        dbg_o = nc.dram_tensor("dbg_o", [128, 4 * 1024], BF16, kind="ExternalOutput").ap()
        dbg_zT = nc.dram_tensor("dbg_zT", [128, 8 * 1664], BF16, kind="ExternalOutput").ap()

    with ExitStack() as CTX:
        sb = lambda n, s, dt: CTX.enter_context(nc.sbuf_tensor(pre + n, s, dt))
        ps = lambda stk, n, s, dt=F32: stk.enter_context(nc.psum_tensor(pre + n, s, dt))
        if fc is None:
            sem = lambda n: Sem(CTX.enter_context(nc.semaphore(n)))
            LD = sem("LD"); GD = sem("GD"); SP = sem("SP")
            TE = sem("TE"); VE = sem("VE"); SE = sem("SE"); OT = sem("OT")
        else:
            sm = fc["sems"]
            LD, GD, SP = sm["LD"], sm["GD"], sm["SP"]
            TE, VE, SE, OT = sm["TE"], sm["VE"], sm["SE"], sm["OT"]

        ident = sb("ident", [128, 128], BF16)
        ident_f = sb("ident_f", [128, 128], F32)
        ones_f = sb("ones_f", [1, 128], F32)
        ones_bf = sb("ones_bf", [1, 128], BF16)
        qst = sb("qst", [128, 512], F32)
        kst = sb("kst", [128, 512], F32)
        lng_sb = sb("lng_sb", [128, 8], F32)
        lt = LnTiles(nc, CTX, tag=pre)
        l2r = sb("l2r", [128, 8], F32)
        Tpack = ps(CTX, "Tpack", [128, 1024], BF16)
        smalls = ps(CTX, "smalls", [128, 512], F32)

        block = fc["block"] if fc else CTX.enter_context(nc.Block())

        R.dma("gpsimd", ident[:], ident_in, GD)
        R.dma("sync", ident_f[:], ident_in, LD)
        R.dma("sync", lng_sb[:], lng_p, LD)
        R.op("vector", "memset", ones_f[:], 1.0)
        R.op("vector", "memset", ones_bf[:], 1.0, inc=VE)
        R.wait("tensor", VE)
        for row_ap, dst in ((qs_r, qst), (ks_r, kst)):
            rsb = sb("row_" + dst.name, [1, 64], F32)
            R.dma("sync", rsb[:], row_ap, LD)
            R.wait("tensor", LD)
            R.op("tensor", "matmul", smalls[:, 0:64], ones_f[:], rsb[:],
                 start=True, stop=True, inc=TE)
            R.wait("vector", TE)
            for h in range(8):
                R.op("vector", "tensor_copy", dst[:, h * 64:(h + 1) * 64],
                     smalls[:, 0:64], inc=(VE if h == 7 else None))
            R.wait("tensor", VE)

        with ExitStack() as PH1:
            sb1 = lambda n, s, dt: PH1.enter_context(nc.sbuf_tensor(pre + n, s, dt))
            wq_sb = sb1("wq_sb", [128, 8 * 1536], BF16)
            wo_sb = sb1("wo_sb", [128, 4 * 1024], BF16)
            xb = [sb1(f"xb{i}", [128, 1024], F32) for i in range(2)]
            zf = sb1("zf", [128, 1024], F32)
            zhat = [sb1(f"zhat{i}", [128, 1024], BF16) for i in range(2)]
            zT = sb1("zT", [128, 8 * 1664], BF16)
            qev = sb1("qev", [128, 512], F32)
            q2s = sb1("q2s", [128, 512], F32)
            rotA = [sb1(f"rotA{i}", [128, 512], F32) for i in range(2)]
            rotB = [sb1(f"rotB{i}", [128, 512], F32) for i in range(2)]
            t1b = sb1("t1b", [128, 512], F32)
            t2b = sb1("t2b", [128, 512], F32)
            q3b = sb1("q3b", [128, 512], BF16)
            qfT = sb1("qfT", [128, 4 * 1024], BF16)
            kfT = sb1("kfT", [128, 4 * 1536], BF16)
            vtm = sb1("vtm", [128, 12 * 512], BF16)
            msk_sb = sb1("msk_sb", [128, 8 * 640], BF16)
            P = sb1("P", [128, 640], BF16)
            PT = sb1("PT", [128, 12 * 1024], BF16)
            oT = sb1("oT", [128, 4 * 1024], BF16)
            rg = sb1("rg", [128, 8], F32)
            row_a = sb1("row_a", [1, 1024], F32)
            rb_sb = sb1("rb_sb", [128, 1024], F32)
            xm = [sb1(f"xm{i}", [128, 1024], F32) for i in range(2)]
            h2t = sb1("h2t", [128, 1024], F32)
            h2n = sb1("h2n", [128, 1024], BF16)
            pps = ps(PH1, "pps", [128, 512])
            Sps = ps(PH1, "Sps", [128, 640])
            ybank = ps(PH1, "ybank", [128, 1024])
            bcb = ps(PH1, "bcb", [128, 512])

            R.dma("gpsimd", wq_sb[:].rearrange("p (c n) -> p c n", n=1536),
                  wqkv.rearrange("(c p) n -> p c n", p=128), GD)
            R.dma("gpsimd", wo_sb[:].rearrange("p (c n) -> p c n", n=1024),
                  wo.rearrange("(c p) n -> p c n", p=128), GD)
            R.wait("tensor", GD)

            def hT(kc, c0, w):
                off = -1 if kc >= 4 else 0
                return zT[:, kc * 1664 + c0 + off: kc * 1664 + c0 + off + w]

            for o in range(4):
                R.wait("gpsimd", VE)
                R.dma("gpsimd", msk_sb[:].rearrange("p (t n) -> p t n", n=640),
                      msk[o].rearrange("t p n -> p t n"), GD)
                R.wait("vector", GD)
                zT3 = zT[:].rearrange("p (c n) -> p c n", n=1664)
                for t in range(13):
                    buf = xb[t % 2]
                    R.wait("sync", VE)
                    R.dma("sync", buf[:], xe[o, t * 128:(t + 1) * 128, :], LD)
                    R.wait("vector", LD)
                    if ln_in:
                        emit_ln_stats(R, lt, buf[:], 1024, VE, SE)
                        R.op("vector", "tensor_scalar", zf[:], buf[:], lt.negm[:],
                             lt.rstd[:], ALU.add, op1=ALU.mult, inc=VE)
                        R.wait("vector", VE)
                        src_f = zf
                    else:
                        src_f = buf
                    R.wait("vector", TE)
                    R.op("vector", "tensor_copy", zhat[t % 2][:], src_f[:], inc=VE)
                    R.wait("tensor", VE)
                    for half in range(2):
                        for c4 in range(4):
                            c = half * 4 + c4
                            R.op("tensor", "transpose", Tpack[:, c4 * 128:(c4 + 1) * 128],
                                 zhat[t % 2][:, c * 128:(c + 1) * 128], ident[:],
                                 inc=(TE if c4 == 3 else None))
                        R.wait("vector", TE)
                        R.op("vector", "tensor_copy",
                             zT3[:, half * 4:half * 4 + 4, t * 128:(t + 1) * 128],
                             Tpack[:, 0:512].rearrange("p (k n) -> p k n", n=128), inc=VE)
                        R.wait("tensor", VE)


                def qk_pipe(kind, tiles, s0, dst_fT, rot_ap, sct, it0=0):
                    inner0 = 0 if kind == "q" else 512
                    nn = 1024 if kind == "q" else 1536
                    for it in range(it0, tiles):
                        s = s0 + it * 128
                        R.wait("sync", VE)
                        R.dma("sync", rotA[it % 2][:],
                              rot_ap[o, 0, it * 128:(it + 1) * 128, :], LD)
                        R.dma("sync", rotB[it % 2][:],
                              rot_ap[o, 1, it * 128:(it + 1) * 128, :], LD)
                        R.wait("tensor", SE)
                        R.wait("tensor", VE)
                        for kc in range(8):
                            R.op("tensor", "matmul", pps[:, 0:512],
                                 hT(kc, s, 128),
                                 wq_sb[:, kc * 1536 + inner0: kc * 1536 + inner0 + 512],
                                 start=(kc == 0), stop=(kc == 7),
                                 inc=(TE if kc == 7 else None))
                        R.wait("vector", TE)
                        R.op("vector", "tensor_copy", qev[:], pps[:, 0:512], inc=VE)
                        R.wait("scalar", VE)
                        R.op("scalar", "activation", t1b[:], qev[:], AF.Square, inc=SE)
                        R.wait("vector", SE)
                        R.op("vector", "tensor_reduce", l2r[:],
                             t1b[:].rearrange("p (h d) -> p h d", d=64),
                             axis=AX.X, op=ALU.add, inc=VE)
                        R.wait("scalar", VE)
                        R.op("scalar", "activation", l2r[:], l2r[:], AF.Sqrt, inc=SE)
                        R.wait("vector", SE)
                        R.op("vector", "tensor_scalar", l2r[:], l2r[:], 1e-12, None,
                             ALU.max, inc=VE)
                        R.wait("vector", VE)
                        R.op("vector", "reciprocal", l2r[:], l2r[:], inc=VE)
                        R.wait("vector", VE)
                        R.wait("vector", LD)
                        for h in range(8):
                            R.op("vector", "scalar_tensor_tensor",
                                 q2s[:, h * 64:(h + 1) * 64],
                                 qev[:, h * 64:(h + 1) * 64],
                                 l2r[:, h:h + 1], sct[:, h * 64:(h + 1) * 64],
                                 ALU.mult, ALU.mult,
                                 inc=(VE if h == 7 else None))
                        R.wait("vector", VE)
                        R.op("vector", "tensor_mul", t1b[:], q2s[:], rotA[it % 2][:])
                        q23 = q2s[:].rearrange("p (h two d) -> p h two d", two=2, d=32)
                        t23 = t2b[:].rearrange("p (h two d) -> p h two d", two=2, d=32)
                        B3 = rotB[it % 2][:].rearrange("p (h two d) -> p h two d",
                                                       two=2, d=32)
                        for h in range(8):
                            R.op("vector", "tensor_tensor", t23[:, h, 0, :],
                                 q23[:, h, 1, :], B3[:, h, 0, :], ALU.mult)
                            R.op("vector", "tensor_tensor", t23[:, h, 1, :],
                                 q23[:, h, 0, :], B3[:, h, 1, :], ALU.mult,
                                 inc=(VE if h == 7 else None))
                        R.wait("vector", VE)
                        R.op("vector", "tensor_add", q3b[:], t1b[:], t2b[:], inc=VE)
                        R.wait("tensor", VE)
                        for ic in range(4):
                            R.op("tensor", "transpose", Tpack[:, ic * 128:(ic + 1) * 128],
                                 q3b[:, ic * 128:(ic + 1) * 128], ident[:],
                                 inc=(TE if ic == 3 else None))
                        R.wait("vector", TE)
                        R.op("vector", "tensor_copy",
                             dst_fT[:].rearrange("p (c n) -> p c n", n=nn)[
                                 :, :, it * 128:(it + 1) * 128],
                             Tpack[:, 0:512].rearrange("p (k n) -> p k n", n=128), inc=VE)
                        R.wait("tensor", VE)

                qk_pipe("q", 8, 640, qfT, rotq, qst, it0=3)
                qk_pipe("k", 12, 128, kfT, rotk, kst)
                for it in range(12):
                    s = 128 + it * 128
                    R.wait("tensor", VE)
                    for kc in range(8):
                        R.op("tensor", "matmul", pps[:, 0:512],
                             hT(kc, s, 128),
                             wq_sb[:, kc * 1536 + 1024: kc * 1536 + 1536],
                             start=(kc == 0), stop=(kc == 7),
                             inc=(TE if kc == 7 else None))
                    R.wait("vector", TE)
                    R.op("vector", "tensor_copy", vtm[:, it * 512:(it + 1) * 512],
                         pps[:, 0:512], inc=VE)
                    R.wait("tensor", VE)

                PT3 = PT[:].rearrange("p (k n) -> p k n", n=1024)
                for ic in range(4):
                    for hh in range(2):
                        h = 2 * ic + hh
                        rows = slice(64 * hh, 64 * hh + 64)
                        for tq in range(3, 8):
                            R.wait("tensor", SE)
                            R.wait("tensor", VE)
                            kcol = ic * 1536 + tq * 128
                            R.op("tensor", "matmul", Sps[:, 0:512],
                                 qfT[rows, ic * 1024 + tq * 128: ic * 1024 + (tq + 1) * 128],
                                 kfT[rows, kcol: kcol + 512], start=True, stop=True)
                            R.op("tensor", "matmul", Sps[:, 512:640],
                                 qfT[rows, ic * 1024 + tq * 128: ic * 1024 + (tq + 1) * 128],
                                 kfT[rows, kcol + 512: kcol + 640], start=True, stop=True,
                                 inc=TE)
                            R.wait("vector", TE)
                            R.op("vector", "tensor_tensor", Sps[:, 0:640], Sps[:, 0:640],
                                 msk_sb[:, tq * 640:(tq + 1) * 640], ALU.add)
                            R.op("vector", "reduce_max", lt.negm[:], Sps[:, 0:640],
                                 axis=AX.X, inc=VE)
                            R.wait("vector", VE)
                            R.op("vector", "tensor_scalar", lt.va[:], lt.negm[:],
                                 -QKS, None, ALU.mult, inc=VE)
                            R.wait("scalar", VE)
                            R.wait("scalar", TE)
                            R.op("scalar", "activation", P[:, 0:640], Sps[:, 0:640],
                                 AF.Exp, bias=lt.va[:], scale=QKS,
                                 accum_out=rg[:, tq:tq + 1], inc=SE)
                            R.wait("tensor", SE)
                            for kt in range(5):
                                R.op("tensor", "transpose",
                                     Tpack[:, kt * 128:(kt + 1) * 128],
                                     P[:, kt * 128:(kt + 1) * 128], ident[:],
                                     inc=(TE if kt == 4 else None))
                            R.wait("vector", TE)
                            R.op("vector", "tensor_copy",
                                 PT3[:, tq:tq + 5, tq * 128:(tq + 1) * 128],
                                 Tpack[:, 0:640].rearrange("p (k n) -> p k n", n=128),
                                 inc=VE)
                        R.wait("tensor", VE)
                        pieces = []
                        for ktG in range(12):
                            lo, hi = max(3, ktG - 4), min(7, ktG)
                            if lo > hi:
                                continue
                            c = lo * 128
                            wdt = (hi - lo + 1) * 128
                            while wdt > 0:
                                n = min(512 - (c % 512), wdt)
                                pieces.append((ktG, c, n))
                                c += n
                                wdt -= n
                        first_in_bank = {}
                        last_in_bank = {}
                        for i, (ktG, c, n) in enumerate(pieces):
                            bk = c // 512
                            first_in_bank.setdefault(bk, i)
                            last_in_bank[bk] = i
                        lasts = set(last_in_bank.values())
                        for i, (ktG, c, n) in enumerate(pieces):
                            bk = c // 512
                            R.op("tensor", "matmul", ybank[rows, c:c + n],
                                 vtm[:, ktG * 512 + h * 64: ktG * 512 + h * 64 + 64],
                                 PT[:, ktG * 1024 + c: ktG * 1024 + c + n],
                                 start=(first_in_bank[bk] == i), stop=(i in lasts),
                                 inc=(TE if i == len(pieces) - 1 else None))
                        R.wait("vector", TE)
                        R.wait("vector", SE)
                        R.op("vector", "tensor_scalar", rg[:], rg[:], 1e-30, None,
                             ALU.add, inc=VE)
                        R.wait("vector", VE)
                        R.op("vector", "reciprocal", rg[:], rg[:], inc=VE)
                        R.wait("tensor", VE)
                        for tq in range(8):
                            R.op("tensor", "transpose",
                                 smalls[0:1, (tq % 4) * 128:(tq % 4) * 128 + 128],
                                 rg[:, tq:tq + 1], ident_f[:],
                                 inc=(TE if tq % 4 == 3 else None))
                            if tq % 4 == 3:
                                R.wait("vector", TE)
                                R.op("vector", "tensor_copy",
                                     row_a[0:1, (tq // 4) * 512:(tq // 4) * 512 + 512],
                                     smalls[0:1, 0:512], inc=VE)
                                R.wait("tensor", VE)
                        for half in range(2):
                            R.op("tensor", "matmul", bcb[:, 0:512],
                                 ones_f[:], row_a[0:1, half * 512:(half + 1) * 512],
                                 start=True, stop=True, inc=TE)
                            R.wait("vector", TE)
                            R.op("vector", "tensor_copy",
                                 rb_sb[:, half * 512:(half + 1) * 512],
                                 bcb[:, 0:512], inc=VE)
                            R.wait("tensor", VE)
                        R.wait("vector", VE)
                        R.op("vector", "tensor_mul", oT[rows, ic * 1024:(ic + 1) * 1024],
                             ybank[rows, 0:1024], rb_sb[rows, :], inc=VE)
                        R.wait("tensor", VE)

                if debug and o == 3:
                    R.wait("gpsimd", VE)
                    R.dma("gpsimd", dbg_q, qfT[:], OT)
                    R.dma("gpsimd", dbg_k, kfT[:], OT)
                    R.dma("gpsimd", dbg_v, vtm[:], OT)
                    R.dma("gpsimd", dbg_o, oT[:], OT)
                    R.dma("gpsimd", dbg_zT, zT[:], OT)
                    R.wait("tensor", OT)
                for tq in range(3, 8):
                    xmb = xm[tq % 2]
                    buf = xb[tq % 2]
                    R.wait("sync", VE)
                    R.dma("sync", buf[:], xe[o, (5 + tq) * 128:(6 + tq) * 128, :], LD)
                    R.wait("vector", LD)
                    if ln_in:
                        emit_ln_stats(R, lt, buf[:], 1024, VE, SE)
                        R.op("vector", "tensor_scalar", zf[:], buf[:], lt.negm[:],
                             lt.rstd[:], ALU.add, op1=ALU.mult, inc=VE)
                        resid = zf
                    else:
                        resid = buf
                    R.wait("vector", SP)
                    for half in range(2):
                        R.wait("tensor", VE)
                        for ic in range(4):
                            R.op("tensor", "matmul", pps[:, 0:512],
                                 oT[:, ic * 1024 + tq * 128: ic * 1024 + tq * 128 + 128],
                                 wo_sb[:, ic * 1024 + half * 512: ic * 1024 + half * 512 + 512],
                                 start=(ic == 0), stop=(ic == 3),
                                 inc=(TE if ic == 3 else None))
                        R.wait("vector", TE)
                        R.op("vector", "tensor_add", xmb[:, half * 512:(half + 1) * 512],
                             pps[:, 0:512],
                             resid[:, half * 512:(half + 1) * 512],
                             inc=VE)
                    R.wait("gpsimd", VE)
                    R.dma("gpsimd",
                          xm_d.ap()[:, o * 8192 + tq * 1024: o * 8192 + (tq + 1) * 1024],
                          xmb[:], SP)
                    if tq >= 4:
                        R.wait("gpsimd", VE)
                        R.dma("gpsimd", h2t[:, 0:512], xmb[:, 0:512], LD)
                        R.dma("gpsimd", h2t[1:128, 512:1024], xmb[0:127, 512:1024], LD)
                        R.dma("gpsimd", h2t[0:1, 512:1024],
                              xm[(tq - 1) % 2][127:128, 512:1024], LD)
                        R.wait("vector", LD)
                        emit_ln_stats(R, lt, h2t[:], 1024, VE, SE)
                        R.op("vector", "tensor_scalar", h2n[:], h2t[:], lt.negm[:],
                             lt.rstd[:], ALU.add, op1=ALU.mult, inc=VE)
                        R.wait("tensor", VE)
                        for half in range(2):
                            for c4 in range(4):
                                c = half * 4 + c4
                                R.op("tensor", "transpose",
                                     Tpack[:, c4 * 128:(c4 + 1) * 128],
                                     h2n[:, c * 128:(c + 1) * 128], ident[:],
                                     inc=(TE if c4 == 3 else None))
                            R.wait("vector", TE)
                            R.wait("vector", SP)
                            for c4 in range(4):
                                c = half * 4 + c4
                                R.op("vector", "tensor_scalar",
                                     qev[:, c4 * 128:(c4 + 1) * 128],
                                     Tpack[:, c4 * 128:(c4 + 1) * 128],
                                     lng_sb[:, c:c + 1], None, ALU.mult,
                                     inc=(VE if c4 == 3 else None))
                            R.wait("gpsimd", VE)
                            h2v = h2_d.ap().rearrange("p (s c n) -> p s c n", s=4, c=8)
                            R.dma("gpsimd",
                                  h2v[:, o, half * 4:half * 4 + 4,
                                      (tq - 4) * 128:(tq - 3) * 128],
                                  qev[:, 0:512].rearrange("p (c n) -> p c n", n=128), SP)
                            R.wait("tensor", SP)
            R.flush(block)

        with ExitStack() as PH2:
            sb2 = lambda n, s, dt: PH2.enter_context(nc.sbuf_tensor(pre + n, s, dt))
            w1_sb = sb2("w1_sb", [128, 8 * 5632], BF16)
            w2_sb = sb2("w2_sb", [128, 22 * 1024], BF16)
            h2s = sb2("h2s", [128, 8 * 512], BF16)
            hT_sb = sb2("hT_sb", [128, 22 * 512], BF16)
            asbf = [sb2(f"asbf{i}", [128, 512], F32) for i in range(2)]
            gbf = [sb2(f"gbf{i}", [128, 512], BF16) for i in range(2)]
            xms = sb2("xms", [128, 4 * 1024], F32)
            xq3b = sb2("xq3b", [128, 1024], F32)
            b1p = sb2("b1p", [1, 512], F32)
            b2_sb = sb2("b2_sb", [1, D], BF16)
            bcol = sb2("bcol", [128, 44], F32)
            lnbc = sb2("lnbc", [128, 8], BF16)
            zb = sb2("zb", [128, 1024], F32)
            pps = [ps(PH2, "ppsb0", [128, 512]), ps(PH2, "ppsb1", [128, 512])]
            pp2 = [ps(PH2, "pp2b0", [128, 512]), ps(PH2, "pp2b1", [128, 512])]
            y2ps = ps(PH2, "y2ps", [128, 1024])
            brow_ps = pps[0]

            R.wait("gpsimd", TE)
            R.wait("gpsimd", VE)
            R.wait("gpsimd", SE)
            R.wait("sync", TE)
            R.wait("sync", VE)
            R.dma("gpsimd", w1_sb[:].rearrange("p (c n) -> p c n", n=5632),
                  w1r.rearrange("(c p) n -> p c n", p=128), GD)
            R.dma("gpsimd", w2_sb[:].rearrange("p (c n) -> p c n", n=1024),
                  w2r.rearrange("(c p) n -> p c n", p=128), GD)
            R.dma("gpsimd", lnbc[:], lnb_p, GD)
            R.dma("gpsimd", b2_sb[:], b2r, GD)
            R.wait("tensor", GD)
            R.wait("tensor", LD)
            R.wait("vector", LD)
            for piece in range(11):
                R.wait("sync", VE)
                R.dma("sync", b1p[:], b1r[0:1, piece * 512:(piece + 1) * 512], LD)
                for kc in range(8):
                    R.op("tensor", "matmul", brow_ps[0:1, 0:512],
                         lnbc[:, kc:kc + 1],
                         w1_sb[:, kc * 5632 + piece * 512: kc * 5632 + (piece + 1) * 512],
                         start=(kc == 0), stop=(kc == 7), inc=(TE if kc == 7 else None))
                R.wait("vector", TE)
                R.wait("vector", LD)
                R.op("vector", "tensor_add", b1p[:], b1p[:], brow_ps[0:1, 0:512], inc=VE)
                R.wait("tensor", VE)
                for c4 in range(4):
                    R.op("tensor", "transpose", smalls[:, c4:c4 + 1],
                         b1p[0:1, c4 * 128:(c4 + 1) * 128], ident_f[0:1, 0:1],
                         inc=(TE if c4 == 3 else None))
                R.wait("vector", TE)
                R.op("vector", "tensor_copy", bcol[:, piece * 4: piece * 4 + 4],
                     smalls[:, 0:4], inc=VE)
                R.wait("tensor", VE)

            for o in range(4):
                R.wait("sync", VE)
                R.dma("sync", h2s[:], h2_d.ap()[:, o * 4096:(o + 1) * 4096], LD)
                R.dma("sync", xms[:], xm_d.ap()[:, o * 8192 + 4096: o * 8192 + 8192], LD)
                R.wait("tensor", LD)
                R.wait("vector", LD)
                se_h = [0, 0]
                ve_h = [0, 0]
                for hc2 in range(22):
                    bi = hc2 % 2
                    R.wait("tensor", SE, se_h[bi])
                    R.wait("tensor", VE, ve_h[bi])
                    for kc in range(8):
                        R.op("tensor", "matmul", pps[bi][:, 0:512],
                             w1_sb[:, kc * 5632 + (2 * hc2) * 128: kc * 5632 + (2 * hc2) * 128 + 128],
                             h2s[:, kc * 512:(kc + 1) * 512],
                             start=(kc == 0), stop=(kc == 7))
                    for kc in range(8):
                        R.op("tensor", "matmul", pp2[bi][:, 0:512],
                             w1_sb[:, kc * 5632 + (2 * hc2 + 1) * 128: kc * 5632 + (2 * hc2 + 1) * 128 + 128],
                             h2s[:, kc * 512:(kc + 1) * 512],
                             start=(kc == 0), stop=(kc == 7),
                             inc=(TE if kc == 7 else None))
                    R.wait("vector", TE)
                    R.wait("scalar", TE)
                    R.op("vector", "tensor_scalar", asbf[bi][:], pps[bi][:, 0:512],
                         bcol[:, 2 * hc2: 2 * hc2 + 1], None, ALU.add)
                    R.op("scalar", "activation", gbf[bi][:], pp2[bi][:, 0:512], AF.Gelu,
                         bias=bcol[:, 2 * hc2 + 1: 2 * hc2 + 2], inc=SE)
                    R.wait("vector", SE)
                    R.op("vector", "tensor_tensor", hT_sb[:, hc2 * 512:(hc2 + 1) * 512],
                         asbf[bi][:], gbf[bi][:], ALU.mult, inc=VE)
                    se_h[bi] = SE.n
                    ve_h[bi] = VE.n
                R.wait("tensor", VE)
                for ts in range(4):
                    tt = o * 4 + ts
                    if sandwich:
                        R.wait("sync", VE)
                        R.dma("sync", xq3b[:], xq3[tt * 128:(tt + 1) * 128, :], LD)
                    for half in range(2):
                        for hc2 in range(22):
                            R.op("tensor", "matmul", y2ps[:, half * 512:(half + 1) * 512],
                                 hT_sb[:, hc2 * 512 + ts * 128: hc2 * 512 + ts * 128 + 128],
                                 w2_sb[:, hc2 * 1024 + half * 512: hc2 * 1024 + half * 512 + 512],
                                 start=(hc2 == 0), stop=False)
                        R.op("tensor", "matmul", y2ps[:, half * 512:(half + 1) * 512],
                             ones_bf[:], b2_sb[0:1, half * 512:(half + 1) * 512],
                             start=False, stop=True, inc=TE)
                    R.wait("vector", TE)
                    R.wait("vector", OT)
                    R.op("vector", "tensor_add", zb[:, 0:512], y2ps[:, 0:512],
                         xms[:, ts * 1024: ts * 1024 + 512])
                    R.op("vector", "tensor_add", zb[:, 512:1024], y2ps[:, 512:1024],
                         xms[:, ts * 1024 + 512: ts * 1024 + 1024], inc=VE)
                    R.wait("tensor", VE)
                    if sandwich:
                        R.wait("vector", LD)
                        emit_ln_stats(R, lt, zb[:], 1024, VE, SE)
                        R.op("vector", "tensor_scalar", zb[:], zb[:], lt.negm[:],
                             lt.rstd[:], ALU.add, op1=ALU.mult, inc=VE)
                        R.wait("vector", VE)
                        R.op("vector", "tensor_add", zb[:], zb[:], xq3b[:], inc=VE)
                    R.wait("gpsimd", VE)
                    R.dma("gpsimd", zout[tt * 128:(tt + 1) * 128, :], zb[:], OT)
            if debug:
                R.dma("gpsimd", dbg_xm, xm_d.ap(), OT)
                R.dma("gpsimd", dbg_h2, h2_d.ap(), OT)
            R.wait("gpsimd", OT)
            R.wait("gpsimd", SP)
            R.flush(block)
    return nc


def _rot_tables(abs_pos, xpos_sign):
    DH = 64
    half = np.arange(0, DH, 2, dtype=np.float64)
    inv = 1.0 / 10000.0 ** (half / DH)
    t = abs_pos.astype(np.float64)
    fr = t[:, None] * inv[None, :]
    freqs = np.concatenate([fr, fr], axis=1)
    base = (half + 0.4 * DH) / (1.4 * DH)
    power = (t - 4096 // 2) / 256.0
    sc = base[None, :] ** power[:, None]
    sc = np.concatenate([sc, sc], axis=1)
    if xpos_sign < 0:
        sc = 1.0 / sc
    A = np.cos(freqs) * sc
    Bs = np.sin(freqs) * sc
    Bs[:, 0:32] *= -1.0
    return A.astype(np.float32), Bs.astype(np.float32)


def stage_D(xin, x3, inputs, layer):
    wqkv = np.ascontiguousarray(inputs["L1_ffn_wqkv"][layer], np.float32)
    wo_ = np.ascontiguousarray(inputs["L1_ffn_wo"][layer], np.float32)
    w1 = np.asarray(inputs["L1_ffn_w1"][layer], np.float32)
    b1 = np.asarray(inputs["L1_ffn_b1"][layer], np.float32)
    w2 = np.asarray(inputs["L1_ffn_w2"][layer], np.float32)
    w1r = np.zeros((D, 2 * 2816), np.float32)
    b1r = np.zeros((1, 2 * 2816), np.float32)
    a_, g_ = w1[:, :2730], w1[:, 2730:]
    ba_, bg_ = b1[:2730], b1[2730:]
    for j in range(22):
        lo, hi = j * 128, min((j + 1) * 128, 2730)
        w1r[:, 2 * j * 128: 2 * j * 128 + (hi - lo)] = a_[:, lo:hi]
        w1r[:, (2 * j + 1) * 128: (2 * j + 1) * 128 + (hi - lo)] = g_[:, lo:hi]
        b1r[0, 2 * j * 128: 2 * j * 128 + (hi - lo)] = ba_[lo:hi]
        b1r[0, (2 * j + 1) * 128: (2 * j + 1) * 128 + (hi - lo)] = bg_[lo:hi]
    w2r = np.zeros((2816, D), np.float32)
    w2r[:2730] = w2
    b2r = np.ascontiguousarray(inputs["L1_ffn_b2"][layer], np.float32).reshape(1, D)
    lnb = np.asarray(inputs["L1_ffn_ln_b"][layer], np.float32)
    lng = np.asarray(inputs["L1_ffn_ln_g"][layer], np.float32)
    lnb_p = np.ascontiguousarray(lnb.reshape(8, 128).T)
    lng_p = np.ascontiguousarray(lng.reshape(8, 128).T)
    qs_r = np.ascontiguousarray(inputs["L1_ffn_q_scale"][layer], np.float32).reshape(1, 64)
    ks_r = np.ascontiguousarray(inputs["L1_ffn_k_scale"][layer], np.float32).reshape(1, 64)
    ident = np.eye(128, dtype=np.float32)
    maps = []
    for c in range(NCORES):
        b, par = c // 2, c % 2
        xe = np.zeros((4, 1664, D), np.float32)
        rq = np.zeros((4, 2, 1024, 512), np.float32)
        rk = np.zeros((4, 2, 1536, 512), np.float32)
        mk = np.full((4, 8, 128, 640), NEG, np.float32)
        for o, A in enumerate(OWN[par]):
            c0 = A * 512
            lo = c0 - 1152
            src_lo, dst_lo = max(0, lo), max(0, -lo)
            xe[o, dst_lo:] = xin[b][src_lo: lo + 1664]
            qpos = np.clip(np.arange(c0 - 512, c0 + 512), 0, None)
            kpos = np.clip(np.arange(c0 - 1024, c0 + 512), 0, None)
            for si, (sign, pos, dst) in enumerate(
                    (((1, qpos, rq)), ((-1, kpos, rk)))):
                Aq, Bq = _rot_tables(pos, sign)
                dst[o, 0] = np.tile(Aq, (1, 8))
                dst[o, 1] = np.tile(Bq, (1, 8))
            for tq in range(8):
                for r in range(128):
                    aq = c0 - 512 + tq * 128 + r
                    ks = c0 - 1024 + tq * 128
                    cc = np.arange(ks, ks + 640)
                    ok = (cc >= 0) & (cc <= aq) & (cc >= aq - 512) & (aq >= 0)
                    mk[o, tq, r, ok] = 0.0
        maps.append({
            "xe": xe, "xq3": np.ascontiguousarray(x3[b][own_rows(par)]),
            "wqkv": wqkv, "wo": wo_, "w1r": w1r, "w2r": w2r, "b1r": b1r,
            "b2r": b2r, "lnb_p": lnb_p, "lng_p": lng_p, "qs_r": qs_r,
            "ks_r": ks_r, "rotq": rq, "rotk": rk, "msk": mk, "ident_in": ident,
        })
    return maps


def run_D(x3, inputs):
    if "D1" not in _NC_CACHE:
        _NC_CACHE["D1"] = build_launch_D(True, False)
        _NC_CACHE["D2"] = build_launch_D(False, True)
    res = run_bass_kernel_spmd(_NC_CACHE["D1"], stage_D(x3, x3, inputs, 0),
                               core_ids=list(range(NCORES)))
    z1 = gather_own(res.results, "zout")
    res = run_bass_kernel_spmd(_NC_CACHE["D2"], stage_D(z1, x3, inputs, 1),
                               core_ids=list(range(NCORES)))
    return gather_own(res.results, "zout")




GROUPS = [[0, 1], [2, 3], [4, 5], [6, 7]]


def emit_barrier(R, sems):
    for eng in Rec.ENGINES:
        for s in sems.values():
            R.wait(eng, s)


def build_fused():
    nc = bass.Bass("TRN2", target_bir_lowering=False)
    R = Rec(nc)
    io = {}

    def inp(name, shape, dt=F32):
        io[name] = nc.dram_tensor(name, shape, dt, kind="ExternalInput").ap()

    inp("ident_in", [128, 128])
    inp("a_xfull", [T, D]); inp("a_xq", [OWNT, D])
    for w in ("a_wq", "a_wk", "a_wv", "a_wo"):
        inp(w, [D, D])
    inp("a_nullk_p", [128, 8]); inp("a_nullv_r", [1, D]); inp("a_lsc_r", [1, 16])
    inp("a_mtail", [16, 5, 128, 128])
    inp("b_wgT", [D, 3072], I32); inp("b_wuT", [D, 3072], I32)
    inp("b_wdT", [3072, D], I32)
    inp("c_wqkv", [D, 3072]); inp("c_wf2", [128, 128]); inp("c_bf_p", [128, 1])
    inp("c_bf_r", [1, 128]); inp("c_wout", [D, D]); inp("c_bout_r", [1, D])
    for li in range(2):
        p = f"d{li}_"
        inp(p + "wqkv", [D, 1536]); inp(p + "wo", [512, D])
        inp(p + "w1r", [D, 5632]); inp(p + "w2r", [2816, D])
        inp(p + "b1r", [1, 5632]); inp(p + "b2r", [1, D])
        inp(p + "lnb_p", [128, 8]); inp(p + "lng_p", [128, 8])
        inp(p + "qs_r", [1, 64]); inp(p + "ks_r", [1, 64])
    inp("d_rotq", [4, 2, 1024, 512]); inp("d_rotk", [4, 2, 1536, 512])
    inp("d_msk", [4, 8, 128, 640])
    inp("dmeta", [1, 128], I32)
    out = nc.dram_tensor("out", [OWNT, D], F32, kind="ExternalOutput").ap()

    x1_d = nc.dram_tensor("x1_d", [OWNT, D], F32)
    x2_d = nc.dram_tensor("x2_d", [OWNT, D], F32)
    x3_d = nc.dram_tensor("x3_d", [OWNT, D], F32)
    z1_d = nc.dram_tensor("z1_d", [OWNT, D], F32)
    x2g = nc.dram_tensor("x2g", [T, D], F32)
    xg_scr = nc.dram_tensor("xg_scr", [T, D], F32)
    xabs = nc.dram_tensor("xabs", [1152 + T, D], F32)
    xe_d = nc.dram_tensor("xe_d", [4, 1664, D], F32)

    with ExitStack() as CTX:
        sems = {n: Sem(CTX.enter_context(nc.semaphore(n)))
                for n in ("LD", "GD", "SP", "TE", "VE", "SE", "OT", "CC")}
        meta_sb = CTX.enter_context(nc.sbuf_tensor("meta_sb", [1, 128], I32))
        zero_sb = CTX.enter_context(nc.sbuf_tensor("zero_sb", [128, 1024], F32))
        dynr = CTX.enter_context(nc.gpsimd.register("dynr"))
        block = CTX.enter_context(nc.Block())

        fcb = {"nc": nc, "R": R, "io": io, "sems": sems, "block": block,
               "x1_d": x1_d, "x2_d": x2_d, "x3_d": x3_d, "z1_d": z1_d,
               "x2g": x2g, "out": out, "xe_d": xe_d}

        LD, GD, OT, CC, VE = (sems[k] for k in ("LD", "GD", "OT", "CC", "VE"))
        R.dma("sync", meta_sb[:], io["dmeta"], LD)
        R.op("vector", "memset", zero_sb[:], 0.0, inc=VE)
        R.wait("gpsimd", VE)
        for zi in range(9):
            R.dma("gpsimd", xabs.ap()[zi * 128:(zi + 1) * 128, :], zero_sb[:], GD)

        def unscramble_and_slot(src_d):
            R.op("gpsimd", "collective_compute", "AllGather", ALU.bypass,
                 replica_groups=GROUPS, ins=[src_d.ap().opt()],
                 outs=[xg_scr.ap().opt()], inc=sems["CC"])
            R.wait("gpsimd", sems["CC"])
            for c in range(8):
                g = gathered_pos(c)
                R.dma("gpsimd", xabs.ap()[1152 + c * 512: 1152 + (c + 1) * 512, :],
                      xg_scr.ap()[g * 512:(g + 1) * 512, :], GD)
            R.wait("gpsimd", GD)
            R.wait("gpsimd", LD)
            for o in range(4):
                R.op("gpsimd", "reg_load", dynr, meta_sb[0:1, o:o + 1])
                R.dma("gpsimd", xe_d.ap()[o],
                      bass.AP(xabs, dynr, [[1024, 1664], [1, 1024]]), GD)

        build_launch_A(fc=fcb)
        emit_barrier(R, sems)
        build_launch_B(fc=fcb)
        emit_barrier(R, sems)
        R.op("gpsimd", "collective_compute", "AllGather", ALU.bypass,
             replica_groups=GROUPS, ins=[x2_d.ap().opt()], outs=[x2g.ap().opt()],
             inc=sems["CC"])
        emit_barrier(R, sems)
        build_launch_C(fc=fcb)
        emit_barrier(R, sems)
        unscramble_and_slot(x3_d)
        emit_barrier(R, sems)
        build_launch_D(True, False, fc=fcb, li=0)
        emit_barrier(R, sems)
        unscramble_and_slot(z1_d)
        emit_barrier(R, sems)
        build_launch_D(False, True, fc=fcb, li=1)
        emit_barrier(R, sems)
        R.flush(block)
    return nc


def gathered_pos(c):
    if c in OWN[0]:
        return OWN[0].index(c)
    return 4 + OWN[1].index(c)


def make_dmeta(par):
    m = np.zeros((1, 128), np.int32)
    for o, A in enumerate(OWN[par]):
        m[0, o] = (A * 512) * D
    return m


def stage_fused(inputs):
    zero4 = np.zeros((4, T, D), np.float32)
    zero2 = np.zeros((OWNT, D), np.float32)
    aM = stage_A(inputs)
    bM = stage_B(zero4, inputs)
    cM = stage_C(zero4, inputs)
    d0M = stage_D(zero4, zero4, inputs, 0)
    d1M = stage_D(zero4, zero4, inputs, 1)
    maps = []
    for c in range(NCORES):
        par = c % 2
        m = {"ident_in": aM[c]["ident_in"], "dmeta": make_dmeta(par)}
        for k, v in aM[c].items():
            if k != "ident_in":
                m["a_" + k] = v
        for k in ("wgT", "wuT", "wdT"):
            m["b_" + k] = bM[c][k]
        for k in ("wqkv", "wf2", "bf_p", "bf_r", "wout", "bout_r"):
            m["c_" + k] = cM[c][k]
        for li, dM in ((0, d0M), (1, d1M)):
            for k in ("wqkv", "wo", "w1r", "w2r", "b1r", "b2r", "lnb_p",
                      "lng_p", "qs_r", "ks_r"):
                m[f"d{li}_" + k] = dM[c][k]
        m["d_rotq"] = d0M[c]["rotq"]
        m["d_rotk"] = d0M[c]["rotk"]
        m["d_msk"] = d0M[c]["msk"]
        maps.append(m)
    return maps


def kernel(**inputs):
    if "FUSED" not in _NC_CACHE:
        _NC_CACHE["FUSED"] = build_fused()
    res = run_bass_kernel_spmd(_NC_CACHE["FUSED"], stage_fused(inputs),
                               core_ids=list(range(NCORES)))
    return gather_own(res.results, "out")


# version 05 (final)
# speedup vs baseline: 1.1514x; 1.0167x over previous
import numpy as np
import ml_dtypes
import concourse.bass as bass
import concourse.mybir as mybir
from concourse.bass_utils import run_bass_kernel_spmd
from contextlib import ExitStack

F32 = mybir.dt.float32
BF16 = mybir.dt.bfloat16
I32 = mybir.dt.int32
AF = mybir.ActivationFunctionType
ALU = mybir.AluOpType
AX = mybir.AxisListType
BF = ml_dtypes.bfloat16

NCORES = 8
T = 4096
D = 1024
OWNT = 2048
NEG = -1e30

OWN = [[0, 3, 4, 7], [1, 2, 5, 6]]


def own_rows(par):
    return np.concatenate([np.arange(c * 512, (c + 1) * 512) for c in OWN[par]])


KK_TAB = [5, 6, 7, 8, 13, 14, 15, 16, 21, 22, 23, 24, 29, 30, 31, 32]


def KK(j):
    return KK_TAB[j]


class Sem:
    def __init__(self, h):
        self.h = h
        self.n = 0


class Rec:
    ENGINES = ("sync", "gpsimd", "tensor", "vector", "scalar")

    def __init__(self, nc):
        self.nc = nc
        self.ops = {e: [] for e in self.ENGINES}

    def on(self, eng, f):
        self.ops[eng].append(f)

    def wait(self, eng, sem, n=None):
        v = sem.n if n is None else n
        if v > 0:
            self.on(eng, lambda e, s=sem.h, v=v: e.wait_ge(s, v))

    def op(self, eng, meth, *a, inc=None, inck=1, **kw):
        if inc is not None:
            inc.n += inck

        import traceback
        site = traceback.extract_stack(limit=3)[0]

        def f(e, meth=meth, a=a, kw=kw, inc=inc, inck=inck, site=site):
            try:
                ins = getattr(e, meth)(*a, **kw)
            except Exception as ex:
                import traceback as _tb
                _tb.print_exc()
                raise RuntimeError(
                    f"op {meth} recorded at {site.filename}:{site.lineno} failed: {ex}"
                ) from ex
            if inc is not None:
                ins.then_inc(inc.h, inck)
        self.on(eng, f)

    def dma(self, eng, out, in_, inc, **kw):
        self.op(eng, "dma_start", inc=inc, inck=16, out=out, in_=in_, **kw)

    def flush(self, block):
        for eng in self.ENGINES:
            if self.ops[eng]:
                ops = self.ops[eng]
                getattr(block, eng)(lambda e, ops=ops: [f(e) for f in ops])
        self.ops = {e: [] for e in self.ENGINES}


class LnTiles:

    def __init__(self, nc, ctx, tag=""):
        sb = lambda n, s, dt=F32: ctx.enter_context(nc.sbuf_tensor(n + tag, s, dt))
        self.stats = sb("ln_stats", [128, 12])
        self.mv = sb("ln_mv", [128, 2])
        self.negm = sb("ln_negm", [128, 1])
        self.va = sb("ln_va", [128, 1])
        self.rstd = sb("ln_rstd", [128, 1])


def emit_ln_stats(R, lt, src, width, VE, SE, eps=1e-5):
    h = width // 2
    R.op("vector", "bn_stats", lt.stats[:, 0:6], src[:, 0:h])
    R.op("vector", "bn_stats", lt.stats[:, 6:12], src[:, h:width], inc=VE)
    R.wait("vector", VE)
    R.op("vector", "bn_aggr", lt.mv[:], lt.stats[:], inc=VE)
    R.wait("vector", VE)
    R.op("vector", "tensor_scalar", lt.negm[:], lt.mv[:, 0:1], -1.0, None, ALU.mult)
    R.op("vector", "tensor_scalar", lt.va[:], lt.mv[:, 1:2], eps, None, ALU.add, inc=VE)
    R.wait("scalar", VE)
    R.op("scalar", "activation", lt.rstd[:], lt.va[:], AF.Sqrt, inc=SE)
    R.wait("vector", SE)
    R.op("vector", "reciprocal", lt.rstd[:], lt.rstd[:], inc=VE)
    R.wait("vector", VE)


def build_launch_A(debug=False, fc=None):
    pre = "" if fc is None else "a_"
    if fc is None:
        nc = bass.Bass("TRN2", target_bir_lowering=False)
        R = Rec(nc)
        xfull = nc.dram_tensor("xfull", [T, D], F32, kind="ExternalInput").ap()
        xq = nc.dram_tensor("xq", [OWNT, D], F32, kind="ExternalInput").ap()
        wq = nc.dram_tensor("wq", [D, D], F32, kind="ExternalInput").ap()
        wk = nc.dram_tensor("wk", [D, D], F32, kind="ExternalInput").ap()
        wv = nc.dram_tensor("wv", [D, D], F32, kind="ExternalInput").ap()
        wo = nc.dram_tensor("wo", [D, D], F32, kind="ExternalInput").ap()
        nullk_p = nc.dram_tensor("nullk_p", [128, 8], F32, kind="ExternalInput").ap()
        nullv_r = nc.dram_tensor("nullv_r", [1, D], F32, kind="ExternalInput").ap()
        lsc_r = nc.dram_tensor("lsc_r", [1, 16], F32, kind="ExternalInput").ap()
        ident_in = nc.dram_tensor("ident_in", [128, 128], F32, kind="ExternalInput").ap()
        mtail = nc.dram_tensor("mtail", [16, 5, 128, 128], F32, kind="ExternalInput").ap()
        x1out = nc.dram_tensor("x1", [OWNT, D], F32, kind="ExternalOutput").ap()
    else:
        nc, R = fc["nc"], fc["R"]
        io = fc["io"]
        xfull, xq = io["a_xfull"], io["a_xq"]
        wq, wk, wv, wo = io["a_wq"], io["a_wk"], io["a_wv"], io["a_wo"]
        nullk_p, nullv_r, lsc_r = io["a_nullk_p"], io["a_nullv_r"], io["a_lsc_r"]
        ident_in, mtail = io["ident_in"], io["a_mtail"]
        x1out = fc["x1_d"].ap()

    kT_d = nc.dram_tensor(pre + "kT_d", [128, 8 * 4096], BF16)
    v_d = nc.dram_tensor(pre + "v_d", [128, 8 * 4096], BF16)
    if debug:
        dbg_k = nc.dram_tensor("dbg_k", [128, 8 * 4096], BF16, kind="ExternalOutput").ap()
        dbg_v = nc.dram_tensor("dbg_v", [128, 8 * 4096], BF16, kind="ExternalOutput").ap()
        dbg_q = nc.dram_tensor("dbg_q", [128, 8 * 2048], BF16, kind="ExternalOutput").ap()
        dbg_yn = nc.dram_tensor("dbg_yn", [4, 128, 8 * 512], BF16, kind="ExternalOutput").ap()
        dbg_xqt = nc.dram_tensor("dbg_xqt", [4, 128, 8 * 512], BF16, kind="ExternalOutput").ap()

    with ExitStack() as CTX:
        sb = lambda n, s, dt: CTX.enter_context(nc.sbuf_tensor(pre + n, s, dt))
        ps = lambda stk, n, s, dt=F32: stk.enter_context(nc.psum_tensor(pre + n, s, dt))
        if fc is None:
            sem = lambda n: Sem(CTX.enter_context(nc.semaphore(n)))
            LD = sem("LD"); GD = sem("GD"); SP = sem("SP")
            TE = sem("TE"); VE = sem("VE"); SE = sem("SE"); OT = sem("OT")
        else:
            sm = fc["sems"]
            LD, GD, SP = sm["LD"], sm["GD"], sm["SP"]
            TE, VE, SE, OT = sm["TE"], sm["VE"], sm["SE"], sm["OT"]

        xq_sb = sb("xq_sb", [128, 16 * 1024], F32)
        qT = sb("qT", [128, 8 * 2048], BF16)
        ident = sb("ident", [128, 128], BF16)
        ident_f = sb("ident_f", [128, 128], F32)
        ones_f = sb("ones_f", [1, 128], F32)
        nullk_sb = sb("nullk_sb", [128, 8], BF16)
        nullv_sb = sb("nullv_sb", [1, D], BF16)
        lsc_sb = sb("lsc_sb", [1, 16], F32)
        scale_b = sb("scale_b", [128, 16], F32)
        scol2 = sb("scol2", [128, 8], F32)
        lt = LnTiles(nc, CTX, tag=pre)

        Tpack = ps(CTX, "Tpack", [128, 1024], BF16)
        smalls = ps(CTX, "smalls", [128, 512], F32)
        bc = ps(CTX, "bc", [128, 512], F32)
        yps = ps(CTX, "yps", [128, 512], F32)

        block = fc["block"] if fc else CTX.enter_context(nc.Block())

        R.dma("gpsimd", ident[:], ident_in, GD)
        R.dma("gpsimd", nullk_sb[:], nullk_p, GD)
        R.dma("gpsimd", nullv_sb[:], nullv_r, GD)
        R.dma("sync", ident_f[:], ident_in, LD)
        R.dma("sync", lsc_sb[:], lsc_r, LD)
        R.dma("sync", xq_sb[:].rearrange("p (t n) -> p t n", n=1024),
              xq.rearrange("(t p) n -> p t n", p=128), LD)
        R.op("vector", "memset", ones_f[:], 1.0, inc=VE)
        R.wait("scalar", LD)
        R.op("scalar", "activation", lsc_sb[:], lsc_sb[:], AF.Exp, inc=SE)
        R.wait("vector", SE)
        R.op("vector", "tensor_scalar", lsc_sb[:], lsc_sb[:], 0.125, None, ALU.mult, inc=VE)
        R.wait("tensor", VE)
        R.op("tensor", "matmul", bc[:, 0:16], ones_f[:], lsc_sb[:], start=True, stop=True, inc=TE)
        R.wait("vector", TE)
        R.op("vector", "tensor_copy", scale_b[:], bc[:, 0:16])
        for ic in range(8):
            R.op("vector", "tensor_copy", scol2[0:64, ic:ic + 1],
                 bc[0:64, 2 * ic:2 * ic + 1])
            R.op("vector", "tensor_copy", scol2[64:128, ic:ic + 1],
                 bc[64:128, 2 * ic + 1:2 * ic + 2], inc=(VE if ic == 7 else None))
        R.op("vector", "engine_nop", inc=VE) if False else None
        VE.n += 0

        with ExitStack() as PH1:
            sb1 = lambda n, s, dt: PH1.enter_context(nc.sbuf_tensor(pre + n, s, dt))
            xb = [sb1(f"xb{i}", [128, 1024], F32) for i in range(2)]
            xhat = [sb1(f"xhat{i}", [128, 1024], BF16) for i in range(2)]
            xTc = [sb1(f"xTc{i}", [128, 8 * 512], BF16) for i in range(2)]
            wk_sb = sb1("wk_sb", [128, 8 * 1024], BF16)
            wv_sb = sb1("wv_sb", [128, 8 * 1024], BF16)
            wq_sb = sb1("wq_sb", [128, 8 * 1024], BF16)
            kspill = sb1("kspill", [128, 8 * 512], BF16)
            vbuf = [sb1(f"vbuf{i}", [128, 1024], BF16) for i in range(2)]
            pps = ps(PH1, "pps", [128, 512], F32)
            pp2 = ps(PH1, "pp2", [128, 512], F32)

            for w_ap, w_sb in ((wk, wk_sb), (wv, wv_sb), (wq, wq_sb)):
                R.dma("gpsimd", w_sb[:].rearrange("p (c n) -> p c n", n=1024),
                      w_ap.rearrange("(c p) n -> p c n", p=128), GD)
            wready = GD.n

            def ln_transpose_tile(src_f32, xhat_t, xT_t, t4):
                emit_ln_stats(R, lt, src_f32, 1024, VE, SE)
                R.wait("vector", TE)
                R.op("vector", "tensor_scalar", xhat_t[:], src_f32, lt.negm[:],
                     lt.rstd[:], ALU.add, op1=ALU.mult, inc=VE)
                R.wait("tensor", VE)
                xT3 = xT_t[:].rearrange("p (c n) -> p c n", n=512)
                for half in range(2):
                    for c4 in range(4):
                        c = half * 4 + c4
                        R.op("tensor", "transpose", Tpack[:, c4 * 128:(c4 + 1) * 128],
                             xhat_t[:, c * 128:(c + 1) * 128], ident[:],
                             inc=(TE if c4 == 3 else None))
                    R.wait("vector", TE)
                    R.op("vector", "tensor_copy",
                         xT3[:, half * 4:half * 4 + 4, t4 * 128:(t4 + 1) * 128],
                         Tpack[:, 0:512].rearrange("p (k n) -> p k n", n=128), inc=VE)
                    R.wait("tensor", VE)

            R.wait("tensor", GD, wready)
            v3 = v_d.ap().rearrange("p (c n) -> p c n", n=4096)
            for tc in range(8):
                for t4 in range(4):
                    tt = tc * 4 + t4
                    buf = xb[tt % 2]
                    R.wait("sync", VE)
                    R.dma("sync", buf[:], xfull[tt * 128:(tt + 1) * 128, :], LD)
                    R.wait("vector", LD)
                    ln_transpose_tile(buf[:], xhat[tt % 2], xTc[tc % 2], t4)
                for ic in range(8):
                    for kc in range(8):
                        R.op("tensor", "matmul", pps[:, 0:512],
                             wk_sb[:, kc * 1024 + ic * 128: kc * 1024 + ic * 128 + 128],
                             xTc[tc % 2][:, kc * 512:(kc + 1) * 512],
                             start=(kc == 0), stop=(kc == 7),
                             inc=(TE if kc == 7 else None))
                    R.wait("vector", TE)
                    R.wait("vector", SP)
                    R.op("vector", "tensor_copy", kspill[:, ic * 512:(ic + 1) * 512],
                         pps[:, 0:512], inc=VE)
                    R.wait("tensor", VE)
                R.wait("gpsimd", VE)
                for ic in range(8):
                    R.dma("gpsimd",
                          kT_d.ap()[:, ic * 4096 + tc * 512: ic * 4096 + (tc + 1) * 512],
                          kspill[:, ic * 512:(ic + 1) * 512], SP)
                for t4 in range(4):
                    tt = tc * 4 + t4
                    for half in range(2):
                        bank = pps if half == 0 else pp2
                        for kc in range(8):
                            R.op("tensor", "matmul", bank[:, 0:512],
                                 xTc[tc % 2][:, kc * 512 + t4 * 128: kc * 512 + t4 * 128 + 128],
                                 wv_sb[:, kc * 1024 + half * 512: kc * 1024 + half * 512 + 512],
                                 start=(kc == 0), stop=(kc == 7),
                                 inc=(TE if kc == 7 else None))
                    R.wait("vector", TE)
                    R.wait("vector", SP)
                    R.op("vector", "tensor_copy", vbuf[tt % 2][:, 0:512], pps[:, 0:512])
                    R.op("vector", "tensor_copy", vbuf[tt % 2][:, 512:1024], pp2[:, 0:512],
                         inc=VE)
                    R.wait("tensor", VE)
                    R.wait("gpsimd", VE)
                    R.dma("gpsimd", v3[:, :, tt * 128:(tt + 1) * 128],
                          vbuf[tt % 2][:].rearrange("p (c n) -> p c n", n=128), SP)

            R.wait("vector", LD)
            for qc in range(4):
                for t4 in range(4):
                    tt = qc * 4 + t4
                    R.wait("vector", VE)
                    ln_transpose_tile(xq_sb[:, tt * 1024:(tt + 1) * 1024],
                                      xhat[tt % 2], xTc[qc % 2], t4)
                if debug:
                    R.wait("gpsimd", VE)
                    R.dma("gpsimd", dbg_xqt[qc], xTc[qc % 2][:], SP)
                    R.wait("tensor", SP)
                for ic in range(8):
                    for kc in range(8):
                        R.op("tensor", "matmul", pps[:, 0:512],
                             wq_sb[:, kc * 1024 + ic * 128: kc * 1024 + ic * 128 + 128],
                             xTc[qc % 2][:, kc * 512:(kc + 1) * 512],
                             start=(kc == 0), stop=(kc == 7),
                             inc=(TE if kc == 7 else None))
                    R.wait("vector", TE)
                    R.op("vector", "tensor_scalar",
                         qT[:, ic * 2048 + qc * 512: ic * 2048 + (qc + 1) * 512],
                         pps[:, 0:512], scol2[:, ic:ic + 1], None, ALU.mult, inc=VE)
                    R.wait("tensor", VE)
            R.flush(block)

        with ExitStack() as PH2:
            sb2 = lambda n, s, dt: PH2.enter_context(nc.sbuf_tensor(pre + n, s, dt))
            Kg = [sb2(f"Kg{i}", [128, 2048], BF16) for i in range(2)]
            Vg = [sb2(f"Vg{i}", [128, 2048], BF16) for i in range(2)]
            P = sb2("P", [128, 2048], BF16)
            PT = sb2("PT", [128, 16 * 512], BF16)
            ynT = sb2("ynT", [128, 8 * 512], BF16)
            y_acc = sb2("y_acc", [128, 512], F32)
            tmp_y = sb2("tmp_y", [128, 512], F32)
            rb_sb = sb2("rb_sb", [128, 512], F32)
            row_a = sb2("row_a", [1, 512], F32)
            row_b = sb2("row_b", [1, 512], F32)
            mt_sb = sb2("mt_sb", [128, 4 * 5 * 128], BF16)
            rg = sb2("rg", [128, 4], F32)
            mg = sb2("mg", [128, 4], F32)
            r_acc = sb2("r_acc", [128, 4], F32)
            m_acc = sb2("m_acc", [128, 4], F32)
            sm1 = sb2("sm1", [128, 4], F32)
            sm2 = sb2("sm2", [128, 4], F32)
            pnull = sb2("pnull", [128, 4], BF16)
            pnullT = sb2("pnullT", [1, 512], BF16)
            wo_sb = sb2("wo_sb", [128, 8 * 1024], BF16)
            attn_f32 = sb2("attn_f32", [128, 1024], F32)
            x1b = sb2("x1b", [128, 1024], F32)
            Sps = ps(PH2, "Sps", [128, 2048], F32)
            PT3 = PT[:].rearrange("p (k n) -> p k n", n=512)

            R.wait("gpsimd", TE)
            R.wait("gpsimd", VE)
            R.wait("sync", TE)
            R.wait("sync", VE)
            R.dma("gpsimd", wo_sb[:].rearrange("p (c n) -> p c n", n=1024),
                  wo.rearrange("(c p) n -> p c n", p=128), GD)

            ld_hist = [0, 0]
            ldi = 0
            kbuf, vbuf = Kg[0], Vg[0]
            for qb in range(4):
                KKs = [KK(4 * qb + qs) for qs in range(4)]
                Wmax = 128 * KKs[3]
                ngroups = 1 if Wmax <= 2048 else 2
                R.wait("gpsimd", VE)
                R.dma("gpsimd", mt_sb[:].rearrange("p (j t n) -> p j t n", t=5, n=128),
                      mtail[4 * qb:4 * qb + 4, :, :, :].rearrange("j t p n -> p j t n"), GD)
                R.wait("vector", GD)
                for ic in range(8):
                    for hh in range(2):
                        h = 2 * ic + hh
                        rows = slice(64 * hh, 64 * hh + 64)
                        scol = scale_b[:, h:h + 1]
                        for g in range(ngroups):
                            g0 = g * 2048
                            gW = min(Wmax, g0 + 2048) - g0
                            if hh == 0 or ngroups > 1:
                                kbuf, vbuf = Kg[ldi % 2], Vg[ldi % 2]
                                R.wait("sync", TE, ld_hist[ldi % 2])
                                R.dma("sync", kbuf[:, 0:gW],
                                      kT_d.ap()[:, ic * 4096 + g0: ic * 4096 + g0 + gW], LD)
                                R.dma("sync", vbuf[:, 0:gW],
                                      v_d.ap()[:, ic * 4096 + g0: ic * 4096 + g0 + gW], LD)
                                ldi += 1
                            R.wait("tensor", LD)
                            for qs in range(4):
                                j = 4 * qb + qs
                                Wj = 128 * KKs[qs]
                                lo, hi = g0, min(Wj, g0 + 2048)
                                if hi <= lo:
                                    continue
                                w = hi - lo
                                lhsT = qT[rows, ic * 2048 + j * 128: ic * 2048 + (j + 1) * 128]
                                R.wait("tensor", SE)
                                R.wait("tensor", VE)
                                mm_list = []
                                c = 0
                                while c < w:
                                    n = min(512, w - c)
                                    mm_list.append((c, n))
                                    c += n
                                for i, (c, n) in enumerate(mm_list):
                                    last = (i == len(mm_list) - 1) and g != 0
                                    R.op("tensor", "matmul", Sps[:, c:c + n], lhsT,
                                         kbuf[rows, c:c + n], start=True, stop=True,
                                         inc=(TE if last else None))
                                if g == 0:
                                    R.op("tensor", "matmul", smalls[:, qs:qs + 1], lhsT,
                                         nullk_sb[rows, ic:ic + 1], start=True, stop=True,
                                         inc=TE)
                                R.wait("vector", TE)
                                tail0 = (KKs[qs] - 5) * 128
                                if lo <= tail0 and tail0 + 640 <= hi:
                                    R.op("vector", "tensor_tensor",
                                         Sps[:, tail0 - lo: tail0 - lo + 640],
                                         Sps[:, tail0 - lo: tail0 - lo + 640],
                                         mt_sb[:, qs * 640: qs * 640 + 640],
                                         ALU.add)
                                R.op("vector", "reduce_max", mg[:, qs:qs + 1], Sps[:, 0:w],
                                     axis=AX.X, inc=VE)
                                R.wait("vector", VE)
                                if g == 0:
                                    R.op("vector", "tensor_max", mg[:, qs:qs + 1],
                                         mg[:, qs:qs + 1], smalls[:, qs:qs + 1], inc=VE)
                                    R.wait("vector", VE)
                                R.op("vector", "tensor_scalar", sm1[:, qs:qs + 1],
                                     mg[:, qs:qs + 1], -1.0, None, ALU.mult, inc=VE)
                                R.wait("scalar", VE)
                                R.wait("scalar", TE)
                                R.op("scalar", "activation", P[:, 0:w], Sps[:, 0:w], AF.Exp,
                                     bias=sm1[:, qs:qs + 1],
                                     accum_out=rg[:, qs:qs + 1], inc=SE)
                                if g == 0:
                                    R.op("scalar", "activation", pnull[:, qs:qs + 1],
                                         smalls[:, qs:qs + 1], AF.Exp,
                                         bias=sm1[:, qs:qs + 1], inc=SE)
                                    R.wait("vector", SE)
                                    R.op("vector", "tensor_add", rg[:, qs:qs + 1],
                                         rg[:, qs:qs + 1], pnull[:, qs:qs + 1], inc=VE)
                                R.wait("tensor", SE)
                                nk = (w + 127) // 128
                                for kb in range(0, nk, 8):
                                    ke = min(kb + 8, nk)
                                    R.wait("tensor", VE)
                                    for kt in range(kb, ke):
                                        R.op("tensor", "transpose",
                                             Tpack[:, (kt - kb) * 128:(kt - kb + 1) * 128],
                                             P[:, kt * 128: min((kt + 1) * 128, w)], ident[:],
                                             inc=(TE if kt == ke - 1 else None))
                                    R.wait("vector", TE)
                                    R.op("vector", "tensor_copy",
                                         PT3[:, kb:ke, qs * 128:(qs + 1) * 128],
                                         Tpack[:, 0:(ke - kb) * 128].rearrange(
                                             "p (k n) -> p k n", n=128), inc=VE)
                            if g == 0:
                                R.wait("tensor", VE)
                                for qs in range(4):
                                    R.op("tensor", "transpose",
                                         Tpack[0:1, qs * 128:(qs + 1) * 128],
                                         pnull[:, qs:qs + 1], ident[:],
                                         inc=(TE if qs == 3 else None))
                                R.wait("vector", TE)
                                R.op("vector", "tensor_copy", pnullT[:],
                                     Tpack[0:1, 0:512], inc=VE)
                            R.wait("tensor", VE)
                            nkg = gW // 128
                            av = []
                            for ktL in range(nkg):
                                ktA = g0 // 128 + ktL
                                qsmin = next((qs for qs in range(4) if KKs[qs] > ktA), None)
                                if qsmin is None:
                                    continue
                                av.append((ktL, qsmin * 128))
                            for i, (ktL, qoff) in enumerate(av):
                                last = (i == len(av) - 1) and g != 0
                                R.op("tensor", "matmul", yps[rows, qoff:512],
                                     vbuf[:, ktL * 128 + hh * 64: ktL * 128 + hh * 64 + 64],
                                     PT3[:, ktL, qoff:512],
                                     start=(i == 0), stop=last,
                                     inc=(TE if last else None))
                            if g == 0:
                                R.op("tensor", "matmul", yps[rows, 0:512],
                                     nullv_sb[0:1, h * 64: h * 64 + 64], pnullT[:],
                                     start=False, stop=True, inc=TE)
                            ld_hist[(ldi - 1) % 2] = TE.n
                            R.wait("vector", TE)
                            R.wait("vector", SE)
                            if g == 0:
                                R.op("vector", "tensor_copy", y_acc[rows, :], yps[rows, :])
                                R.op("vector", "tensor_copy", r_acc[:], rg[:])
                                R.op("vector", "tensor_copy", m_acc[:], mg[:], inc=VE)
                                R.wait("tensor", VE)
                            else:
                                R.op("vector", "tensor_max", sm2[:], m_acc[:], mg[:], inc=VE)
                                R.wait("vector", VE)
                                R.op("vector", "tensor_sub", sm1[:], m_acc[:], sm2[:])
                                R.op("vector", "tensor_sub", mg[:], mg[:], sm2[:])
                                R.op("vector", "tensor_copy", m_acc[:], sm2[:], inc=VE)
                                R.wait("scalar", VE)
                                R.op("scalar", "activation", sm1[:], sm1[:], AF.Exp)
                                R.op("scalar", "activation", mg[:], mg[:], AF.Exp, inc=SE)
                                R.wait("vector", SE)
                                R.op("vector", "tensor_mul", r_acc[:], r_acc[:], sm1[:])
                                R.op("vector", "tensor_mul", rg[:], rg[:], mg[:], inc=VE)
                                R.wait("vector", VE)
                                R.op("vector", "tensor_add", r_acc[:], r_acc[:], rg[:],
                                     inc=VE)
                                R.wait("tensor", VE)
                                for rowt, srcontainter in ((row_a, sm1), (row_b, mg)):
                                    for qs in range(4):
                                        R.op("tensor", "transpose",
                                             smalls[0:1, qs * 128:(qs + 1) * 128],
                                             srcontainter[:, qs:qs + 1], ident_f[:],
                                             inc=(TE if qs == 3 else None))
                                    R.wait("vector", TE)
                                    R.op("vector", "tensor_copy",
                                         rowt[:], smalls[0:1, 0:512], inc=VE)
                                    R.wait("tensor", VE)
                                R.op("tensor", "matmul", bc[:, 0:512],
                                     ones_f[:], row_a[:],
                                     start=True, stop=True, inc=TE)
                                R.wait("vector", TE)
                                R.op("vector", "tensor_copy", rb_sb[:],
                                     bc[:, 0:512], inc=VE)
                                R.wait("tensor", VE)
                                R.op("tensor", "matmul", bc[:, 0:512],
                                     ones_f[:], row_b[:],
                                     start=True, stop=True, inc=TE)
                                R.wait("vector", TE)
                                R.op("vector", "tensor_copy", tmp_y[:], bc[:, 0:512])
                                R.op("vector", "tensor_mul", y_acc[rows, :], y_acc[rows, :],
                                     rb_sb[rows, :], inc=VE)
                                R.wait("vector", VE)
                                R.op("vector", "tensor_mul", tmp_y[rows, :], yps[rows, :],
                                     tmp_y[rows, :], inc=VE)
                                R.wait("vector", VE)
                                R.op("vector", "tensor_add", y_acc[rows, :], y_acc[rows, :],
                                     tmp_y[rows, :], inc=VE)
                                R.wait("tensor", VE)
                        R.op("vector", "reciprocal", r_acc[:], r_acc[:], inc=VE)
                        R.wait("tensor", VE)
                        for qs in range(4):
                            R.op("tensor", "transpose", smalls[0:1, qs * 128:(qs + 1) * 128],
                                 r_acc[:, qs:qs + 1], ident_f[:],
                                 inc=(TE if qs == 3 else None))
                        R.wait("vector", TE)
                        R.op("vector", "tensor_copy", row_a[:], smalls[0:1, 0:512],
                             inc=VE)
                        R.wait("tensor", VE)
                        R.op("tensor", "matmul", bc[:, 0:512], ones_f[:],
                             row_a[:], start=True, stop=True, inc=TE)
                        R.wait("vector", TE)
                        R.op("vector", "tensor_copy", rb_sb[:], bc[:, 0:512], inc=VE)
                        R.wait("vector", VE)
                        R.op("vector", "tensor_mul", ynT[rows, ic * 512:(ic + 1) * 512],
                             y_acc[rows, :], rb_sb[rows, :], inc=VE)
                        R.wait("tensor", VE)

                if debug:
                    R.wait("gpsimd", VE)
                    R.dma("gpsimd", dbg_yn[qb], ynT[:], OT)
                R.wait("tensor", GD)
                for ts in range(4):
                    tt = qb * 4 + ts
                    R.wait("scalar", VE)
                    for half in range(2):
                        for ic in range(8):
                            R.op("tensor", "matmul", bc[:, 0:512],
                                 ynT[:, ic * 512 + ts * 128: ic * 512 + ts * 128 + 128],
                                 wo_sb[:, ic * 1024 + half * 512: ic * 1024 + half * 512 + 512],
                                 start=(ic == 0), stop=(ic == 7),
                                 inc=(TE if ic == 7 else None))
                        R.wait("scalar", TE)
                        R.op("scalar", "copy", attn_f32[:, half * 512:(half + 1) * 512],
                             bc[:, 0:512], inc=SE)
                        R.wait("tensor", SE)
                    R.wait("vector", SE)
                    emit_ln_stats(R, lt, attn_f32[:], 1024, VE, SE)
                    R.op("vector", "tensor_scalar", attn_f32[:], attn_f32[:], lt.negm[:],
                         lt.rstd[:], ALU.add, op1=ALU.mult, inc=VE)
                    R.wait("vector", VE)
                    R.op("vector", "tensor_add", x1b[:], attn_f32[:],
                         xq_sb[:, tt * 1024:(tt + 1) * 1024], inc=VE)
                    R.wait("gpsimd", VE)
                    R.dma("gpsimd", x1out[tt * 128:(tt + 1) * 128, :], x1b[:], OT)
                    R.wait("vector", OT)

            if debug:
                R.dma("gpsimd", dbg_k, kT_d.ap(), OT)
                R.dma("gpsimd", dbg_v, v_d.ap(), OT)
                R.dma("gpsimd", dbg_q, qT[:], OT)
            R.wait("gpsimd", OT)
            R.wait("gpsimd", SP)
            R.flush(block)
    return nc


def make_mtail(par):
    tri = np.triu(np.full((128, 128), NEG, np.float32), 1)
    zero = np.zeros((128, 128), np.float32)
    allm = np.full((128, 128), NEG, np.float32)
    out = np.empty((16, 5, 128, 128), np.float32)
    for j in range(16):
        A = OWN[par][j // 4]
        V = 4 * A + (j % 4) + 1
        for s in range(5):
            kt = KK(j) - 5 + s
            out[j, s] = zero if kt < V - 1 else (tri if kt == V - 1 else allm)
    return out


def stage_A(inputs):
    x = np.ascontiguousarray(inputs["L0_attn_x"], np.float32)
    nk = np.ascontiguousarray(inputs["L0_attn_null_k"], np.float32).reshape(16, 64)
    nv = np.ascontiguousarray(inputs["L0_attn_null_v"], np.float32).reshape(16, 64)
    nullk_p = np.ascontiguousarray(
        nk.reshape(8, 2, 64).transpose(1, 2, 0).reshape(128, 8))
    nullv_r = np.ascontiguousarray(nv.reshape(1, 1024))
    lsc_r = np.ascontiguousarray(
        inputs["L0_attn_logit_scale"], np.float32).reshape(1, 16)
    ident = np.eye(128, dtype=np.float32)
    mt = [make_mtail(0), make_mtail(1)]
    maps = []
    for c in range(NCORES):
        b, par = c // 2, c % 2
        rows = own_rows(par)
        maps.append({
            "xfull": x[b],
            "xq": np.ascontiguousarray(x[b][rows]),
            "wq": np.ascontiguousarray(inputs["L0_attn_Wq"], np.float32),
            "wk": np.ascontiguousarray(inputs["L0_attn_Wk"], np.float32),
            "wv": np.ascontiguousarray(inputs["L0_attn_Wv"], np.float32),
            "wo": np.ascontiguousarray(inputs["L0_attn_Wo"], np.float32),
            "nullk_p": nullk_p, "nullv_r": nullv_r, "lsc_r": lsc_r,
            "ident_in": ident, "mtail": mt[par],
        })
    return maps


def gather_own(results, key):
    out = np.empty((4, T, D), np.float32)
    for c in range(NCORES):
        b, par = c // 2, c % 2
        out[b][own_rows(par)] = results[c][key]
    return out


_NC_CACHE = {}


def run_A(inputs, debug=False):
    key = ("A", debug)
    if key not in _NC_CACHE:
        _NC_CACHE[key] = build_launch_A(debug)
    maps = stage_A(inputs)
    res = run_bass_kernel_spmd(_NC_CACHE[key], maps, core_ids=list(range(NCORES)))
    if debug:
        return gather_own(res.results, "x1"), res.results
    return gather_own(res.results, "x1")



def build_launch_B(fc=None):
    pre = "" if fc is None else "b_"
    S127 = 1.0 / 127.0
    if fc is None:
        nc = bass.Bass("TRN2", target_bir_lowering=False)
        R = Rec(nc)
        x1i = nc.dram_tensor("x1i", [OWNT, D], F32, kind="ExternalInput").ap()
        wgT = nc.dram_tensor("wgT", [D, 3072], I32, kind="ExternalInput").ap()
        wuT = nc.dram_tensor("wuT", [D, 3072], I32, kind="ExternalInput").ap()
        wdT = nc.dram_tensor("wdT", [3072, D], I32, kind="ExternalInput").ap()
        ident_in = nc.dram_tensor("ident_in", [128, 128], F32, kind="ExternalInput").ap()
        x2out = nc.dram_tensor("x2", [OWNT, D], F32, kind="ExternalOutput").ap()
    else:
        nc, R = fc["nc"], fc["R"]
        io = fc["io"]
        x1i = fc["x1_d"].ap()
        wgT, wuT, wdT = io["b_wgT"], io["b_wuT"], io["b_wdT"]
        ident_in = io["ident_in"]
        x2out = fc["x2_d"].ap()

    with ExitStack() as CTX:
        sb = lambda n, s, dt: CTX.enter_context(nc.sbuf_tensor(pre + n, s, dt))
        ps = lambda n, s, dt=F32: CTX.enter_context(nc.psum_tensor(pre + n, s, dt))
        if fc is None:
            sem = lambda n: Sem(CTX.enter_context(nc.semaphore(n)))
            LD = sem("LD"); GD = sem("GD")
            TE = sem("TE"); VE = sem("VE"); SE = sem("SE"); OT = sem("OT")
        else:
            sm = fc["sems"]
            LD, GD = sm["LD"], sm["GD"]
            TE, VE, SE, OT = sm["TE"], sm["VE"], sm["SE"], sm["OT"]

        wg_sb = sb("wg_sb", [128, 8 * 3072], BF16)
        wu_sb = sb("wu_sb", [128, 8 * 3072], BF16)
        wd_sb = [sb(f"wd_sb{i}", [128, 24 * 128], BF16) for i in range(2)]
        xc = sb("xc", [128, 4 * 1024], F32)
        xhat = [sb(f"xhat{i}", [128, 1024], BF16) for i in range(2)]
        xTc = sb("xTc", [128, 8 * 512], BF16)
        Bt = sb("Bt", [128, 24 * 512], BF16)
        asb = [sb(f"asb{i}", [128, 512], BF16) for i in range(2)]
        ytb = sb("ytb", [128, 8 * 512], BF16)
        yf = sb("yf", [128, 1024], F32)
        x2b = sb("x2b", [128, 1024], F32)
        ident = sb("ident", [128, 128], BF16)
        lt = LnTiles(nc, CTX, tag=pre)
        pps = [ps("pps0", [128, 512]), ps("pps1", [128, 512])]
        pp2 = [ps("pp20", [128, 512]), ps("pp21", [128, 512])]
        Tpack = ps("Tpack", [128, 1024], BF16)
        dps = ps("dps", [128, 512])

        block = fc["block"] if fc else CTX.enter_context(nc.Block())

        R.dma("gpsimd", ident[:], ident_in, GD)
        R.dma("gpsimd", wg_sb[:].rearrange("p (c n) -> p c n", n=3072),
              wgT.rearrange("(c p) n -> p c n", p=128), GD)
        R.dma("gpsimd", wu_sb[:].rearrange("p (c n) -> p c n", n=3072),
              wuT.rearrange("(c p) n -> p c n", p=128), GD)
        R.wait("tensor", GD)
        wdT3 = wdT.rearrange("(c p) n -> p c n", p=128)

        def ln_transpose_tile(src_f32, xhat_t, xT_t, t4):
            emit_ln_stats(R, lt, src_f32, 1024, VE, SE)
            R.wait("vector", TE)
            R.op("vector", "tensor_scalar", xhat_t[:], src_f32, lt.negm[:],
                 lt.rstd[:], ALU.add, op1=ALU.mult, inc=VE)
            R.wait("tensor", VE)
            xT3 = xT_t[:].rearrange("p (c n) -> p c n", n=512)
            for half in range(2):
                for c4 in range(4):
                    c = half * 4 + c4
                    R.op("tensor", "transpose", Tpack[:, c4 * 128:(c4 + 1) * 128],
                         xhat_t[:, c * 128:(c + 1) * 128], ident[:],
                         inc=(TE if c4 == 3 else None))
                R.wait("vector", TE)
                R.op("vector", "tensor_copy",
                     xT3[:, half * 4:half * 4 + 4, t4 * 128:(t4 + 1) * 128],
                     Tpack[:, 0:512].rearrange("p (k n) -> p k n", n=128), inc=VE)
                R.wait("tensor", VE)

        wd_hist = [0, 0]
        for qc in range(4):
            R.wait("sync", VE)
            R.wait("sync", OT)
            R.dma("sync", xc[:].rearrange("p (t n) -> p t n", n=1024),
                  x1i[qc * 512:(qc + 1) * 512, :].rearrange("(t p) n -> p t n", p=128), LD)
            R.wait("vector", LD)
            for t4 in range(4):
                ln_transpose_tile(xc[:, t4 * 1024:(t4 + 1) * 1024], xhat[t4 % 2], xTc, t4)
            se_hist = [0, 0]
            ve_hist = [0, 0]
            for hc in range(24):
                bidx = hc % 2
                R.wait("tensor", SE, se_hist[bidx])
                R.wait("tensor", VE, ve_hist[bidx])
                for kc in range(8):
                    R.op("tensor", "matmul", pps[bidx][:, 0:512],
                         wg_sb[:, kc * 3072 + hc * 128: kc * 3072 + hc * 128 + 128],
                         xTc[:, kc * 512:(kc + 1) * 512],
                         start=(kc == 0), stop=(kc == 7))
                for kc in range(8):
                    R.op("tensor", "matmul", pp2[bidx][:, 0:512],
                         wu_sb[:, kc * 3072 + hc * 128: kc * 3072 + hc * 128 + 128],
                         xTc[:, kc * 512:(kc + 1) * 512],
                         start=(kc == 0), stop=(kc == 7),
                         inc=(TE if kc == 7 else None))
                R.wait("scalar", TE)
                R.op("scalar", "activation", asb[bidx][:], pps[bidx][:, 0:512], AF.Silu,
                     scale=S127, inc=SE)
                R.wait("vector", SE)
                R.op("vector", "tensor_tensor", Bt[:, hc * 512:(hc + 1) * 512],
                     pp2[bidx][:, 0:512], asb[bidx][:], ALU.mult, inc=VE)
                se_hist[bidx] = SE.n
                ve_hist[bidx] = VE.n
            R.wait("tensor", VE)
            for oc in range(8):
                wdb = wd_sb[oc % 2]
                R.wait("gpsimd", TE, wd_hist[oc % 2])
                R.dma("gpsimd", wdb[:].rearrange("p (c n) -> p c n", n=128),
                      wdT3[:, :, oc * 128:(oc + 1) * 128], GD)
                R.wait("tensor", GD)
                R.wait("tensor", SE)
                for hc in range(24):
                    R.op("tensor", "matmul", dps[:, 0:512],
                         wdb[:, hc * 128:(hc + 1) * 128],
                         Bt[:, hc * 512:(hc + 1) * 512],
                         start=(hc == 0), stop=(hc == 23),
                         inc=(TE if hc == 23 else None))
                wd_hist[oc % 2] = TE.n
                R.wait("scalar", TE)
                R.op("scalar", "mul", ytb[:, oc * 512:(oc + 1) * 512], dps[:, 0:512],
                     S127 * S127, inc=SE)
            for ts in range(4):
                R.wait("tensor", SE)
                R.wait("tensor", VE)
                for half in range(2):
                    for c4 in range(4):
                        oc = half * 4 + c4
                        R.op("tensor", "transpose", Tpack[:, c4 * 128:(c4 + 1) * 128],
                             ytb[:, oc * 512 + ts * 128: oc * 512 + ts * 128 + 128],
                             ident[:], inc=(TE if c4 == 3 else None))
                    R.wait("vector", TE)
                    R.op("vector", "tensor_copy",
                         yf[:, half * 512:(half + 1) * 512],
                         Tpack[:, 0:512], inc=VE)
                    R.wait("tensor", VE)
                emit_ln_stats(R, lt, yf[:], 1024, VE, SE)
                R.op("vector", "tensor_scalar", yf[:], yf[:], lt.negm[:],
                     lt.rstd[:], ALU.add, op1=ALU.mult, inc=VE)
                R.wait("vector", VE)
                R.wait("vector", OT)
                R.op("vector", "tensor_add", x2b[:], yf[:],
                     xc[:, ts * 1024:(ts + 1) * 1024], inc=VE)
                R.wait("gpsimd", VE)
                tt = qc * 4 + ts
                R.dma("gpsimd", x2out[tt * 128:(tt + 1) * 128, :], x2b[:], OT)
        R.wait("gpsimd", OT)
        R.flush(block)
    return nc


def stage_B(x1, inputs):
    wgT = np.ascontiguousarray(inputs["L0_ffn_w_gate"].T)
    wuT = np.ascontiguousarray(inputs["L0_ffn_w_up"].T)
    wdT = np.ascontiguousarray(inputs["L0_ffn_w_down"].T)
    ident = np.eye(128, dtype=np.float32)
    maps = []
    for c in range(NCORES):
        b, par = c // 2, c % 2
        maps.append({
            "x1i": np.ascontiguousarray(x1[b][own_rows(par)]),
            "wgT": wgT, "wuT": wuT, "wdT": wdT, "ident_in": ident,
        })
    return maps


def run_B(x1, inputs):
    if "B" not in _NC_CACHE:
        _NC_CACHE["B"] = build_launch_B()
    res = run_bass_kernel_spmd(_NC_CACHE["B"], stage_B(x1, inputs),
                               core_ids=list(range(NCORES)))
    return gather_own(res.results, "x2")



def build_launch_C(maxphase=3, fc=None):
    pre = "" if fc is None else "c_"
    if fc is None:
        nc = bass.Bass("TRN2", target_bir_lowering=False)
        R = Rec(nc)
        xfull = nc.dram_tensor("xfull", [T, D], F32, kind="ExternalInput").ap()
        xq = nc.dram_tensor("xq", [OWNT, D], F32, kind="ExternalInput").ap()
        wqkv = nc.dram_tensor("wqkv", [D, 3072], F32, kind="ExternalInput").ap()
        wf2 = nc.dram_tensor("wf2", [128, 128], F32, kind="ExternalInput").ap()
        bf_p = nc.dram_tensor("bf_p", [128, 1], F32, kind="ExternalInput").ap()
        bf_r = nc.dram_tensor("bf_r", [1, 128], F32, kind="ExternalInput").ap()
        wout = nc.dram_tensor("wout", [D, D], F32, kind="ExternalInput").ap()
        bout_r = nc.dram_tensor("bout_r", [1, D], F32, kind="ExternalInput").ap()
        ident_in = nc.dram_tensor("ident_in", [128, 128], F32, kind="ExternalInput").ap()
        x3out = nc.dram_tensor("x3", [OWNT, D], F32, kind="ExternalOutput").ap()
    else:
        nc, R = fc["nc"], fc["R"]
        io = fc["io"]
        xfull = fc["x2g"].ap()
        xq = fc["x2_d"].ap()
        wqkv, wf2, bf_p, bf_r = io["c_wqkv"], io["c_wf2"], io["c_bf_p"], io["c_bf_r"]
        wout, bout_r = io["c_wout"], io["c_bout_r"]
        ident_in = io["ident_in"]
        x3out = fc["x3_d"].ap()

    with ExitStack() as CTX:
        sb = lambda n, s, dt: CTX.enter_context(nc.sbuf_tensor(pre + n, s, dt))
        ps = lambda n, s, dt=F32: CTX.enter_context(nc.psum_tensor(pre + n, s, dt))
        if fc is None:
            sem = lambda n: Sem(CTX.enter_context(nc.semaphore(n)))
            LD = sem("LD"); GD = sem("GD")
            TE = sem("TE"); VE = sem("VE"); SE = sem("SE"); OT = sem("OT")
        else:
            sm = fc["sems"]
            LD, GD = sm["LD"], sm["GD"]
            TE, VE, SE, OT = sm["TE"], sm["VE"], sm["SE"], sm["OT"]

        wq_sb = sb("wq_sb", [128, 8 * 3072], BF16)
        wout_sb = sb("wout_sb", [128, 8 * 1024], BF16)
        wf_sb = sb("wf_sb", [128, 128], BF16)
        bfp_sb = sb("bfp_sb", [128, 1], F32)
        bfr_sb = sb("bfr_sb", [1, 128], BF16)
        bout_sb = sb("bout_sb", [1, D], BF16)
        ident = sb("ident", [128, 128], BF16)
        ones_b = sb("ones_b", [1, 128], BF16)
        ones_c = sb("ones_c", [128, 1], BF16)
        xb = [sb(f"xb{i}", [128, 1024], F32) for i in range(2)]
        xhat = [sb(f"xhat{i}", [128, 1024], BF16) for i in range(2)]
        xTc = sb("xTc", [128, 8 * 512], BF16)
        kTp = sb("kTp", [128, 8 * 512], BF16)
        Kp = sb("Kp", [128, 4 * 1024], BF16)
        Vp = sb("Vp", [128, 4 * 1024], BF16)
        kv_acc = sb("kv_acc", [128, 8 * 65], F32)
        kvb = sb("kvb", [128, 16 * 64], BF16)
        ksb = sb("ksb", [128, 16], BF16)
        qpT = sb("qpT", [128, 8 * 2048], BF16)
        nrm = sb("nrm", [128, 16], F32)
        obn = sb("obn", [128, 1024], BF16)
        oT = sb("oT", [128, 8 * 512], BF16)
        xqc = sb("xqc", [128, 4 * 1024], F32)
        af = sb("af", [128, 1024], F32)
        x3b = sb("x3b", [128, 1024], F32)
        lt = LnTiles(nc, CTX, tag=pre)
        pps = ps("pps", [128, 512])
        pp2 = ps("pp2", [128, 512])
        Tpack = ps("Tpack", [128, 1024], BF16)
        fps = ps("fps", [128, 128])
        kvps = ps("kvps", [64, 65])
        ops1 = ps("ops1", [128, 512])
        ops2 = ps("ops2", [128, 512])
        smalls = ps("smalls", [128, 512])

        block = fc["block"] if fc else CTX.enter_context(nc.Block())

        R.dma("gpsimd", ident[:], ident_in, GD)
        R.dma("gpsimd", wf_sb[:], wf2, GD)
        R.dma("gpsimd", bfr_sb[:], bf_r, GD)
        R.dma("gpsimd", bout_sb[:], bout_r, GD)
        R.dma("sync", bfp_sb[:], bf_p, LD)
        R.dma("gpsimd", wq_sb[:].rearrange("p (c n) -> p c n", n=3072),
              wqkv.rearrange("(c p) n -> p c n", p=128), GD)
        R.dma("gpsimd", wout_sb[:].rearrange("p (c n) -> p c n", n=1024),
              wout.rearrange("(c p) n -> p c n", p=128), GD)
        R.op("vector", "memset", ones_b[:], 1.0)
        R.op("vector", "memset", ones_c[:], 1.0)
        R.op("vector", "memset", kv_acc[:], 0.0, inc=VE)
        R.wait("tensor", GD)
        R.wait("tensor", LD)
        R.wait("tensor", VE)

        def ln_transpose_tile(src_f32, xhat_t, xT_t, t4):
            emit_ln_stats(R, lt, src_f32, 1024, VE, SE)
            R.wait("vector", TE)
            R.op("vector", "tensor_scalar", xhat_t[:], src_f32, lt.negm[:],
                 lt.rstd[:], ALU.add, op1=ALU.mult, inc=VE)
            R.wait("tensor", VE)
            xT3 = xT_t[:].rearrange("p (c n) -> p c n", n=512)
            for half in range(2):
                for c4 in range(4):
                    c = half * 4 + c4
                    R.op("tensor", "transpose", Tpack[:, c4 * 128:(c4 + 1) * 128],
                         xhat_t[:, c * 128:(c + 1) * 128], ident[:],
                         inc=(TE if c4 == 3 else None))
                R.wait("vector", TE)
                R.op("vector", "tensor_copy",
                     xT3[:, half * 4:half * 4 + 4, t4 * 128:(t4 + 1) * 128],
                     Tpack[:, 0:512].rearrange("p (k n) -> p k n", n=128), inc=VE)
                R.wait("tensor", VE)

        for tc in range(8):
            for t4 in range(4):
                tt = tc * 4 + t4
                R.wait("sync", VE)
                R.dma("sync", xb[tt % 2][:], xfull[tt * 128:(tt + 1) * 128, :], LD)
                R.wait("vector", LD)
                ln_transpose_tile(xb[tt % 2][:], xhat[tt % 2], xTc, t4)
            for ic in range(8):
                R.wait("tensor", SE)
                for kc in range(8):
                    R.op("tensor", "matmul", pps[:, 0:512],
                         wq_sb[:, kc * 3072 + 1024 + ic * 128: kc * 3072 + 1024 + ic * 128 + 128],
                         xTc[:, kc * 512:(kc + 1) * 512],
                         start=(kc == 0), stop=(kc == 7),
                         inc=(TE if kc == 7 else None))
                R.wait("vector", TE)
                R.op("vector", "tensor_copy", kTp[:, ic * 512:(ic + 1) * 512],
                     pps[:, 0:512], inc=VE)
                R.wait("tensor", VE)
            for t4 in range(4):
                for half in range(2):
                    bank = pps if half == 0 else pp2
                    for kc in range(8):
                        R.op("tensor", "matmul", bank[:, 0:512],
                             xTc[:, kc * 512 + t4 * 128: kc * 512 + t4 * 128 + 128],
                             wq_sb[:, kc * 3072 + 2048 + half * 512: kc * 3072 + 2048 + half * 512 + 512],
                             start=(kc == 0), stop=(kc == 7),
                             inc=(TE if kc == 7 else None))
                R.wait("vector", TE)
                R.op("vector", "tensor_copy", Vp[:, t4 * 1024: t4 * 1024 + 512], pps[:, 0:512])
                R.op("vector", "tensor_copy", Vp[:, t4 * 1024 + 512: t4 * 1024 + 1024],
                     pp2[:, 0:512], inc=VE)
                R.wait("tensor", VE)
            for ic in range(8):
                for t4 in range(4):
                    R.wait("tensor", SE)
                    R.op("tensor", "matmul", fps[:, 0:128],
                         kTp[:, ic * 512 + t4 * 128: ic * 512 + t4 * 128 + 128],
                         wf_sb[:, 0:128], start=True, stop=False)
                    R.op("tensor", "matmul", fps[:, 0:128],
                         ones_b[0:1, 0:128], bfr_sb[:],
                         start=False, stop=True, inc=TE)
                    R.wait("scalar", TE)
                    R.op("scalar", "activation",
                         Kp[:, t4 * 1024 + ic * 128: t4 * 1024 + ic * 128 + 128],
                         fps[:, 0:128], AF.Relu, inc=SE)
            R.wait("tensor", SE)
            for ic in range(8):
                for hh in range(2):
                    h = 2 * ic + hh
                    rows = slice(64 * hh, 64 * hh + 64)
                    for t4 in range(4):
                        R.op("tensor", "matmul", kvps[:, 0:64],
                             Kp[:, t4 * 1024 + h * 64: t4 * 1024 + h * 64 + 64],
                             Vp[:, t4 * 1024 + h * 64: t4 * 1024 + h * 64 + 64],
                             start=(t4 == 0), stop=False)
                        R.op("tensor", "matmul", kvps[:, 64:65],
                             Kp[:, t4 * 1024 + h * 64: t4 * 1024 + h * 64 + 64],
                             ones_c[:], start=False, stop=(t4 == 3),
                             inc=(TE if t4 == 3 else None))
                    R.wait("vector", TE)
                    R.op("vector", "tensor_tensor",
                         kv_acc[rows, ic * 65:(ic + 1) * 65],
                         kv_acc[rows, ic * 65:(ic + 1) * 65],
                         kvps[:, 0:65], ALU.add, inc=VE)
                    R.wait("tensor", VE)

        if maxphase < 2:
            R.wait("gpsimd", VE)
            R.flush(block)
            return nc
        R.op("vector", "memset", kvb[:], 0.0)
        R.op("vector", "memset", ksb[:], 0.0, inc=VE)
        R.wait("vector", VE)
        kva = kv_acc[:].rearrange("p (c n) -> p c n", n=65)
        kvb3 = kvb[:].rearrange("p (c n) -> p c n", n=64)
        for ic in range(8):
            for hh in range(2):
                h = 2 * ic + hh
                rows = slice(64 * hh, 64 * hh + 64)
                R.op("vector", "tensor_copy", kvb3[rows, h, :], kva[rows, ic, 0:64])
                R.op("vector", "tensor_copy", ksb[rows, h:h + 1], kva[rows, ic, 64:65],
                     inc=(VE if h == 15 else None))

        for qc in range(4):
            for t4 in range(4):
                tt = qc * 4 + t4
                R.wait("sync", VE)
                R.dma("sync", xb[tt % 2][:], xq[tt * 128:(tt + 1) * 128, :], LD)
                R.wait("vector", LD)
                ln_transpose_tile(xb[tt % 2][:], xhat[tt % 2], xTc, t4)
            for ic in range(8):
                R.wait("tensor", SE)
                for kc in range(8):
                    R.op("tensor", "matmul", pps[:, 0:512],
                         wq_sb[:, kc * 3072 + ic * 128: kc * 3072 + ic * 128 + 128],
                         xTc[:, kc * 512:(kc + 1) * 512],
                         start=(kc == 0), stop=(kc == 7),
                         inc=(TE if kc == 7 else None))
                R.wait("vector", TE)
                R.op("vector", "tensor_copy", kTp[:, ic * 512:(ic + 1) * 512],
                     pps[:, 0:512], inc=VE)
                R.wait("tensor", VE)
                R.op("tensor", "matmul", pp2[:, 0:512], wf_sb[:, 0:128],
                     kTp[:, ic * 512:(ic + 1) * 512],
                     start=True, stop=True, inc=TE)
                R.wait("scalar", TE)
                R.op("scalar", "activation",
                     qpT[:, ic * 2048 + qc * 512: ic * 2048 + (qc + 1) * 512],
                     pp2[:, 0:512], AF.Relu, bias=bfp_sb[:], inc=SE)
                R.wait("tensor", SE)

        if maxphase < 2.2:
            R.wait("gpsimd", VE)
            R.flush(block)
            return nc
        for qc in range(4):
            R.wait("sync", OT)
            R.wait("sync", VE)
            R.dma("sync", xqc[:].rearrange("p (t n) -> p t n", n=1024),
                  xq[qc * 512:(qc + 1) * 512, :].rearrange("(t p) n -> p t n", p=128), LD)
            for ts in range(4):
                R.wait("tensor", SE)
                R.wait("tensor", VE)
                for ic in range(8):
                    lhsT = qpT[:, ic * 2048 + qc * 512 + ts * 128:
                               ic * 2048 + qc * 512 + ts * 128 + 128]
                    for hh in range(2):
                        h = 2 * ic + hh
                        bank = ops1 if h < 8 else ops2
                        col = (h % 8) * 64
                        R.op("tensor", "matmul", bank[:, col:col + 64], lhsT,
                             kvb[:, h * 64:(h + 1) * 64], start=True, stop=True)
                        R.op("tensor", "matmul", smalls[:, h:h + 1], lhsT,
                             ksb[:, h:h + 1], start=True, stop=True,
                             inc=(TE if h == 15 else None))
                R.wait("vector", TE)
                if maxphase < 2.22:
                    continue
                R.op("vector", "tensor_scalar", nrm[:], smalls[:, 0:16], 1e-6, None,
                     ALU.add, inc=VE)
                R.wait("vector", VE)
                R.op("vector", "reciprocal", nrm[:], nrm[:], inc=VE)
                R.wait("vector", VE)
                if maxphase < 2.24:
                    continue
                for h in range(16):
                    bank = ops1 if h < 8 else ops2
                    col = (h % 8) * 64
                    R.op("vector", "tensor_scalar", obn[:, h * 64:(h + 1) * 64],
                         bank[:, col:col + 64], nrm[:, h:h + 1], None, ALU.mult,
                         inc=(VE if h == 15 else None))
                R.wait("tensor", VE)
                if maxphase < 2.5:
                    continue
                for half in range(2):
                    for c4 in range(4):
                        c = half * 4 + c4
                        R.op("tensor", "transpose", Tpack[:, c4 * 128:(c4 + 1) * 128],
                             obn[:, c * 128:(c + 1) * 128], ident[:],
                             inc=(TE if c4 == 3 else None))
                    R.wait("vector", TE)
                    R.op("vector", "tensor_copy",
                         oT[:].rearrange("p (c n) -> p c n", n=512)[
                             :, half * 4:half * 4 + 4, ts * 128:(ts + 1) * 128],
                         Tpack[:, 0:512].rearrange("p (k n) -> p k n", n=128), inc=VE)
                    R.wait("tensor", VE)
            if maxphase < 2.75:
                continue
            R.wait("vector", LD)
            for ts in range(4):
                tt = qc * 4 + ts
                R.wait("scalar", VE)
                for half in range(2):
                    for ic in range(8):
                        R.op("tensor", "matmul", pps[:, 0:512],
                             oT[:, ic * 512 + ts * 128: ic * 512 + ts * 128 + 128],
                             wout_sb[:, ic * 1024 + half * 512: ic * 1024 + half * 512 + 512],
                             start=(ic == 0), stop=False)
                    R.op("tensor", "matmul", pps[:, 0:512], ones_b[0:1, 0:128],
                         bout_sb[0:1, half * 512:(half + 1) * 512],
                         start=False, stop=True, inc=TE)
                    R.wait("scalar", TE)
                    R.op("scalar", "copy", af[:, half * 512:(half + 1) * 512],
                         pps[:, 0:512], inc=SE)
                    R.wait("tensor", SE)
                R.wait("vector", SE)
                emit_ln_stats(R, lt, af[:], 1024, VE, SE)
                R.op("vector", "tensor_scalar", af[:], af[:], lt.negm[:],
                     lt.rstd[:], ALU.add, op1=ALU.mult, inc=VE)
                R.wait("vector", VE)
                R.wait("vector", OT)
                R.op("vector", "tensor_add", x3b[:], af[:],
                     xqc[:, ts * 1024:(ts + 1) * 1024], inc=VE)
                R.wait("gpsimd", VE)
                R.dma("gpsimd", x3out[tt * 128:(tt + 1) * 128, :], x3b[:], OT)
        R.wait("gpsimd", OT)
        R.flush(block)
    return nc


def stage_C(x2, inputs):
    wf = np.ascontiguousarray(inputs["L1_attn_W_f"], np.float32)
    wf2 = np.zeros((128, 128), np.float32)
    wf2[0:64, 0:64] = wf
    wf2[64:128, 64:128] = wf
    bf = np.ascontiguousarray(inputs["L1_attn_b_f"], np.float32)
    bf_p = np.concatenate([bf, bf]).reshape(128, 1)
    bf_r = np.concatenate([bf, bf]).reshape(1, 128)
    bout_r = np.ascontiguousarray(inputs["L1_attn_b_out"], np.float32).reshape(1, D)
    ident = np.eye(128, dtype=np.float32)
    maps = []
    for c in range(NCORES):
        b, par = c // 2, c % 2
        maps.append({
            "xfull": x2[b],
            "xq": np.ascontiguousarray(x2[b][own_rows(par)]),
            "wqkv": np.ascontiguousarray(inputs["L1_attn_W_qkv"], np.float32),
            "wf2": wf2, "bf_p": bf_p, "bf_r": bf_r,
            "wout": np.ascontiguousarray(inputs["L1_attn_W_out"], np.float32),
            "bout_r": bout_r, "ident_in": ident,
        })
    return maps


def run_C(x2, inputs, maxphase=3):
    if ("C", maxphase) not in _NC_CACHE:
        _NC_CACHE[("C", maxphase)] = build_launch_C(maxphase)
    _NC_CACHE["C"] = _NC_CACHE[("C", maxphase)]
    res = run_bass_kernel_spmd(_NC_CACHE["C"], stage_C(x2, inputs),
                               core_ids=list(range(NCORES)))
    return gather_own(res.results, "x3")


def kernel(**inputs):
    x1 = run_A(inputs)
    x2 = run_B(x1, inputs)
    x3 = run_C(x2, inputs)
    return x3

def build_launch_D(ln_in, sandwich, debug=False, fc=None, li=0):
    pre = "" if fc is None else f"d{li}_"
    QKS = 8.0
    if fc is None:
        nc = bass.Bass("TRN2", target_bir_lowering=False)
        R = Rec(nc)
        xe = nc.dram_tensor("xe", [4, 1664, D], F32, kind="ExternalInput").ap()
        xq3 = nc.dram_tensor("xq3", [OWNT, D], F32, kind="ExternalInput").ap()
        wqkv = nc.dram_tensor("wqkv", [D, 1536], F32, kind="ExternalInput").ap()
        wo = nc.dram_tensor("wo", [512, D], F32, kind="ExternalInput").ap()
        w1r = nc.dram_tensor("w1r", [D, 5632], F32, kind="ExternalInput").ap()
        w2r = nc.dram_tensor("w2r", [2816, D], F32, kind="ExternalInput").ap()
        b1r = nc.dram_tensor("b1r", [1, 5632], F32, kind="ExternalInput").ap()
        b2r = nc.dram_tensor("b2r", [1, D], F32, kind="ExternalInput").ap()
        lnb_p = nc.dram_tensor("lnb_p", [128, 8], F32, kind="ExternalInput").ap()
        lng_p = nc.dram_tensor("lng_p", [128, 8], F32, kind="ExternalInput").ap()
        qs_r = nc.dram_tensor("qs_r", [1, 64], F32, kind="ExternalInput").ap()
        ks_r = nc.dram_tensor("ks_r", [1, 64], F32, kind="ExternalInput").ap()
        rotq = nc.dram_tensor("rotq", [4, 2, 1024, 512], F32, kind="ExternalInput").ap()
        rotk = nc.dram_tensor("rotk", [4, 2, 1536, 512], F32, kind="ExternalInput").ap()
        msk = nc.dram_tensor("msk", [4, 8, 128, 640], F32, kind="ExternalInput").ap()
        ident_in = nc.dram_tensor("ident_in", [128, 128], F32, kind="ExternalInput").ap()
        zout = nc.dram_tensor("zout", [OWNT, D], F32, kind="ExternalOutput").ap()
    else:
        nc, R = fc["nc"], fc["R"]
        io = fc["io"]
        xe = fc["xe_d"].ap()
        xq3 = fc["x3_d"].ap()
        wqkv, wo = io[pre + "wqkv"], io[pre + "wo"]
        w1r, w2r, b1r, b2r = io[pre + "w1r"], io[pre + "w2r"], io[pre + "b1r"], io[pre + "b2r"]
        lnb_p, lng_p = io[pre + "lnb_p"], io[pre + "lng_p"]
        qs_r, ks_r = io[pre + "qs_r"], io[pre + "ks_r"]
        rotq, rotk, msk = io["d_rotq"], io["d_rotk"], io["d_msk"]
        ident_in = io["ident_in"]
        zout = (fc["z1_d"].ap() if li == 0 else fc["out"])

    xm_d = nc.dram_tensor(pre + "xm_d", [128, 4 * 8 * 1024], F32)
    h2_d = nc.dram_tensor(pre + "h2_d", [128, 4 * 8 * 512], BF16)
    if debug:
        dbg_xm = nc.dram_tensor("dbg_xm", [128, 4 * 8 * 1024], F32, kind="ExternalOutput").ap()
        dbg_h2 = nc.dram_tensor("dbg_h2", [128, 4 * 8 * 512], BF16, kind="ExternalOutput").ap()
        dbg_q = nc.dram_tensor("dbg_q", [128, 4 * 1024], BF16, kind="ExternalOutput").ap()
        dbg_k = nc.dram_tensor("dbg_k", [128, 4 * 1536], BF16, kind="ExternalOutput").ap()
        dbg_v = nc.dram_tensor("dbg_v", [128, 12 * 512], BF16, kind="ExternalOutput").ap()
        dbg_o = nc.dram_tensor("dbg_o", [128, 4 * 1024], BF16, kind="ExternalOutput").ap()
        dbg_zT = nc.dram_tensor("dbg_zT", [128, 8 * 1664], BF16, kind="ExternalOutput").ap()

    with ExitStack() as CTX:
        sb = lambda n, s, dt: CTX.enter_context(nc.sbuf_tensor(pre + n, s, dt))
        ps = lambda stk, n, s, dt=F32: stk.enter_context(nc.psum_tensor(pre + n, s, dt))
        if fc is None:
            sem = lambda n: Sem(CTX.enter_context(nc.semaphore(n)))
            LD = sem("LD"); GD = sem("GD"); SP = sem("SP")
            TE = sem("TE"); VE = sem("VE"); SE = sem("SE"); OT = sem("OT")
        else:
            sm = fc["sems"]
            LD, GD, SP = sm["LD"], sm["GD"], sm["SP"]
            TE, VE, SE, OT = sm["TE"], sm["VE"], sm["SE"], sm["OT"]

        ident = sb("ident", [128, 128], BF16)
        ident_f = sb("ident_f", [128, 128], F32)
        ones_f = sb("ones_f", [1, 128], F32)
        ones_bf = sb("ones_bf", [1, 128], BF16)
        qst = sb("qst", [128, 512], F32)
        kst = sb("kst", [128, 512], F32)
        lng_sb = sb("lng_sb", [128, 8], F32)
        lt = LnTiles(nc, CTX, tag=pre)
        l2r = sb("l2r", [128, 8], F32)
        Tpack = ps(CTX, "Tpack", [128, 1024], BF16)
        smalls = ps(CTX, "smalls", [128, 512], F32)

        block = fc["block"] if fc else CTX.enter_context(nc.Block())

        R.dma("gpsimd", ident[:], ident_in, GD)
        R.dma("sync", ident_f[:], ident_in, LD)
        R.dma("sync", lng_sb[:], lng_p, LD)
        R.op("vector", "memset", ones_f[:], 1.0)
        R.op("vector", "memset", ones_bf[:], 1.0, inc=VE)
        R.wait("tensor", VE)
        for row_ap, dst in ((qs_r, qst), (ks_r, kst)):
            rsb = sb("row_" + dst.name, [1, 64], F32)
            R.dma("sync", rsb[:], row_ap, LD)
            R.wait("tensor", LD)
            R.op("tensor", "matmul", smalls[:, 0:64], ones_f[:], rsb[:],
                 start=True, stop=True, inc=TE)
            R.wait("vector", TE)
            for h in range(8):
                R.op("vector", "tensor_copy", dst[:, h * 64:(h + 1) * 64],
                     smalls[:, 0:64], inc=(VE if h == 7 else None))
            R.wait("tensor", VE)

        with ExitStack() as PH1:
            sb1 = lambda n, s, dt: PH1.enter_context(nc.sbuf_tensor(pre + n, s, dt))
            wq_sb = sb1("wq_sb", [128, 8 * 1536], BF16)
            wo_sb = sb1("wo_sb", [128, 4 * 1024], BF16)
            xb = [sb1(f"xb{i}", [128, 1024], F32) for i in range(2)]
            zf = sb1("zf", [128, 1024], F32)
            zhat = [sb1(f"zhat{i}", [128, 1024], BF16) for i in range(2)]
            zT = sb1("zT", [128, 8 * 1664], BF16)
            qev = sb1("qev", [128, 512], F32)
            q2s = sb1("q2s", [128, 512], F32)
            rotA = [sb1(f"rotA{i}", [128, 512], F32) for i in range(2)]
            rotB = [sb1(f"rotB{i}", [128, 512], F32) for i in range(2)]
            t1b = sb1("t1b", [128, 512], F32)
            t2b = sb1("t2b", [128, 512], F32)
            q3b = sb1("q3b", [128, 512], BF16)
            qfT = sb1("qfT", [128, 4 * 1024], BF16)
            kfT = sb1("kfT", [128, 4 * 1536], BF16)
            vtm = sb1("vtm", [128, 12 * 512], BF16)
            msk_sb = sb1("msk_sb", [128, 8 * 640], BF16)
            P = sb1("P", [128, 640], BF16)
            PT = sb1("PT", [128, 12 * 1024], BF16)
            oT = sb1("oT", [128, 4 * 1024], BF16)
            rg = sb1("rg", [128, 8], F32)
            row_a = sb1("row_a", [1, 1024], F32)
            rb_sb = sb1("rb_sb", [128, 1024], F32)
            xm = [sb1(f"xm{i}", [128, 1024], F32) for i in range(2)]
            h2t = sb1("h2t", [128, 1024], F32)
            h2n = sb1("h2n", [128, 1024], BF16)
            pps = ps(PH1, "pps", [128, 512])
            Sps = ps(PH1, "Sps", [128, 640])
            ybank = ps(PH1, "ybank", [128, 1024])
            bcb = ps(PH1, "bcb", [128, 512])

            R.dma("gpsimd", wq_sb[:].rearrange("p (c n) -> p c n", n=1536),
                  wqkv.rearrange("(c p) n -> p c n", p=128), GD)
            R.dma("gpsimd", wo_sb[:].rearrange("p (c n) -> p c n", n=1024),
                  wo.rearrange("(c p) n -> p c n", p=128), GD)
            R.wait("tensor", GD)

            def hT(kc, c0, w):
                off = -1 if kc >= 4 else 0
                return zT[:, kc * 1664 + c0 + off: kc * 1664 + c0 + off + w]

            for o in range(4):
                R.wait("gpsimd", VE)
                R.dma("gpsimd", msk_sb[:].rearrange("p (t n) -> p t n", n=640),
                      msk[o].rearrange("t p n -> p t n"), GD)
                R.wait("vector", GD)
                zT3 = zT[:].rearrange("p (c n) -> p c n", n=1664)
                for t in range(13):
                    buf = xb[t % 2]
                    R.wait("sync", VE)
                    R.dma("sync", buf[:], xe[o, t * 128:(t + 1) * 128, :], LD)
                    R.wait("vector", LD)
                    if ln_in:
                        emit_ln_stats(R, lt, buf[:], 1024, VE, SE)
                        R.op("vector", "tensor_scalar", zf[:], buf[:], lt.negm[:],
                             lt.rstd[:], ALU.add, op1=ALU.mult, inc=VE)
                        R.wait("vector", VE)
                        src_f = zf
                    else:
                        src_f = buf
                    R.wait("vector", TE)
                    R.op("vector", "tensor_copy", zhat[t % 2][:], src_f[:], inc=VE)
                    R.wait("tensor", VE)
                    for half in range(2):
                        for c4 in range(4):
                            c = half * 4 + c4
                            R.op("tensor", "transpose", Tpack[:, c4 * 128:(c4 + 1) * 128],
                                 zhat[t % 2][:, c * 128:(c + 1) * 128], ident[:],
                                 inc=(TE if c4 == 3 else None))
                        R.wait("vector", TE)
                        R.op("vector", "tensor_copy",
                             zT3[:, half * 4:half * 4 + 4, t * 128:(t + 1) * 128],
                             Tpack[:, 0:512].rearrange("p (k n) -> p k n", n=128), inc=VE)
                        R.wait("tensor", VE)


                def qk_pipe(kind, tiles, s0, dst_fT, rot_ap, sct, it0=0):
                    inner0 = 0 if kind == "q" else 512
                    nn = 1024 if kind == "q" else 1536
                    for it in range(it0, tiles):
                        s = s0 + it * 128
                        R.wait("sync", VE)
                        R.dma("sync", rotA[it % 2][:],
                              rot_ap[o, 0, it * 128:(it + 1) * 128, :], LD)
                        R.dma("sync", rotB[it % 2][:],
                              rot_ap[o, 1, it * 128:(it + 1) * 128, :], LD)
                        R.wait("tensor", SE)
                        R.wait("tensor", VE)
                        for kc in range(8):
                            R.op("tensor", "matmul", pps[:, 0:512],
                                 hT(kc, s, 128),
                                 wq_sb[:, kc * 1536 + inner0: kc * 1536 + inner0 + 512],
                                 start=(kc == 0), stop=(kc == 7),
                                 inc=(TE if kc == 7 else None))
                        R.wait("vector", TE)
                        R.op("vector", "tensor_copy", qev[:], pps[:, 0:512], inc=VE)
                        R.wait("scalar", VE)
                        R.op("scalar", "activation", t1b[:], qev[:], AF.Square, inc=SE)
                        R.wait("vector", SE)
                        R.op("vector", "tensor_reduce", l2r[:],
                             t1b[:].rearrange("p (h d) -> p h d", d=64),
                             axis=AX.X, op=ALU.add, inc=VE)
                        R.wait("scalar", VE)
                        R.op("scalar", "activation", l2r[:], l2r[:], AF.Sqrt, inc=SE)
                        R.wait("vector", SE)
                        R.op("vector", "tensor_scalar", l2r[:], l2r[:], 1e-12, None,
                             ALU.max, inc=VE)
                        R.wait("vector", VE)
                        R.op("vector", "reciprocal", l2r[:], l2r[:], inc=VE)
                        R.wait("vector", VE)
                        R.wait("vector", LD)
                        for h in range(8):
                            R.op("vector", "scalar_tensor_tensor",
                                 q2s[:, h * 64:(h + 1) * 64],
                                 qev[:, h * 64:(h + 1) * 64],
                                 l2r[:, h:h + 1], sct[:, h * 64:(h + 1) * 64],
                                 ALU.mult, ALU.mult,
                                 inc=(VE if h == 7 else None))
                        R.wait("vector", VE)
                        R.op("vector", "tensor_mul", t1b[:], q2s[:], rotA[it % 2][:])
                        q23 = q2s[:].rearrange("p (h two d) -> p h two d", two=2, d=32)
                        t23 = t2b[:].rearrange("p (h two d) -> p h two d", two=2, d=32)
                        B3 = rotB[it % 2][:].rearrange("p (h two d) -> p h two d",
                                                       two=2, d=32)
                        for h in range(8):
                            R.op("vector", "tensor_tensor", t23[:, h, 0, :],
                                 q23[:, h, 1, :], B3[:, h, 0, :], ALU.mult)
                            R.op("vector", "tensor_tensor", t23[:, h, 1, :],
                                 q23[:, h, 0, :], B3[:, h, 1, :], ALU.mult,
                                 inc=(VE if h == 7 else None))
                        R.wait("vector", VE)
                        R.op("vector", "tensor_add", q3b[:], t1b[:], t2b[:], inc=VE)
                        R.wait("tensor", VE)
                        for ic in range(4):
                            R.op("tensor", "transpose", Tpack[:, ic * 128:(ic + 1) * 128],
                                 q3b[:, ic * 128:(ic + 1) * 128], ident[:],
                                 inc=(TE if ic == 3 else None))
                        R.wait("vector", TE)
                        R.op("vector", "tensor_copy",
                             dst_fT[:].rearrange("p (c n) -> p c n", n=nn)[
                                 :, :, it * 128:(it + 1) * 128],
                             Tpack[:, 0:512].rearrange("p (k n) -> p k n", n=128), inc=VE)
                        R.wait("tensor", VE)

                qk_pipe("q", 8, 640, qfT, rotq, qst, it0=3)
                qk_pipe("k", 12, 128, kfT, rotk, kst)
                for it in range(12):
                    s = 128 + it * 128
                    R.wait("tensor", VE)
                    for kc in range(8):
                        R.op("tensor", "matmul", pps[:, 0:512],
                             hT(kc, s, 128),
                             wq_sb[:, kc * 1536 + 1024: kc * 1536 + 1536],
                             start=(kc == 0), stop=(kc == 7),
                             inc=(TE if kc == 7 else None))
                    R.wait("vector", TE)
                    R.op("vector", "tensor_copy", vtm[:, it * 512:(it + 1) * 512],
                         pps[:, 0:512], inc=VE)
                    R.wait("tensor", VE)

                PT3 = PT[:].rearrange("p (k n) -> p k n", n=1024)
                for ic in range(4):
                    for hh in range(2):
                        h = 2 * ic + hh
                        rows = slice(64 * hh, 64 * hh + 64)
                        for tq in range(3, 8):
                            R.wait("tensor", SE)
                            R.wait("tensor", VE)
                            kcol = ic * 1536 + tq * 128
                            R.op("tensor", "matmul", Sps[:, 0:512],
                                 qfT[rows, ic * 1024 + tq * 128: ic * 1024 + (tq + 1) * 128],
                                 kfT[rows, kcol: kcol + 512], start=True, stop=True)
                            R.op("tensor", "matmul", Sps[:, 512:640],
                                 qfT[rows, ic * 1024 + tq * 128: ic * 1024 + (tq + 1) * 128],
                                 kfT[rows, kcol + 512: kcol + 640], start=True, stop=True,
                                 inc=TE)
                            R.wait("vector", TE)
                            R.op("vector", "tensor_tensor", Sps[:, 0:640], Sps[:, 0:640],
                                 msk_sb[:, tq * 640:(tq + 1) * 640], ALU.add)
                            R.op("vector", "reduce_max", lt.negm[:], Sps[:, 0:640],
                                 axis=AX.X, inc=VE)
                            R.wait("vector", VE)
                            R.op("vector", "tensor_scalar", lt.va[:], lt.negm[:],
                                 -QKS, None, ALU.mult, inc=VE)
                            R.wait("scalar", VE)
                            R.wait("scalar", TE)
                            R.op("scalar", "activation", P[:, 0:640], Sps[:, 0:640],
                                 AF.Exp, bias=lt.va[:], scale=QKS,
                                 accum_out=rg[:, tq:tq + 1], inc=SE)
                            R.wait("tensor", SE)
                            for kt in range(5):
                                R.op("tensor", "transpose",
                                     Tpack[:, kt * 128:(kt + 1) * 128],
                                     P[:, kt * 128:(kt + 1) * 128], ident[:],
                                     inc=(TE if kt == 4 else None))
                            R.wait("vector", TE)
                            R.op("vector", "tensor_copy",
                                 PT3[:, tq:tq + 5, tq * 128:(tq + 1) * 128],
                                 Tpack[:, 0:640].rearrange("p (k n) -> p k n", n=128),
                                 inc=VE)
                        R.wait("tensor", VE)
                        pieces = []
                        for ktG in range(12):
                            lo, hi = max(3, ktG - 4), min(7, ktG)
                            if lo > hi:
                                continue
                            c = lo * 128
                            wdt = (hi - lo + 1) * 128
                            while wdt > 0:
                                n = min(512 - (c % 512), wdt)
                                pieces.append((ktG, c, n))
                                c += n
                                wdt -= n
                        first_in_bank = {}
                        last_in_bank = {}
                        for i, (ktG, c, n) in enumerate(pieces):
                            bk = c // 512
                            first_in_bank.setdefault(bk, i)
                            last_in_bank[bk] = i
                        lasts = set(last_in_bank.values())
                        for i, (ktG, c, n) in enumerate(pieces):
                            bk = c // 512
                            R.op("tensor", "matmul", ybank[rows, c:c + n],
                                 vtm[:, ktG * 512 + h * 64: ktG * 512 + h * 64 + 64],
                                 PT[:, ktG * 1024 + c: ktG * 1024 + c + n],
                                 start=(first_in_bank[bk] == i), stop=(i in lasts),
                                 inc=(TE if i == len(pieces) - 1 else None))
                        R.wait("vector", TE)
                        R.wait("vector", SE)
                        R.op("vector", "tensor_scalar", rg[:], rg[:], 1e-30, None,
                             ALU.add, inc=VE)
                        R.wait("vector", VE)
                        R.op("vector", "reciprocal", rg[:], rg[:], inc=VE)
                        R.wait("tensor", VE)
                        for tq in range(8):
                            R.op("tensor", "transpose",
                                 smalls[0:1, (tq % 4) * 128:(tq % 4) * 128 + 128],
                                 rg[:, tq:tq + 1], ident_f[:],
                                 inc=(TE if tq % 4 == 3 else None))
                            if tq % 4 == 3:
                                R.wait("vector", TE)
                                R.op("vector", "tensor_copy",
                                     row_a[0:1, (tq // 4) * 512:(tq // 4) * 512 + 512],
                                     smalls[0:1, 0:512], inc=VE)
                                R.wait("tensor", VE)
                        for half in range(2):
                            R.op("tensor", "matmul", bcb[:, 0:512],
                                 ones_f[:], row_a[0:1, half * 512:(half + 1) * 512],
                                 start=True, stop=True, inc=TE)
                            R.wait("vector", TE)
                            R.op("vector", "tensor_copy",
                                 rb_sb[:, half * 512:(half + 1) * 512],
                                 bcb[:, 0:512], inc=VE)
                            R.wait("tensor", VE)
                        R.wait("vector", VE)
                        R.op("vector", "tensor_mul", oT[rows, ic * 1024:(ic + 1) * 1024],
                             ybank[rows, 0:1024], rb_sb[rows, :], inc=VE)
                        R.wait("tensor", VE)

                if debug and o == 3:
                    R.wait("gpsimd", VE)
                    R.dma("gpsimd", dbg_q, qfT[:], OT)
                    R.dma("gpsimd", dbg_k, kfT[:], OT)
                    R.dma("gpsimd", dbg_v, vtm[:], OT)
                    R.dma("gpsimd", dbg_o, oT[:], OT)
                    R.dma("gpsimd", dbg_zT, zT[:], OT)
                    R.wait("tensor", OT)
                for tq in range(3, 8):
                    xmb = xm[tq % 2]
                    buf = xb[tq % 2]
                    R.wait("sync", VE)
                    R.dma("sync", buf[:], xe[o, (5 + tq) * 128:(6 + tq) * 128, :], LD)
                    R.wait("vector", LD)
                    if ln_in:
                        emit_ln_stats(R, lt, buf[:], 1024, VE, SE)
                        R.op("vector", "tensor_scalar", zf[:], buf[:], lt.negm[:],
                             lt.rstd[:], ALU.add, op1=ALU.mult, inc=VE)
                        resid = zf
                    else:
                        resid = buf
                    R.wait("vector", SP)
                    for half in range(2):
                        R.wait("tensor", VE)
                        for ic in range(4):
                            R.op("tensor", "matmul", pps[:, 0:512],
                                 oT[:, ic * 1024 + tq * 128: ic * 1024 + tq * 128 + 128],
                                 wo_sb[:, ic * 1024 + half * 512: ic * 1024 + half * 512 + 512],
                                 start=(ic == 0), stop=(ic == 3),
                                 inc=(TE if ic == 3 else None))
                        R.wait("vector", TE)
                        R.op("vector", "tensor_add", xmb[:, half * 512:(half + 1) * 512],
                             pps[:, 0:512],
                             resid[:, half * 512:(half + 1) * 512],
                             inc=VE)
                    R.wait("gpsimd", VE)
                    R.dma("gpsimd",
                          xm_d.ap()[:, o * 8192 + tq * 1024: o * 8192 + (tq + 1) * 1024],
                          xmb[:], SP)
                    if tq >= 4:
                        R.wait("gpsimd", VE)
                        R.dma("gpsimd", h2t[:, 0:512], xmb[:, 0:512], LD)
                        R.dma("gpsimd", h2t[1:128, 512:1024], xmb[0:127, 512:1024], LD)
                        R.dma("gpsimd", h2t[0:1, 512:1024],
                              xm[(tq - 1) % 2][127:128, 512:1024], LD)
                        R.wait("vector", LD)
                        emit_ln_stats(R, lt, h2t[:], 1024, VE, SE)
                        R.op("vector", "tensor_scalar", h2n[:], h2t[:], lt.negm[:],
                             lt.rstd[:], ALU.add, op1=ALU.mult, inc=VE)
                        R.wait("tensor", VE)
                        for half in range(2):
                            for c4 in range(4):
                                c = half * 4 + c4
                                R.op("tensor", "transpose",
                                     Tpack[:, c4 * 128:(c4 + 1) * 128],
                                     h2n[:, c * 128:(c + 1) * 128], ident[:],
                                     inc=(TE if c4 == 3 else None))
                            R.wait("vector", TE)
                            R.wait("vector", SP)
                            for c4 in range(4):
                                c = half * 4 + c4
                                R.op("vector", "tensor_scalar",
                                     qev[:, c4 * 128:(c4 + 1) * 128],
                                     Tpack[:, c4 * 128:(c4 + 1) * 128],
                                     lng_sb[:, c:c + 1], None, ALU.mult,
                                     inc=(VE if c4 == 3 else None))
                            R.wait("gpsimd", VE)
                            h2v = h2_d.ap().rearrange("p (s c n) -> p s c n", s=4, c=8)
                            R.dma("gpsimd",
                                  h2v[:, o, half * 4:half * 4 + 4,
                                      (tq - 4) * 128:(tq - 3) * 128],
                                  qev[:, 0:512].rearrange("p (c n) -> p c n", n=128), SP)
                            R.wait("tensor", SP)
            R.flush(block)

        with ExitStack() as PH2:
            sb2 = lambda n, s, dt: PH2.enter_context(nc.sbuf_tensor(pre + n, s, dt))
            w1_sb = sb2("w1_sb", [128, 8 * 5632], BF16)
            w2_sb = sb2("w2_sb", [128, 22 * 1024], BF16)
            h2s = sb2("h2s", [128, 8 * 512], BF16)
            hT_sb = sb2("hT_sb", [128, 22 * 512], BF16)
            asbf = [sb2(f"asbf{i}", [128, 512], F32) for i in range(2)]
            gbf = [sb2(f"gbf{i}", [128, 512], BF16) for i in range(2)]
            xms = sb2("xms", [128, 4 * 1024], F32)
            xq3b = sb2("xq3b", [128, 1024], F32)
            b1p = sb2("b1p", [1, 512], F32)
            b2_sb = sb2("b2_sb", [1, D], BF16)
            bcol = sb2("bcol", [128, 44], F32)
            lnbc = sb2("lnbc", [128, 8], BF16)
            zb = sb2("zb", [128, 1024], F32)
            pps = [ps(PH2, "ppsb0", [128, 512]), ps(PH2, "ppsb1", [128, 512])]
            pp2 = [ps(PH2, "pp2b0", [128, 512]), ps(PH2, "pp2b1", [128, 512])]
            y2ps = ps(PH2, "y2ps", [128, 1024])
            brow_ps = pps[0]

            R.wait("gpsimd", TE)
            R.wait("gpsimd", VE)
            R.wait("gpsimd", SE)
            R.wait("sync", TE)
            R.wait("sync", VE)
            R.dma("gpsimd", w1_sb[:].rearrange("p (c n) -> p c n", n=5632),
                  w1r.rearrange("(c p) n -> p c n", p=128), GD)
            R.dma("gpsimd", w2_sb[:].rearrange("p (c n) -> p c n", n=1024),
                  w2r.rearrange("(c p) n -> p c n", p=128), GD)
            R.dma("gpsimd", lnbc[:], lnb_p, GD)
            R.dma("gpsimd", b2_sb[:], b2r, GD)
            R.wait("tensor", GD)
            R.wait("tensor", LD)
            R.wait("vector", LD)
            for piece in range(11):
                R.wait("sync", VE)
                R.dma("sync", b1p[:], b1r[0:1, piece * 512:(piece + 1) * 512], LD)
                for kc in range(8):
                    R.op("tensor", "matmul", brow_ps[0:1, 0:512],
                         lnbc[:, kc:kc + 1],
                         w1_sb[:, kc * 5632 + piece * 512: kc * 5632 + (piece + 1) * 512],
                         start=(kc == 0), stop=(kc == 7), inc=(TE if kc == 7 else None))
                R.wait("vector", TE)
                R.wait("vector", LD)
                R.op("vector", "tensor_add", b1p[:], b1p[:], brow_ps[0:1, 0:512], inc=VE)
                R.wait("tensor", VE)
                for c4 in range(4):
                    R.op("tensor", "transpose", smalls[:, c4:c4 + 1],
                         b1p[0:1, c4 * 128:(c4 + 1) * 128], ident_f[0:1, 0:1],
                         inc=(TE if c4 == 3 else None))
                R.wait("vector", TE)
                R.op("vector", "tensor_copy", bcol[:, piece * 4: piece * 4 + 4],
                     smalls[:, 0:4], inc=VE)
                R.wait("tensor", VE)

            for o in range(4):
                R.wait("sync", VE)
                R.dma("sync", h2s[:], h2_d.ap()[:, o * 4096:(o + 1) * 4096], LD)
                R.dma("sync", xms[:], xm_d.ap()[:, o * 8192 + 4096: o * 8192 + 8192], LD)
                R.wait("tensor", LD)
                R.wait("vector", LD)
                se_h = [0, 0]
                ve_h = [0, 0]
                for hc2 in range(22):
                    bi = hc2 % 2
                    R.wait("tensor", SE, se_h[bi])
                    R.wait("tensor", VE, ve_h[bi])
                    for kc in range(8):
                        R.op("tensor", "matmul", pps[bi][:, 0:512],
                             w1_sb[:, kc * 5632 + (2 * hc2) * 128: kc * 5632 + (2 * hc2) * 128 + 128],
                             h2s[:, kc * 512:(kc + 1) * 512],
                             start=(kc == 0), stop=(kc == 7))
                    for kc in range(8):
                        R.op("tensor", "matmul", pp2[bi][:, 0:512],
                             w1_sb[:, kc * 5632 + (2 * hc2 + 1) * 128: kc * 5632 + (2 * hc2 + 1) * 128 + 128],
                             h2s[:, kc * 512:(kc + 1) * 512],
                             start=(kc == 0), stop=(kc == 7),
                             inc=(TE if kc == 7 else None))
                    R.wait("vector", TE)
                    R.wait("scalar", TE)
                    R.op("vector", "tensor_scalar", asbf[bi][:], pps[bi][:, 0:512],
                         bcol[:, 2 * hc2: 2 * hc2 + 1], None, ALU.add)
                    R.op("scalar", "activation", gbf[bi][:], pp2[bi][:, 0:512], AF.Gelu,
                         bias=bcol[:, 2 * hc2 + 1: 2 * hc2 + 2], inc=SE)
                    R.wait("vector", SE)
                    R.op("vector", "tensor_tensor", hT_sb[:, hc2 * 512:(hc2 + 1) * 512],
                         asbf[bi][:], gbf[bi][:], ALU.mult, inc=VE)
                    se_h[bi] = SE.n
                    ve_h[bi] = VE.n
                R.wait("tensor", VE)
                for ts in range(4):
                    tt = o * 4 + ts
                    if sandwich:
                        R.wait("sync", VE)
                        R.dma("sync", xq3b[:], xq3[tt * 128:(tt + 1) * 128, :], LD)
                    for half in range(2):
                        for hc2 in range(22):
                            R.op("tensor", "matmul", y2ps[:, half * 512:(half + 1) * 512],
                                 hT_sb[:, hc2 * 512 + ts * 128: hc2 * 512 + ts * 128 + 128],
                                 w2_sb[:, hc2 * 1024 + half * 512: hc2 * 1024 + half * 512 + 512],
                                 start=(hc2 == 0), stop=False)
                        R.op("tensor", "matmul", y2ps[:, half * 512:(half + 1) * 512],
                             ones_bf[:], b2_sb[0:1, half * 512:(half + 1) * 512],
                             start=False, stop=True, inc=TE)
                    R.wait("vector", TE)
                    R.wait("vector", OT)
                    R.op("vector", "tensor_add", zb[:, 0:512], y2ps[:, 0:512],
                         xms[:, ts * 1024: ts * 1024 + 512])
                    R.op("vector", "tensor_add", zb[:, 512:1024], y2ps[:, 512:1024],
                         xms[:, ts * 1024 + 512: ts * 1024 + 1024], inc=VE)
                    R.wait("tensor", VE)
                    if sandwich:
                        R.wait("vector", LD)
                        emit_ln_stats(R, lt, zb[:], 1024, VE, SE)
                        R.op("vector", "tensor_scalar", zb[:], zb[:], lt.negm[:],
                             lt.rstd[:], ALU.add, op1=ALU.mult, inc=VE)
                        R.wait("vector", VE)
                        R.op("vector", "tensor_add", zb[:], zb[:], xq3b[:], inc=VE)
                    R.wait("gpsimd", VE)
                    R.dma("gpsimd", zout[tt * 128:(tt + 1) * 128, :], zb[:], OT)
            if debug:
                R.dma("gpsimd", dbg_xm, xm_d.ap(), OT)
                R.dma("gpsimd", dbg_h2, h2_d.ap(), OT)
            R.wait("gpsimd", OT)
            R.wait("gpsimd", SP)
            R.flush(block)
    return nc


def _rot_tables(abs_pos, xpos_sign):
    DH = 64
    half = np.arange(0, DH, 2, dtype=np.float64)
    inv = 1.0 / 10000.0 ** (half / DH)
    t = abs_pos.astype(np.float64)
    fr = t[:, None] * inv[None, :]
    freqs = np.concatenate([fr, fr], axis=1)
    base = (half + 0.4 * DH) / (1.4 * DH)
    power = (t - 4096 // 2) / 256.0
    sc = base[None, :] ** power[:, None]
    sc = np.concatenate([sc, sc], axis=1)
    if xpos_sign < 0:
        sc = 1.0 / sc
    A = np.cos(freqs) * sc
    Bs = np.sin(freqs) * sc
    Bs[:, 0:32] *= -1.0
    return A.astype(np.float32), Bs.astype(np.float32)


def stage_D(xin, x3, inputs, layer):
    wqkv = np.ascontiguousarray(inputs["L1_ffn_wqkv"][layer], np.float32)
    wo_ = np.ascontiguousarray(inputs["L1_ffn_wo"][layer], np.float32)
    w1 = np.asarray(inputs["L1_ffn_w1"][layer], np.float32)
    b1 = np.asarray(inputs["L1_ffn_b1"][layer], np.float32)
    w2 = np.asarray(inputs["L1_ffn_w2"][layer], np.float32)
    w1r = np.zeros((D, 2 * 2816), np.float32)
    b1r = np.zeros((1, 2 * 2816), np.float32)
    a_, g_ = w1[:, :2730], w1[:, 2730:]
    ba_, bg_ = b1[:2730], b1[2730:]
    for j in range(22):
        lo, hi = j * 128, min((j + 1) * 128, 2730)
        w1r[:, 2 * j * 128: 2 * j * 128 + (hi - lo)] = a_[:, lo:hi]
        w1r[:, (2 * j + 1) * 128: (2 * j + 1) * 128 + (hi - lo)] = g_[:, lo:hi]
        b1r[0, 2 * j * 128: 2 * j * 128 + (hi - lo)] = ba_[lo:hi]
        b1r[0, (2 * j + 1) * 128: (2 * j + 1) * 128 + (hi - lo)] = bg_[lo:hi]
    w2r = np.zeros((2816, D), np.float32)
    w2r[:2730] = w2
    b2r = np.ascontiguousarray(inputs["L1_ffn_b2"][layer], np.float32).reshape(1, D)
    lnb = np.asarray(inputs["L1_ffn_ln_b"][layer], np.float32)
    lng = np.asarray(inputs["L1_ffn_ln_g"][layer], np.float32)
    lnb_p = np.ascontiguousarray(lnb.reshape(8, 128).T)
    lng_p = np.ascontiguousarray(lng.reshape(8, 128).T)
    qs_r = np.ascontiguousarray(inputs["L1_ffn_q_scale"][layer], np.float32).reshape(1, 64)
    ks_r = np.ascontiguousarray(inputs["L1_ffn_k_scale"][layer], np.float32).reshape(1, 64)
    ident = np.eye(128, dtype=np.float32)
    maps = []
    for c in range(NCORES):
        b, par = c // 2, c % 2
        xe = np.zeros((4, 1664, D), np.float32)
        rq = np.zeros((4, 2, 1024, 512), np.float32)
        rk = np.zeros((4, 2, 1536, 512), np.float32)
        mk = np.full((4, 8, 128, 640), NEG, np.float32)
        for o, A in enumerate(OWN[par]):
            c0 = A * 512
            lo = c0 - 1152
            src_lo, dst_lo = max(0, lo), max(0, -lo)
            xe[o, dst_lo:] = xin[b][src_lo: lo + 1664]
            qpos = np.clip(np.arange(c0 - 512, c0 + 512), 0, None)
            kpos = np.clip(np.arange(c0 - 1024, c0 + 512), 0, None)
            for si, (sign, pos, dst) in enumerate(
                    (((1, qpos, rq)), ((-1, kpos, rk)))):
                Aq, Bq = _rot_tables(pos, sign)
                dst[o, 0] = np.tile(Aq, (1, 8))
                dst[o, 1] = np.tile(Bq, (1, 8))
            for tq in range(8):
                for r in range(128):
                    aq = c0 - 512 + tq * 128 + r
                    ks = c0 - 1024 + tq * 128
                    cc = np.arange(ks, ks + 640)
                    ok = (cc >= 0) & (cc <= aq) & (cc >= aq - 512) & (aq >= 0)
                    mk[o, tq, r, ok] = 0.0
        maps.append({
            "xe": xe, "xq3": np.ascontiguousarray(x3[b][own_rows(par)]),
            "wqkv": wqkv, "wo": wo_, "w1r": w1r, "w2r": w2r, "b1r": b1r,
            "b2r": b2r, "lnb_p": lnb_p, "lng_p": lng_p, "qs_r": qs_r,
            "ks_r": ks_r, "rotq": rq, "rotk": rk, "msk": mk, "ident_in": ident,
        })
    return maps


def run_D(x3, inputs):
    if "D1" not in _NC_CACHE:
        _NC_CACHE["D1"] = build_launch_D(True, False)
        _NC_CACHE["D2"] = build_launch_D(False, True)
    res = run_bass_kernel_spmd(_NC_CACHE["D1"], stage_D(x3, x3, inputs, 0),
                               core_ids=list(range(NCORES)))
    z1 = gather_own(res.results, "zout")
    res = run_bass_kernel_spmd(_NC_CACHE["D2"], stage_D(z1, x3, inputs, 1),
                               core_ids=list(range(NCORES)))
    return gather_own(res.results, "zout")




GROUPS = [[0, 1], [2, 3], [4, 5], [6, 7]]


def emit_barrier(R, sems):
    for eng in Rec.ENGINES:
        for s in sems.values():
            R.wait(eng, s)


def build_fused():
    nc = bass.Bass("TRN2", target_bir_lowering=False)
    R = Rec(nc)
    io = {}

    def inp(name, shape, dt=F32):
        io[name] = nc.dram_tensor(name, shape, dt, kind="ExternalInput").ap()

    inp("ident_in", [128, 128])
    inp("a_xfull", [T, D]); inp("a_xq", [OWNT, D])
    for w in ("a_wq", "a_wk", "a_wv", "a_wo"):
        inp(w, [D, D])
    inp("a_nullk_p", [128, 8]); inp("a_nullv_r", [1, D]); inp("a_lsc_r", [1, 16])
    inp("a_mtail", [16, 5, 128, 128])
    inp("b_wgT", [D, 3072], I32); inp("b_wuT", [D, 3072], I32)
    inp("b_wdT", [3072, D], I32)
    inp("c_wqkv", [D, 3072]); inp("c_wf2", [128, 128]); inp("c_bf_p", [128, 1])
    inp("c_bf_r", [1, 128]); inp("c_wout", [D, D]); inp("c_bout_r", [1, D])
    for li in range(2):
        p = f"d{li}_"
        inp(p + "wqkv", [D, 1536]); inp(p + "wo", [512, D])
        inp(p + "w1r", [D, 5632]); inp(p + "w2r", [2816, D])
        inp(p + "b1r", [1, 5632]); inp(p + "b2r", [1, D])
        inp(p + "lnb_p", [128, 8]); inp(p + "lng_p", [128, 8])
        inp(p + "qs_r", [1, 64]); inp(p + "ks_r", [1, 64])
    inp("d_rotq", [4, 2, 1024, 512]); inp("d_rotk", [4, 2, 1536, 512])
    inp("d_msk", [4, 8, 128, 640])
    inp("dmeta", [1, 128], I32)
    out = nc.dram_tensor("out", [OWNT, D], F32, kind="ExternalOutput").ap()

    x1_d = nc.dram_tensor("x1_d", [OWNT, D], F32)
    x2_d = nc.dram_tensor("x2_d", [OWNT, D], F32)
    x3_d = nc.dram_tensor("x3_d", [OWNT, D], F32)
    z1_d = nc.dram_tensor("z1_d", [OWNT, D], F32)
    x2g = nc.dram_tensor("x2g", [T, D], F32)
    xg_scr = nc.dram_tensor("xg_scr", [T, D], F32)
    xabs = nc.dram_tensor("xabs", [1152 + T, D], F32)
    xe_d = nc.dram_tensor("xe_d", [4, 1664, D], F32)

    with ExitStack() as CTX:
        sems = {n: Sem(CTX.enter_context(nc.semaphore(n)))
                for n in ("LD", "GD", "SP", "TE", "VE", "SE", "OT", "CC")}
        meta_sb = CTX.enter_context(nc.sbuf_tensor("meta_sb", [1, 128], I32))
        zero_sb = CTX.enter_context(nc.sbuf_tensor("zero_sb", [128, 1024], F32))
        dynr = CTX.enter_context(nc.gpsimd.register("dynr"))
        block = CTX.enter_context(nc.Block())

        fcb = {"nc": nc, "R": R, "io": io, "sems": sems, "block": block,
               "x1_d": x1_d, "x2_d": x2_d, "x3_d": x3_d, "z1_d": z1_d,
               "x2g": x2g, "out": out, "xe_d": xe_d}

        LD, GD, OT, CC, VE = (sems[k] for k in ("LD", "GD", "OT", "CC", "VE"))
        R.dma("sync", meta_sb[:], io["dmeta"], LD)
        R.op("vector", "memset", zero_sb[:], 0.0, inc=VE)
        R.wait("gpsimd", VE)
        for zi in range(9):
            R.dma("gpsimd", xabs.ap()[zi * 128:(zi + 1) * 128, :], zero_sb[:], GD)

        def unscramble_and_slot(src_d):
            R.op("gpsimd", "collective_compute", "AllGather", ALU.bypass,
                 replica_groups=GROUPS, ins=[src_d.ap().opt()],
                 outs=[xg_scr.ap().opt()], inc=sems["CC"])
            R.wait("gpsimd", sems["CC"])
            for c in range(8):
                g = gathered_pos(c)
                R.dma("gpsimd", xabs.ap()[1152 + c * 512: 1152 + (c + 1) * 512, :],
                      xg_scr.ap()[g * 512:(g + 1) * 512, :], GD)
            R.wait("gpsimd", GD)
            R.wait("gpsimd", LD)
            for o in range(4):
                R.op("gpsimd", "reg_load", dynr, meta_sb[0:1, o:o + 1])
                R.dma("gpsimd", xe_d.ap()[o],
                      bass.AP(xabs, dynr, [[1024, 1664], [1, 1024]]), GD)

        build_launch_A(fc=fcb)
        emit_barrier(R, sems)
        build_launch_B(fc=fcb)
        emit_barrier(R, sems)
        R.op("gpsimd", "collective_compute", "AllGather", ALU.bypass,
             replica_groups=GROUPS, ins=[x2_d.ap().opt()], outs=[x2g.ap().opt()],
             inc=sems["CC"])
        emit_barrier(R, sems)
        build_launch_C(fc=fcb)
        emit_barrier(R, sems)
        unscramble_and_slot(x3_d)
        emit_barrier(R, sems)
        build_launch_D(True, False, fc=fcb, li=0)
        emit_barrier(R, sems)
        unscramble_and_slot(z1_d)
        emit_barrier(R, sems)
        build_launch_D(False, True, fc=fcb, li=1)
        emit_barrier(R, sems)
        R.flush(block)
    return nc


def gathered_pos(c):
    if c in OWN[0]:
        return OWN[0].index(c)
    return 4 + OWN[1].index(c)


def make_dmeta(par):
    m = np.zeros((1, 128), np.int32)
    for o, A in enumerate(OWN[par]):
        m[0, o] = (A * 512) * D
    return m


def stage_fused(inputs):
    zero4 = np.zeros((4, T, D), np.float32)
    zero2 = np.zeros((OWNT, D), np.float32)
    aM = stage_A(inputs)
    bM = stage_B(zero4, inputs)
    cM = stage_C(zero4, inputs)
    d0M = stage_D(zero4, zero4, inputs, 0)
    d1M = stage_D(zero4, zero4, inputs, 1)
    maps = []
    for c in range(NCORES):
        par = c % 2
        m = {"ident_in": aM[c]["ident_in"], "dmeta": make_dmeta(par)}
        for k, v in aM[c].items():
            if k != "ident_in":
                m["a_" + k] = v
        for k in ("wgT", "wuT", "wdT"):
            m["b_" + k] = bM[c][k]
        for k in ("wqkv", "wf2", "bf_p", "bf_r", "wout", "bout_r"):
            m["c_" + k] = cM[c][k]
        for li, dM in ((0, d0M), (1, d1M)):
            for k in ("wqkv", "wo", "w1r", "w2r", "b1r", "b2r", "lnb_p",
                      "lng_p", "qs_r", "ks_r"):
                m[f"d{li}_" + k] = dM[c][k]
        m["d_rotq"] = d0M[c]["rotq"]
        m["d_rotk"] = d0M[c]["rotk"]
        m["d_msk"] = d0M[c]["msk"]
        maps.append(m)
    return maps


def kernel(**inputs):
    if "FUSED" not in _NC_CACHE:
        _NC_CACHE["FUSED"] = build_fused()
    res = run_bass_kernel_spmd(_NC_CACHE["FUSED"], stage_fused(inputs),
                               core_ids=list(range(NCORES)))
    return gather_own(res.results, "out")
